# version 04
# baseline (speedup 1.0000x reference)
import numpy as np
from contextlib import ExitStack
from dataclasses import dataclass

import concourse.bass as bass
import concourse.mybir as mybir
from concourse.bass_utils import run_bass_kernel_spmd
from concourse.dve_spec import Spec, Src0, Src1, C0, C1, C2, Zero, One, maxx, minn, AluOp
from concourse.dve_spec import lower as dve_lower
from concourse.dve_ops import DveOp, OPS
import concourse.dve_ops as dve_ops_mod

F32 = mybir.dt.float32
I16 = mybir.dt.int16
I32 = mybir.dt.int32
AF = mybir.ActivationFunctionType
ALU = mybir.AluOpType
AX = mybir.AxisListType

MAGIC = float(np.float32(1.5 * 2.0 ** 23))


def _register_op(name, spec, reference):
    for op in OPS:
        if op.name == name:
            return op
    shas, perf_en = {}, {}
    for ver in ("v3", "v4"):
        try:
            shas[ver] = dve_lower(spec, ver=ver, perf_en=False, name=name).sha(ver)
            perf_en[ver] = False
        except Exception:
            pass
    op = DveOp(name, spec, subdim=False, uops_sha=shas, perf_en=perf_en,
               reference=reference)
    OPS.append(op)
    dve_ops_mod._SUB_OPCODE_FOR_NAME[name] = (
        dve_ops_mod._CUSTOM_DVE_ROW_BASE + len(OPS) - 1)
    dve_ops_mod.CUSTOM_DVE_REFS[name] = op.ref
    dve_ops_mod.CUSTOM_DVE_SPECS[name] = spec
    return op


RCS = _register_op(
    "MK_RCS",
    Spec(body=minn(maxx((Src0 * C0 + C2) - C2, Zero - (C1 + One)), C1)),
    lambda in0, in1, s0, s1, imm2: np.clip(np.rint(in0 * s0), -s1 - 1.0, s1),
)
SUBMUL = _register_op(
    "MK_SUBMUL",
    Spec(body=(Src0 - C0) * Src1, accum=AluOp.MAX),
    lambda in0, in1, s0, s1, imm2: (
        (in0 - s0) * in1,
        np.max((in0 - s0) * in1, axis=-1, keepdims=True),
    ),
)


@dataclass(frozen=True)
class Cfg:
    D: int = 1024
    H: int = 3072
    NH: int = 16
    T: int = 2048
    n_cores: int = 8
    eps: float = 1e-5
    stages: int = 3

    P = 128
    HDIM = 64

    @property
    def DC(self):
        return self.D // 128

    @property
    def HC(self):
        return self.H // 128

    @property
    def NTT(self):
        return self.T // 128

    @property
    def S(self):
        return 2 * self.T

    @property
    def NKT(self):
        return self.S // 128

    @property
    def TN(self):
        return min(512, self.T)

    @property
    def NTC(self):
        return self.T // self.TN

    @property
    def NHG(self):
        return self.NH // 2

    @property
    def NW(self):
        return min(512, self.D)

    @property
    def ND2(self):
        return self.D // self.NW


FULL = Cfg()
MINI = Cfg(D=256, H=768, NH=4, T=256)
MIDI = Cfg(D=1024, H=1536, NH=16, T=1024)


class Sem:
    def __init__(self, es, nc, name):
        self.h = es.enter_context(nc.semaphore(name))
        self.v = 0

    def inc(self, ins, n=1):
        ins.then_inc(self.h, n)
        self.v += n
        return self.v


class Builder:
    def __init__(self, cfg: Cfg, dbg=None):
        self.dbg = dbg
        self.ffn_i = 0
        self.dbg_done = False
        self.c = cfg
        nc = bass.Bass("TRN2", target_bir_lowering=False, debug=False)
        self.nc = nc
        self.uid = 0
        c, P = cfg, cfg.P
        self.x0_d = nc.dram_tensor("x0", [c.T, c.D], F32, kind="ExternalInput").ap()
        self.wqk_d = nc.dram_tensor("wqk", [2 * c.DC, P, c.DC, P], F32, kind="ExternalInput").ap()
        self.wv_d = nc.dram_tensor("wv", [c.ND2, c.DC, P, c.NW], F32, kind="ExternalInput").ap()
        self.bqk_d = nc.dram_tensor("bqk", [2 * c.DC, P], F32, kind="ExternalInput").ap()
        self.bv_d = nc.dram_tensor("bv", [c.D], F32, kind="ExternalInput").ap()
        self.wp_d = nc.dram_tensor("wp", [c.D, c.D], F32, kind="ExternalInput").ap()
        self.bp_d = nc.dram_tensor("bp", [c.D], F32, kind="ExternalInput").ap()
        self.wu_d = nc.dram_tensor("wu", [c.HC, P, c.DC, P], F32, kind="ExternalInput").ap()
        self.wg_d = nc.dram_tensor("wg", [c.HC, P, c.DC, P], F32, kind="ExternalInput").ap()
        self.wd_d = nc.dram_tensor("wd", [c.H, c.D], F32, kind="ExternalInput").ap()
        self.ci_d = nc.dram_tensor("coreinfo", [2], I32, kind="ExternalInput").ap()
        self.y_d = nc.dram_tensor("y", [c.T, c.D], F32, kind="ExternalOutput").ap()
        self.wuq_d = nc.dram_tensor("wuq", [c.HC, P, c.DC, P], F32).ap()
        self.wgq_d = nc.dram_tensor("wgq", [c.HC, P, c.DC, P], F32).ap()
        self.wdq_d = nc.dram_tensor("wdq", [c.H, c.D], F32).ap()
        self.mup_d = nc.dram_tensor("mup", [c.HC, P, c.T], F32).ap()
        self.mgate_d = nc.dram_tensor("mgate", [c.HC, P, c.T], F32).ap()
        self.kup_d = nc.dram_tensor("kup", [c.NTC, c.HC, P, c.TN], I16).ap()
        self.kgate_d = nc.dram_tensor("kgate", [c.NTC, c.HC, P, c.TN], I16).ap()
        self.kg2_d = nc.dram_tensor("kg2", [c.NTC, c.HC, P, c.TN], I16).ap()
        self.qT_d = nc.dram_tensor("qTd", [c.DC, P, c.T], F32).ap()
        self.kloc_d = nc.dram_tensor("kloc", [c.DC, P, c.T], F32).ap()
        self.vloc_d = nc.dram_tensor("vloc", [c.NTT, P, c.D], F32).ap()
        self.kall_d = nc.dram_tensor("kall", [c.n_cores, c.DC, P, c.T], F32, addr_space="Shared").ap()
        self.vall_d = nc.dram_tensor("vall", [c.n_cores, c.NTT, P, c.D], F32, addr_space="Shared").ap()
        self.kpair_d = nc.dram_tensor("kpair", [2, c.DC, P, c.T], F32).ap()
        self.vpair_d = nc.dram_tensor("vpair", [2, c.NTT, P, c.D], F32).ap()
        self.lraw_d = nc.dram_tensor("lraw", [2 * 512], F32).ap()
        self.lrec_d = nc.dram_tensor("lrec", [2 * 512], F32).ap()
        self.ar_in = [nc.dram_tensor(f"arin{i}", [256], F32).ap() for i in range(8)]
        self.ar_out = [nc.dram_tensor(f"arout{i}", [256], F32).ap() for i in range(8)]
        self.n_ar = 0

        self.es = ExitStack()
        self.es.__enter__()
        es = self.es
        sb = lambda n, sh, dt=F32: es.enter_context(nc.sbuf_tensor(n, sh, dt))
        self.xres = sb("xres", [P, c.NTT * c.D])
        self.stat = sb("stat", [P, 8 * c.NTT])
        self.slots = sb("slots", [P, 256])
        self.scal = sb("scal", [P, 32])
        self.arb = sb("arb", [P, 256])
        self.ident = sb("ident", [P, P])
        self.bvb = sb("bvb", [P, c.D])
        self.bpb = sb("bpb", [P, c.D])
        self.bqks = sb("bqks", [P, 2 * c.DC])
        self.cis = sb("cis", [1, 2], I32)
        self.psA = es.enter_context(nc.psum_tensor("psA", [P, 2048], F32))
        self.psB = es.enter_context(nc.psum_tensor("psB", [P, 2048], F32))
        self.sy = Sem(es, nc, "sy")
        self.gd = Sem(es, nc, "gd")
        self.cc = Sem(es, nc, "cc")
        self.tm = Sem(es, nc, "tm")
        self.vv = Sem(es, nc, "vv")
        self.ss = Sem(es, nc, "ss")
        self.gg = Sem(es, nc, "gg")
        self.wsl = [Sem(es, nc, "wsl0"), Sem(es, nc, "wsl1")]
        self.isl = [Sem(es, nc, "isl0"), Sem(es, nc, "isl1")]
        self.osl = [Sem(es, nc, "osl0"), Sem(es, nc, "osl1")]

    def sc(self, i):
        return self.scal[:, i:i + 1]

    def name(self, stem):
        self.uid += 1
        return f"{stem}{self.uid}"

    def pslot(self, i, width):
        t = self.psA if i % 2 == 0 else self.psB
        off = ((i // 2) % 2) * 1024
        return t[:, off:off + width]

    def vstep(self, v, ins):
        ins.then_inc(self.vv.h, 1)
        self.vv.v += 1
        v.wait_ge(self.vv.h, self.vv.v)

    def finish(self):
        self.es.__exit__(None, None, None)
        return self.nc

    def emit(self):
        c, nc = self.c, self.nc
        with nc.Block() as blk:
            @blk.sync
            def _(sy):
                x0v = self.x0_d.rearrange("(t p) d -> t p d", p=c.P)
                for t in range(c.NTT):
                    self.sy.inc(sy.dma_start(out=self.xres[:, t * c.D:(t + 1) * c.D], in_=x0v[t]), 16)
                self.sy.inc(sy.dma_start(out=self.bvb[:], in_=self.bv_d.unsqueeze(0).to_broadcast([c.P, c.D])), 16)
                self.sy.inc(sy.dma_start(out=self.bpb[:], in_=self.bp_d.unsqueeze(0).to_broadcast([c.P, c.D])), 16)
                idd = nc.inline_tensor(np.eye(c.P, dtype=np.float32), "identnp").ap()
                self.sy.inc(sy.dma_start(out=self.ident[:], in_=idd), 16)
                for m in range(2 * c.DC):
                    self.sy.inc(sy.dma_start(out=self.bqks[:, m:m + 1], in_=self.bqk_d[m].unsqueeze(1)), 16)
                self.sy.inc(sy.dma_start(out=self.cis[:], in_=self.ci_d.unsqueeze(0)), 16)
                sy.wait_ge(self.sy.h, self.sy.v)

        self.emit_dequant()
        self.emit_ffn()
        if c.stages >= 2 and not self.dbg_done:
            self.emit_attention()
        if c.stages >= 3 and not self.dbg_done:
            self.emit_ffn()

        if not self.dbg_done:
            with nc.Block() as blk:
                @blk.sync
                def _(sy):
                    yv = self.y_d.rearrange("(t p) d -> t p d", p=c.P)
                    for t in range(c.NTT):
                        self.sy.inc(sy.dma_start(out=yv[t], in_=self.xres[:, t * c.D:(t + 1) * c.D]), 16)
                    sy.wait_ge(self.sy.h, self.sy.v)

    def emit_dequant(self):
        c, s, nc = self.c, self, self.nc
        P = c.P
        W = 512 if c.D >= 512 else c.D
        es2 = ExitStack()
        wa = es2.enter_context(nc.sbuf_tensor(self.name("dqa"), [P, 2 * W], F32))
        wt = es2.enter_context(nc.sbuf_tensor(self.name("dqt"), [P, W], F32))
        bm = es2.enter_context(nc.sbuf_tensor(self.name("dqm"), [P, W // 2], F32))
        ki = es2.enter_context(nc.sbuf_tensor(self.name("dqi"), [P, W], I16))
        nb = W // 32

        tiles = []
        for wsrc, wdst in [(s.wu_d, s.wuq_d), (s.wg_d, s.wgq_d)]:
            for m in range(c.HC):
                for kc in range(c.DC):
                    v4 = wsrc[m, :, kc, :]
                    d4 = wdst[m, :, kc, :]
                    for j in range(0, P, W):
                        w_ = min(W, P - j)
                        tiles.append((v4[:, j:j + w_], d4[:, j:j + w_], w_))
        wdv_s = s.wd_d.rearrange("(hc p) d -> hc p d", p=P)
        wdv_q = s.wdq_d.rearrange("(hc p) d -> hc p d", p=P)
        for hc in range(c.HC):
            for j in range(0, c.D, W):
                w_ = min(W, c.D - j)
                tiles.append((wdv_s[hc, :, j:j + w_], wdv_q[hc, :, j:j + w_], w_))

        with nc.Block() as blk:
            n = len(tiles)
            VPT = 7
            ld_evt = []
            st_evt = []
            vvb = s.vv.v

            @blk.sync
            def _(sy):
                for i in range(n + 1):
                    if i < n:
                        src_ap, dst_ap, w_ = tiles[i]
                        if i >= 2:
                            sy.wait_ge(s.vv.h, vvb + (i - 1) * VPT)
                            sy.wait_ge(s.osl[i % 2].h, st_evt[i - 2])
                        il = s.isl[i % 2]
                        il.inc(sy.dma_start(out=wa[:, (i % 2) * W:(i % 2) * W + w_], in_=src_ap), 16)
                        ld_evt.append(il.v)
                    if i >= 1:
                        j = i - 1
                        sy.wait_ge(s.vv.h, vvb + (j + 1) * VPT)
                        ol = s.osl[j % 2]
                        ol.inc(sy.dma_start(out=tiles[j][1],
                                            in_=wa[:, (j % 2) * W:(j % 2) * W + tiles[j][2]]), 16)
                        st_evt.append(ol.v)
                sy.wait_ge(s.osl[0].h, s.osl[0].v)
                sy.wait_ge(s.osl[1].h, s.osl[1].v)

            @blk.vector
            def _(v):
                for i, (src_ap, dst_ap, w_) in enumerate(tiles):
                    nb_ = w_ // 32
                    o = (i % 2) * W
                    v.wait_ge(s.isl[i % 2].h, ld_evt[i])
                    self.vstep(v, v.transpose(out=wt[:, 0:w_], in_=wa[:, o:o + w_]))
                    self.vstep(v, v.reduce_max(
                        out=bm[:, 0:nb_],
                        in_=wt[:, 0:w_].rearrange("p (g e) -> p g e", e=32),
                        axis=AX.X, apply_absolute_value=True))
                    self.vstep(v, v.tensor_scalar(out=bm[:, 0:nb_], in0=bm[:, 0:nb_],
                                                  scalar1=1.0 / 7.0, scalar2=1e-12,
                                                  op0=ALU.mult, op1=ALU.max))
                    self.vstep(v, v.reciprocal(out=bm[:, nb:nb + nb_], in_=bm[:, 0:nb_]))
                    self.vstep(v, v.tensor_tensor(
                        out=ki[:, 0:w_].rearrange("p (g e) -> p g e", e=32),
                        in0=wt[:, 0:w_].rearrange("p (g e) -> p g e", e=32),
                        in1=bm[:, nb:nb + nb_].rearrange("p (g o) -> p g o", o=1).to_broadcast([P, nb_, 32]),
                        op=ALU.mult))
                    self.vstep(v, v.tensor_tensor(
                        out=wt[:, 0:w_].rearrange("p (g e) -> p g e", e=32),
                        in0=ki[:, 0:w_].rearrange("p (g e) -> p g e", e=32),
                        in1=bm[:, 0:nb_].rearrange("p (g o) -> p g o", o=1).to_broadcast([P, nb_, 32]),
                        op=ALU.mult))
                    self.vstep(v, v.transpose(out=wa[:, o:o + w_], in_=wt[:, 0:w_]))
        es2.close()

    def emit_ln_stats(self, lnsc):
        c, s = self.c, self
        N = c.NTT
        nc = self.nc
        with nc.Block() as blk:
            @blk.vector
            def _(v):
                for t in range(N):
                    xt = s.xres[:, t * c.D:(t + 1) * c.D]
                    ins = v.reduce_sum(out=s.stat[:, t:t + 1], in_=xt, axis=AX.X)
                s.vv.inc(ins)

            @blk.scalar
            def _(sc):
                ssb = s.ss.v
                for t in range(N):
                    xt = s.xres[:, t * c.D:(t + 1) * c.D]
                    if t >= 1:
                        sc.wait_ge(s.ss.h, ssb + t)
                    sc.activation(out=lnsc[:, 0:c.D], in_=xt, func=AF.Square,
                                  accum_out=s.stat[:, N + t:N + t + 1]).then_inc(s.ss.h, 1)
                    s.ss.v += 1

        with nc.Block() as blk:
            @blk.vector
            def _(v):
                st = s.stat
                self.vstep(v, v.tensor_scalar(out=st[:, 2 * N:3 * N], in0=st[:, 0:N],
                                              scalar1=1.0 / c.D, scalar2=None, op0=ALU.mult))
                self.vstep(v, v.scalar_tensor_tensor(out=st[:, 3 * N:4 * N], in0=st[:, 2 * N:3 * N],
                                                     scalar=-1.0, in1=st[:, 2 * N:3 * N],
                                                     op0=ALU.mult, op1=ALU.mult))
                self.vstep(v, v.scalar_tensor_tensor(out=st[:, 3 * N:4 * N], in0=st[:, N:2 * N],
                                                     scalar=1.0 / c.D, in1=st[:, 3 * N:4 * N],
                                                     op0=ALU.mult, op1=ALU.add))
                self.vstep(v, v.tensor_scalar(out=st[:, 3 * N:4 * N], in0=st[:, 3 * N:4 * N],
                                              scalar1=c.eps, scalar2=None, op0=ALU.add))
                s.vv.inc(v.engine_nop())

            @blk.scalar
            def _(sc):
                sc.wait_ge(s.vv.h, s.vv.v)
                st = s.stat
                s.ss.inc(sc.activation(out=st[:, 4 * N:5 * N], in_=st[:, 3 * N:4 * N],
                                       func=AF.Sqrt))

            @blk.vector
            def _(v):
                v.wait_ge(s.ss.h, s.ss.v)
                st = s.stat
                self.vstep(v, v.reciprocal(out=st[:, 5 * N:6 * N], in_=st[:, 4 * N:5 * N]))
                self.vstep(v, v.tensor_mul(out=st[:, 6 * N:7 * N], in0=st[:, 5 * N:6 * N],
                                           in1=st[:, 5 * N:6 * N]))
                self.vstep(v, v.tensor_mul(out=st[:, 6 * N:7 * N], in0=st[:, 6 * N:7 * N],
                                           in1=st[:, 3 * N:4 * N]))
                self.vstep(v, v.tensor_scalar(out=st[:, 6 * N:7 * N], in0=st[:, 6 * N:7 * N],
                                              scalar1=-0.5, scalar2=1.5, op0=ALU.mult, op1=ALU.add))
                self.vstep(v, v.tensor_mul(out=st[:, 4 * N:5 * N], in0=st[:, 5 * N:6 * N],
                                           in1=st[:, 6 * N:7 * N]))
                s.vv.inc(v.engine_nop())

    def emit_armax(self, ranges, outs):
        c, s, nc = self.c, self, self.nc
        idx = self.n_ar % 8
        self.n_ar += 1
        k = len(ranges)
        with nc.Block() as blk:
            @blk.vector
            def _(v):
                for i, (st0, n) in enumerate(ranges):
                    ins = v.reduce_max(out=s.scal[:, 30 - i:31 - i],
                                       in_=s.slots[:, st0:st0 + n], axis=AX.X)
                s.vv.inc(ins)

            @blk.gpsimd
            def _(g):
                g.wait_ge(s.vv.h, s.vv.v)
                for i in range(k):
                    s.gd.inc(g.dma_start(out=s.ar_in[idx][i * 128:(i + 1) * 128],
                                         in_=s.scal[:, 30 - i:31 - i]), 16)
                g.wait_ge(s.gd.h, s.gd.v)
                s.cc.inc(g.collective_compute(
                    "AllReduce", ALU.max, replica_groups=[list(range(c.n_cores))],
                    ins=[s.ar_in[idx][0:k * 128].opt()],
                    outs=[s.ar_out[idx][0:k * 128].opt()]), 1)
                g.wait_ge(s.cc.h, s.cc.v)
                for i in range(k):
                    s.gd.inc(g.dma_start(
                        out=s.arb[:, i * 128:(i + 1) * 128],
                        in_=s.ar_out[idx][i * 128:(i + 1) * 128].unsqueeze(0).to_broadcast([c.P, 128])), 16)

            gdv = s.gd.v

            @blk.vector
            def _(v):
                v.wait_ge(s.gd.h, gdv)
                for i in range(k):
                    ins = v.reduce_max(out=outs[i], in_=s.arb[:, i * 128:(i + 1) * 128], axis=AX.X)
                s.vv.inc(ins)

    def dump(self, ap, nelem):
        nc, s = self.nc, self
        yflat = self.y_d.rearrange("a b -> (a b)")
        with nc.Block() as blk:
            @blk.sync
            def _(sy):
                s.sy.inc(sy.dma_start(out=yflat[0:nelem], in_=ap), 16)
                sy.wait_ge(s.sy.h, s.sy.v)
        self.dbg_done = True

    def dbg_hit(self, tag):
        return (not self.dbg_done) and self.dbg is not None and \
            self.dbg[0] == tag and self.dbg[1] == self.ffn_i

    def emit_ffn(self):
        c, s, nc = self.c, self, self.nc
        self.ffn_i += 1
        P, N = c.P, c.NTT
        es2 = ExitStack()
        esln = ExitStack()
        fmA = es2.enter_context(nc.sbuf_tensor(self.name("fmA"), [P, c.DC * c.T], F32))
        lnsc = esln.enter_context(nc.sbuf_tensor(self.name("lnsc"), [P, 2 * c.D], F32))
        xlnh = esln.enter_context(nc.sbuf_tensor(self.name("xlnh"), [P, max(1, N // 2) * c.D], F32))

        self.emit_ln_stats(lnsc)

        with nc.Block() as blk:
            @blk.vector
            def _(v):
                for t in range(N):
                    self.vstep(v, v.scalar_tensor_tensor(
                        out=lnsc[:, 0:c.D],
                        in0=s.xres[:, t * c.D:(t + 1) * c.D],
                        scalar=s.stat[:, 2 * N + t:2 * N + t + 1],
                        in1=s.stat[:, 4 * N + t:4 * N + t + 1].to_broadcast([c.P, c.D]),
                        op0=ALU.subtract, op1=ALU.mult))
                    self.vstep(v, v.reduce_max(out=s.slots[:, t:t + 1], in_=lnsc[:, 0:c.D],
                                               axis=AX.X, apply_absolute_value=True))

        self.emit_armax([(0, N)], [self.sc(25)])
        with nc.Block() as blk:
            @blk.vector
            def _(v):
                v.wait_ge(s.vv.h, s.vv.v)
                self.vstep(v, v.tensor_scalar(out=self.sc(1), in0=self.sc(25), scalar1=1.0 / 32767.0,
                                              scalar2=1e-12, op0=ALU.mult, op1=ALU.max))
                self.vstep(v, v.reciprocal(out=self.sc(2), in_=self.sc(1)))
                s.vv.inc(v.engine_nop())

        self.emit_lnT(fmA, xlnh, rcs_scale=self.sc(2))
        esln.close()
        if self.dbg_hit("k1T"):
            self.dump(fmA[:], c.P * c.DC * c.T)
            es2.close()
            return

        self.emit_upgate(fmA)
        es2.close()

        self.emit_armax([(0, c.HC * c.NTC), (128, c.HC * c.NTC)],
                        [self.sc(30), self.sc(29)])
        with nc.Block() as blk:
            @blk.vector
            def _(v):
                v.wait_ge(s.vv.h, s.vv.v)
                self.vstep(v, v.scalar_tensor_tensor(out=self.sc(20), in0=self.sc(30),
                                                     scalar=1.0 / 32767.0, in1=self.sc(1),
                                                     op0=ALU.mult, op1=ALU.mult))
                self.vstep(v, v.tensor_scalar(out=self.sc(20), in0=self.sc(20), scalar1=1e-12,
                                              scalar2=None, op0=ALU.max))
                self.vstep(v, v.reciprocal(out=self.sc(28), in_=self.sc(20)))
                self.vstep(v, v.tensor_mul(out=self.sc(4), in0=self.sc(28), in1=self.sc(1)))
                self.vstep(v, v.scalar_tensor_tensor(out=self.sc(21), in0=self.sc(29),
                                                     scalar=1.0 / 32767.0, in1=self.sc(1),
                                                     op0=ALU.mult, op1=ALU.mult))
                self.vstep(v, v.tensor_scalar(out=self.sc(21), in0=self.sc(21), scalar1=1e-12,
                                              scalar2=None, op0=ALU.max))
                self.vstep(v, v.reciprocal(out=self.sc(28), in_=self.sc(21)))
                self.vstep(v, v.tensor_mul(out=self.sc(5), in0=self.sc(28), in1=self.sc(1)))
                self.vstep(v, v.tensor_copy(out=self.sc(6), in_=self.sc(21)))
                s.vv.inc(v.engine_nop())

        if self.dbg_hit("scal_ar1"):
            self.dump(s.scal[:], c.P * 32)
            return
        self.emit_f3()
        self.emit_armax([(0, c.HC)], [self.sc(30)])
        with nc.Block() as blk:
            @blk.vector
            def _(v):
                v.wait_ge(s.vv.h, s.vv.v)
                self.vstep(v, v.scalar_tensor_tensor(out=self.sc(22), in0=self.sc(6),
                                                     scalar=1.0 / 65536.0, in1=self.sc(30),
                                                     op0=ALU.mult, op1=ALU.mult))
                self.vstep(v, v.tensor_scalar(out=self.sc(22), in0=self.sc(22),
                                              scalar1=1.0 / 32767.0, scalar2=1e-12,
                                              op0=ALU.mult, op1=ALU.max))
                self.vstep(v, v.reciprocal(out=self.sc(28), in_=self.sc(22)))
                self.vstep(v, v.scalar_tensor_tensor(out=self.sc(7), in0=self.sc(6),
                                                     scalar=1.0 / 65536.0, in1=self.sc(28),
                                                     op0=ALU.mult, op1=ALU.mult))
                self.vstep(v, v.tensor_mul(out=self.sc(8), in0=self.sc(22), in1=self.sc(20)))
                s.vv.inc(v.engine_nop())

        if self.dbg_hit("scal_ar2"):
            self.dump(s.scal[:], c.P * 32)
            return
        self.emit_f4()
        self.emit_armax([(0, c.HC)], [self.sc(30)])
        with nc.Block() as blk:
            @blk.vector
            def _(v):
                v.wait_ge(s.vv.h, s.vv.v)
                self.vstep(v, v.scalar_tensor_tensor(out=self.sc(23), in0=self.sc(30),
                                                     scalar=1.0 / 32767.0, in1=self.sc(8),
                                                     op0=ALU.mult, op1=ALU.mult))
                self.vstep(v, v.tensor_scalar(out=self.sc(23), in0=self.sc(23), scalar1=1e-12,
                                              scalar2=None, op0=ALU.max))
                self.vstep(v, v.reciprocal(out=self.sc(28), in_=self.sc(23)))
                self.vstep(v, v.tensor_mul(out=self.sc(9), in0=self.sc(28), in1=self.sc(8)))
                self.vstep(v, v.tensor_scalar(out=self.sc(10), in0=self.sc(23), scalar1=0.5,
                                              scalar2=None, op0=ALU.mult))
                s.vv.inc(v.engine_nop())

        if self.dbg_hit("scal_ar3"):
            self.dump(s.scal[:], c.P * 32)
            return
        self.emit_f5()

    def emit_lnT(self, fmA, xlnh, rcs_scale):
        c, s, nc = self.c, self, self.nc
        P, N = c.P, c.NTT
        nh = max(1, N // 2)
        nbank = 4
        W = c.DC * P
        EV = 3 if rcs_scale is not None else 1
        esl = ExitStack()
        tmpf = esl.enter_context(nc.sbuf_tensor(self.name("ltf"), [P, W], F32))
        tmpi = esl.enter_context(nc.sbuf_tensor(self.name("lti"), [P, W], I16))
        for half in range(N // nh):
            t0 = half * nh
            with nc.Block() as blk:
                @blk.vector
                def _(v, t0=t0):
                    tmb = s.tm.v
                    for i in range(nh):
                        t = t0 + i
                        v.scalar_tensor_tensor(
                            out=xlnh[:, i * c.D:(i + 1) * c.D],
                            in0=s.xres[:, t * c.D:(t + 1) * c.D],
                            scalar=s.stat[:, 2 * N + t:2 * N + t + 1],
                            in1=s.stat[:, 4 * N + t:4 * N + t + 1].to_broadcast([P, c.D]),
                            op0=ALU.subtract, op1=ALU.mult).then_inc(s.vv.h, 1)
                        s.vv.v += 1
                    for i in range(nh):
                        t = t0 + i
                        v.wait_ge(s.tm.h, tmb + i + 1)
                        bank = self.pslot(i % nbank, W)
                        o = fmA.rearrange("p (cc tt) -> p cc tt", cc=c.DC)
                        if rcs_scale is not None:
                            self.vstep(v, v.tensor_scalar(out=tmpf[:], in0=bank,
                                                          scalar1=rcs_scale, scalar2=None,
                                                          op0=ALU.mult))
                            self.vstep(v, v.tensor_copy(out=tmpi[:], in_=tmpf[:]))
                            v.tensor_copy(out=o[:, :, t * P:(t + 1) * P],
                                          in_=tmpi[:].rearrange("p (cc tt) -> p cc tt", cc=c.DC)
                                          ).then_inc(s.vv.h, 1)
                            s.vv.v += 1
                        else:
                            v.tensor_copy(out=o[:, :, t * P:(t + 1) * P], in_=bank
                                          ).then_inc(s.vv.h, 1)
                            s.vv.v += 1

                vvb = s.vv.v - (1 + EV) * nh

                @blk.tensor
                def _(te, t0=t0, vvb=vvb):
                    for i in range(nh):
                        te.wait_ge(s.vv.h, vvb + i + 1)
                        if i >= nbank:
                            te.wait_ge(s.vv.h, vvb + nh + EV * (i - nbank) + 1)
                        bank = self.pslot(i % nbank, W)
                        for cc in range(c.DC):
                            ti = te.transpose(bank[:, cc * P:(cc + 1) * P],
                                              xlnh[:, i * c.D + cc * P:i * c.D + (cc + 1) * P],
                                              s.ident[:])
                        s.tm.inc(ti)
        esl.close()

    def emit_upgate(self, fmA):
        c, s, nc = self.c, self, self.nc
        P = c.P
        es2 = ExitStack()
        wbuf = es2.enter_context(nc.sbuf_tensor(self.name("wb"), [P, 2 * c.DC * P], F32))
        mst = es2.enter_context(nc.sbuf_tensor(self.name("ms"), [P, 2 * c.T], F32))
        nb = min(4, 2048 // c.TN)

        for wd_, md_, slot0 in [(s.wuq_d, s.mup_d, 0), (s.wgq_d, s.mgate_d, 128)]:
            with nc.Block() as blk:
                tmb0 = s.tm.v
                vvb0 = s.vv.v
                oslb = [s.osl[0].v, s.osl[1].v]
                wevt = []
                sevt = []

                @blk.sync
                def _(sy, wd_=wd_, md_=md_):
                    for M in range(c.HC + 1):
                        if M < c.HC:
                            if M >= 2:
                                sy.wait_ge(s.tm.h, tmb0 + (M - 1) * c.NTC)
                            w = s.wsl[M % 2]
                            w.inc(sy.dma_start(
                                out=wbuf[:, (M % 2) * c.DC * P:(M % 2 + 1) * c.DC * P],
                                in_=wd_[M]), 16)
                            wevt.append(w.v)
                        if M >= 1:
                            sy.wait_ge(s.vv.h, vvb0 + 2 * M * c.NTC)
                            o = s.osl[(M - 1) % 2]
                            o.inc(sy.dma_start(
                                out=md_[M - 1],
                                in_=mst[:, ((M - 1) % 2) * c.T:((M - 1) % 2 + 1) * c.T]), 16)
                            sevt.append(o.v)

                @blk.tensor
                def _(te):
                    for M in range(c.HC):
                        w = s.wsl[M % 2]
                        te.wait_ge(w.h, wevt[M])
                        for n in range(c.NTC):
                            g = M * c.NTC + n
                            if g >= nb:
                                te.wait_ge(s.vv.h, vvb0 + 2 * (g - nb) + 1)
                            bank = self.pslot(g % nb, c.TN)
                            for kc in range(c.DC):
                                ti = te.matmul(
                                    bank,
                                    wbuf[:, (M % 2) * c.DC * P + kc * P:(M % 2) * c.DC * P + (kc + 1) * P],
                                    fmA[:, kc * c.T + n * c.TN:kc * c.T + (n + 1) * c.TN],
                                    start=(kc == 0), stop=(kc == c.DC - 1))
                            s.tm.inc(ti)

                @blk.vector
                def _(v, slot0=slot0):
                    v.wait_ge(s.osl[0].h, oslb[0])
                    v.wait_ge(s.osl[1].h, oslb[1])
                    for M in range(c.HC):
                        if M >= 2:
                            o = s.osl[M % 2]
                            v.wait_ge(o.h, sevt[M - 2])
                        for n in range(c.NTC):
                            g = M * c.NTC + n
                            v.wait_ge(s.tm.h, tmb0 + g + 1)
                            bank = self.pslot(g % nb, c.TN)
                            dst = mst[:, (M % 2) * c.T + n * c.TN:(M % 2) * c.T + (n + 1) * c.TN]
                            self.vstep(v, v.tensor_scalar(
                                out=dst, in0=bank, scalar1=1.0, scalar2=None, op0=ALU.mult))
                            v.reduce_max(out=s.slots[:, slot0 + g:slot0 + g + 1],
                                         in_=dst, axis=AX.X,
                                         apply_absolute_value=True).then_inc(s.vv.h, 1)
                            s.vv.v += 1
        with nc.Block() as blk:
            @blk.sync
            def _(sy):
                sy.wait_ge(s.osl[0].h, s.osl[0].v)
                sy.wait_ge(s.osl[1].h, s.osl[1].v)
        es2.close()

    def emit_f3(self):
        c, s, nc = self.c, self, self.nc
        P = c.P
        es2 = ExitStack()
        mst = es2.enter_context(nc.sbuf_tensor(self.name("f3m"), [P, 4 * c.T], F32))
        kst = es2.enter_context(nc.sbuf_tensor(self.name("f3k"), [P, 4 * c.T], I16))
        sg = es2.enter_context(nc.sbuf_tensor(self.name("f3s"), [P, 2 * c.T], F32))
        jj = es2.enter_context(nc.sbuf_tensor(self.name("f3j"), [P, 2 * c.T], F32))
        pp = es2.enter_context(nc.sbuf_tensor(self.name("f3p"), [P, c.T], F32))
        tpf = es2.enter_context(nc.sbuf_tensor(self.name("f3t"), [P, c.T], F32))
        ju16 = es2.enter_context(nc.sbuf_tensor(self.name("f3u"), [P, c.T], mybir.dt.uint16))

        with nc.Block() as blk:
            vvb0 = s.vv.v
            ssb0 = s.ss.v
            VPC = 9
            ievt = []
            oevt = []

            @blk.sync
            def _(sy):
                for cc in range(c.HC + 1):
                    if cc < c.HC:
                        sl = cc % 2
                        if cc >= 2:
                            sy.wait_ge(s.vv.h, vvb0 + (cc - 2) * VPC + 3)
                        i = s.isl[sl]
                        i.inc(sy.dma_start(out=mst[:, sl * c.T:(sl + 1) * c.T], in_=s.mup_d[cc]), 16)
                        i.inc(sy.dma_start(out=mst[:, (2 + sl) * c.T:(3 + sl) * c.T], in_=s.mgate_d[cc]), 16)
                        ievt.append(i.v)
                    if cc >= 1:
                        sl = (cc - 1) % 2
                        sy.wait_ge(s.vv.h, vvb0 + (cc - 1) * VPC + 4)
                        o = s.osl[sl]
                        for n in range(c.NTC):
                            o.inc(sy.dma_start(
                                out=s.kup_d[n, cc - 1],
                                in_=kst[:, sl * c.T + n * c.TN:sl * c.T + (n + 1) * c.TN]), 16)
                            o.inc(sy.dma_start(
                                out=s.kgate_d[n, cc - 1],
                                in_=kst[:, (2 + sl) * c.T + n * c.TN:(2 + sl) * c.T + (n + 1) * c.TN]), 16)
                        oevt.append(o.v)

            @blk.vector
            def _(v):
                for cc in range(c.HC):
                    sl = cc % 2
                    v.wait_ge(s.isl[sl].h, ievt[cc])
                    if cc >= 2:
                        v.wait_ge(s.osl[sl].h, oevt[cc - 2])
                    self.vstep(v, v.tensor_scalar(out=tpf[:], in0=mst[:, sl * c.T:(sl + 1) * c.T],
                                                  scalar1=self.sc(4), scalar2=None, op0=ALU.mult))
                    self.vstep(v, v.tensor_copy(out=kst[:, sl * c.T:(sl + 1) * c.T], in_=tpf[:]))
                    self.vstep(v, v.tensor_scalar(out=tpf[:], in0=mst[:, (2 + sl) * c.T:(3 + sl) * c.T],
                                                  scalar1=self.sc(5), scalar2=None, op0=ALU.mult))
                    self.vstep(v, v.tensor_copy(out=kst[:, (2 + sl) * c.T:(3 + sl) * c.T], in_=tpf[:]))
                    v.wait_ge(s.ss.h, ssb0 + cc + 1)
                    self.vstep(v, v.tensor_scalar(out=tpf[:], in0=sg[:, sl * c.T:(sl + 1) * c.T],
                                                  scalar1=65536.0, scalar2=None, op0=ALU.mult))
                    self.vstep(v, v.tensor_copy(out=ju16[:], in_=tpf[:]))
                    self.vstep(v, v.tensor_copy(out=jj[:, sl * c.T:(sl + 1) * c.T], in_=ju16[:]))
                    self.vstep(v, v.tensor_mul(out=pp[:], in0=kst[:, (2 + sl) * c.T:(3 + sl) * c.T],
                                               in1=jj[:, sl * c.T:(sl + 1) * c.T]))
                    self.vstep(v, v.reduce_max(out=s.slots[:, cc:cc + 1], in_=pp[:], axis=AX.X,
                                               apply_absolute_value=True))

            @blk.scalar
            def _(sc):
                for cc in range(c.HC):
                    sl = cc % 2
                    sc.wait_ge(s.vv.h, vvb0 + cc * VPC + 4)
                    if cc >= 2:
                        sc.wait_ge(s.ss.h, ssb0 + cc - 1)
                    sc.activation(out=sg[:, sl * c.T:(sl + 1) * c.T],
                                  in_=kst[:, (2 + sl) * c.T:(3 + sl) * c.T],
                                  func=AF.Sigmoid, scale=self.sc(6)).then_inc(s.ss.h, 1)
                    s.ss.v += 1

        with nc.Block() as blk:
            @blk.sync
            def _(sy):
                sy.wait_ge(s.osl[0].h, s.osl[0].v)
                sy.wait_ge(s.osl[1].h, s.osl[1].v)
        es2.close()

    def emit_f4(self):
        c, s, nc = self.c, self, self.nc
        P = c.P
        es2 = ExitStack()
        kin = es2.enter_context(nc.sbuf_tensor(self.name("f4a"), [P, 4 * c.T], I16))
        sg = es2.enter_context(nc.sbuf_tensor(self.name("f4s"), [P, 2 * c.T], F32))
        jj = es2.enter_context(nc.sbuf_tensor(self.name("f4j"), [P, c.T], F32))
        pp = es2.enter_context(nc.sbuf_tensor(self.name("f4p"), [P, c.T], F32))
        k2 = es2.enter_context(nc.sbuf_tensor(self.name("f4k"), [P, 2 * c.T], I16))
        tpf = es2.enter_context(nc.sbuf_tensor(self.name("f4t"), [P, c.T], F32))
        ju16 = es2.enter_context(nc.sbuf_tensor(self.name("f4u"), [P, c.T], mybir.dt.uint16))

        with nc.Block() as blk:
            vvb0 = s.vv.v
            ssb0 = s.ss.v
            VPC = 8
            ievt = []
            oevt = []

            @blk.sync
            def _(sy):
                for cc in range(c.HC + 1):
                    if cc < c.HC:
                        sl = cc % 2
                        if cc >= 2:
                            sy.wait_ge(s.vv.h, vvb0 + (cc - 2) * VPC + 7)
                        i = s.isl[sl]
                        for n in range(c.NTC):
                            i.inc(sy.dma_start(
                                out=kin[:, sl * c.T + n * c.TN:sl * c.T + (n + 1) * c.TN],
                                in_=s.kgate_d[n, cc]), 16)
                            i.inc(sy.dma_start(
                                out=kin[:, (2 + sl) * c.T + n * c.TN:(2 + sl) * c.T + (n + 1) * c.TN],
                                in_=s.kup_d[n, cc]), 16)
                        ievt.append(i.v)
                    if cc >= 1:
                        sl = (cc - 1) % 2
                        sy.wait_ge(s.vv.h, vvb0 + (cc - 1) * VPC + 6)
                        o = s.osl[sl]
                        for n in range(c.NTC):
                            o.inc(sy.dma_start(
                                out=s.kg2_d[n, cc - 1],
                                in_=k2[:, sl * c.T + n * c.TN:sl * c.T + (n + 1) * c.TN]), 16)
                        oevt.append(o.v)

            @blk.scalar
            def _(sc):
                for cc in range(c.HC):
                    sl = cc % 2
                    sc.wait_ge(s.isl[sl].h, ievt[cc])
                    if cc >= 2:
                        sc.wait_ge(s.ss.h, ssb0 + cc - 1)
                        sc.wait_ge(s.vv.h, vvb0 + (cc - 2) * VPC + 1)
                    sc.activation(out=sg[:, sl * c.T:(sl + 1) * c.T],
                                  in_=kin[:, sl * c.T:(sl + 1) * c.T],
                                  func=AF.Sigmoid, scale=self.sc(6)).then_inc(s.ss.h, 1)
                    s.ss.v += 1

            @blk.vector
            def _(v):
                for cc in range(c.HC):
                    sl = cc % 2
                    v.wait_ge(s.ss.h, ssb0 + cc + 1)
                    self.vstep(v, v.tensor_scalar(out=tpf[:], in0=sg[:, sl * c.T:(sl + 1) * c.T],
                                                  scalar1=65536.0, scalar2=None, op0=ALU.mult))
                    self.vstep(v, v.tensor_copy(out=ju16[:], in_=tpf[:]))
                    self.vstep(v, v.tensor_copy(out=jj[:], in_=ju16[:]))
                    self.vstep(v, v.tensor_mul(out=pp[:], in0=kin[:, sl * c.T:(sl + 1) * c.T],
                                               in1=jj[:]))
                    if cc >= 2:
                        v.wait_ge(s.osl[sl].h, oevt[cc - 2])
                    self.vstep(v, v.tensor_scalar(out=tpf[:], in0=pp[:],
                                                  scalar1=self.sc(7), scalar2=None, op0=ALU.mult))
                    self.vstep(v, v.tensor_copy(out=k2[:, sl * c.T:(sl + 1) * c.T], in_=tpf[:]))
                    self.vstep(v, v.tensor_mul(out=pp[:], in0=k2[:, sl * c.T:(sl + 1) * c.T],
                                               in1=kin[:, (2 + sl) * c.T:(3 + sl) * c.T]))
                    self.vstep(v, v.reduce_max(out=s.slots[:, cc:cc + 1], in_=pp[:], axis=AX.X,
                                               apply_absolute_value=True))

        with nc.Block() as blk:
            @blk.sync
            def _(sy):
                sy.wait_ge(s.osl[0].h, s.osl[0].v)
                sy.wait_ge(s.osl[1].h, s.osl[1].v)
        es2.close()

    def emit_f5(self):
        c, s, nc = self.c, self, self.nc
        if self.dbg_hit("xres_pre_f5"):
            self.dump(self.xres[:, 0:4 * c.D], c.P * 4 * c.D)
            return
        P = c.P
        ntile = c.TN // P
        es2 = ExitStack()
        k2st = es2.enter_context(nc.sbuf_tensor(self.name("f5a"), [P, c.HC * c.TN], I16))
        kust = es2.enter_context(nc.sbuf_tensor(self.name("f5b"), [P, c.HC * c.TN], I16))
        qko = es2.enter_context(nc.sbuf_tensor(self.name("f5q"), [P, c.HC * c.TN], F32))
        ko2 = qko
        wdb = es2.enter_context(nc.sbuf_tensor(self.name("f5w"), [P, 2 * c.D], F32))
        wdv = s.wdq_d.rearrange("(hc p) d -> hc p d", p=P)

        for tc in range(c.NTC):
            with nc.Block() as blk:
                iv = []

                @blk.sync
                def _(sy, tc=tc):
                    i = s.isl[0]
                    for cc in range(c.HC):
                        i.inc(sy.dma_start(out=k2st[:, cc * c.TN:(cc + 1) * c.TN],
                                           in_=s.kg2_d[tc, cc]), 16)
                        i.inc(sy.dma_start(out=kust[:, cc * c.TN:(cc + 1) * c.TN],
                                           in_=s.kup_d[tc, cc]), 16)
                    iv.append(i.v)

                @blk.vector
                def _(v):
                    v.wait_ge(s.isl[0].h, iv[0])
                    self.vstep(v, v.tensor_mul(out=qko[:], in0=k2st[:], in1=kust[:]))
                    self.vstep(v, v.tensor_scalar(out=qko[:], in0=qko[:], scalar1=self.sc(9),
                                                  scalar2=None, op0=ALU.mult))
                    self.vstep(v, v.tensor_copy(out=k2st[:], in_=qko[:]))
                    self.vstep(v, v.tensor_copy(out=ko2[:], in_=k2st[:]))

            if self.dbg_hit("ko_tc") and tc == self.dbg[2]:
                self.dump(qko[:], c.P * c.HC * c.TN)
                break
            slow = self.dbg is not None and self.dbg[0] == "f5slow" and self.dbg[1] == self.ffn_i
            with nc.Block() as blk:
                tmb0 = s.tm.v
                wevt = []

                @blk.sync
                def _(sy, tc=tc, slow=slow):
                    for cc in range(c.HC):
                        if slow:
                            if cc >= 1:
                                sy.wait_ge(s.tm.h, tmb0 + cc)
                            w = s.wsl[0]
                            w.inc(sy.dma_start(out=wdb[:, 0:c.D], in_=wdv[cc]), 16)
                            wevt.append(w.v)
                            continue
                        if cc >= 2:
                            sy.wait_ge(s.tm.h, tmb0 + (cc - 1))
                        w = s.wsl[cc % 2]
                        w.inc(sy.dma_start(out=wdb[:, (cc % 2) * c.D:(cc % 2 + 1) * c.D],
                                           in_=wdv[cc]), 16)
                        wevt.append(w.v)

                @blk.tensor
                def _(te, tc=tc, slow=slow):
                    for cc in range(c.HC):
                        wsl_i = 0 if slow else cc % 2
                        woff = 0 if slow else (cc % 2) * c.D
                        te.wait_ge(s.wsl[wsl_i].h, wevt[cc])
                        for t in range(ntile):
                            bank = s.psA if t % 2 == 0 else s.psB
                            boff = (t // 2) * c.D
                            for nn in range(c.ND2):
                                ti = te.matmul(
                                    bank[:, boff + nn * c.NW:boff + (nn + 1) * c.NW],
                                    ko2[:, cc * c.TN + t * P:cc * c.TN + (t + 1) * P],
                                    wdb[:, woff + nn * c.NW:woff + (nn + 1) * c.NW],
                                    start=(cc == 0), stop=(cc == c.HC - 1))
                        s.tm.inc(ti)

                @blk.vector
                def _(v, tc=tc):
                    v.wait_ge(s.tm.h, s.tm.v)
                    abt = (self.dbg is not None and self.dbg[0] == "abtest"
                           and self.dbg[1] == self.ffn_i and tc == 0)
                    for t in range(ntile):
                        bank = s.psA if t % 2 == 0 else s.psB
                        boff = (t // 2) * c.D
                        gt = tc * ntile + t
                        if abt:
                            self.vstep(v, v.tensor_scalar(
                                out=qko[:, t * c.D:(t + 1) * c.D],
                                in0=bank[:, boff:boff + c.D], scalar1=self.sc(10),
                                scalar2=None, op0=ALU.mult))
                            self.vstep(v, v.scalar_tensor_tensor(
                                out=qko[:, (4 + t) * c.D:(5 + t) * c.D],
                                in0=bank[:, boff:boff + c.D], scalar=self.sc(10),
                                in1=s.xres[:, gt * c.D:(gt + 1) * c.D],
                                op0=ALU.mult, op1=ALU.add))
                            self.vstep(v, v.scalar_tensor_tensor(
                                out=s.xres[:, gt * c.D:(gt + 1) * c.D],
                                in0=bank[:, boff:boff + c.D], scalar=self.sc(10),
                                in1=s.xres[:, gt * c.D:(gt + 1) * c.D],
                                op0=ALU.mult, op1=ALU.add))
                            continue
                        if self.dbg is not None and self.dbg[0] == "mdown" and self.dbg[1] == self.ffn_i:
                            self.vstep(v, v.tensor_scalar(
                                out=s.xres[:, gt * c.D:(gt + 1) * c.D],
                                in0=bank[:, boff:boff + c.D], scalar1=self.sc(10),
                                scalar2=None, op0=ALU.mult))
                        else:
                            self.vstep(v, v.scalar_tensor_tensor(
                                out=s.xres[:, gt * c.D:(gt + 1) * c.D],
                                in0=bank[:, boff:boff + c.D], scalar=self.sc(10),
                                in1=s.xres[:, gt * c.D:(gt + 1) * c.D],
                                op0=ALU.mult, op1=ALU.add))
            if self.dbg is not None and self.dbg[0] == "mdown" and self.dbg[1] == self.ffn_i and tc == self.dbg[2] and not self.dbg_done:
                self.dump(s.xres[:, 0:4 * c.D], c.P * 4 * c.D)
                break
            if self.dbg is not None and self.dbg[0] == "abtest" and self.dbg[1] == self.ffn_i and tc == 0 and not self.dbg_done:
                self.dump(qko[:, 0:8 * c.D], c.P * 8 * c.D)
                break
            if self.dbg is not None and self.dbg[0] == "f5slow" and self.dbg[1] == self.ffn_i and tc == 0 and not self.dbg_done:
                self.dump(s.xres[:, 0:4 * c.D], c.P * 4 * c.D)
                break
            if self.dbg_hit("after_f5_tc") and tc == self.dbg[2]:
                self.dump(s.xres[:, 0:4 * c.D], c.P * 4 * c.D)
                break
        es2.close()

    def emit_attention(self):
        c, s, nc = self.c, self, self.nc
        P, N = c.P, c.NTT
        es2 = ExitStack()
        esln = ExitStack()
        fmA = es2.enter_context(nc.sbuf_tensor(self.name("afm"), [P, c.DC * c.T], F32))
        lnsc = esln.enter_context(nc.sbuf_tensor(self.name("alns"), [P, 2 * c.D], F32))
        xlnh = esln.enter_context(nc.sbuf_tensor(self.name("axln"), [P, max(1, N // 2) * c.D], F32))

        self.emit_ln_stats(lnsc)
        self.emit_lnT(fmA, xlnh, rcs_scale=None)
        esln.close()

        es3 = ExitStack()
        wbuf = es3.enter_context(nc.sbuf_tensor(self.name("awb"), [P, max(2 * c.DC * P, c.DC * c.NW)], F32))
        qst = es3.enter_context(nc.sbuf_tensor(self.name("aqs"), [P, 2 * c.T], F32))
        nb = min(4, 2048 // c.TN)
        with nc.Block() as blk:
            tmb0 = s.tm.v
            vvb0 = s.vv.v
            wevt = []
            sevt = []

            @blk.sync
            def _(sy):
                for M in range(2 * c.DC + 1):
                    if M < 2 * c.DC:
                        if M >= 2:
                            sy.wait_ge(s.tm.h, tmb0 + (M - 1) * c.NTC)
                        w = s.wsl[M % 2]
                        w.inc(sy.dma_start(
                            out=wbuf[:, (M % 2) * c.DC * P:(M % 2 + 1) * c.DC * P],
                            in_=s.wqk_d[M]), 16)
                        wevt.append(w.v)
                    if M >= 1:
                        sy.wait_ge(s.vv.h, vvb0 + M * c.NTC)
                        dst = s.qT_d[M - 1] if (M - 1) < c.DC else s.kloc_d[M - 1 - c.DC]
                        o = s.osl[(M - 1) % 2]
                        o.inc(sy.dma_start(
                            out=dst, in_=qst[:, ((M - 1) % 2) * c.T:((M - 1) % 2 + 1) * c.T]), 16)
                        sevt.append(o.v)

            @blk.tensor
            def _(te):
                for M in range(2 * c.DC):
                    te.wait_ge(s.wsl[M % 2].h, wevt[M])
                    for n in range(c.NTC):
                        g = M * c.NTC + n
                        if g >= nb:
                            te.wait_ge(s.vv.h, vvb0 + (g - nb) + 1)
                        bank = self.pslot(g % nb, c.TN)
                        for kc in range(c.DC):
                            ti = te.matmul(
                                bank,
                                wbuf[:, (M % 2) * c.DC * P + kc * P:(M % 2) * c.DC * P + (kc + 1) * P],
                                fmA[:, kc * c.T + n * c.TN:kc * c.T + (n + 1) * c.TN],
                                start=(kc == 0), stop=(kc == c.DC - 1))
                        s.tm.inc(ti)

            @blk.vector
            def _(v):
                for M in range(2 * c.DC):
                    if M >= 2:
                        v.wait_ge(s.osl[M % 2].h, sevt[M - 2])
                    for n in range(c.NTC):
                        g = M * c.NTC + n
                        v.wait_ge(s.tm.h, tmb0 + g + 1)
                        bank = self.pslot(g % nb, c.TN)
                        v.tensor_scalar(
                            out=qst[:, (M % 2) * c.T + n * c.TN:(M % 2) * c.T + (n + 1) * c.TN],
                            in0=bank, scalar1=s.bqks[:, M:M + 1], scalar2=None,
                            op0=ALU.add).then_inc(s.vv.h, 1)
                        s.vv.v += 1

        with nc.Block() as blk:
            tmb0 = s.tm.v
            vvb0 = s.vv.v
            oslb = [s.osl[0].v, s.osl[1].v]
            wevt = []
            sevt = []

            @blk.sync
            def _(sy):
                for h2 in range(c.ND2):
                    if h2 >= 1:
                        sy.wait_ge(s.tm.h, tmb0 + h2 * c.NTT)
                    w = s.wsl[h2 % 2]
                    for kc in range(c.DC):
                        w.inc(sy.dma_start(out=wbuf[:, kc * c.NW:(kc + 1) * c.NW],
                                           in_=s.wv_d[h2, kc]), 16)
                    wevt.append(w.v)
                    for Mt in range(c.NTT):
                        g = h2 * c.NTT + Mt
                        sy.wait_ge(s.vv.h, vvb0 + g + 1)
                        o = s.osl[g % 2]
                        o.inc(sy.dma_start(
                            out=s.vloc_d[Mt, :, h2 * c.NW:(h2 + 1) * c.NW],
                            in_=qst[:, (g % 2) * c.T:(g % 2) * c.T + c.NW]), 16)
                        sevt.append(o.v)

            @blk.tensor
            def _(te):
                for h2 in range(c.ND2):
                    te.wait_ge(s.wsl[h2 % 2].h, wevt[h2])
                    for Mt in range(c.NTT):
                        g = h2 * c.NTT + Mt
                        if g >= 4:
                            te.wait_ge(s.vv.h, vvb0 + (g - 4) + 1)
                        bank = self.pslot(g % 4, c.NW)
                        for kc in range(c.DC):
                            ti = te.matmul(bank,
                                           fmA[:, kc * c.T + Mt * P:kc * c.T + (Mt + 1) * P],
                                           wbuf[:, kc * c.NW:(kc + 1) * c.NW],
                                           start=(kc == 0), stop=(kc == c.DC - 1))
                        s.tm.inc(ti)

            @blk.vector
            def _(v):
                v.wait_ge(s.osl[0].h, oslb[0])
                v.wait_ge(s.osl[1].h, oslb[1])
                for h2 in range(c.ND2):
                    for Mt in range(c.NTT):
                        g = h2 * c.NTT + Mt
                        v.wait_ge(s.tm.h, tmb0 + g + 1)
                        if g >= 2:
                            v.wait_ge(s.osl[g % 2].h, sevt[g - 2])
                        bank = self.pslot(g % 4, c.NW)
                        v.tensor_add(out=qst[:, (g % 2) * c.T:(g % 2) * c.T + c.NW],
                                     in0=bank,
                                     in1=s.bvb[:, h2 * c.NW:(h2 + 1) * c.NW]).then_inc(s.vv.h, 1)
                        s.vv.v += 1
        es3.close()
        es2.close()

        with nc.Block() as blk:
            @blk.gpsimd
            def _(g):
                g.wait_ge(s.osl[0].h, s.osl[0].v)
                g.wait_ge(s.osl[1].h, s.osl[1].v)
                s.cc.inc(g.collective_compute(
                    "AllGather", ALU.bypass, replica_groups=[list(range(c.n_cores))],
                    ins=[s.kloc_d.opt()], outs=[s.kall_d.opt()]), 1)
                s.cc.inc(g.collective_compute(
                    "AllGather", ALU.bypass, replica_groups=[list(range(c.n_cores))],
                    ins=[s.vloc_d.opt()], outs=[s.vall_d.opt()]), 1)
                g.wait_ge(s.cc.h, s.cc.v)

        with nc.Block() as blk:
            @blk.sync
            def _(sy):
                with sy.register(self.name("pr0")) as r0, sy.register(self.name("pr1")) as r1:
                    sy.reg_load(r0, s.cis[0:1, 0:1])
                    sy.reg_load(r1, s.cis[0:1, 1:2])
                    sl0 = sy.snap(r0, min_val=0, max_val=c.n_cores - 1)
                    sl1 = sy.snap(r1, min_val=0, max_val=c.n_cores - 1)
                    for i, slot in enumerate([sl0, sl1]):
                        s.sy.inc(sy.dma_start(out=self.kpair_d[i],
                                              in_=s.kall_d[bass.ds(slot, 1)].squeeze(0)), 16)
                        s.sy.inc(sy.dma_start(out=self.vpair_d[i],
                                              in_=s.vall_d[bass.ds(slot, 1)].squeeze(0)), 16)
                sy.wait_ge(s.sy.h, s.sy.v)

        self.emit_flash()

    def emit_flash(self):
        c, s, nc = self.c, self, self.nc
        P, HD = c.P, c.HDIM
        QC = c.TN
        NQC = c.T // QC
        qtile = QC // P
        NKT = c.NKT
        half_kt = NKT // 2
        es2 = ExitStack()
        qcb = es2.enter_context(nc.sbuf_tensor(self.name("fqc"), [P, c.DC * QC], F32))
        kbl = es2.enter_context(nc.sbuf_tensor(self.name("fkb"), [P, 2 * P], F32))
        vbl = es2.enter_context(nc.sbuf_tensor(self.name("fvb"), [P, 2 * 130], F32))
        ptl = es2.enter_context(nc.sbuf_tensor(self.name("fpt"), [P, 2 * 2 * QC], F32))
        atn = es2.enter_context(nc.sbuf_tensor(self.name("fat"), [P, c.DC * QC], F32))
        x1b = es2.enter_context(nc.sbuf_tensor(self.name("fx1"), [P, qtile * c.D], F32))
        lsb = es2.enter_context(nc.sbuf_tensor(self.name("fls"), [P, 4 * QC], F32))
        lbc = es2.enter_context(nc.sbuf_tensor(self.name("flb"), [P, QC], F32))
        wpb = es2.enter_context(nc.sbuf_tensor(self.name("fwp"), [P, 2 * c.D], F32))
        wpv = s.wp_d.rearrange("(kc p) d -> kc p d", p=P)

        with nc.Block() as blk:
            @blk.vector
            def _(v):
                for sl in range(2):
                    for hh in range(2):
                        ins = v.memset(vbl[:, sl * 130 + hh * 65 + 64:sl * 130 + hh * 65 + 65], 1.0)
                s.vv.inc(ins)

        for qc in range(NQC):
            with nc.Block() as blk:
                @blk.sync
                def _(sy, qc=qc):
                    for cc in range(c.DC):
                        s.sy.inc(sy.dma_start(out=qcb[:, cc * QC:(cc + 1) * QC],
                                              in_=s.qT_d[cc, :, qc * QC:(qc + 1) * QC]), 16)
                    sy.wait_ge(s.sy.h, s.sy.v)

                @blk.gpsimd
                def _(g, qc=qc):
                    for t in range(qtile):
                        gt = qc * qtile + t
                        g.tensor_add(out=x1b[:, t * c.D:(t + 1) * c.D],
                                     in0=s.xres[:, gt * c.D:(gt + 1) * c.D],
                                     in1=s.bpb[:]).then_inc(s.gg.h, 1)
                        s.gg.v += 1

            for hg in range(c.NHG):
                pvb = s.psB if hg % 2 == 0 else s.psA
                scb = s.psA if hg % 2 == 0 else s.psB
                with nc.Block() as blk:
                    tmb0 = s.tm.v
                    ssb0 = s.ss.v
                    kevt = []

                    @blk.sync
                    def _(sy, qc=qc, hg=hg):
                        for kt in range(NKT):
                            sp = kt // half_kt
                            ktl = kt % half_kt
                            sl = kt % 2
                            if kt >= 2:
                                sy.wait_ge(s.tm.h, tmb0 + 2 * (kt - 1))
                            i = s.isl[sl]
                            i.inc(sy.dma_start(
                                out=kbl[:, sl * P:(sl + 1) * P],
                                in_=s.kpair_d[sp, hg, :, ktl * P:(ktl + 1) * P]), 16)
                            vsrc = s.vpair_d[sp, ktl, :, hg * P:hg * P + 2 * HD]
                            vsrc = vsrc.rearrange("p (h d) -> p h d", h=2)
                            vdst = vbl[:, sl * 130:(sl + 1) * 130].rearrange(
                                "p (h e) -> p h e", h=2)[:, :, 0:HD]
                            i.inc(sy.dma_start(out=vdst, in_=vsrc), 16)
                            kevt.append(i.v)

                    @blk.tensor
                    def _(te, qc=qc, hg=hg):
                        for kt in range(NKT):
                            sl = kt % 2
                            te.wait_ge(s.isl[sl].h, kevt[kt])
                            if kt >= 2:
                                te.wait_ge(s.ss.h, ssb0 + (kt - 1))
                            for hh in range(2):
                                ti = te.matmul(
                                    scb[:, sl * 1024 + hh * 512:sl * 1024 + hh * 512 + QC],
                                    kbl[hh * HD:(hh + 1) * HD, sl * P:(sl + 1) * P],
                                    qcb[hh * HD:(hh + 1) * HD, hg * QC:(hg + 1) * QC],
                                    start=True, stop=True, tile_position=(hh * HD, 0))
                            s.tm.inc(ti)
                            te.wait_ge(s.ss.h, ssb0 + kt + 1)
                            for hh in range(2):
                                ti = te.matmul(
                                    pvb[0:65, hh * 1024:hh * 1024 + QC],
                                    vbl[:, sl * 130 + hh * 65:sl * 130 + (hh + 1) * 65],
                                    ptl[:, sl * 2 * QC + hh * QC:sl * 2 * QC + (hh + 1) * QC],
                                    start=(kt == 0), stop=(kt == NKT - 1))
                            s.tm.inc(ti)

                    @blk.scalar
                    def _(sc, qc=qc, hg=hg):
                        for kt in range(NKT):
                            sl = kt % 2
                            sc.wait_ge(s.tm.h, tmb0 + 2 * kt + 1)
                            if kt >= 2:
                                sc.wait_ge(s.tm.h, tmb0 + 2 * (kt - 2) + 2)
                            sin = scb[:, sl * 1024:(sl + 1) * 1024].rearrange(
                                "p (h x) -> p h x", h=2)[:, :, 0:QC]
                            pout = ptl[:, sl * 2 * QC:(sl + 1) * 2 * QC].rearrange(
                                "p (h x) -> p h x", h=2)
                            sc.activation(out=pout, in_=sin,
                                          func=AF.Exp, scale=0.125).then_inc(s.ss.h, 1)
                            s.ss.v += 1

                with nc.Block() as blk:
                    syb_e = s.sy.v

                    @blk.vector
                    def _(v, hg=hg):
                        v.wait_ge(s.sy.h, syb_e)
                        v.tensor_copy(out=lsb[64:65, 0:QC], in_=pvb[64:65, 0:QC])
                        s.vv.inc(v.tensor_copy(out=lsb[64:65, QC:2 * QC],
                                               in_=pvb[64:65, 1024:1024 + QC]))

                    @blk.sync
                    def _(sy):
                        sy.wait_ge(s.vv.h, s.vv.v)
                        s.sy.inc(sy.dma_start(out=s.lraw_d[0:2 * QC], in_=lsb[64:65, 0:2 * QC]), 16)

                    @blk.gpsimd
                    def _(g):
                        g.wait_ge(s.sy.h, s.sy.v)
                        lr = s.lraw_d[0:2 * QC].rearrange("(a b) -> a b", a=2)
                        s.gd.inc(g.dma_start(out=lsb[0:2, 0:QC], in_=lr), 16)

                    gdv1 = s.gd.v

                    @blk.vector
                    def _(v):
                        v.wait_ge(s.gd.h, gdv1)
                        s.vv.inc(v.reciprocal(out=lsb[0:2, QC:2 * QC], in_=lsb[0:2, 0:QC]))

                    @blk.sync
                    def _(sy):
                        sy.wait_ge(s.vv.h, s.vv.v)
                        s.sy.inc(sy.dma_start(out=s.lrec_d[0:2 * QC], in_=lsb[0:2, QC:2 * QC]), 16)

                    @blk.gpsimd
                    def _(g):
                        g.wait_ge(s.sy.h, s.sy.v)
                        for hh in range(2):
                            s.gd.inc(g.dma_start(
                                out=lbc[hh * HD:(hh + 1) * HD, 0:QC],
                                in_=s.lrec_d[hh * QC:(hh + 1) * QC].unsqueeze(0).to_broadcast([HD, QC])), 16)

                    gdv2 = s.gd.v

                    @blk.vector
                    def _(v, qc=qc, hg=hg):
                        v.wait_ge(s.gd.h, gdv2)
                        for hh in range(2):
                            h = 2 * hg + hh
                            cch, rr = h // 2, (h % 2) * HD
                            v.tensor_mul(out=atn[rr:rr + HD, cch * QC:(cch + 1) * QC],
                                         in0=pvb[0:HD, hh * 1024:hh * 1024 + QC],
                                         in1=lbc[hh * HD:(hh + 1) * HD, 0:QC]).then_inc(s.vv.h, 1)
                            s.vv.v += 1

            with nc.Block() as blk:
                wevt = []

                @blk.sync
                def _(sy, qc=qc):
                    tmb0 = s.tm.v
                    for kc in range(c.DC):
                        if kc >= 2:
                            sy.wait_ge(s.tm.h, tmb0 + (kc - 1))
                        w = s.wsl[kc % 2]
                        w.inc(sy.dma_start(out=wpb[:, (kc % 2) * c.D:(kc % 2 + 1) * c.D],
                                           in_=wpv[kc]), 16)
                        wevt.append(w.v)

                @blk.tensor
                def _(te, qc=qc):
                    for kc in range(c.DC):
                        te.wait_ge(s.wsl[kc % 2].h, wevt[kc])
                        for t in range(qtile):
                            bank = s.psA if t % 2 == 0 else s.psB
                            boff = (t // 2) * c.D
                            for nn in range(c.ND2):
                                ti = te.matmul(
                                    bank[:, boff + nn * c.NW:boff + (nn + 1) * c.NW],
                                    atn[:, kc * QC + t * P:kc * QC + (t + 1) * P],
                                    wpb[:, (kc % 2) * c.D + nn * c.NW:(kc % 2) * c.D + (nn + 1) * c.NW],
                                    start=(kc == 0), stop=(kc == c.DC - 1))
                        s.tm.inc(ti)

                @blk.vector
                def _(v, qc=qc):
                    v.wait_ge(s.tm.h, s.tm.v)
                    v.wait_ge(s.gg.h, s.gg.v)
                    for t in range(qtile):
                        bank = s.psA if t % 2 == 0 else s.psB
                        boff = (t // 2) * c.D
                        gt = qc * qtile + t
                        self.vstep(v, v.tensor_add(out=s.xres[:, gt * c.D:(gt + 1) * c.D],
                                                   in0=bank[:, boff:boff + c.D],
                                                   in1=x1b[:, t * c.D:(t + 1) * c.D]))
        es2.close()



def prep_inputs(cfg, L0_attn_x, L0_attn_w_qkv, L0_attn_b_qkv, L0_attn_w_proj,
                L0_attn_b_proj, L0_ffn_w_gate, L0_ffn_w_up, L0_ffn_w_down):
    c = cfg
    P = c.P
    x = np.ascontiguousarray(np.asarray(L0_attn_x, dtype=np.float32))
    D = c.D
    wqkv = np.asarray(L0_attn_w_qkv, dtype=np.float32)

    def tile_lhsT(wT, n_m):
        arr = wT.reshape(c.DC, P, n_m, P)
        return np.ascontiguousarray(arr.transpose(2, 1, 0, 3))

    wqk = tile_lhsT(wqkv[:, 0:2 * D], 2 * c.DC)
    wv = np.ascontiguousarray(
        wqkv[:, 2 * D:3 * D].reshape(c.DC, P, c.ND2, c.NW).transpose(2, 0, 1, 3))
    bqkv = np.asarray(L0_attn_b_qkv, dtype=np.float32)
    bqk = np.ascontiguousarray(bqkv[0:2 * D].reshape(2 * c.DC, P))
    bv = np.ascontiguousarray(bqkv[2 * D:3 * D])
    wp = np.ascontiguousarray(np.asarray(L0_attn_w_proj, dtype=np.float32))
    bp = np.ascontiguousarray(np.asarray(L0_attn_b_proj, dtype=np.float32))
    wu = tile_lhsT(np.ascontiguousarray(np.asarray(L0_ffn_w_up, dtype=np.float32).T), c.HC)
    wg = tile_lhsT(np.ascontiguousarray(np.asarray(L0_ffn_w_gate, dtype=np.float32).T), c.HC)
    wd = np.ascontiguousarray(np.asarray(L0_ffn_w_down, dtype=np.float32).T)

    in_maps = []
    for core in range(c.n_cores):
        b, h = core // 2, core % 2
        xs = np.ascontiguousarray(x[b, h * c.T:(h + 1) * c.T, :])
        ci = np.array([2 * b, 2 * b + 1], dtype=np.int32)
        in_maps.append(dict(x0=xs, wqk=wqk, wv=wv, bqk=bqk, bv=bv, wp=wp, bp=bp,
                            wu=wu, wg=wg, wd=wd, coreinfo=ci))
    return in_maps


def build_program(cfg, dbg=None):
    b = Builder(cfg, dbg=dbg)
    b.emit()
    return b.finish()


def kernel(L0_attn_x, L0_attn_w_qkv, L0_attn_b_qkv, L0_attn_w_proj, L0_attn_b_proj,
           L0_ffn_w_gate, L0_ffn_w_up, L0_ffn_w_down):
    cfg = FULL
    in_maps = prep_inputs(cfg, L0_attn_x, L0_attn_w_qkv, L0_attn_b_qkv,
                          L0_attn_w_proj, L0_attn_b_proj, L0_ffn_w_gate,
                          L0_ffn_w_up, L0_ffn_w_down)
    nc = build_program(cfg)
    res = run_bass_kernel_spmd(nc, in_maps, core_ids=list(range(cfg.n_cores)))
    x = np.asarray(L0_attn_x)
    Bb, S, D = x.shape
    out = np.empty((Bb, S, D), dtype=np.float32)
    for core in range(cfg.n_cores):
        bb, h = core // 2, core % 2
        out[bb, h * cfg.T:(h + 1) * cfg.T, :] = res.results[core]["y"]
    return out


# version 05
# speedup vs baseline: 1.0398x; 1.0398x over previous
import numpy as np
from contextlib import ExitStack
from dataclasses import dataclass

import concourse.bass as bass
import concourse.mybir as mybir
from concourse.bass_utils import run_bass_kernel_spmd
from concourse.dve_spec import Spec, Src0, Src1, C0, C1, C2, Zero, One, maxx, minn, AluOp
from concourse.dve_spec import lower as dve_lower
from concourse.dve_ops import DveOp, OPS
import concourse.dve_ops as dve_ops_mod

F32 = mybir.dt.float32
I16 = mybir.dt.int16
I32 = mybir.dt.int32
AF = mybir.ActivationFunctionType
ALU = mybir.AluOpType
AX = mybir.AxisListType

MAGIC = float(np.float32(1.5 * 2.0 ** 23))


def _register_op(name, spec, reference):
    for op in OPS:
        if op.name == name:
            return op
    shas, perf_en = {}, {}
    for ver in ("v3", "v4"):
        try:
            shas[ver] = dve_lower(spec, ver=ver, perf_en=False, name=name).sha(ver)
            perf_en[ver] = False
        except Exception:
            pass
    op = DveOp(name, spec, subdim=False, uops_sha=shas, perf_en=perf_en,
               reference=reference)
    OPS.append(op)
    dve_ops_mod._SUB_OPCODE_FOR_NAME[name] = (
        dve_ops_mod._CUSTOM_DVE_ROW_BASE + len(OPS) - 1)
    dve_ops_mod.CUSTOM_DVE_REFS[name] = op.ref
    dve_ops_mod.CUSTOM_DVE_SPECS[name] = spec
    return op


RCS = _register_op(
    "MK_RCS",
    Spec(body=minn(maxx((Src0 * C0 + C2) - C2, Zero - (C1 + One)), C1)),
    lambda in0, in1, s0, s1, imm2: np.clip(np.rint(in0 * s0), -s1 - 1.0, s1),
)
SUBMUL = _register_op(
    "MK_SUBMUL",
    Spec(body=(Src0 - C0) * Src1, accum=AluOp.MAX),
    lambda in0, in1, s0, s1, imm2: (
        (in0 - s0) * in1,
        np.max((in0 - s0) * in1, axis=-1, keepdims=True),
    ),
)


@dataclass(frozen=True)
class Cfg:
    D: int = 1024
    H: int = 3072
    NH: int = 16
    T: int = 2048
    n_cores: int = 8
    eps: float = 1e-5
    stages: int = 3

    P = 128
    HDIM = 64

    @property
    def DC(self):
        return self.D // 128

    @property
    def HC(self):
        return self.H // 128

    @property
    def NTT(self):
        return self.T // 128

    @property
    def S(self):
        return 2 * self.T

    @property
    def NKT(self):
        return self.S // 128

    @property
    def TN(self):
        return min(512, self.T)

    @property
    def NTC(self):
        return self.T // self.TN

    @property
    def NHG(self):
        return self.NH // 2

    @property
    def NW(self):
        return min(512, self.D)

    @property
    def ND2(self):
        return self.D // self.NW


FULL = Cfg()
MINI = Cfg(D=256, H=768, NH=4, T=256)
MIDI = Cfg(D=1024, H=1536, NH=16, T=1024)


class Sem:
    def __init__(self, es, nc, name):
        self.h = es.enter_context(nc.semaphore(name))
        self.v = 0

    def inc(self, ins, n=1):
        ins.then_inc(self.h, n)
        self.v += n
        return self.v


class Builder:
    def __init__(self, cfg: Cfg, dbg=None):
        self.dbg = dbg
        self.ffn_i = 0
        self.dbg_done = False
        self.c = cfg
        nc = bass.Bass("TRN2", target_bir_lowering=False, debug=False)
        self.nc = nc
        self.uid = 0
        c, P = cfg, cfg.P
        self.x0_d = nc.dram_tensor("x0", [c.T, c.D], F32, kind="ExternalInput").ap()
        self.wqk_d = nc.dram_tensor("wqk", [2 * c.DC, P, c.DC, P], F32, kind="ExternalInput").ap()
        self.wv_d = nc.dram_tensor("wv", [c.ND2, c.DC, P, c.NW], F32, kind="ExternalInput").ap()
        self.bqk_d = nc.dram_tensor("bqk", [2 * c.DC, P], F32, kind="ExternalInput").ap()
        self.bv_d = nc.dram_tensor("bv", [c.D], F32, kind="ExternalInput").ap()
        self.wp_d = nc.dram_tensor("wp", [c.D, c.D], F32, kind="ExternalInput").ap()
        self.bp_d = nc.dram_tensor("bp", [c.D], F32, kind="ExternalInput").ap()
        self.wu_d = nc.dram_tensor("wu", [c.HC, P, c.DC, P], F32, kind="ExternalInput").ap()
        self.wg_d = nc.dram_tensor("wg", [c.HC, P, c.DC, P], F32, kind="ExternalInput").ap()
        self.wd_d = nc.dram_tensor("wd", [c.H, c.D], F32, kind="ExternalInput").ap()
        self.y_d = nc.dram_tensor("y", [c.T, c.D], F32, kind="ExternalOutput").ap()
        self.wuq_d = nc.dram_tensor("wuq", [c.HC, P, c.DC, P], F32).ap()
        self.wgq_d = nc.dram_tensor("wgq", [c.HC, P, c.DC, P], F32).ap()
        self.wdq_d = nc.dram_tensor("wdq", [c.H, c.D], F32).ap()
        self.mup_d = nc.dram_tensor("mup", [c.HC, P, c.T], F32).ap()
        self.mgate_d = nc.dram_tensor("mgate", [c.HC, P, c.T], F32).ap()
        self.kup_d = nc.dram_tensor("kup", [c.NTC, c.HC, P, c.TN], I16).ap()
        self.kgate_d = nc.dram_tensor("kgate", [c.NTC, c.HC, P, c.TN], I16).ap()
        self.kg2_d = nc.dram_tensor("kg2", [c.NTC, c.HC, P, c.TN], I16).ap()
        self.qT_d = nc.dram_tensor("qTd", [c.DC, P, c.T], F32).ap()
        self.kloc_d = nc.dram_tensor("kloc", [c.DC, P, c.T], F32).ap()
        self.vloc_d = nc.dram_tensor("vloc", [c.NTT, P, c.D], F32).ap()
        self.kpair_d = nc.dram_tensor("kpair", [2, c.DC, P, c.T], F32).ap()
        self.vpair_d = nc.dram_tensor("vpair", [2, c.NTT, P, c.D], F32).ap()
        self.lraw_d = nc.dram_tensor("lraw", [2 * 512], F32).ap()
        self.lrec_d = nc.dram_tensor("lrec", [2 * 512], F32).ap()
        self.ar_in = [nc.dram_tensor(f"arin{i}", [256], F32).ap() for i in range(8)]
        self.ar_out = [nc.dram_tensor(f"arout{i}", [256], F32).ap() for i in range(8)]
        self.n_ar = 0

        self.es = ExitStack()
        self.es.__enter__()
        es = self.es
        sb = lambda n, sh, dt=F32: es.enter_context(nc.sbuf_tensor(n, sh, dt))
        self.xres = sb("xres", [P, c.NTT * c.D])
        self.stat = sb("stat", [P, 8 * c.NTT])
        self.slots = sb("slots", [P, 256])
        self.scal = sb("scal", [P, 32])
        self.arb = sb("arb", [P, 256])
        self.ident = sb("ident", [P, P])
        self.bvb = sb("bvb", [P, c.D])
        self.bpb = sb("bpb", [P, c.D])
        self.bqks = sb("bqks", [P, 2 * c.DC])
        self.psA = es.enter_context(nc.psum_tensor("psA", [P, 2048], F32))
        self.psB = es.enter_context(nc.psum_tensor("psB", [P, 2048], F32))
        self.sy = Sem(es, nc, "sy")
        self.gd = Sem(es, nc, "gd")
        self.cc = Sem(es, nc, "cc")
        self.tm = Sem(es, nc, "tm")
        self.vv = Sem(es, nc, "vv")
        self.ss = Sem(es, nc, "ss")
        self.gg = Sem(es, nc, "gg")
        self.wsl = [Sem(es, nc, "wsl0"), Sem(es, nc, "wsl1")]
        self.isl = [Sem(es, nc, "isl0"), Sem(es, nc, "isl1")]
        self.osl = [Sem(es, nc, "osl0"), Sem(es, nc, "osl1")]

    def sc(self, i):
        return self.scal[:, i:i + 1]

    def name(self, stem):
        self.uid += 1
        return f"{stem}{self.uid}"

    def pslot(self, i, width):
        t = self.psA if i % 2 == 0 else self.psB
        off = ((i // 2) % 2) * 1024
        return t[:, off:off + width]

    def vstep(self, v, ins):
        ins.then_inc(self.vv.h, 1)
        self.vv.v += 1
        v.wait_ge(self.vv.h, self.vv.v)

    def finish(self):
        self.es.__exit__(None, None, None)
        return self.nc

    def emit(self):
        c, nc = self.c, self.nc
        with nc.Block() as blk:
            @blk.sync
            def _(sy):
                x0v = self.x0_d.rearrange("(t p) d -> t p d", p=c.P)
                for t in range(c.NTT):
                    self.sy.inc(sy.dma_start(out=self.xres[:, t * c.D:(t + 1) * c.D], in_=x0v[t]), 16)
                self.sy.inc(sy.dma_start(out=self.bvb[:], in_=self.bv_d.unsqueeze(0).to_broadcast([c.P, c.D])), 16)
                self.sy.inc(sy.dma_start(out=self.bpb[:], in_=self.bp_d.unsqueeze(0).to_broadcast([c.P, c.D])), 16)
                idd = nc.inline_tensor(np.eye(c.P, dtype=np.float32), "identnp").ap()
                self.sy.inc(sy.dma_start(out=self.ident[:], in_=idd), 16)
                for m in range(2 * c.DC):
                    self.sy.inc(sy.dma_start(out=self.bqks[:, m:m + 1], in_=self.bqk_d[m].unsqueeze(1)), 16)
                sy.wait_ge(self.sy.h, self.sy.v)

        self.emit_dequant()
        self.emit_ffn()
        if c.stages >= 2 and not self.dbg_done:
            self.emit_attention()
        if c.stages >= 3 and not self.dbg_done:
            self.emit_ffn()

        if not self.dbg_done:
            with nc.Block() as blk:
                @blk.sync
                def _(sy):
                    yv = self.y_d.rearrange("(t p) d -> t p d", p=c.P)
                    for t in range(c.NTT):
                        self.sy.inc(sy.dma_start(out=yv[t], in_=self.xres[:, t * c.D:(t + 1) * c.D]), 16)
                    sy.wait_ge(self.sy.h, self.sy.v)

    def emit_dequant(self):
        c, s, nc = self.c, self, self.nc
        P = c.P
        W = 512 if c.D >= 512 else c.D
        es2 = ExitStack()
        wa = es2.enter_context(nc.sbuf_tensor(self.name("dqa"), [P, 2 * W], F32))
        wt = es2.enter_context(nc.sbuf_tensor(self.name("dqt"), [P, W], F32))
        bm = es2.enter_context(nc.sbuf_tensor(self.name("dqm"), [P, W // 2], F32))
        ki = es2.enter_context(nc.sbuf_tensor(self.name("dqi"), [P, W], I16))
        nb = W // 32

        tiles = []
        for wsrc, wdst in [(s.wu_d, s.wuq_d), (s.wg_d, s.wgq_d)]:
            for m in range(c.HC):
                for kc in range(c.DC):
                    v4 = wsrc[m, :, kc, :]
                    d4 = wdst[m, :, kc, :]
                    for j in range(0, P, W):
                        w_ = min(W, P - j)
                        tiles.append((v4[:, j:j + w_], d4[:, j:j + w_], w_))
        wdv_s = s.wd_d.rearrange("(hc p) d -> hc p d", p=P)
        wdv_q = s.wdq_d.rearrange("(hc p) d -> hc p d", p=P)
        for hc in range(c.HC):
            for j in range(0, c.D, W):
                w_ = min(W, c.D - j)
                tiles.append((wdv_s[hc, :, j:j + w_], wdv_q[hc, :, j:j + w_], w_))

        with nc.Block() as blk:
            n = len(tiles)
            VPT = 7
            ld_evt = []
            st_evt = []
            vvb = s.vv.v

            @blk.sync
            def _(sy):
                for i in range(n + 1):
                    if i < n:
                        src_ap, dst_ap, w_ = tiles[i]
                        if i >= 2:
                            sy.wait_ge(s.vv.h, vvb + (i - 1) * VPT)
                            sy.wait_ge(s.osl[i % 2].h, st_evt[i - 2])
                        il = s.isl[i % 2]
                        il.inc(sy.dma_start(out=wa[:, (i % 2) * W:(i % 2) * W + w_], in_=src_ap), 16)
                        ld_evt.append(il.v)
                    if i >= 1:
                        j = i - 1
                        sy.wait_ge(s.vv.h, vvb + (j + 1) * VPT)
                        ol = s.osl[j % 2]
                        ol.inc(sy.dma_start(out=tiles[j][1],
                                            in_=wa[:, (j % 2) * W:(j % 2) * W + tiles[j][2]]), 16)
                        st_evt.append(ol.v)
                sy.wait_ge(s.osl[0].h, s.osl[0].v)
                sy.wait_ge(s.osl[1].h, s.osl[1].v)

            @blk.vector
            def _(v):
                for i, (src_ap, dst_ap, w_) in enumerate(tiles):
                    nb_ = w_ // 32
                    o = (i % 2) * W
                    v.wait_ge(s.isl[i % 2].h, ld_evt[i])
                    self.vstep(v, v.transpose(out=wt[:, 0:w_], in_=wa[:, o:o + w_]))
                    self.vstep(v, v.reduce_max(
                        out=bm[:, 0:nb_],
                        in_=wt[:, 0:w_].rearrange("p (g e) -> p g e", e=32),
                        axis=AX.X, apply_absolute_value=True))
                    self.vstep(v, v.tensor_scalar(out=bm[:, 0:nb_], in0=bm[:, 0:nb_],
                                                  scalar1=1.0 / 7.0, scalar2=1e-12,
                                                  op0=ALU.mult, op1=ALU.max))
                    self.vstep(v, v.reciprocal(out=bm[:, nb:nb + nb_], in_=bm[:, 0:nb_]))
                    self.vstep(v, v.tensor_tensor(
                        out=ki[:, 0:w_].rearrange("p (g e) -> p g e", e=32),
                        in0=wt[:, 0:w_].rearrange("p (g e) -> p g e", e=32),
                        in1=bm[:, nb:nb + nb_].rearrange("p (g o) -> p g o", o=1).to_broadcast([P, nb_, 32]),
                        op=ALU.mult))
                    self.vstep(v, v.tensor_tensor(
                        out=wt[:, 0:w_].rearrange("p (g e) -> p g e", e=32),
                        in0=ki[:, 0:w_].rearrange("p (g e) -> p g e", e=32),
                        in1=bm[:, 0:nb_].rearrange("p (g o) -> p g o", o=1).to_broadcast([P, nb_, 32]),
                        op=ALU.mult))
                    self.vstep(v, v.transpose(out=wa[:, o:o + w_], in_=wt[:, 0:w_]))
        es2.close()

    def emit_ln_stats(self, lnsc):
        c, s = self.c, self
        N = c.NTT
        nc = self.nc
        with nc.Block() as blk:
            @blk.vector
            def _(v):
                for t in range(N):
                    xt = s.xres[:, t * c.D:(t + 1) * c.D]
                    ins = v.reduce_sum(out=s.stat[:, t:t + 1], in_=xt, axis=AX.X)
                s.vv.inc(ins)

            @blk.scalar
            def _(sc):
                ssb = s.ss.v
                for t in range(N):
                    xt = s.xres[:, t * c.D:(t + 1) * c.D]
                    if t >= 1:
                        sc.wait_ge(s.ss.h, ssb + t)
                    sc.activation(out=lnsc[:, 0:c.D], in_=xt, func=AF.Square,
                                  accum_out=s.stat[:, N + t:N + t + 1]).then_inc(s.ss.h, 1)
                    s.ss.v += 1

        with nc.Block() as blk:
            @blk.vector
            def _(v):
                st = s.stat
                self.vstep(v, v.tensor_scalar(out=st[:, 2 * N:3 * N], in0=st[:, 0:N],
                                              scalar1=1.0 / c.D, scalar2=None, op0=ALU.mult))
                self.vstep(v, v.scalar_tensor_tensor(out=st[:, 3 * N:4 * N], in0=st[:, 2 * N:3 * N],
                                                     scalar=-1.0, in1=st[:, 2 * N:3 * N],
                                                     op0=ALU.mult, op1=ALU.mult))
                self.vstep(v, v.scalar_tensor_tensor(out=st[:, 3 * N:4 * N], in0=st[:, N:2 * N],
                                                     scalar=1.0 / c.D, in1=st[:, 3 * N:4 * N],
                                                     op0=ALU.mult, op1=ALU.add))
                self.vstep(v, v.tensor_scalar(out=st[:, 3 * N:4 * N], in0=st[:, 3 * N:4 * N],
                                              scalar1=c.eps, scalar2=None, op0=ALU.add))
                s.vv.inc(v.engine_nop())

            @blk.scalar
            def _(sc):
                sc.wait_ge(s.vv.h, s.vv.v)
                st = s.stat
                s.ss.inc(sc.activation(out=st[:, 4 * N:5 * N], in_=st[:, 3 * N:4 * N],
                                       func=AF.Sqrt))

            @blk.vector
            def _(v):
                v.wait_ge(s.ss.h, s.ss.v)
                st = s.stat
                self.vstep(v, v.reciprocal(out=st[:, 5 * N:6 * N], in_=st[:, 4 * N:5 * N]))
                self.vstep(v, v.tensor_mul(out=st[:, 6 * N:7 * N], in0=st[:, 5 * N:6 * N],
                                           in1=st[:, 5 * N:6 * N]))
                self.vstep(v, v.tensor_mul(out=st[:, 6 * N:7 * N], in0=st[:, 6 * N:7 * N],
                                           in1=st[:, 3 * N:4 * N]))
                self.vstep(v, v.tensor_scalar(out=st[:, 6 * N:7 * N], in0=st[:, 6 * N:7 * N],
                                              scalar1=-0.5, scalar2=1.5, op0=ALU.mult, op1=ALU.add))
                self.vstep(v, v.tensor_mul(out=st[:, 4 * N:5 * N], in0=st[:, 5 * N:6 * N],
                                           in1=st[:, 6 * N:7 * N]))
                s.vv.inc(v.engine_nop())

    def emit_armax(self, ranges, outs):
        c, s, nc = self.c, self, self.nc
        idx = self.n_ar % 8
        self.n_ar += 1
        k = len(ranges)
        with nc.Block() as blk:
            @blk.vector
            def _(v):
                for i, (st0, n) in enumerate(ranges):
                    ins = v.reduce_max(out=s.scal[:, 30 - i:31 - i],
                                       in_=s.slots[:, st0:st0 + n], axis=AX.X)
                s.vv.inc(ins)

            @blk.gpsimd
            def _(g):
                g.wait_ge(s.vv.h, s.vv.v)
                for i in range(k):
                    s.gd.inc(g.dma_start(out=s.ar_in[idx][i * 128:(i + 1) * 128],
                                         in_=s.scal[:, 30 - i:31 - i]), 16)
                g.wait_ge(s.gd.h, s.gd.v)
                s.cc.inc(g.collective_compute(
                    "AllReduce", ALU.max, replica_groups=[list(range(c.n_cores))],
                    ins=[s.ar_in[idx][0:k * 128].opt()],
                    outs=[s.ar_out[idx][0:k * 128].opt()]), 1)
                g.wait_ge(s.cc.h, s.cc.v)
                for i in range(k):
                    s.gd.inc(g.dma_start(
                        out=s.arb[:, i * 128:(i + 1) * 128],
                        in_=s.ar_out[idx][i * 128:(i + 1) * 128].unsqueeze(0).to_broadcast([c.P, 128])), 16)

            gdv = s.gd.v

            @blk.vector
            def _(v):
                v.wait_ge(s.gd.h, gdv)
                for i in range(k):
                    ins = v.reduce_max(out=outs[i], in_=s.arb[:, i * 128:(i + 1) * 128], axis=AX.X)
                s.vv.inc(ins)

    def dump(self, ap, nelem):
        nc, s = self.nc, self
        yflat = self.y_d.rearrange("a b -> (a b)")
        with nc.Block() as blk:
            @blk.sync
            def _(sy):
                s.sy.inc(sy.dma_start(out=yflat[0:nelem], in_=ap), 16)
                sy.wait_ge(s.sy.h, s.sy.v)
        self.dbg_done = True

    def dbg_hit(self, tag):
        return (not self.dbg_done) and self.dbg is not None and \
            self.dbg[0] == tag and self.dbg[1] == self.ffn_i

    def emit_ffn(self):
        c, s, nc = self.c, self, self.nc
        self.ffn_i += 1
        P, N = c.P, c.NTT
        es2 = ExitStack()
        esln = ExitStack()
        fmA = es2.enter_context(nc.sbuf_tensor(self.name("fmA"), [P, c.DC * c.T], F32))
        lnsc = esln.enter_context(nc.sbuf_tensor(self.name("lnsc"), [P, 2 * c.D], F32))
        xlnh = esln.enter_context(nc.sbuf_tensor(self.name("xlnh"), [P, max(1, N // 2) * c.D], F32))

        self.emit_ln_stats(lnsc)

        with nc.Block() as blk:
            @blk.vector
            def _(v):
                for t in range(N):
                    self.vstep(v, v.scalar_tensor_tensor(
                        out=lnsc[:, 0:c.D],
                        in0=s.xres[:, t * c.D:(t + 1) * c.D],
                        scalar=s.stat[:, 2 * N + t:2 * N + t + 1],
                        in1=s.stat[:, 4 * N + t:4 * N + t + 1].to_broadcast([c.P, c.D]),
                        op0=ALU.subtract, op1=ALU.mult))
                    self.vstep(v, v.reduce_max(out=s.slots[:, t:t + 1], in_=lnsc[:, 0:c.D],
                                               axis=AX.X, apply_absolute_value=True))

        self.emit_armax([(0, N)], [self.sc(25)])
        with nc.Block() as blk:
            @blk.vector
            def _(v):
                v.wait_ge(s.vv.h, s.vv.v)
                self.vstep(v, v.tensor_scalar(out=self.sc(1), in0=self.sc(25), scalar1=1.0 / 32767.0,
                                              scalar2=1e-12, op0=ALU.mult, op1=ALU.max))
                self.vstep(v, v.reciprocal(out=self.sc(2), in_=self.sc(1)))
                s.vv.inc(v.engine_nop())

        self.emit_lnT(fmA, xlnh, rcs_scale=self.sc(2))
        esln.close()
        if self.dbg_hit("k1T"):
            self.dump(fmA[:], c.P * c.DC * c.T)
            es2.close()
            return

        self.emit_upgate(fmA)
        es2.close()

        self.emit_armax([(0, c.HC * c.NTC), (128, c.HC * c.NTC)],
                        [self.sc(30), self.sc(29)])
        with nc.Block() as blk:
            @blk.vector
            def _(v):
                v.wait_ge(s.vv.h, s.vv.v)
                self.vstep(v, v.scalar_tensor_tensor(out=self.sc(20), in0=self.sc(30),
                                                     scalar=1.0 / 32767.0, in1=self.sc(1),
                                                     op0=ALU.mult, op1=ALU.mult))
                self.vstep(v, v.tensor_scalar(out=self.sc(20), in0=self.sc(20), scalar1=1e-12,
                                              scalar2=None, op0=ALU.max))
                self.vstep(v, v.reciprocal(out=self.sc(28), in_=self.sc(20)))
                self.vstep(v, v.tensor_mul(out=self.sc(4), in0=self.sc(28), in1=self.sc(1)))
                self.vstep(v, v.scalar_tensor_tensor(out=self.sc(21), in0=self.sc(29),
                                                     scalar=1.0 / 32767.0, in1=self.sc(1),
                                                     op0=ALU.mult, op1=ALU.mult))
                self.vstep(v, v.tensor_scalar(out=self.sc(21), in0=self.sc(21), scalar1=1e-12,
                                              scalar2=None, op0=ALU.max))
                self.vstep(v, v.reciprocal(out=self.sc(28), in_=self.sc(21)))
                self.vstep(v, v.tensor_mul(out=self.sc(5), in0=self.sc(28), in1=self.sc(1)))
                self.vstep(v, v.tensor_copy(out=self.sc(6), in_=self.sc(21)))
                s.vv.inc(v.engine_nop())

        if self.dbg_hit("scal_ar1"):
            self.dump(s.scal[:], c.P * 32)
            return
        self.emit_f3()
        self.emit_armax([(0, c.HC)], [self.sc(30)])
        with nc.Block() as blk:
            @blk.vector
            def _(v):
                v.wait_ge(s.vv.h, s.vv.v)
                self.vstep(v, v.scalar_tensor_tensor(out=self.sc(22), in0=self.sc(6),
                                                     scalar=1.0 / 65536.0, in1=self.sc(30),
                                                     op0=ALU.mult, op1=ALU.mult))
                self.vstep(v, v.tensor_scalar(out=self.sc(22), in0=self.sc(22),
                                              scalar1=1.0 / 32767.0, scalar2=1e-12,
                                              op0=ALU.mult, op1=ALU.max))
                self.vstep(v, v.reciprocal(out=self.sc(28), in_=self.sc(22)))
                self.vstep(v, v.scalar_tensor_tensor(out=self.sc(7), in0=self.sc(6),
                                                     scalar=1.0 / 65536.0, in1=self.sc(28),
                                                     op0=ALU.mult, op1=ALU.mult))
                self.vstep(v, v.tensor_mul(out=self.sc(8), in0=self.sc(22), in1=self.sc(20)))
                s.vv.inc(v.engine_nop())

        if self.dbg_hit("scal_ar2"):
            self.dump(s.scal[:], c.P * 32)
            return
        self.emit_f4()
        self.emit_armax([(0, c.HC)], [self.sc(30)])
        with nc.Block() as blk:
            @blk.vector
            def _(v):
                v.wait_ge(s.vv.h, s.vv.v)
                self.vstep(v, v.scalar_tensor_tensor(out=self.sc(23), in0=self.sc(30),
                                                     scalar=1.0 / 32767.0, in1=self.sc(8),
                                                     op0=ALU.mult, op1=ALU.mult))
                self.vstep(v, v.tensor_scalar(out=self.sc(23), in0=self.sc(23), scalar1=1e-12,
                                              scalar2=None, op0=ALU.max))
                self.vstep(v, v.reciprocal(out=self.sc(28), in_=self.sc(23)))
                self.vstep(v, v.tensor_mul(out=self.sc(9), in0=self.sc(28), in1=self.sc(8)))
                self.vstep(v, v.tensor_scalar(out=self.sc(10), in0=self.sc(23), scalar1=0.5,
                                              scalar2=None, op0=ALU.mult))
                s.vv.inc(v.engine_nop())

        if self.dbg_hit("scal_ar3"):
            self.dump(s.scal[:], c.P * 32)
            return
        self.emit_f5()

    def emit_lnT(self, fmA, xlnh, rcs_scale):
        c, s, nc = self.c, self, self.nc
        P, N = c.P, c.NTT
        nh = max(1, N // 2)
        nbank = 4
        W = c.DC * P
        EV = 3 if rcs_scale is not None else 1
        esl = ExitStack()
        tmpf = esl.enter_context(nc.sbuf_tensor(self.name("ltf"), [P, W], F32))
        tmpi = esl.enter_context(nc.sbuf_tensor(self.name("lti"), [P, W], I16))
        for half in range(N // nh):
            t0 = half * nh
            with nc.Block() as blk:
                @blk.vector
                def _(v, t0=t0):
                    tmb = s.tm.v
                    for i in range(nh):
                        t = t0 + i
                        v.scalar_tensor_tensor(
                            out=xlnh[:, i * c.D:(i + 1) * c.D],
                            in0=s.xres[:, t * c.D:(t + 1) * c.D],
                            scalar=s.stat[:, 2 * N + t:2 * N + t + 1],
                            in1=s.stat[:, 4 * N + t:4 * N + t + 1].to_broadcast([P, c.D]),
                            op0=ALU.subtract, op1=ALU.mult).then_inc(s.vv.h, 1)
                        s.vv.v += 1
                    for i in range(nh):
                        t = t0 + i
                        v.wait_ge(s.tm.h, tmb + i + 1)
                        bank = self.pslot(i % nbank, W)
                        o = fmA.rearrange("p (cc tt) -> p cc tt", cc=c.DC)
                        if rcs_scale is not None:
                            self.vstep(v, v.tensor_scalar(out=tmpf[:], in0=bank,
                                                          scalar1=rcs_scale, scalar2=None,
                                                          op0=ALU.mult))
                            self.vstep(v, v.tensor_copy(out=tmpi[:], in_=tmpf[:]))
                            v.tensor_copy(out=o[:, :, t * P:(t + 1) * P],
                                          in_=tmpi[:].rearrange("p (cc tt) -> p cc tt", cc=c.DC)
                                          ).then_inc(s.vv.h, 1)
                            s.vv.v += 1
                        else:
                            v.tensor_copy(out=o[:, :, t * P:(t + 1) * P], in_=bank
                                          ).then_inc(s.vv.h, 1)
                            s.vv.v += 1

                vvb = s.vv.v - (1 + EV) * nh

                @blk.tensor
                def _(te, t0=t0, vvb=vvb):
                    for i in range(nh):
                        te.wait_ge(s.vv.h, vvb + i + 1)
                        if i >= nbank:
                            te.wait_ge(s.vv.h, vvb + nh + EV * (i - nbank) + 1)
                        bank = self.pslot(i % nbank, W)
                        for cc in range(c.DC):
                            ti = te.transpose(bank[:, cc * P:(cc + 1) * P],
                                              xlnh[:, i * c.D + cc * P:i * c.D + (cc + 1) * P],
                                              s.ident[:])
                        s.tm.inc(ti)
        esl.close()

    def emit_upgate(self, fmA):
        c, s, nc = self.c, self, self.nc
        P = c.P
        es2 = ExitStack()
        wbuf = es2.enter_context(nc.sbuf_tensor(self.name("wb"), [P, 2 * c.DC * P], F32))
        mst = es2.enter_context(nc.sbuf_tensor(self.name("ms"), [P, 2 * c.T], F32))
        nb = min(4, 2048 // c.TN)

        for wd_, md_, slot0 in [(s.wuq_d, s.mup_d, 0), (s.wgq_d, s.mgate_d, 128)]:
            with nc.Block() as blk:
                tmb0 = s.tm.v
                vvb0 = s.vv.v
                oslb = [s.osl[0].v, s.osl[1].v]
                wevt = []
                sevt = []

                @blk.sync
                def _(sy, wd_=wd_, md_=md_):
                    for M in range(c.HC + 1):
                        if M < c.HC:
                            if M >= 2:
                                sy.wait_ge(s.tm.h, tmb0 + (M - 1) * c.NTC)
                            w = s.wsl[M % 2]
                            w.inc(sy.dma_start(
                                out=wbuf[:, (M % 2) * c.DC * P:(M % 2 + 1) * c.DC * P],
                                in_=wd_[M]), 16)
                            wevt.append(w.v)
                        if M >= 1:
                            sy.wait_ge(s.vv.h, vvb0 + 2 * M * c.NTC)
                            o = s.osl[(M - 1) % 2]
                            o.inc(sy.dma_start(
                                out=md_[M - 1],
                                in_=mst[:, ((M - 1) % 2) * c.T:((M - 1) % 2 + 1) * c.T]), 16)
                            sevt.append(o.v)

                @blk.tensor
                def _(te):
                    for M in range(c.HC):
                        w = s.wsl[M % 2]
                        te.wait_ge(w.h, wevt[M])
                        for n in range(c.NTC):
                            g = M * c.NTC + n
                            if g >= nb:
                                te.wait_ge(s.vv.h, vvb0 + 2 * (g - nb) + 1)
                            bank = self.pslot(g % nb, c.TN)
                            for kc in range(c.DC):
                                ti = te.matmul(
                                    bank,
                                    wbuf[:, (M % 2) * c.DC * P + kc * P:(M % 2) * c.DC * P + (kc + 1) * P],
                                    fmA[:, kc * c.T + n * c.TN:kc * c.T + (n + 1) * c.TN],
                                    start=(kc == 0), stop=(kc == c.DC - 1))
                            s.tm.inc(ti)

                @blk.vector
                def _(v, slot0=slot0):
                    v.wait_ge(s.osl[0].h, oslb[0])
                    v.wait_ge(s.osl[1].h, oslb[1])
                    for M in range(c.HC):
                        if M >= 2:
                            o = s.osl[M % 2]
                            v.wait_ge(o.h, sevt[M - 2])
                        for n in range(c.NTC):
                            g = M * c.NTC + n
                            v.wait_ge(s.tm.h, tmb0 + g + 1)
                            bank = self.pslot(g % nb, c.TN)
                            dst = mst[:, (M % 2) * c.T + n * c.TN:(M % 2) * c.T + (n + 1) * c.TN]
                            self.vstep(v, v.tensor_scalar(
                                out=dst, in0=bank, scalar1=1.0, scalar2=None, op0=ALU.mult))
                            v.reduce_max(out=s.slots[:, slot0 + g:slot0 + g + 1],
                                         in_=dst, axis=AX.X,
                                         apply_absolute_value=True).then_inc(s.vv.h, 1)
                            s.vv.v += 1
        with nc.Block() as blk:
            @blk.sync
            def _(sy):
                sy.wait_ge(s.osl[0].h, s.osl[0].v)
                sy.wait_ge(s.osl[1].h, s.osl[1].v)
        es2.close()

    def emit_f3(self):
        c, s, nc = self.c, self, self.nc
        P = c.P
        es2 = ExitStack()
        mst = es2.enter_context(nc.sbuf_tensor(self.name("f3m"), [P, 4 * c.T], F32))
        kst = es2.enter_context(nc.sbuf_tensor(self.name("f3k"), [P, 4 * c.T], I16))
        sg = es2.enter_context(nc.sbuf_tensor(self.name("f3s"), [P, 2 * c.T], F32))
        jj = es2.enter_context(nc.sbuf_tensor(self.name("f3j"), [P, 2 * c.T], F32))
        pp = es2.enter_context(nc.sbuf_tensor(self.name("f3p"), [P, c.T], F32))
        tpf = es2.enter_context(nc.sbuf_tensor(self.name("f3t"), [P, c.T], F32))
        ju16 = es2.enter_context(nc.sbuf_tensor(self.name("f3u"), [P, c.T], mybir.dt.uint16))

        with nc.Block() as blk:
            vvb0 = s.vv.v
            ssb0 = s.ss.v
            VPC = 9
            ievt = []
            oevt = []

            @blk.sync
            def _(sy):
                for cc in range(c.HC + 1):
                    if cc < c.HC:
                        sl = cc % 2
                        if cc >= 2:
                            sy.wait_ge(s.vv.h, vvb0 + (cc - 2) * VPC + 3)
                        i = s.isl[sl]
                        i.inc(sy.dma_start(out=mst[:, sl * c.T:(sl + 1) * c.T], in_=s.mup_d[cc]), 16)
                        i.inc(sy.dma_start(out=mst[:, (2 + sl) * c.T:(3 + sl) * c.T], in_=s.mgate_d[cc]), 16)
                        ievt.append(i.v)
                    if cc >= 1:
                        sl = (cc - 1) % 2
                        sy.wait_ge(s.vv.h, vvb0 + (cc - 1) * VPC + 4)
                        o = s.osl[sl]
                        for n in range(c.NTC):
                            o.inc(sy.dma_start(
                                out=s.kup_d[n, cc - 1],
                                in_=kst[:, sl * c.T + n * c.TN:sl * c.T + (n + 1) * c.TN]), 16)
                            o.inc(sy.dma_start(
                                out=s.kgate_d[n, cc - 1],
                                in_=kst[:, (2 + sl) * c.T + n * c.TN:(2 + sl) * c.T + (n + 1) * c.TN]), 16)
                        oevt.append(o.v)

            @blk.vector
            def _(v):
                for cc in range(c.HC):
                    sl = cc % 2
                    v.wait_ge(s.isl[sl].h, ievt[cc])
                    if cc >= 2:
                        v.wait_ge(s.osl[sl].h, oevt[cc - 2])
                    self.vstep(v, v.tensor_scalar(out=tpf[:], in0=mst[:, sl * c.T:(sl + 1) * c.T],
                                                  scalar1=self.sc(4), scalar2=None, op0=ALU.mult))
                    self.vstep(v, v.tensor_copy(out=kst[:, sl * c.T:(sl + 1) * c.T], in_=tpf[:]))
                    self.vstep(v, v.tensor_scalar(out=tpf[:], in0=mst[:, (2 + sl) * c.T:(3 + sl) * c.T],
                                                  scalar1=self.sc(5), scalar2=None, op0=ALU.mult))
                    self.vstep(v, v.tensor_copy(out=kst[:, (2 + sl) * c.T:(3 + sl) * c.T], in_=tpf[:]))
                    v.wait_ge(s.ss.h, ssb0 + cc + 1)
                    self.vstep(v, v.tensor_scalar(out=tpf[:], in0=sg[:, sl * c.T:(sl + 1) * c.T],
                                                  scalar1=65536.0, scalar2=None, op0=ALU.mult))
                    self.vstep(v, v.tensor_copy(out=ju16[:], in_=tpf[:]))
                    self.vstep(v, v.tensor_copy(out=jj[:, sl * c.T:(sl + 1) * c.T], in_=ju16[:]))
                    self.vstep(v, v.tensor_mul(out=pp[:], in0=kst[:, (2 + sl) * c.T:(3 + sl) * c.T],
                                               in1=jj[:, sl * c.T:(sl + 1) * c.T]))
                    self.vstep(v, v.reduce_max(out=s.slots[:, cc:cc + 1], in_=pp[:], axis=AX.X,
                                               apply_absolute_value=True))

            @blk.scalar
            def _(sc):
                for cc in range(c.HC):
                    sl = cc % 2
                    sc.wait_ge(s.vv.h, vvb0 + cc * VPC + 4)
                    if cc >= 2:
                        sc.wait_ge(s.ss.h, ssb0 + cc - 1)
                    sc.activation(out=sg[:, sl * c.T:(sl + 1) * c.T],
                                  in_=kst[:, (2 + sl) * c.T:(3 + sl) * c.T],
                                  func=AF.Sigmoid, scale=self.sc(6)).then_inc(s.ss.h, 1)
                    s.ss.v += 1

        with nc.Block() as blk:
            @blk.sync
            def _(sy):
                sy.wait_ge(s.osl[0].h, s.osl[0].v)
                sy.wait_ge(s.osl[1].h, s.osl[1].v)
        es2.close()

    def emit_f4(self):
        c, s, nc = self.c, self, self.nc
        P = c.P
        es2 = ExitStack()
        kin = es2.enter_context(nc.sbuf_tensor(self.name("f4a"), [P, 4 * c.T], I16))
        sg = es2.enter_context(nc.sbuf_tensor(self.name("f4s"), [P, 2 * c.T], F32))
        jj = es2.enter_context(nc.sbuf_tensor(self.name("f4j"), [P, c.T], F32))
        pp = es2.enter_context(nc.sbuf_tensor(self.name("f4p"), [P, c.T], F32))
        k2 = es2.enter_context(nc.sbuf_tensor(self.name("f4k"), [P, 2 * c.T], I16))
        tpf = es2.enter_context(nc.sbuf_tensor(self.name("f4t"), [P, c.T], F32))
        ju16 = es2.enter_context(nc.sbuf_tensor(self.name("f4u"), [P, c.T], mybir.dt.uint16))

        with nc.Block() as blk:
            vvb0 = s.vv.v
            ssb0 = s.ss.v
            VPC = 8
            ievt = []
            oevt = []

            @blk.sync
            def _(sy):
                for cc in range(c.HC + 1):
                    if cc < c.HC:
                        sl = cc % 2
                        if cc >= 2:
                            sy.wait_ge(s.vv.h, vvb0 + (cc - 2) * VPC + 7)
                        i = s.isl[sl]
                        for n in range(c.NTC):
                            i.inc(sy.dma_start(
                                out=kin[:, sl * c.T + n * c.TN:sl * c.T + (n + 1) * c.TN],
                                in_=s.kgate_d[n, cc]), 16)
                            i.inc(sy.dma_start(
                                out=kin[:, (2 + sl) * c.T + n * c.TN:(2 + sl) * c.T + (n + 1) * c.TN],
                                in_=s.kup_d[n, cc]), 16)
                        ievt.append(i.v)
                    if cc >= 1:
                        sl = (cc - 1) % 2
                        sy.wait_ge(s.vv.h, vvb0 + (cc - 1) * VPC + 6)
                        o = s.osl[sl]
                        for n in range(c.NTC):
                            o.inc(sy.dma_start(
                                out=s.kg2_d[n, cc - 1],
                                in_=k2[:, sl * c.T + n * c.TN:sl * c.T + (n + 1) * c.TN]), 16)
                        oevt.append(o.v)

            @blk.scalar
            def _(sc):
                for cc in range(c.HC):
                    sl = cc % 2
                    sc.wait_ge(s.isl[sl].h, ievt[cc])
                    if cc >= 2:
                        sc.wait_ge(s.ss.h, ssb0 + cc - 1)
                        sc.wait_ge(s.vv.h, vvb0 + (cc - 2) * VPC + 1)
                    sc.activation(out=sg[:, sl * c.T:(sl + 1) * c.T],
                                  in_=kin[:, sl * c.T:(sl + 1) * c.T],
                                  func=AF.Sigmoid, scale=self.sc(6)).then_inc(s.ss.h, 1)
                    s.ss.v += 1

            @blk.vector
            def _(v):
                for cc in range(c.HC):
                    sl = cc % 2
                    v.wait_ge(s.ss.h, ssb0 + cc + 1)
                    self.vstep(v, v.tensor_scalar(out=tpf[:], in0=sg[:, sl * c.T:(sl + 1) * c.T],
                                                  scalar1=65536.0, scalar2=None, op0=ALU.mult))
                    self.vstep(v, v.tensor_copy(out=ju16[:], in_=tpf[:]))
                    self.vstep(v, v.tensor_copy(out=jj[:], in_=ju16[:]))
                    self.vstep(v, v.tensor_mul(out=pp[:], in0=kin[:, sl * c.T:(sl + 1) * c.T],
                                               in1=jj[:]))
                    if cc >= 2:
                        v.wait_ge(s.osl[sl].h, oevt[cc - 2])
                    self.vstep(v, v.tensor_scalar(out=tpf[:], in0=pp[:],
                                                  scalar1=self.sc(7), scalar2=None, op0=ALU.mult))
                    self.vstep(v, v.tensor_copy(out=k2[:, sl * c.T:(sl + 1) * c.T], in_=tpf[:]))
                    self.vstep(v, v.tensor_mul(out=pp[:], in0=k2[:, sl * c.T:(sl + 1) * c.T],
                                               in1=kin[:, (2 + sl) * c.T:(3 + sl) * c.T]))
                    self.vstep(v, v.reduce_max(out=s.slots[:, cc:cc + 1], in_=pp[:], axis=AX.X,
                                               apply_absolute_value=True))

        with nc.Block() as blk:
            @blk.sync
            def _(sy):
                sy.wait_ge(s.osl[0].h, s.osl[0].v)
                sy.wait_ge(s.osl[1].h, s.osl[1].v)
        es2.close()

    def emit_f5(self):
        c, s, nc = self.c, self, self.nc
        if self.dbg_hit("xres_pre_f5"):
            self.dump(self.xres[:, 0:4 * c.D], c.P * 4 * c.D)
            return
        P = c.P
        ntile = c.TN // P
        es2 = ExitStack()
        k2st = es2.enter_context(nc.sbuf_tensor(self.name("f5a"), [P, c.HC * c.TN], I16))
        kust = es2.enter_context(nc.sbuf_tensor(self.name("f5b"), [P, c.HC * c.TN], I16))
        qko = es2.enter_context(nc.sbuf_tensor(self.name("f5q"), [P, c.HC * c.TN], F32))
        ko2 = qko
        wdb = es2.enter_context(nc.sbuf_tensor(self.name("f5w"), [P, 2 * c.D], F32))
        wdv = s.wdq_d.rearrange("(hc p) d -> hc p d", p=P)

        for tc in range(c.NTC):
            with nc.Block() as blk:
                iv = []

                @blk.sync
                def _(sy, tc=tc):
                    i = s.isl[0]
                    for cc in range(c.HC):
                        i.inc(sy.dma_start(out=k2st[:, cc * c.TN:(cc + 1) * c.TN],
                                           in_=s.kg2_d[tc, cc]), 16)
                        i.inc(sy.dma_start(out=kust[:, cc * c.TN:(cc + 1) * c.TN],
                                           in_=s.kup_d[tc, cc]), 16)
                    iv.append(i.v)

                @blk.vector
                def _(v):
                    v.wait_ge(s.isl[0].h, iv[0])
                    self.vstep(v, v.tensor_mul(out=qko[:], in0=k2st[:], in1=kust[:]))
                    self.vstep(v, v.tensor_scalar(out=qko[:], in0=qko[:], scalar1=self.sc(9),
                                                  scalar2=None, op0=ALU.mult))
                    self.vstep(v, v.tensor_copy(out=k2st[:], in_=qko[:]))
                    self.vstep(v, v.tensor_copy(out=ko2[:], in_=k2st[:]))

            if self.dbg_hit("ko_tc") and tc == self.dbg[2]:
                self.dump(qko[:], c.P * c.HC * c.TN)
                break
            slow = self.dbg is not None and self.dbg[0] == "f5slow" and self.dbg[1] == self.ffn_i
            with nc.Block() as blk:
                tmb0 = s.tm.v
                wevt = []

                @blk.sync
                def _(sy, tc=tc, slow=slow):
                    for cc in range(c.HC):
                        if slow:
                            if cc >= 1:
                                sy.wait_ge(s.tm.h, tmb0 + cc)
                            w = s.wsl[0]
                            w.inc(sy.dma_start(out=wdb[:, 0:c.D], in_=wdv[cc]), 16)
                            wevt.append(w.v)
                            continue
                        if cc >= 2:
                            sy.wait_ge(s.tm.h, tmb0 + (cc - 1))
                        w = s.wsl[cc % 2]
                        w.inc(sy.dma_start(out=wdb[:, (cc % 2) * c.D:(cc % 2 + 1) * c.D],
                                           in_=wdv[cc]), 16)
                        wevt.append(w.v)

                @blk.tensor
                def _(te, tc=tc, slow=slow):
                    for cc in range(c.HC):
                        wsl_i = 0 if slow else cc % 2
                        woff = 0 if slow else (cc % 2) * c.D
                        te.wait_ge(s.wsl[wsl_i].h, wevt[cc])
                        for t in range(ntile):
                            bank = s.psA if t % 2 == 0 else s.psB
                            boff = (t // 2) * c.D
                            for nn in range(c.ND2):
                                ti = te.matmul(
                                    bank[:, boff + nn * c.NW:boff + (nn + 1) * c.NW],
                                    ko2[:, cc * c.TN + t * P:cc * c.TN + (t + 1) * P],
                                    wdb[:, woff + nn * c.NW:woff + (nn + 1) * c.NW],
                                    start=(cc == 0), stop=(cc == c.HC - 1))
                        s.tm.inc(ti)

                @blk.vector
                def _(v, tc=tc):
                    v.wait_ge(s.tm.h, s.tm.v)
                    abt = (self.dbg is not None and self.dbg[0] == "abtest"
                           and self.dbg[1] == self.ffn_i and tc == 0)
                    for t in range(ntile):
                        bank = s.psA if t % 2 == 0 else s.psB
                        boff = (t // 2) * c.D
                        gt = tc * ntile + t
                        if abt:
                            self.vstep(v, v.tensor_scalar(
                                out=qko[:, t * c.D:(t + 1) * c.D],
                                in0=bank[:, boff:boff + c.D], scalar1=self.sc(10),
                                scalar2=None, op0=ALU.mult))
                            self.vstep(v, v.scalar_tensor_tensor(
                                out=qko[:, (4 + t) * c.D:(5 + t) * c.D],
                                in0=bank[:, boff:boff + c.D], scalar=self.sc(10),
                                in1=s.xres[:, gt * c.D:(gt + 1) * c.D],
                                op0=ALU.mult, op1=ALU.add))
                            self.vstep(v, v.scalar_tensor_tensor(
                                out=s.xres[:, gt * c.D:(gt + 1) * c.D],
                                in0=bank[:, boff:boff + c.D], scalar=self.sc(10),
                                in1=s.xres[:, gt * c.D:(gt + 1) * c.D],
                                op0=ALU.mult, op1=ALU.add))
                            continue
                        if self.dbg is not None and self.dbg[0] == "mdown" and self.dbg[1] == self.ffn_i:
                            self.vstep(v, v.tensor_scalar(
                                out=s.xres[:, gt * c.D:(gt + 1) * c.D],
                                in0=bank[:, boff:boff + c.D], scalar1=self.sc(10),
                                scalar2=None, op0=ALU.mult))
                        else:
                            self.vstep(v, v.scalar_tensor_tensor(
                                out=s.xres[:, gt * c.D:(gt + 1) * c.D],
                                in0=bank[:, boff:boff + c.D], scalar=self.sc(10),
                                in1=s.xres[:, gt * c.D:(gt + 1) * c.D],
                                op0=ALU.mult, op1=ALU.add))
            if self.dbg is not None and self.dbg[0] == "mdown" and self.dbg[1] == self.ffn_i and tc == self.dbg[2] and not self.dbg_done:
                self.dump(s.xres[:, 0:4 * c.D], c.P * 4 * c.D)
                break
            if self.dbg is not None and self.dbg[0] == "abtest" and self.dbg[1] == self.ffn_i and tc == 0 and not self.dbg_done:
                self.dump(qko[:, 0:8 * c.D], c.P * 8 * c.D)
                break
            if self.dbg is not None and self.dbg[0] == "f5slow" and self.dbg[1] == self.ffn_i and tc == 0 and not self.dbg_done:
                self.dump(s.xres[:, 0:4 * c.D], c.P * 4 * c.D)
                break
            if self.dbg_hit("after_f5_tc") and tc == self.dbg[2]:
                self.dump(s.xres[:, 0:4 * c.D], c.P * 4 * c.D)
                break
        es2.close()

    def emit_attention(self):
        c, s, nc = self.c, self, self.nc
        P, N = c.P, c.NTT
        es2 = ExitStack()
        esln = ExitStack()
        fmA = es2.enter_context(nc.sbuf_tensor(self.name("afm"), [P, c.DC * c.T], F32))
        lnsc = esln.enter_context(nc.sbuf_tensor(self.name("alns"), [P, 2 * c.D], F32))
        xlnh = esln.enter_context(nc.sbuf_tensor(self.name("axln"), [P, max(1, N // 2) * c.D], F32))

        self.emit_ln_stats(lnsc)
        self.emit_lnT(fmA, xlnh, rcs_scale=None)
        esln.close()

        es3 = ExitStack()
        wbuf = es3.enter_context(nc.sbuf_tensor(self.name("awb"), [P, max(2 * c.DC * P, c.DC * c.NW)], F32))
        qst = es3.enter_context(nc.sbuf_tensor(self.name("aqs"), [P, 2 * c.T], F32))
        nb = min(4, 2048 // c.TN)
        with nc.Block() as blk:
            tmb0 = s.tm.v
            vvb0 = s.vv.v
            wevt = []
            sevt = []

            @blk.sync
            def _(sy):
                for M in range(2 * c.DC + 1):
                    if M < 2 * c.DC:
                        if M >= 2:
                            sy.wait_ge(s.tm.h, tmb0 + (M - 1) * c.NTC)
                        w = s.wsl[M % 2]
                        w.inc(sy.dma_start(
                            out=wbuf[:, (M % 2) * c.DC * P:(M % 2 + 1) * c.DC * P],
                            in_=s.wqk_d[M]), 16)
                        wevt.append(w.v)
                    if M >= 1:
                        sy.wait_ge(s.vv.h, vvb0 + M * c.NTC)
                        dst = s.qT_d[M - 1] if (M - 1) < c.DC else s.kloc_d[M - 1 - c.DC]
                        o = s.osl[(M - 1) % 2]
                        o.inc(sy.dma_start(
                            out=dst, in_=qst[:, ((M - 1) % 2) * c.T:((M - 1) % 2 + 1) * c.T]), 16)
                        sevt.append(o.v)

            @blk.tensor
            def _(te):
                for M in range(2 * c.DC):
                    te.wait_ge(s.wsl[M % 2].h, wevt[M])
                    for n in range(c.NTC):
                        g = M * c.NTC + n
                        if g >= nb:
                            te.wait_ge(s.vv.h, vvb0 + (g - nb) + 1)
                        bank = self.pslot(g % nb, c.TN)
                        for kc in range(c.DC):
                            ti = te.matmul(
                                bank,
                                wbuf[:, (M % 2) * c.DC * P + kc * P:(M % 2) * c.DC * P + (kc + 1) * P],
                                fmA[:, kc * c.T + n * c.TN:kc * c.T + (n + 1) * c.TN],
                                start=(kc == 0), stop=(kc == c.DC - 1))
                        s.tm.inc(ti)

            @blk.vector
            def _(v):
                for M in range(2 * c.DC):
                    if M >= 2:
                        v.wait_ge(s.osl[M % 2].h, sevt[M - 2])
                    for n in range(c.NTC):
                        g = M * c.NTC + n
                        v.wait_ge(s.tm.h, tmb0 + g + 1)
                        bank = self.pslot(g % nb, c.TN)
                        v.tensor_scalar(
                            out=qst[:, (M % 2) * c.T + n * c.TN:(M % 2) * c.T + (n + 1) * c.TN],
                            in0=bank, scalar1=s.bqks[:, M:M + 1], scalar2=None,
                            op0=ALU.add).then_inc(s.vv.h, 1)
                        s.vv.v += 1

        with nc.Block() as blk:
            tmb0 = s.tm.v
            vvb0 = s.vv.v
            oslb = [s.osl[0].v, s.osl[1].v]
            wevt = []
            sevt = []

            @blk.sync
            def _(sy):
                for h2 in range(c.ND2):
                    if h2 >= 1:
                        sy.wait_ge(s.tm.h, tmb0 + h2 * c.NTT)
                    w = s.wsl[h2 % 2]
                    for kc in range(c.DC):
                        w.inc(sy.dma_start(out=wbuf[:, kc * c.NW:(kc + 1) * c.NW],
                                           in_=s.wv_d[h2, kc]), 16)
                    wevt.append(w.v)
                    for Mt in range(c.NTT):
                        g = h2 * c.NTT + Mt
                        sy.wait_ge(s.vv.h, vvb0 + g + 1)
                        o = s.osl[g % 2]
                        o.inc(sy.dma_start(
                            out=s.vloc_d[Mt, :, h2 * c.NW:(h2 + 1) * c.NW],
                            in_=qst[:, (g % 2) * c.T:(g % 2) * c.T + c.NW]), 16)
                        sevt.append(o.v)

            @blk.tensor
            def _(te):
                for h2 in range(c.ND2):
                    te.wait_ge(s.wsl[h2 % 2].h, wevt[h2])
                    for Mt in range(c.NTT):
                        g = h2 * c.NTT + Mt
                        if g >= 4:
                            te.wait_ge(s.vv.h, vvb0 + (g - 4) + 1)
                        bank = self.pslot(g % 4, c.NW)
                        for kc in range(c.DC):
                            ti = te.matmul(bank,
                                           fmA[:, kc * c.T + Mt * P:kc * c.T + (Mt + 1) * P],
                                           wbuf[:, kc * c.NW:(kc + 1) * c.NW],
                                           start=(kc == 0), stop=(kc == c.DC - 1))
                        s.tm.inc(ti)

            @blk.vector
            def _(v):
                v.wait_ge(s.osl[0].h, oslb[0])
                v.wait_ge(s.osl[1].h, oslb[1])
                for h2 in range(c.ND2):
                    for Mt in range(c.NTT):
                        g = h2 * c.NTT + Mt
                        v.wait_ge(s.tm.h, tmb0 + g + 1)
                        if g >= 2:
                            v.wait_ge(s.osl[g % 2].h, sevt[g - 2])
                        bank = self.pslot(g % 4, c.NW)
                        v.tensor_add(out=qst[:, (g % 2) * c.T:(g % 2) * c.T + c.NW],
                                     in0=bank,
                                     in1=s.bvb[:, h2 * c.NW:(h2 + 1) * c.NW]).then_inc(s.vv.h, 1)
                        s.vv.v += 1

            @blk.gpsimd
            def _(g):
                g.wait_ge(s.osl[0].h, oslb[0])
                g.wait_ge(s.osl[1].h, oslb[1])
                s.cc.inc(g.collective_compute(
                    "AllGather", ALU.bypass, replica_groups=self.pair_groups(),
                    ins=[s.kloc_d.opt()], outs=[s.kpair_d.opt()]), 1)
                g.wait_ge(s.cc.h, s.cc.v)
        es3.close()
        es2.close()

        with nc.Block() as blk:
            @blk.gpsimd
            def _(g):
                g.wait_ge(s.osl[0].h, s.osl[0].v)
                g.wait_ge(s.osl[1].h, s.osl[1].v)
                s.cc.inc(g.collective_compute(
                    "AllGather", ALU.bypass, replica_groups=self.pair_groups(),
                    ins=[s.vloc_d.opt()], outs=[s.vpair_d.opt()]), 1)
                g.wait_ge(s.cc.h, s.cc.v)

        self.emit_flash()

    def pair_groups(self):
        return [[2 * b, 2 * b + 1] for b in range(self.c.n_cores // 2)]

    def emit_flash(self):
        c, s, nc = self.c, self, self.nc
        P, HD = c.P, c.HDIM
        QC = c.TN
        NQC = c.T // QC
        qtile = QC // P
        NKT = c.NKT
        half_kt = NKT // 2
        es2 = ExitStack()
        qcb = es2.enter_context(nc.sbuf_tensor(self.name("fqc"), [P, c.DC * QC], F32))
        kbl = es2.enter_context(nc.sbuf_tensor(self.name("fkb"), [P, 2 * P], F32))
        vbl = es2.enter_context(nc.sbuf_tensor(self.name("fvb"), [P, 2 * 130], F32))
        ptl = es2.enter_context(nc.sbuf_tensor(self.name("fpt"), [P, 2 * 2 * QC], F32))
        atn = es2.enter_context(nc.sbuf_tensor(self.name("fat"), [P, c.DC * QC], F32))
        x1b = es2.enter_context(nc.sbuf_tensor(self.name("fx1"), [P, qtile * c.D], F32))
        lsb = es2.enter_context(nc.sbuf_tensor(self.name("fls"), [P, 4 * QC], F32))
        lbc = es2.enter_context(nc.sbuf_tensor(self.name("flb"), [P, QC], F32))
        wpb = es2.enter_context(nc.sbuf_tensor(self.name("fwp"), [P, 2 * c.D], F32))
        wpv = s.wp_d.rearrange("(kc p) d -> kc p d", p=P)

        with nc.Block() as blk:
            @blk.vector
            def _(v):
                for sl in range(2):
                    for hh in range(2):
                        ins = v.memset(vbl[:, sl * 130 + hh * 65 + 64:sl * 130 + hh * 65 + 65], 1.0)
                s.vv.inc(ins)

        for qc in range(NQC):
            with nc.Block() as blk:
                @blk.sync
                def _(sy, qc=qc):
                    for cc in range(c.DC):
                        s.sy.inc(sy.dma_start(out=qcb[:, cc * QC:(cc + 1) * QC],
                                              in_=s.qT_d[cc, :, qc * QC:(qc + 1) * QC]), 16)
                    sy.wait_ge(s.sy.h, s.sy.v)

                @blk.gpsimd
                def _(g, qc=qc):
                    for t in range(qtile):
                        gt = qc * qtile + t
                        g.tensor_add(out=x1b[:, t * c.D:(t + 1) * c.D],
                                     in0=s.xres[:, gt * c.D:(gt + 1) * c.D],
                                     in1=s.bpb[:]).then_inc(s.gg.h, 1)
                        s.gg.v += 1

            for hg in range(c.NHG):
                pvb = s.psB if hg % 2 == 0 else s.psA
                scb = s.psA if hg % 2 == 0 else s.psB
                with nc.Block() as blk:
                    tmb0 = s.tm.v
                    ssb0 = s.ss.v
                    kevt = []

                    @blk.sync
                    def _(sy, qc=qc, hg=hg):
                        for kt in range(NKT):
                            sp = kt // half_kt
                            ktl = kt % half_kt
                            sl = kt % 2
                            if kt >= 2:
                                sy.wait_ge(s.tm.h, tmb0 + 2 * (kt - 1))
                            i = s.isl[sl]
                            i.inc(sy.dma_start(
                                out=kbl[:, sl * P:(sl + 1) * P],
                                in_=s.kpair_d[sp, hg, :, ktl * P:(ktl + 1) * P]), 16)
                            vsrc = s.vpair_d[sp, ktl, :, hg * P:hg * P + 2 * HD]
                            vsrc = vsrc.rearrange("p (h d) -> p h d", h=2)
                            vdst = vbl[:, sl * 130:(sl + 1) * 130].rearrange(
                                "p (h e) -> p h e", h=2)[:, :, 0:HD]
                            i.inc(sy.dma_start(out=vdst, in_=vsrc), 16)
                            kevt.append(i.v)

                    @blk.tensor
                    def _(te, qc=qc, hg=hg):
                        for kt in range(NKT):
                            sl = kt % 2
                            te.wait_ge(s.isl[sl].h, kevt[kt])
                            if kt >= 2:
                                te.wait_ge(s.ss.h, ssb0 + (kt - 1))
                            for hh in range(2):
                                ti = te.matmul(
                                    scb[:, sl * 1024 + hh * 512:sl * 1024 + hh * 512 + QC],
                                    kbl[hh * HD:(hh + 1) * HD, sl * P:(sl + 1) * P],
                                    qcb[hh * HD:(hh + 1) * HD, hg * QC:(hg + 1) * QC],
                                    start=True, stop=True, tile_position=(hh * HD, 0))
                            s.tm.inc(ti)
                            te.wait_ge(s.ss.h, ssb0 + kt + 1)
                            for hh in range(2):
                                ti = te.matmul(
                                    pvb[0:65, hh * 1024:hh * 1024 + QC],
                                    vbl[:, sl * 130 + hh * 65:sl * 130 + (hh + 1) * 65],
                                    ptl[:, sl * 2 * QC + hh * QC:sl * 2 * QC + (hh + 1) * QC],
                                    start=(kt == 0), stop=(kt == NKT - 1))
                            s.tm.inc(ti)

                    @blk.scalar
                    def _(sc, qc=qc, hg=hg):
                        for kt in range(NKT):
                            sl = kt % 2
                            sc.wait_ge(s.tm.h, tmb0 + 2 * kt + 1)
                            if kt >= 2:
                                sc.wait_ge(s.tm.h, tmb0 + 2 * (kt - 2) + 2)
                            sin = scb[:, sl * 1024:(sl + 1) * 1024].rearrange(
                                "p (h x) -> p h x", h=2)[:, :, 0:QC]
                            pout = ptl[:, sl * 2 * QC:(sl + 1) * 2 * QC].rearrange(
                                "p (h x) -> p h x", h=2)
                            sc.activation(out=pout, in_=sin,
                                          func=AF.Exp, scale=0.125).then_inc(s.ss.h, 1)
                            s.ss.v += 1

                with nc.Block() as blk:
                    syb_e = s.sy.v

                    @blk.vector
                    def _(v, hg=hg):
                        v.wait_ge(s.sy.h, syb_e)
                        v.tensor_copy(out=lsb[64:65, 0:QC], in_=pvb[64:65, 0:QC])
                        s.vv.inc(v.tensor_copy(out=lsb[64:65, QC:2 * QC],
                                               in_=pvb[64:65, 1024:1024 + QC]))

                    @blk.sync
                    def _(sy):
                        sy.wait_ge(s.vv.h, s.vv.v)
                        s.sy.inc(sy.dma_start(out=s.lraw_d[0:2 * QC], in_=lsb[64:65, 0:2 * QC]), 16)

                    @blk.gpsimd
                    def _(g):
                        g.wait_ge(s.sy.h, s.sy.v)
                        lr = s.lraw_d[0:2 * QC].rearrange("(a b) -> a b", a=2)
                        s.gd.inc(g.dma_start(out=lsb[0:2, 0:QC], in_=lr), 16)

                    gdv1 = s.gd.v

                    @blk.vector
                    def _(v):
                        v.wait_ge(s.gd.h, gdv1)
                        s.vv.inc(v.reciprocal(out=lsb[0:2, QC:2 * QC], in_=lsb[0:2, 0:QC]))

                    @blk.sync
                    def _(sy):
                        sy.wait_ge(s.vv.h, s.vv.v)
                        s.sy.inc(sy.dma_start(out=s.lrec_d[0:2 * QC], in_=lsb[0:2, QC:2 * QC]), 16)

                    @blk.gpsimd
                    def _(g):
                        g.wait_ge(s.sy.h, s.sy.v)
                        for hh in range(2):
                            s.gd.inc(g.dma_start(
                                out=lbc[hh * HD:(hh + 1) * HD, 0:QC],
                                in_=s.lrec_d[hh * QC:(hh + 1) * QC].unsqueeze(0).to_broadcast([HD, QC])), 16)

                    gdv2 = s.gd.v

                    @blk.vector
                    def _(v, qc=qc, hg=hg):
                        v.wait_ge(s.gd.h, gdv2)
                        for hh in range(2):
                            h = 2 * hg + hh
                            cch, rr = h // 2, (h % 2) * HD
                            v.tensor_mul(out=atn[rr:rr + HD, cch * QC:(cch + 1) * QC],
                                         in0=pvb[0:HD, hh * 1024:hh * 1024 + QC],
                                         in1=lbc[hh * HD:(hh + 1) * HD, 0:QC]).then_inc(s.vv.h, 1)
                            s.vv.v += 1

            with nc.Block() as blk:
                wevt = []

                @blk.sync
                def _(sy, qc=qc):
                    tmb0 = s.tm.v
                    for kc in range(c.DC):
                        if kc >= 2:
                            sy.wait_ge(s.tm.h, tmb0 + (kc - 1))
                        w = s.wsl[kc % 2]
                        w.inc(sy.dma_start(out=wpb[:, (kc % 2) * c.D:(kc % 2 + 1) * c.D],
                                           in_=wpv[kc]), 16)
                        wevt.append(w.v)

                @blk.tensor
                def _(te, qc=qc):
                    for kc in range(c.DC):
                        te.wait_ge(s.wsl[kc % 2].h, wevt[kc])
                        for t in range(qtile):
                            bank = s.psA if t % 2 == 0 else s.psB
                            boff = (t // 2) * c.D
                            for nn in range(c.ND2):
                                ti = te.matmul(
                                    bank[:, boff + nn * c.NW:boff + (nn + 1) * c.NW],
                                    atn[:, kc * QC + t * P:kc * QC + (t + 1) * P],
                                    wpb[:, (kc % 2) * c.D + nn * c.NW:(kc % 2) * c.D + (nn + 1) * c.NW],
                                    start=(kc == 0), stop=(kc == c.DC - 1))
                        s.tm.inc(ti)

                @blk.vector
                def _(v, qc=qc):
                    v.wait_ge(s.tm.h, s.tm.v)
                    v.wait_ge(s.gg.h, s.gg.v)
                    for t in range(qtile):
                        bank = s.psA if t % 2 == 0 else s.psB
                        boff = (t // 2) * c.D
                        gt = qc * qtile + t
                        self.vstep(v, v.tensor_add(out=s.xres[:, gt * c.D:(gt + 1) * c.D],
                                                   in0=bank[:, boff:boff + c.D],
                                                   in1=x1b[:, t * c.D:(t + 1) * c.D]))
        es2.close()



def prep_inputs(cfg, L0_attn_x, L0_attn_w_qkv, L0_attn_b_qkv, L0_attn_w_proj,
                L0_attn_b_proj, L0_ffn_w_gate, L0_ffn_w_up, L0_ffn_w_down):
    c = cfg
    P = c.P
    x = np.ascontiguousarray(np.asarray(L0_attn_x, dtype=np.float32))
    D = c.D
    wqkv = np.asarray(L0_attn_w_qkv, dtype=np.float32)

    def tile_lhsT(wT, n_m):
        arr = wT.reshape(c.DC, P, n_m, P)
        return np.ascontiguousarray(arr.transpose(2, 1, 0, 3))

    wqk = tile_lhsT(wqkv[:, 0:2 * D], 2 * c.DC)
    wv = np.ascontiguousarray(
        wqkv[:, 2 * D:3 * D].reshape(c.DC, P, c.ND2, c.NW).transpose(2, 0, 1, 3))
    bqkv = np.asarray(L0_attn_b_qkv, dtype=np.float32)
    bqk = np.ascontiguousarray(bqkv[0:2 * D].reshape(2 * c.DC, P))
    bv = np.ascontiguousarray(bqkv[2 * D:3 * D])
    wp = np.ascontiguousarray(np.asarray(L0_attn_w_proj, dtype=np.float32))
    bp = np.ascontiguousarray(np.asarray(L0_attn_b_proj, dtype=np.float32))
    wu = tile_lhsT(np.ascontiguousarray(np.asarray(L0_ffn_w_up, dtype=np.float32).T), c.HC)
    wg = tile_lhsT(np.ascontiguousarray(np.asarray(L0_ffn_w_gate, dtype=np.float32).T), c.HC)
    wd = np.ascontiguousarray(np.asarray(L0_ffn_w_down, dtype=np.float32).T)

    in_maps = []
    for core in range(c.n_cores):
        b, h = core // 2, core % 2
        xs = np.ascontiguousarray(x[b, h * c.T:(h + 1) * c.T, :])
        in_maps.append(dict(x0=xs, wqk=wqk, wv=wv, bqk=bqk, bv=bv, wp=wp, bp=bp,
                            wu=wu, wg=wg, wd=wd))
    return in_maps


def build_program(cfg, dbg=None):
    b = Builder(cfg, dbg=dbg)
    b.emit()
    return b.finish()


def kernel(L0_attn_x, L0_attn_w_qkv, L0_attn_b_qkv, L0_attn_w_proj, L0_attn_b_proj,
           L0_ffn_w_gate, L0_ffn_w_up, L0_ffn_w_down):
    cfg = FULL
    in_maps = prep_inputs(cfg, L0_attn_x, L0_attn_w_qkv, L0_attn_b_qkv,
                          L0_attn_w_proj, L0_attn_b_proj, L0_ffn_w_gate,
                          L0_ffn_w_up, L0_ffn_w_down)
    nc = build_program(cfg)
    res = run_bass_kernel_spmd(nc, in_maps, core_ids=list(range(cfg.n_cores)))
    x = np.asarray(L0_attn_x)
    Bb, S, D = x.shape
    out = np.empty((Bb, S, D), dtype=np.float32)
    for core in range(cfg.n_cores):
        bb, h = core // 2, core % 2
        out[bb, h * cfg.T:(h + 1) * cfg.T, :] = res.results[core]["y"]
    return out


# version 06
# speedup vs baseline: 1.0653x; 1.0245x over previous
import numpy as np
from contextlib import ExitStack
from dataclasses import dataclass

import concourse.bass as bass
import concourse.mybir as mybir
from concourse.bass_utils import run_bass_kernel_spmd
from concourse.dve_spec import Spec, Src0, Src1, C0, C1, C2, Zero, One, maxx, minn, AluOp
from concourse.dve_spec import lower as dve_lower
from concourse.dve_ops import DveOp, OPS
import concourse.dve_ops as dve_ops_mod

F32 = mybir.dt.float32
I16 = mybir.dt.int16
I32 = mybir.dt.int32
AF = mybir.ActivationFunctionType
ALU = mybir.AluOpType
AX = mybir.AxisListType

MAGIC = float(np.float32(1.5 * 2.0 ** 23))


def _register_op(name, spec, reference):
    for op in OPS:
        if op.name == name:
            return op
    shas, perf_en = {}, {}
    for ver in ("v3", "v4"):
        try:
            shas[ver] = dve_lower(spec, ver=ver, perf_en=False, name=name).sha(ver)
            perf_en[ver] = False
        except Exception:
            pass
    op = DveOp(name, spec, subdim=False, uops_sha=shas, perf_en=perf_en,
               reference=reference)
    OPS.append(op)
    dve_ops_mod._SUB_OPCODE_FOR_NAME[name] = (
        dve_ops_mod._CUSTOM_DVE_ROW_BASE + len(OPS) - 1)
    dve_ops_mod.CUSTOM_DVE_REFS[name] = op.ref
    dve_ops_mod.CUSTOM_DVE_SPECS[name] = spec
    return op


RCS = _register_op(
    "MK_RCS",
    Spec(body=minn(maxx((Src0 * C0 + C2) - C2, Zero - (C1 + One)), C1)),
    lambda in0, in1, s0, s1, imm2: np.clip(np.rint(in0 * s0), -s1 - 1.0, s1),
)
SUBMUL = _register_op(
    "MK_SUBMUL",
    Spec(body=(Src0 - C0) * Src1, accum=AluOp.MAX),
    lambda in0, in1, s0, s1, imm2: (
        (in0 - s0) * in1,
        np.max((in0 - s0) * in1, axis=-1, keepdims=True),
    ),
)


@dataclass(frozen=True)
class Cfg:
    D: int = 1024
    H: int = 3072
    NH: int = 16
    T: int = 2048
    n_cores: int = 8
    eps: float = 1e-5
    stages: int = 3

    P = 128
    HDIM = 64

    @property
    def DC(self):
        return self.D // 128

    @property
    def HC(self):
        return self.H // 128

    @property
    def NTT(self):
        return self.T // 128

    @property
    def S(self):
        return 2 * self.T

    @property
    def NKT(self):
        return self.S // 128

    @property
    def TN(self):
        return min(512, self.T)

    @property
    def NTC(self):
        return self.T // self.TN

    @property
    def NHG(self):
        return self.NH // 2

    @property
    def NW(self):
        return min(512, self.D)

    @property
    def ND2(self):
        return self.D // self.NW


FULL = Cfg()
MINI = Cfg(D=256, H=768, NH=4, T=256)
MIDI = Cfg(D=1024, H=1536, NH=16, T=1024)


class Sem:
    def __init__(self, es, nc, name):
        self.h = es.enter_context(nc.semaphore(name))
        self.v = 0

    def inc(self, ins, n=1):
        ins.then_inc(self.h, n)
        self.v += n
        return self.v


class Builder:
    def __init__(self, cfg: Cfg, dbg=None):
        self.dbg = dbg
        self.ffn_i = 0
        self.dbg_done = False
        self.c = cfg
        nc = bass.Bass("TRN2", target_bir_lowering=False, debug=False)
        self.nc = nc
        self.uid = 0
        c, P = cfg, cfg.P
        self.x0_d = nc.dram_tensor("x0", [c.T, c.D], F32, kind="ExternalInput").ap()
        self.wqk_d = nc.dram_tensor("wqk", [2 * c.DC, P, c.DC, P], F32, kind="ExternalInput").ap()
        self.wv_d = nc.dram_tensor("wv", [c.ND2, c.DC, P, c.NW], F32, kind="ExternalInput").ap()
        self.bqk_d = nc.dram_tensor("bqk", [2 * c.DC, P], F32, kind="ExternalInput").ap()
        self.bv_d = nc.dram_tensor("bv", [c.D], F32, kind="ExternalInput").ap()
        self.wp_d = nc.dram_tensor("wp", [c.D, c.D], F32, kind="ExternalInput").ap()
        self.bp_d = nc.dram_tensor("bp", [c.D], F32, kind="ExternalInput").ap()
        self.wu_d = nc.dram_tensor("wu", [c.HC, P, c.DC, P], F32, kind="ExternalInput").ap()
        self.wg_d = nc.dram_tensor("wg", [c.HC, P, c.DC, P], F32, kind="ExternalInput").ap()
        self.wd_d = nc.dram_tensor("wd", [c.H, c.D], F32, kind="ExternalInput").ap()
        self.y_d = nc.dram_tensor("y", [c.T, c.D], F32, kind="ExternalOutput").ap()
        self.wuq_d = nc.dram_tensor("wuq", [c.HC, P, c.DC, P], F32).ap()
        self.wgq_d = nc.dram_tensor("wgq", [c.HC, P, c.DC, P], F32).ap()
        self.wdq_d = nc.dram_tensor("wdq", [c.H, c.D], F32).ap()
        self.mup_d = nc.dram_tensor("mup", [c.HC, P, c.T], F32).ap()
        self.mgate_d = nc.dram_tensor("mgate", [c.HC, P, c.T], F32).ap()
        self.kup_d = nc.dram_tensor("kup", [c.NTC, c.HC, P, c.TN], I16).ap()
        self.kgate_d = nc.dram_tensor("kgate", [c.NTC, c.HC, P, c.TN], I16).ap()
        self.kg2_d = nc.dram_tensor("kg2", [c.NTC, c.HC, P, c.TN], I16).ap()
        self.qT_d = nc.dram_tensor("qTd", [c.DC, P, c.T], F32).ap()
        self.kloc_d = nc.dram_tensor("kloc", [c.DC, P, c.T], F32).ap()
        self.vloc_d = nc.dram_tensor("vloc", [c.NTT, P, c.D], F32).ap()
        self.kpair_d = nc.dram_tensor("kpair", [2, c.DC, P, c.T], F32).ap()
        self.vpair_d = nc.dram_tensor("vpair", [2, c.NTT, P, c.D], F32).ap()
        self.lraw_d = nc.dram_tensor("lraw", [2 * 512], F32).ap()
        self.lrec_d = nc.dram_tensor("lrec", [2 * 512], F32).ap()
        self.ar_in = [nc.dram_tensor(f"arin{i}", [256], F32).ap() for i in range(8)]
        self.ar_out = [nc.dram_tensor(f"arout{i}", [256], F32).ap() for i in range(8)]
        self.n_ar = 0

        self.es = ExitStack()
        self.es.__enter__()
        es = self.es
        sb = lambda n, sh, dt=F32: es.enter_context(nc.sbuf_tensor(n, sh, dt))
        self.xres = sb("xres", [P, c.NTT * c.D])
        self.stat = sb("stat", [P, 8 * c.NTT])
        self.slots = sb("slots", [P, 256])
        self.scal = sb("scal", [P, 32])
        self.arb = sb("arb", [P, 256])
        self.ident = sb("ident", [P, P])
        self.bvb = sb("bvb", [P, c.D])
        self.bpb = sb("bpb", [P, c.D])
        self.bqks = sb("bqks", [P, 2 * c.DC])
        self.psA = es.enter_context(nc.psum_tensor("psA", [P, 2048], F32))
        self.psB = es.enter_context(nc.psum_tensor("psB", [P, 2048], F32))
        self.sy = Sem(es, nc, "sy")
        self.gd = Sem(es, nc, "gd")
        self.cc = Sem(es, nc, "cc")
        self.tm = Sem(es, nc, "tm")
        self.vv = Sem(es, nc, "vv")
        self.ss = Sem(es, nc, "ss")
        self.gg = Sem(es, nc, "gg")
        self.wsl = [Sem(es, nc, "wsl0"), Sem(es, nc, "wsl1")]
        self.isl = [Sem(es, nc, "isl0"), Sem(es, nc, "isl1")]
        self.osl = [Sem(es, nc, "osl0"), Sem(es, nc, "osl1")]

    def sc(self, i):
        return self.scal[:, i:i + 1]

    def name(self, stem):
        self.uid += 1
        return f"{stem}{self.uid}"

    def pslot(self, i, width):
        t = self.psA if i % 2 == 0 else self.psB
        off = ((i // 2) % 2) * 1024
        return t[:, off:off + width]

    def vstep(self, v, ins):
        ins.then_inc(self.vv.h, 1)
        self.vv.v += 1
        v.wait_ge(self.vv.h, self.vv.v)

    def finish(self):
        self.es.__exit__(None, None, None)
        return self.nc

    def emit(self):
        c, nc = self.c, self.nc
        with nc.Block() as blk:
            @blk.sync
            def _(sy):
                x0v = self.x0_d.rearrange("(t p) d -> t p d", p=c.P)
                for t in range(c.NTT):
                    self.sy.inc(sy.dma_start(out=self.xres[:, t * c.D:(t + 1) * c.D], in_=x0v[t]), 16)
                self.sy.inc(sy.dma_start(out=self.bvb[:], in_=self.bv_d.unsqueeze(0).to_broadcast([c.P, c.D])), 16)
                self.sy.inc(sy.dma_start(out=self.bpb[:], in_=self.bp_d.unsqueeze(0).to_broadcast([c.P, c.D])), 16)
                idd = nc.inline_tensor(np.eye(c.P, dtype=np.float32), "identnp").ap()
                self.sy.inc(sy.dma_start(out=self.ident[:], in_=idd), 16)
                for m in range(2 * c.DC):
                    self.sy.inc(sy.dma_start(out=self.bqks[:, m:m + 1], in_=self.bqk_d[m].unsqueeze(1)), 16)
                sy.wait_ge(self.sy.h, self.sy.v)

        self.emit_dequant()
        self.emit_ffn()
        if c.stages >= 2 and not self.dbg_done:
            self.emit_attention()
        if c.stages >= 3 and not self.dbg_done:
            self.emit_ffn()

        if not self.dbg_done:
            with nc.Block() as blk:
                @blk.sync
                def _(sy):
                    yv = self.y_d.rearrange("(t p) d -> t p d", p=c.P)
                    for t in range(c.NTT):
                        self.sy.inc(sy.dma_start(out=yv[t], in_=self.xres[:, t * c.D:(t + 1) * c.D]), 16)
                    sy.wait_ge(self.sy.h, self.sy.v)

    def emit_dequant(self):
        c, s, nc = self.c, self, self.nc
        P = c.P
        W = 512 if c.D >= 512 else c.D
        es2 = ExitStack()
        wa = es2.enter_context(nc.sbuf_tensor(self.name("dqa"), [P, 2 * W], F32))
        wt = es2.enter_context(nc.sbuf_tensor(self.name("dqt"), [P, W], F32))
        bm = es2.enter_context(nc.sbuf_tensor(self.name("dqm"), [P, W // 2], F32))
        ki = es2.enter_context(nc.sbuf_tensor(self.name("dqi"), [P, W], I16))
        nb = W // 32

        tiles = []
        for wsrc, wdst in [(s.wu_d, s.wuq_d), (s.wg_d, s.wgq_d)]:
            for m in range(c.HC):
                for kc in range(c.DC):
                    v4 = wsrc[m, :, kc, :]
                    d4 = wdst[m, :, kc, :]
                    for j in range(0, P, W):
                        w_ = min(W, P - j)
                        tiles.append((v4[:, j:j + w_], d4[:, j:j + w_], w_))
        wdv_s = s.wd_d.rearrange("(hc p) d -> hc p d", p=P)
        wdv_q = s.wdq_d.rearrange("(hc p) d -> hc p d", p=P)
        for hc in range(c.HC):
            for j in range(0, c.D, W):
                w_ = min(W, c.D - j)
                tiles.append((wdv_s[hc, :, j:j + w_], wdv_q[hc, :, j:j + w_], w_))

        with nc.Block() as blk:
            n = len(tiles)
            VPT = 7
            ld_evt = []
            st_evt = []
            vvb = s.vv.v

            @blk.sync
            def _(sy):
                for i in range(n + 1):
                    if i < n:
                        src_ap, dst_ap, w_ = tiles[i]
                        if i >= 2:
                            sy.wait_ge(s.vv.h, vvb + (i - 1) * VPT)
                            sy.wait_ge(s.osl[i % 2].h, st_evt[i - 2])
                        il = s.isl[i % 2]
                        il.inc(sy.dma_start(out=wa[:, (i % 2) * W:(i % 2) * W + w_], in_=src_ap), 16)
                        ld_evt.append(il.v)
                    if i >= 1:
                        j = i - 1
                        sy.wait_ge(s.vv.h, vvb + (j + 1) * VPT)
                        ol = s.osl[j % 2]
                        ol.inc(sy.dma_start(out=tiles[j][1],
                                            in_=wa[:, (j % 2) * W:(j % 2) * W + tiles[j][2]]), 16)
                        st_evt.append(ol.v)
                sy.wait_ge(s.osl[0].h, s.osl[0].v)
                sy.wait_ge(s.osl[1].h, s.osl[1].v)

            @blk.vector
            def _(v):
                for i, (src_ap, dst_ap, w_) in enumerate(tiles):
                    nb_ = w_ // 32
                    o = (i % 2) * W
                    v.wait_ge(s.isl[i % 2].h, ld_evt[i])
                    self.vstep(v, v.transpose(out=wt[:, 0:w_], in_=wa[:, o:o + w_]))
                    self.vstep(v, v.reduce_max(
                        out=bm[:, 0:nb_],
                        in_=wt[:, 0:w_].rearrange("p (g e) -> p g e", e=32),
                        axis=AX.X, apply_absolute_value=True))
                    self.vstep(v, v.tensor_scalar(out=bm[:, 0:nb_], in0=bm[:, 0:nb_],
                                                  scalar1=1.0 / 7.0, scalar2=1e-12,
                                                  op0=ALU.mult, op1=ALU.max))
                    self.vstep(v, v.reciprocal(out=bm[:, nb:nb + nb_], in_=bm[:, 0:nb_]))
                    self.vstep(v, v.tensor_tensor(
                        out=ki[:, 0:w_].rearrange("p (g e) -> p g e", e=32),
                        in0=wt[:, 0:w_].rearrange("p (g e) -> p g e", e=32),
                        in1=bm[:, nb:nb + nb_].rearrange("p (g o) -> p g o", o=1).to_broadcast([P, nb_, 32]),
                        op=ALU.mult))
                    self.vstep(v, v.tensor_tensor(
                        out=wt[:, 0:w_].rearrange("p (g e) -> p g e", e=32),
                        in0=ki[:, 0:w_].rearrange("p (g e) -> p g e", e=32),
                        in1=bm[:, 0:nb_].rearrange("p (g o) -> p g o", o=1).to_broadcast([P, nb_, 32]),
                        op=ALU.mult))
                    self.vstep(v, v.transpose(out=wa[:, o:o + w_], in_=wt[:, 0:w_]))
        es2.close()

    def emit_ln_stats(self, lnsc):
        c, s = self.c, self
        N = c.NTT
        nc = self.nc
        with nc.Block() as blk:
            @blk.vector
            def _(v):
                for t in range(N):
                    xt = s.xres[:, t * c.D:(t + 1) * c.D]
                    ins = v.reduce_sum(out=s.stat[:, t:t + 1], in_=xt, axis=AX.X)
                s.vv.inc(ins)

            @blk.scalar
            def _(sc):
                ssb = s.ss.v
                for t in range(N):
                    xt = s.xres[:, t * c.D:(t + 1) * c.D]
                    if t >= 1:
                        sc.wait_ge(s.ss.h, ssb + t)
                    sc.activation(out=lnsc[:, 0:c.D], in_=xt, func=AF.Square,
                                  accum_out=s.stat[:, N + t:N + t + 1]).then_inc(s.ss.h, 1)
                    s.ss.v += 1

        with nc.Block() as blk:
            @blk.vector
            def _(v):
                st = s.stat
                self.vstep(v, v.tensor_scalar(out=st[:, 2 * N:3 * N], in0=st[:, 0:N],
                                              scalar1=1.0 / c.D, scalar2=None, op0=ALU.mult))
                self.vstep(v, v.scalar_tensor_tensor(out=st[:, 3 * N:4 * N], in0=st[:, 2 * N:3 * N],
                                                     scalar=-1.0, in1=st[:, 2 * N:3 * N],
                                                     op0=ALU.mult, op1=ALU.mult))
                self.vstep(v, v.scalar_tensor_tensor(out=st[:, 3 * N:4 * N], in0=st[:, N:2 * N],
                                                     scalar=1.0 / c.D, in1=st[:, 3 * N:4 * N],
                                                     op0=ALU.mult, op1=ALU.add))
                self.vstep(v, v.tensor_scalar(out=st[:, 3 * N:4 * N], in0=st[:, 3 * N:4 * N],
                                              scalar1=c.eps, scalar2=None, op0=ALU.add))
                s.vv.inc(v.engine_nop())

            @blk.scalar
            def _(sc):
                sc.wait_ge(s.vv.h, s.vv.v)
                st = s.stat
                s.ss.inc(sc.activation(out=st[:, 4 * N:5 * N], in_=st[:, 3 * N:4 * N],
                                       func=AF.Sqrt))

            @blk.vector
            def _(v):
                v.wait_ge(s.ss.h, s.ss.v)
                st = s.stat
                self.vstep(v, v.reciprocal(out=st[:, 5 * N:6 * N], in_=st[:, 4 * N:5 * N]))
                self.vstep(v, v.tensor_mul(out=st[:, 6 * N:7 * N], in0=st[:, 5 * N:6 * N],
                                           in1=st[:, 5 * N:6 * N]))
                self.vstep(v, v.tensor_mul(out=st[:, 6 * N:7 * N], in0=st[:, 6 * N:7 * N],
                                           in1=st[:, 3 * N:4 * N]))
                self.vstep(v, v.tensor_scalar(out=st[:, 6 * N:7 * N], in0=st[:, 6 * N:7 * N],
                                              scalar1=-0.5, scalar2=1.5, op0=ALU.mult, op1=ALU.add))
                self.vstep(v, v.tensor_mul(out=st[:, 4 * N:5 * N], in0=st[:, 5 * N:6 * N],
                                           in1=st[:, 6 * N:7 * N]))
                s.vv.inc(v.engine_nop())

    def emit_armax(self, ranges, outs):
        c, s, nc = self.c, self, self.nc
        idx = self.n_ar % 8
        self.n_ar += 1
        k = len(ranges)
        with nc.Block() as blk:
            @blk.vector
            def _(v):
                for i, (st0, n) in enumerate(ranges):
                    ins = v.reduce_max(out=s.scal[:, 30 - i:31 - i],
                                       in_=s.slots[:, st0:st0 + n], axis=AX.X)
                s.vv.inc(ins)

            @blk.gpsimd
            def _(g):
                g.wait_ge(s.vv.h, s.vv.v)
                for i in range(k):
                    s.gd.inc(g.dma_start(out=s.ar_in[idx][i * 128:(i + 1) * 128],
                                         in_=s.scal[:, 30 - i:31 - i]), 16)
                g.wait_ge(s.gd.h, s.gd.v)
                s.cc.inc(g.collective_compute(
                    "AllReduce", ALU.max, replica_groups=[list(range(c.n_cores))],
                    ins=[s.ar_in[idx][0:k * 128].opt()],
                    outs=[s.ar_out[idx][0:k * 128].opt()]), 1)
                g.wait_ge(s.cc.h, s.cc.v)
                for i in range(k):
                    s.gd.inc(g.dma_start(
                        out=s.arb[:, i * 128:(i + 1) * 128],
                        in_=s.ar_out[idx][i * 128:(i + 1) * 128].unsqueeze(0).to_broadcast([c.P, 128])), 16)

            gdv = s.gd.v

            @blk.vector
            def _(v):
                v.wait_ge(s.gd.h, gdv)
                for i in range(k):
                    ins = v.reduce_max(out=outs[i], in_=s.arb[:, i * 128:(i + 1) * 128], axis=AX.X)
                s.vv.inc(ins)

    def dump(self, ap, nelem):
        nc, s = self.nc, self
        yflat = self.y_d.rearrange("a b -> (a b)")
        with nc.Block() as blk:
            @blk.sync
            def _(sy):
                s.sy.inc(sy.dma_start(out=yflat[0:nelem], in_=ap), 16)
                sy.wait_ge(s.sy.h, s.sy.v)
        self.dbg_done = True

    def dbg_hit(self, tag):
        return (not self.dbg_done) and self.dbg is not None and \
            self.dbg[0] == tag and self.dbg[1] == self.ffn_i

    def emit_ffn(self):
        c, s, nc = self.c, self, self.nc
        self.ffn_i += 1
        P, N = c.P, c.NTT
        es2 = ExitStack()
        esln = ExitStack()
        fmA = es2.enter_context(nc.sbuf_tensor(self.name("fmA"), [P, c.DC * c.T], F32))
        lnsc = esln.enter_context(nc.sbuf_tensor(self.name("lnsc"), [P, 2 * c.D], F32))
        xlnh = esln.enter_context(nc.sbuf_tensor(self.name("xlnh"), [P, max(1, N // 2) * c.D], F32))

        self.emit_ln_stats(lnsc)

        with nc.Block() as blk:
            @blk.vector
            def _(v):
                for t in range(N):
                    self.vstep(v, v.scalar_tensor_tensor(
                        out=lnsc[:, 0:c.D],
                        in0=s.xres[:, t * c.D:(t + 1) * c.D],
                        scalar=s.stat[:, 2 * N + t:2 * N + t + 1],
                        in1=s.stat[:, 4 * N + t:4 * N + t + 1].to_broadcast([c.P, c.D]),
                        op0=ALU.subtract, op1=ALU.mult))
                    self.vstep(v, v.reduce_max(out=s.slots[:, t:t + 1], in_=lnsc[:, 0:c.D],
                                               axis=AX.X, apply_absolute_value=True))

        self.emit_armax([(0, N)], [self.sc(25)])
        with nc.Block() as blk:
            @blk.vector
            def _(v):
                v.wait_ge(s.vv.h, s.vv.v)
                self.vstep(v, v.tensor_scalar(out=self.sc(1), in0=self.sc(25), scalar1=1.0 / 32767.0,
                                              scalar2=1e-12, op0=ALU.mult, op1=ALU.max))
                self.vstep(v, v.reciprocal(out=self.sc(2), in_=self.sc(1)))
                s.vv.inc(v.engine_nop())

        self.emit_lnT(fmA, xlnh, rcs_scale=self.sc(2))
        esln.close()
        if self.dbg_hit("k1T"):
            self.dump(fmA[:], c.P * c.DC * c.T)
            es2.close()
            return

        self.emit_upgate(fmA)
        es2.close()

        self.emit_armax([(0, c.HC * c.NTC), (128, c.HC * c.NTC)],
                        [self.sc(30), self.sc(29)])
        with nc.Block() as blk:
            @blk.vector
            def _(v):
                v.wait_ge(s.vv.h, s.vv.v)
                self.vstep(v, v.scalar_tensor_tensor(out=self.sc(20), in0=self.sc(30),
                                                     scalar=1.0 / 32767.0, in1=self.sc(1),
                                                     op0=ALU.mult, op1=ALU.mult))
                self.vstep(v, v.tensor_scalar(out=self.sc(20), in0=self.sc(20), scalar1=1e-12,
                                              scalar2=None, op0=ALU.max))
                self.vstep(v, v.reciprocal(out=self.sc(28), in_=self.sc(20)))
                self.vstep(v, v.tensor_mul(out=self.sc(4), in0=self.sc(28), in1=self.sc(1)))
                self.vstep(v, v.scalar_tensor_tensor(out=self.sc(21), in0=self.sc(29),
                                                     scalar=1.0 / 32767.0, in1=self.sc(1),
                                                     op0=ALU.mult, op1=ALU.mult))
                self.vstep(v, v.tensor_scalar(out=self.sc(21), in0=self.sc(21), scalar1=1e-12,
                                              scalar2=None, op0=ALU.max))
                self.vstep(v, v.reciprocal(out=self.sc(28), in_=self.sc(21)))
                self.vstep(v, v.tensor_mul(out=self.sc(5), in0=self.sc(28), in1=self.sc(1)))
                self.vstep(v, v.tensor_copy(out=self.sc(6), in_=self.sc(21)))
                s.vv.inc(v.engine_nop())

        if self.dbg_hit("scal_ar1"):
            self.dump(s.scal[:], c.P * 32)
            return
        self.emit_f3()
        self.emit_armax([(0, c.HC)], [self.sc(30)])
        with nc.Block() as blk:
            @blk.vector
            def _(v):
                v.wait_ge(s.vv.h, s.vv.v)
                self.vstep(v, v.scalar_tensor_tensor(out=self.sc(22), in0=self.sc(6),
                                                     scalar=1.0 / 65536.0, in1=self.sc(30),
                                                     op0=ALU.mult, op1=ALU.mult))
                self.vstep(v, v.tensor_scalar(out=self.sc(22), in0=self.sc(22),
                                              scalar1=1.0 / 32767.0, scalar2=1e-12,
                                              op0=ALU.mult, op1=ALU.max))
                self.vstep(v, v.reciprocal(out=self.sc(28), in_=self.sc(22)))
                self.vstep(v, v.scalar_tensor_tensor(out=self.sc(7), in0=self.sc(6),
                                                     scalar=1.0 / 65536.0, in1=self.sc(28),
                                                     op0=ALU.mult, op1=ALU.mult))
                self.vstep(v, v.tensor_mul(out=self.sc(8), in0=self.sc(22), in1=self.sc(20)))
                s.vv.inc(v.engine_nop())

        if self.dbg_hit("scal_ar2"):
            self.dump(s.scal[:], c.P * 32)
            return
        self.emit_f4()
        self.emit_armax([(0, c.HC)], [self.sc(30)])
        with nc.Block() as blk:
            @blk.vector
            def _(v):
                v.wait_ge(s.vv.h, s.vv.v)
                self.vstep(v, v.scalar_tensor_tensor(out=self.sc(23), in0=self.sc(30),
                                                     scalar=1.0 / 32767.0, in1=self.sc(8),
                                                     op0=ALU.mult, op1=ALU.mult))
                self.vstep(v, v.tensor_scalar(out=self.sc(23), in0=self.sc(23), scalar1=1e-12,
                                              scalar2=None, op0=ALU.max))
                self.vstep(v, v.reciprocal(out=self.sc(28), in_=self.sc(23)))
                self.vstep(v, v.tensor_mul(out=self.sc(9), in0=self.sc(28), in1=self.sc(8)))
                self.vstep(v, v.tensor_scalar(out=self.sc(10), in0=self.sc(23), scalar1=0.5,
                                              scalar2=None, op0=ALU.mult))
                s.vv.inc(v.engine_nop())

        if self.dbg_hit("scal_ar3"):
            self.dump(s.scal[:], c.P * 32)
            return
        self.emit_f5()

    def emit_lnT(self, fmA, xlnh, rcs_scale):
        c, s, nc = self.c, self, self.nc
        P, N = c.P, c.NTT
        nh = max(1, N // 2)
        nbank = 4
        W = c.DC * P
        EV = 3 if rcs_scale is not None else 1
        esl = ExitStack()
        tmpf = esl.enter_context(nc.sbuf_tensor(self.name("ltf"), [P, W], F32))
        tmpi = esl.enter_context(nc.sbuf_tensor(self.name("lti"), [P, W], I16))
        for half in range(N // nh):
            t0 = half * nh
            with nc.Block() as blk:
                @blk.vector
                def _(v, t0=t0):
                    tmb = s.tm.v
                    for i in range(nh):
                        t = t0 + i
                        v.scalar_tensor_tensor(
                            out=xlnh[:, i * c.D:(i + 1) * c.D],
                            in0=s.xres[:, t * c.D:(t + 1) * c.D],
                            scalar=s.stat[:, 2 * N + t:2 * N + t + 1],
                            in1=s.stat[:, 4 * N + t:4 * N + t + 1].to_broadcast([P, c.D]),
                            op0=ALU.subtract, op1=ALU.mult).then_inc(s.vv.h, 1)
                        s.vv.v += 1
                    for i in range(nh):
                        t = t0 + i
                        v.wait_ge(s.tm.h, tmb + i + 1)
                        bank = self.pslot(i % nbank, W)
                        o = fmA.rearrange("p (cc tt) -> p cc tt", cc=c.DC)
                        if rcs_scale is not None:
                            self.vstep(v, v.tensor_scalar(out=tmpf[:], in0=bank,
                                                          scalar1=rcs_scale, scalar2=None,
                                                          op0=ALU.mult))
                            self.vstep(v, v.tensor_copy(out=tmpi[:], in_=tmpf[:]))
                            v.tensor_copy(out=o[:, :, t * P:(t + 1) * P],
                                          in_=tmpi[:].rearrange("p (cc tt) -> p cc tt", cc=c.DC)
                                          ).then_inc(s.vv.h, 1)
                            s.vv.v += 1
                        else:
                            v.tensor_copy(out=o[:, :, t * P:(t + 1) * P], in_=bank
                                          ).then_inc(s.vv.h, 1)
                            s.vv.v += 1

                vvb = s.vv.v - (1 + EV) * nh

                @blk.tensor
                def _(te, t0=t0, vvb=vvb):
                    for i in range(nh):
                        te.wait_ge(s.vv.h, vvb + i + 1)
                        if i >= nbank:
                            te.wait_ge(s.vv.h, vvb + nh + EV * (i - nbank) + 1)
                        bank = self.pslot(i % nbank, W)
                        for cc in range(c.DC):
                            ti = te.transpose(bank[:, cc * P:(cc + 1) * P],
                                              xlnh[:, i * c.D + cc * P:i * c.D + (cc + 1) * P],
                                              s.ident[:])
                        s.tm.inc(ti)
        esl.close()

    def emit_upgate(self, fmA):
        c, s, nc = self.c, self, self.nc
        P = c.P
        es2 = ExitStack()
        wbuf = es2.enter_context(nc.sbuf_tensor(self.name("wb"), [P, 2 * c.DC * P], F32))
        mst = es2.enter_context(nc.sbuf_tensor(self.name("ms"), [P, 2 * c.T], F32))
        nb = min(4, 2048 // c.TN)

        for wd_, md_, slot0 in [(s.wuq_d, s.mup_d, 0), (s.wgq_d, s.mgate_d, 128)]:
            with nc.Block() as blk:
                tmb0 = s.tm.v
                vvb0 = s.vv.v
                oslb = [s.osl[0].v, s.osl[1].v]
                wevt = []
                sevt = []

                @blk.sync
                def _(sy, wd_=wd_, md_=md_):
                    for M in range(c.HC + 1):
                        if M < c.HC:
                            if M >= 2:
                                sy.wait_ge(s.tm.h, tmb0 + (M - 1) * c.NTC)
                            w = s.wsl[M % 2]
                            w.inc(sy.dma_start(
                                out=wbuf[:, (M % 2) * c.DC * P:(M % 2 + 1) * c.DC * P],
                                in_=wd_[M]), 16)
                            wevt.append(w.v)
                        if M >= 1:
                            sy.wait_ge(s.vv.h, vvb0 + 2 * M * c.NTC)
                            o = s.osl[(M - 1) % 2]
                            o.inc(sy.dma_start(
                                out=md_[M - 1],
                                in_=mst[:, ((M - 1) % 2) * c.T:((M - 1) % 2 + 1) * c.T]), 16)
                            sevt.append(o.v)

                @blk.tensor
                def _(te):
                    for M in range(c.HC):
                        w = s.wsl[M % 2]
                        te.wait_ge(w.h, wevt[M])
                        for n in range(c.NTC):
                            g = M * c.NTC + n
                            if g >= nb:
                                te.wait_ge(s.vv.h, vvb0 + 2 * (g - nb) + 1)
                            bank = self.pslot(g % nb, c.TN)
                            for kc in range(c.DC):
                                ti = te.matmul(
                                    bank,
                                    wbuf[:, (M % 2) * c.DC * P + kc * P:(M % 2) * c.DC * P + (kc + 1) * P],
                                    fmA[:, kc * c.T + n * c.TN:kc * c.T + (n + 1) * c.TN],
                                    start=(kc == 0), stop=(kc == c.DC - 1))
                            s.tm.inc(ti)

                @blk.vector
                def _(v, slot0=slot0):
                    v.wait_ge(s.osl[0].h, oslb[0])
                    v.wait_ge(s.osl[1].h, oslb[1])
                    for M in range(c.HC):
                        if M >= 2:
                            o = s.osl[M % 2]
                            v.wait_ge(o.h, sevt[M - 2])
                        for n in range(c.NTC):
                            g = M * c.NTC + n
                            v.wait_ge(s.tm.h, tmb0 + g + 1)
                            bank = self.pslot(g % nb, c.TN)
                            dst = mst[:, (M % 2) * c.T + n * c.TN:(M % 2) * c.T + (n + 1) * c.TN]
                            self.vstep(v, v.tensor_scalar(
                                out=dst, in0=bank, scalar1=1.0, scalar2=None, op0=ALU.mult))
                            v.reduce_max(out=s.slots[:, slot0 + g:slot0 + g + 1],
                                         in_=dst, axis=AX.X,
                                         apply_absolute_value=True).then_inc(s.vv.h, 1)
                            s.vv.v += 1
        with nc.Block() as blk:
            @blk.sync
            def _(sy):
                sy.wait_ge(s.osl[0].h, s.osl[0].v)
                sy.wait_ge(s.osl[1].h, s.osl[1].v)
        es2.close()

    def emit_f3(self):
        c, s, nc = self.c, self, self.nc
        P = c.P
        es2 = ExitStack()
        mst = es2.enter_context(nc.sbuf_tensor(self.name("f3m"), [P, 4 * c.T], F32))
        kst = es2.enter_context(nc.sbuf_tensor(self.name("f3k"), [P, 4 * c.T], I16))
        sg = es2.enter_context(nc.sbuf_tensor(self.name("f3s"), [P, 2 * c.T], F32))
        jj = es2.enter_context(nc.sbuf_tensor(self.name("f3j"), [P, 2 * c.T], F32))
        pp = es2.enter_context(nc.sbuf_tensor(self.name("f3p"), [P, c.T], F32))
        tpf = es2.enter_context(nc.sbuf_tensor(self.name("f3t"), [P, c.T], F32))
        ju16 = es2.enter_context(nc.sbuf_tensor(self.name("f3u"), [P, c.T], mybir.dt.uint16))

        with nc.Block() as blk:
            vvb0 = s.vv.v
            ssb0 = s.ss.v
            VPC = 9
            ievt = []
            oevt = []

            @blk.sync
            def _(sy):
                for cc in range(c.HC + 1):
                    if cc < c.HC:
                        sl = cc % 2
                        if cc >= 2:
                            sy.wait_ge(s.vv.h, vvb0 + (cc - 2) * VPC + 3)
                        i = s.isl[sl]
                        i.inc(sy.dma_start(out=mst[:, sl * c.T:(sl + 1) * c.T], in_=s.mup_d[cc]), 16)
                        i.inc(sy.dma_start(out=mst[:, (2 + sl) * c.T:(3 + sl) * c.T], in_=s.mgate_d[cc]), 16)
                        ievt.append(i.v)
                    if cc >= 1:
                        sl = (cc - 1) % 2
                        sy.wait_ge(s.vv.h, vvb0 + (cc - 1) * VPC + 4)
                        o = s.osl[sl]
                        for n in range(c.NTC):
                            o.inc(sy.dma_start(
                                out=s.kup_d[n, cc - 1],
                                in_=kst[:, sl * c.T + n * c.TN:sl * c.T + (n + 1) * c.TN]), 16)
                            o.inc(sy.dma_start(
                                out=s.kgate_d[n, cc - 1],
                                in_=kst[:, (2 + sl) * c.T + n * c.TN:(2 + sl) * c.T + (n + 1) * c.TN]), 16)
                        oevt.append(o.v)

            @blk.vector
            def _(v):
                for cc in range(c.HC):
                    sl = cc % 2
                    v.wait_ge(s.isl[sl].h, ievt[cc])
                    if cc >= 2:
                        v.wait_ge(s.osl[sl].h, oevt[cc - 2])
                    self.vstep(v, v.tensor_scalar(out=tpf[:], in0=mst[:, sl * c.T:(sl + 1) * c.T],
                                                  scalar1=self.sc(4), scalar2=None, op0=ALU.mult))
                    self.vstep(v, v.tensor_copy(out=kst[:, sl * c.T:(sl + 1) * c.T], in_=tpf[:]))
                    self.vstep(v, v.tensor_scalar(out=tpf[:], in0=mst[:, (2 + sl) * c.T:(3 + sl) * c.T],
                                                  scalar1=self.sc(5), scalar2=None, op0=ALU.mult))
                    self.vstep(v, v.tensor_copy(out=kst[:, (2 + sl) * c.T:(3 + sl) * c.T], in_=tpf[:]))
                    v.wait_ge(s.ss.h, ssb0 + cc + 1)
                    self.vstep(v, v.tensor_scalar(out=tpf[:], in0=sg[:, sl * c.T:(sl + 1) * c.T],
                                                  scalar1=65536.0, scalar2=None, op0=ALU.mult))
                    self.vstep(v, v.tensor_copy(out=ju16[:], in_=tpf[:]))
                    self.vstep(v, v.tensor_copy(out=jj[:, sl * c.T:(sl + 1) * c.T], in_=ju16[:]))
                    self.vstep(v, v.tensor_mul(out=pp[:], in0=kst[:, (2 + sl) * c.T:(3 + sl) * c.T],
                                               in1=jj[:, sl * c.T:(sl + 1) * c.T]))
                    self.vstep(v, v.reduce_max(out=s.slots[:, cc:cc + 1], in_=pp[:], axis=AX.X,
                                               apply_absolute_value=True))

            @blk.scalar
            def _(sc):
                for cc in range(c.HC):
                    sl = cc % 2
                    sc.wait_ge(s.vv.h, vvb0 + cc * VPC + 4)
                    if cc >= 2:
                        sc.wait_ge(s.ss.h, ssb0 + cc - 1)
                    sc.activation(out=sg[:, sl * c.T:(sl + 1) * c.T],
                                  in_=kst[:, (2 + sl) * c.T:(3 + sl) * c.T],
                                  func=AF.Sigmoid, scale=self.sc(6)).then_inc(s.ss.h, 1)
                    s.ss.v += 1

        with nc.Block() as blk:
            @blk.sync
            def _(sy):
                sy.wait_ge(s.osl[0].h, s.osl[0].v)
                sy.wait_ge(s.osl[1].h, s.osl[1].v)
        es2.close()

    def emit_f4(self):
        c, s, nc = self.c, self, self.nc
        P = c.P
        es2 = ExitStack()
        kin = es2.enter_context(nc.sbuf_tensor(self.name("f4a"), [P, 4 * c.T], I16))
        sg = es2.enter_context(nc.sbuf_tensor(self.name("f4s"), [P, 2 * c.T], F32))
        jj = es2.enter_context(nc.sbuf_tensor(self.name("f4j"), [P, c.T], F32))
        pp = es2.enter_context(nc.sbuf_tensor(self.name("f4p"), [P, c.T], F32))
        k2 = es2.enter_context(nc.sbuf_tensor(self.name("f4k"), [P, 2 * c.T], I16))
        tpf = es2.enter_context(nc.sbuf_tensor(self.name("f4t"), [P, c.T], F32))
        ju16 = es2.enter_context(nc.sbuf_tensor(self.name("f4u"), [P, c.T], mybir.dt.uint16))

        with nc.Block() as blk:
            vvb0 = s.vv.v
            ssb0 = s.ss.v
            VPC = 8
            ievt = []
            oevt = []

            @blk.sync
            def _(sy):
                for cc in range(c.HC + 1):
                    if cc < c.HC:
                        sl = cc % 2
                        if cc >= 2:
                            sy.wait_ge(s.vv.h, vvb0 + (cc - 2) * VPC + 7)
                        i = s.isl[sl]
                        for n in range(c.NTC):
                            i.inc(sy.dma_start(
                                out=kin[:, sl * c.T + n * c.TN:sl * c.T + (n + 1) * c.TN],
                                in_=s.kgate_d[n, cc]), 16)
                            i.inc(sy.dma_start(
                                out=kin[:, (2 + sl) * c.T + n * c.TN:(2 + sl) * c.T + (n + 1) * c.TN],
                                in_=s.kup_d[n, cc]), 16)
                        ievt.append(i.v)
                    if cc >= 1:
                        sl = (cc - 1) % 2
                        sy.wait_ge(s.vv.h, vvb0 + (cc - 1) * VPC + 6)
                        o = s.osl[sl]
                        for n in range(c.NTC):
                            o.inc(sy.dma_start(
                                out=s.kg2_d[n, cc - 1],
                                in_=k2[:, sl * c.T + n * c.TN:sl * c.T + (n + 1) * c.TN]), 16)
                        oevt.append(o.v)

            @blk.scalar
            def _(sc):
                for cc in range(c.HC):
                    sl = cc % 2
                    sc.wait_ge(s.isl[sl].h, ievt[cc])
                    if cc >= 2:
                        sc.wait_ge(s.ss.h, ssb0 + cc - 1)
                        sc.wait_ge(s.vv.h, vvb0 + (cc - 2) * VPC + 1)
                    sc.activation(out=sg[:, sl * c.T:(sl + 1) * c.T],
                                  in_=kin[:, sl * c.T:(sl + 1) * c.T],
                                  func=AF.Sigmoid, scale=self.sc(6)).then_inc(s.ss.h, 1)
                    s.ss.v += 1

            @blk.vector
            def _(v):
                for cc in range(c.HC):
                    sl = cc % 2
                    v.wait_ge(s.ss.h, ssb0 + cc + 1)
                    self.vstep(v, v.tensor_scalar(out=tpf[:], in0=sg[:, sl * c.T:(sl + 1) * c.T],
                                                  scalar1=65536.0, scalar2=None, op0=ALU.mult))
                    self.vstep(v, v.tensor_copy(out=ju16[:], in_=tpf[:]))
                    self.vstep(v, v.tensor_copy(out=jj[:], in_=ju16[:]))
                    self.vstep(v, v.tensor_mul(out=pp[:], in0=kin[:, sl * c.T:(sl + 1) * c.T],
                                               in1=jj[:]))
                    if cc >= 2:
                        v.wait_ge(s.osl[sl].h, oevt[cc - 2])
                    self.vstep(v, v.tensor_scalar(out=tpf[:], in0=pp[:],
                                                  scalar1=self.sc(7), scalar2=None, op0=ALU.mult))
                    self.vstep(v, v.tensor_copy(out=k2[:, sl * c.T:(sl + 1) * c.T], in_=tpf[:]))
                    self.vstep(v, v.tensor_mul(out=pp[:], in0=k2[:, sl * c.T:(sl + 1) * c.T],
                                               in1=kin[:, (2 + sl) * c.T:(3 + sl) * c.T]))
                    self.vstep(v, v.reduce_max(out=s.slots[:, cc:cc + 1], in_=pp[:], axis=AX.X,
                                               apply_absolute_value=True))

        with nc.Block() as blk:
            @blk.sync
            def _(sy):
                sy.wait_ge(s.osl[0].h, s.osl[0].v)
                sy.wait_ge(s.osl[1].h, s.osl[1].v)
        es2.close()

    def emit_f5(self):
        c, s, nc = self.c, self, self.nc
        if self.dbg_hit("xres_pre_f5"):
            self.dump(self.xres[:, 0:4 * c.D], c.P * 4 * c.D)
            return
        P = c.P
        ntile = c.TN // P
        es2 = ExitStack()
        k2st = es2.enter_context(nc.sbuf_tensor(self.name("f5a"), [P, c.HC * c.TN], I16))
        kust = es2.enter_context(nc.sbuf_tensor(self.name("f5b"), [P, c.HC * c.TN], I16))
        qko = es2.enter_context(nc.sbuf_tensor(self.name("f5q"), [P, c.HC * c.TN], F32))
        ko2 = qko
        wdb = es2.enter_context(nc.sbuf_tensor(self.name("f5w"), [P, 2 * c.D], F32))
        wdv = s.wdq_d.rearrange("(hc p) d -> hc p d", p=P)

        for tc in range(c.NTC):
            with nc.Block() as blk:
                iv = []

                @blk.sync
                def _(sy, tc=tc):
                    i = s.isl[0]
                    for cc in range(c.HC):
                        i.inc(sy.dma_start(out=k2st[:, cc * c.TN:(cc + 1) * c.TN],
                                           in_=s.kg2_d[tc, cc]), 16)
                        i.inc(sy.dma_start(out=kust[:, cc * c.TN:(cc + 1) * c.TN],
                                           in_=s.kup_d[tc, cc]), 16)
                    iv.append(i.v)

                @blk.vector
                def _(v):
                    v.wait_ge(s.isl[0].h, iv[0])
                    self.vstep(v, v.tensor_mul(out=qko[:], in0=k2st[:], in1=kust[:]))
                    self.vstep(v, v.tensor_scalar(out=qko[:], in0=qko[:], scalar1=self.sc(9),
                                                  scalar2=None, op0=ALU.mult))
                    self.vstep(v, v.tensor_copy(out=k2st[:], in_=qko[:]))
                    self.vstep(v, v.tensor_copy(out=ko2[:], in_=k2st[:]))

            if self.dbg_hit("ko_tc") and tc == self.dbg[2]:
                self.dump(qko[:], c.P * c.HC * c.TN)
                break
            slow = self.dbg is not None and self.dbg[0] == "f5slow" and self.dbg[1] == self.ffn_i
            with nc.Block() as blk:
                tmb0 = s.tm.v
                wevt = []

                @blk.sync
                def _(sy, tc=tc, slow=slow):
                    for cc in range(c.HC):
                        if slow:
                            if cc >= 1:
                                sy.wait_ge(s.tm.h, tmb0 + cc)
                            w = s.wsl[0]
                            w.inc(sy.dma_start(out=wdb[:, 0:c.D], in_=wdv[cc]), 16)
                            wevt.append(w.v)
                            continue
                        if cc >= 2:
                            sy.wait_ge(s.tm.h, tmb0 + (cc - 1))
                        w = s.wsl[cc % 2]
                        w.inc(sy.dma_start(out=wdb[:, (cc % 2) * c.D:(cc % 2 + 1) * c.D],
                                           in_=wdv[cc]), 16)
                        wevt.append(w.v)

                @blk.tensor
                def _(te, tc=tc, slow=slow):
                    for cc in range(c.HC):
                        wsl_i = 0 if slow else cc % 2
                        woff = 0 if slow else (cc % 2) * c.D
                        te.wait_ge(s.wsl[wsl_i].h, wevt[cc])
                        for t in range(ntile):
                            bank = s.psA if t % 2 == 0 else s.psB
                            boff = (t // 2) * c.D
                            for nn in range(c.ND2):
                                ti = te.matmul(
                                    bank[:, boff + nn * c.NW:boff + (nn + 1) * c.NW],
                                    ko2[:, cc * c.TN + t * P:cc * c.TN + (t + 1) * P],
                                    wdb[:, woff + nn * c.NW:woff + (nn + 1) * c.NW],
                                    start=(cc == 0), stop=(cc == c.HC - 1))
                        s.tm.inc(ti)

                @blk.vector
                def _(v, tc=tc):
                    v.wait_ge(s.tm.h, s.tm.v)
                    abt = (self.dbg is not None and self.dbg[0] == "abtest"
                           and self.dbg[1] == self.ffn_i and tc == 0)
                    for t in range(ntile):
                        bank = s.psA if t % 2 == 0 else s.psB
                        boff = (t // 2) * c.D
                        gt = tc * ntile + t
                        if abt:
                            self.vstep(v, v.tensor_scalar(
                                out=qko[:, t * c.D:(t + 1) * c.D],
                                in0=bank[:, boff:boff + c.D], scalar1=self.sc(10),
                                scalar2=None, op0=ALU.mult))
                            self.vstep(v, v.scalar_tensor_tensor(
                                out=qko[:, (4 + t) * c.D:(5 + t) * c.D],
                                in0=bank[:, boff:boff + c.D], scalar=self.sc(10),
                                in1=s.xres[:, gt * c.D:(gt + 1) * c.D],
                                op0=ALU.mult, op1=ALU.add))
                            self.vstep(v, v.scalar_tensor_tensor(
                                out=s.xres[:, gt * c.D:(gt + 1) * c.D],
                                in0=bank[:, boff:boff + c.D], scalar=self.sc(10),
                                in1=s.xres[:, gt * c.D:(gt + 1) * c.D],
                                op0=ALU.mult, op1=ALU.add))
                            continue
                        if self.dbg is not None and self.dbg[0] == "mdown" and self.dbg[1] == self.ffn_i:
                            self.vstep(v, v.tensor_scalar(
                                out=s.xres[:, gt * c.D:(gt + 1) * c.D],
                                in0=bank[:, boff:boff + c.D], scalar1=self.sc(10),
                                scalar2=None, op0=ALU.mult))
                        else:
                            self.vstep(v, v.scalar_tensor_tensor(
                                out=s.xres[:, gt * c.D:(gt + 1) * c.D],
                                in0=bank[:, boff:boff + c.D], scalar=self.sc(10),
                                in1=s.xres[:, gt * c.D:(gt + 1) * c.D],
                                op0=ALU.mult, op1=ALU.add))
            if self.dbg is not None and self.dbg[0] == "mdown" and self.dbg[1] == self.ffn_i and tc == self.dbg[2] and not self.dbg_done:
                self.dump(s.xres[:, 0:4 * c.D], c.P * 4 * c.D)
                break
            if self.dbg is not None and self.dbg[0] == "abtest" and self.dbg[1] == self.ffn_i and tc == 0 and not self.dbg_done:
                self.dump(qko[:, 0:8 * c.D], c.P * 8 * c.D)
                break
            if self.dbg is not None and self.dbg[0] == "f5slow" and self.dbg[1] == self.ffn_i and tc == 0 and not self.dbg_done:
                self.dump(s.xres[:, 0:4 * c.D], c.P * 4 * c.D)
                break
            if self.dbg_hit("after_f5_tc") and tc == self.dbg[2]:
                self.dump(s.xres[:, 0:4 * c.D], c.P * 4 * c.D)
                break
        es2.close()

    def emit_attention(self):
        c, s, nc = self.c, self, self.nc
        P, N = c.P, c.NTT
        es2 = ExitStack()
        esln = ExitStack()
        fmA = es2.enter_context(nc.sbuf_tensor(self.name("afm"), [P, c.DC * c.T], F32))
        lnsc = esln.enter_context(nc.sbuf_tensor(self.name("alns"), [P, 2 * c.D], F32))
        xlnh = esln.enter_context(nc.sbuf_tensor(self.name("axln"), [P, max(1, N // 2) * c.D], F32))

        self.emit_ln_stats(lnsc)
        self.emit_lnT(fmA, xlnh, rcs_scale=None)
        esln.close()

        es3 = ExitStack()
        wbuf = es3.enter_context(nc.sbuf_tensor(self.name("awb"), [P, max(2 * c.DC * P, c.DC * c.NW)], F32))
        qst = es3.enter_context(nc.sbuf_tensor(self.name("aqs"), [P, 2 * c.T], F32))
        nb = min(4, 2048 // c.TN)
        with nc.Block() as blk:
            tmb0 = s.tm.v
            vvb0 = s.vv.v
            wevt = []
            sevt = []

            @blk.sync
            def _(sy):
                for M in range(2 * c.DC + 1):
                    if M < 2 * c.DC:
                        if M >= 2:
                            sy.wait_ge(s.tm.h, tmb0 + (M - 1) * c.NTC)
                        w = s.wsl[M % 2]
                        w.inc(sy.dma_start(
                            out=wbuf[:, (M % 2) * c.DC * P:(M % 2 + 1) * c.DC * P],
                            in_=s.wqk_d[M]), 16)
                        wevt.append(w.v)
                    if M >= 1:
                        sy.wait_ge(s.vv.h, vvb0 + M * c.NTC)
                        dst = s.qT_d[M - 1] if (M - 1) < c.DC else s.kloc_d[M - 1 - c.DC]
                        o = s.osl[(M - 1) % 2]
                        o.inc(sy.dma_start(
                            out=dst, in_=qst[:, ((M - 1) % 2) * c.T:((M - 1) % 2 + 1) * c.T]), 16)
                        sevt.append(o.v)

            @blk.tensor
            def _(te):
                for M in range(2 * c.DC):
                    te.wait_ge(s.wsl[M % 2].h, wevt[M])
                    for n in range(c.NTC):
                        g = M * c.NTC + n
                        if g >= nb:
                            te.wait_ge(s.vv.h, vvb0 + (g - nb) + 1)
                        bank = self.pslot(g % nb, c.TN)
                        for kc in range(c.DC):
                            ti = te.matmul(
                                bank,
                                wbuf[:, (M % 2) * c.DC * P + kc * P:(M % 2) * c.DC * P + (kc + 1) * P],
                                fmA[:, kc * c.T + n * c.TN:kc * c.T + (n + 1) * c.TN],
                                start=(kc == 0), stop=(kc == c.DC - 1))
                        s.tm.inc(ti)

            @blk.vector
            def _(v):
                for M in range(2 * c.DC):
                    if M >= 2:
                        v.wait_ge(s.osl[M % 2].h, sevt[M - 2])
                    for n in range(c.NTC):
                        g = M * c.NTC + n
                        v.wait_ge(s.tm.h, tmb0 + g + 1)
                        bank = self.pslot(g % nb, c.TN)
                        v.tensor_scalar(
                            out=qst[:, (M % 2) * c.T + n * c.TN:(M % 2) * c.T + (n + 1) * c.TN],
                            in0=bank, scalar1=s.bqks[:, M:M + 1], scalar2=None,
                            op0=ALU.add).then_inc(s.vv.h, 1)
                        s.vv.v += 1

        with nc.Block() as blk:
            tmb0 = s.tm.v
            vvb0 = s.vv.v
            oslb = [s.osl[0].v, s.osl[1].v]
            wevt = []
            sevt = []

            @blk.sync
            def _(sy):
                for h2 in range(c.ND2):
                    if h2 >= 1:
                        sy.wait_ge(s.tm.h, tmb0 + h2 * c.NTT)
                    w = s.wsl[h2 % 2]
                    for kc in range(c.DC):
                        w.inc(sy.dma_start(out=wbuf[:, kc * c.NW:(kc + 1) * c.NW],
                                           in_=s.wv_d[h2, kc]), 16)
                    wevt.append(w.v)
                    for Mt in range(c.NTT):
                        g = h2 * c.NTT + Mt
                        sy.wait_ge(s.vv.h, vvb0 + g + 1)
                        o = s.osl[g % 2]
                        o.inc(sy.dma_start(
                            out=s.vloc_d[Mt, :, h2 * c.NW:(h2 + 1) * c.NW],
                            in_=qst[:, (g % 2) * c.T:(g % 2) * c.T + c.NW]), 16)
                        sevt.append(o.v)

            @blk.tensor
            def _(te):
                for h2 in range(c.ND2):
                    te.wait_ge(s.wsl[h2 % 2].h, wevt[h2])
                    for Mt in range(c.NTT):
                        g = h2 * c.NTT + Mt
                        if g >= 4:
                            te.wait_ge(s.vv.h, vvb0 + (g - 4) + 1)
                        bank = self.pslot(g % 4, c.NW)
                        for kc in range(c.DC):
                            ti = te.matmul(bank,
                                           fmA[:, kc * c.T + Mt * P:kc * c.T + (Mt + 1) * P],
                                           wbuf[:, kc * c.NW:(kc + 1) * c.NW],
                                           start=(kc == 0), stop=(kc == c.DC - 1))
                        s.tm.inc(ti)

            @blk.vector
            def _(v):
                v.wait_ge(s.osl[0].h, oslb[0])
                v.wait_ge(s.osl[1].h, oslb[1])
                for h2 in range(c.ND2):
                    for Mt in range(c.NTT):
                        g = h2 * c.NTT + Mt
                        v.wait_ge(s.tm.h, tmb0 + g + 1)
                        if g >= 2:
                            v.wait_ge(s.osl[g % 2].h, sevt[g - 2])
                        bank = self.pslot(g % 4, c.NW)
                        v.tensor_add(out=qst[:, (g % 2) * c.T:(g % 2) * c.T + c.NW],
                                     in0=bank,
                                     in1=s.bvb[:, h2 * c.NW:(h2 + 1) * c.NW]).then_inc(s.vv.h, 1)
                        s.vv.v += 1

            @blk.gpsimd
            def _(g):
                g.wait_ge(s.osl[0].h, oslb[0])
                g.wait_ge(s.osl[1].h, oslb[1])
                s.cc.inc(g.collective_compute(
                    "AllGather", ALU.bypass, replica_groups=self.pair_groups(),
                    ins=[s.kloc_d.opt()], outs=[s.kpair_d.opt()]), 1)
                g.wait_ge(s.cc.h, s.cc.v)
        es3.close()
        es2.close()

        with nc.Block() as blk:
            @blk.gpsimd
            def _(g):
                g.wait_ge(s.osl[0].h, s.osl[0].v)
                g.wait_ge(s.osl[1].h, s.osl[1].v)
                s.cc.inc(g.collective_compute(
                    "AllGather", ALU.bypass, replica_groups=self.pair_groups(),
                    ins=[s.vloc_d.opt()], outs=[s.vpair_d.opt()]), 1)
                g.wait_ge(s.cc.h, s.cc.v)

        self.emit_flash()

    def pair_groups(self):
        return [[2 * b, 2 * b + 1] for b in range(self.c.n_cores // 2)]

    def emit_flash(self):
        c, s, nc = self.c, self, self.nc
        P, HD = c.P, c.HDIM
        QC = c.TN
        NQC = c.T // QC
        qtile = QC // P
        NKT = c.NKT
        half_kt = NKT // 2
        es2 = ExitStack()
        qcb = es2.enter_context(nc.sbuf_tensor(self.name("fqc"), [P, c.DC * QC], F32))
        kbl = es2.enter_context(nc.sbuf_tensor(self.name("fkb"), [P, 2 * P], F32))
        vbl = es2.enter_context(nc.sbuf_tensor(self.name("fvb"), [P, 2 * 130], F32))
        ptl = es2.enter_context(nc.sbuf_tensor(self.name("fpt"), [P, 2 * 2 * QC], F32))
        atn = es2.enter_context(nc.sbuf_tensor(self.name("fat"), [P, c.DC * QC], F32))
        x1b = es2.enter_context(nc.sbuf_tensor(self.name("fx1"), [P, qtile * c.D], F32))
        lsb = es2.enter_context(nc.sbuf_tensor(self.name("fls"), [P, 4 * QC], F32))
        lbc = es2.enter_context(nc.sbuf_tensor(self.name("flb"), [P, QC], F32))
        wpb = es2.enter_context(nc.sbuf_tensor(self.name("fwp"), [P, 2 * c.D], F32))
        wpv = s.wp_d.rearrange("(kc p) d -> kc p d", p=P)

        with nc.Block() as blk:
            @blk.vector
            def _(v):
                for sl in range(2):
                    for hh in range(2):
                        ins = v.memset(vbl[:, sl * 130 + hh * 65 + 64:sl * 130 + hh * 65 + 65], 1.0)
                s.vv.inc(ins)

        for qc in range(NQC):
            with nc.Block() as blk:
                @blk.sync
                def _(sy, qc=qc):
                    for cc in range(c.DC):
                        s.sy.inc(sy.dma_start(out=qcb[:, cc * QC:(cc + 1) * QC],
                                              in_=s.qT_d[cc, :, qc * QC:(qc + 1) * QC]), 16)
                    sy.wait_ge(s.sy.h, s.sy.v)

                @blk.gpsimd
                def _(g, qc=qc):
                    for t in range(qtile):
                        gt = qc * qtile + t
                        g.tensor_add(out=x1b[:, t * c.D:(t + 1) * c.D],
                                     in0=s.xres[:, gt * c.D:(gt + 1) * c.D],
                                     in1=s.bpb[:]).then_inc(s.gg.h, 1)
                        s.gg.v += 1

            pvb, scb = s.psB, s.psA
            with nc.Block() as blk:
                tmb0 = s.tm.v
                ssb0 = s.ss.v
                vvb0 = s.vv.v
                gdb0 = s.gd.v
                kevt = []

                @blk.sync
                def _(sy, qc=qc):
                    for hg in range(c.NHG):
                        for kt in range(NKT):
                            g = hg * NKT + kt
                            sp = kt // half_kt
                            ktl = kt % half_kt
                            sl = g % 2
                            if g >= 2:
                                sy.wait_ge(s.tm.h, tmb0 + 2 * (g - 1))
                            i = s.isl[sl]
                            i.inc(sy.dma_start(
                                out=kbl[:, sl * P:(sl + 1) * P],
                                in_=s.kpair_d[sp, hg, :, ktl * P:(ktl + 1) * P]), 16)
                            vsrc = s.vpair_d[sp, ktl, :, hg * P:hg * P + 2 * HD]
                            vsrc = vsrc.rearrange("p (h d) -> p h d", h=2)
                            vdst = vbl[:, sl * 130:(sl + 1) * 130].rearrange(
                                "p (h e) -> p h e", h=2)[:, :, 0:HD]
                            i.inc(sy.dma_start(out=vdst, in_=vsrc), 16)
                            kevt.append(i.v)

                @blk.tensor
                def _(te, qc=qc):
                    for hg in range(c.NHG):
                        pvo = (hg % 2) * 1024
                        for kt in range(NKT):
                            g = hg * NKT + kt
                            sl = g % 2
                            te.wait_ge(s.isl[sl].h, kevt[g])
                            if g >= 2:
                                te.wait_ge(s.ss.h, ssb0 + (g - 1))
                            for hh in range(2):
                                ti = te.matmul(
                                    scb[:, sl * 1024 + hh * 512:sl * 1024 + hh * 512 + QC],
                                    kbl[hh * HD:(hh + 1) * HD, sl * P:(sl + 1) * P],
                                    qcb[hh * HD:(hh + 1) * HD, hg * QC:(hg + 1) * QC],
                                    start=True, stop=True, tile_position=(hh * HD, 0))
                            s.tm.inc(ti)
                            te.wait_ge(s.ss.h, ssb0 + g + 1)
                            if kt == 0 and hg >= 2:
                                te.wait_ge(s.vv.h, vvb0 + 4 * (hg - 1))
                            for hh in range(2):
                                ti = te.matmul(
                                    pvb[0:65, pvo + hh * 512:pvo + hh * 512 + QC],
                                    vbl[:, sl * 130 + hh * 65:sl * 130 + (hh + 1) * 65],
                                    ptl[:, sl * 2 * QC + hh * QC:sl * 2 * QC + (hh + 1) * QC],
                                    start=(kt == 0), stop=(kt == NKT - 1))
                            s.tm.inc(ti)

                @blk.scalar
                def _(sc, qc=qc):
                    for g in range(c.NHG * NKT):
                        sl = g % 2
                        sc.wait_ge(s.tm.h, tmb0 + 2 * g + 1)
                        if g >= 2:
                            sc.wait_ge(s.tm.h, tmb0 + 2 * (g - 2) + 2)
                        sin = scb[:, sl * 1024:(sl + 1) * 1024].rearrange(
                            "p (h x) -> p h x", h=2)[:, :, 0:QC]
                        pout = ptl[:, sl * 2 * QC:(sl + 1) * 2 * QC].rearrange(
                            "p (h x) -> p h x", h=2)
                        sc.activation(out=pout, in_=sin,
                                      func=AF.Exp, scale=0.125).then_inc(s.ss.h, 1)
                        s.ss.v += 1

                @blk.vector
                def _(v, qc=qc):
                    for hg in range(c.NHG):
                        pvo = (hg % 2) * 1024
                        v.wait_ge(s.tm.h, tmb0 + 2 * (hg + 1) * NKT)
                        v.tensor_copy(out=lsb[64:65, 0:QC], in_=pvb[64:65, pvo:pvo + QC])
                        v.tensor_copy(out=lsb[64:65, QC:2 * QC],
                                      in_=pvb[64:65, pvo + 512:pvo + 512 + QC]
                                      ).then_inc(s.vv.h, 1)
                        s.vv.v += 1
                        v.wait_ge(s.gd.h, gdb0 + 80 * hg + 32)
                        v.reciprocal(out=lsb[0:2, QC:2 * QC],
                                     in_=lsb[0:2, 0:QC]).then_inc(s.vv.h, 1)
                        s.vv.v += 1
                        v.wait_ge(s.gd.h, gdb0 + 80 * hg + 80)
                        for hh in range(2):
                            h = 2 * hg + hh
                            cch, rr = h // 2, (h % 2) * HD
                            v.tensor_mul(out=atn[rr:rr + HD, cch * QC:(cch + 1) * QC],
                                         in0=pvb[0:HD, pvo + hh * 512:pvo + hh * 512 + QC],
                                         in1=lbc[hh * HD:(hh + 1) * HD, 0:QC]).then_inc(s.vv.h, 1)
                            s.vv.v += 1

                @blk.gpsimd
                def _(g_, qc=qc):
                    for hg in range(c.NHG):
                        if hg >= 1:
                            g_.wait_ge(s.gd.h, gdb0 + 80 * hg)
                        g_.wait_ge(s.vv.h, vvb0 + 4 * hg + 1)
                        s.gd.inc(g_.dma_start(out=s.lraw_d[0:2 * QC],
                                              in_=lsb[64:65, 0:2 * QC]), 16)
                        g_.wait_ge(s.gd.h, gdb0 + 80 * hg + 16)
                        lr = s.lraw_d[0:2 * QC].rearrange("(a b) -> a b", a=2)
                        s.gd.inc(g_.dma_start(out=lsb[0:2, 0:QC], in_=lr), 16)
                        g_.wait_ge(s.vv.h, vvb0 + 4 * hg + 2)
                        s.gd.inc(g_.dma_start(out=s.lrec_d[0:2 * QC],
                                              in_=lsb[0:2, QC:2 * QC]), 16)
                        g_.wait_ge(s.gd.h, gdb0 + 80 * hg + 48)
                        for hh in range(2):
                            s.gd.inc(g_.dma_start(
                                out=lbc[hh * HD:(hh + 1) * HD, 0:QC],
                                in_=s.lrec_d[hh * QC:(hh + 1) * QC].unsqueeze(0).to_broadcast([HD, QC])), 16)

            with nc.Block() as blk:
                wevt = []

                @blk.sync
                def _(sy, qc=qc):
                    tmb0 = s.tm.v
                    for kc in range(c.DC):
                        if kc >= 2:
                            sy.wait_ge(s.tm.h, tmb0 + (kc - 1))
                        w = s.wsl[kc % 2]
                        w.inc(sy.dma_start(out=wpb[:, (kc % 2) * c.D:(kc % 2 + 1) * c.D],
                                           in_=wpv[kc]), 16)
                        wevt.append(w.v)

                @blk.tensor
                def _(te, qc=qc):
                    for kc in range(c.DC):
                        te.wait_ge(s.wsl[kc % 2].h, wevt[kc])
                        for t in range(qtile):
                            bank = s.psA if t % 2 == 0 else s.psB
                            boff = (t // 2) * c.D
                            for nn in range(c.ND2):
                                ti = te.matmul(
                                    bank[:, boff + nn * c.NW:boff + (nn + 1) * c.NW],
                                    atn[:, kc * QC + t * P:kc * QC + (t + 1) * P],
                                    wpb[:, (kc % 2) * c.D + nn * c.NW:(kc % 2) * c.D + (nn + 1) * c.NW],
                                    start=(kc == 0), stop=(kc == c.DC - 1))
                        s.tm.inc(ti)

                @blk.vector
                def _(v, qc=qc):
                    v.wait_ge(s.tm.h, s.tm.v)
                    v.wait_ge(s.gg.h, s.gg.v)
                    for t in range(qtile):
                        bank = s.psA if t % 2 == 0 else s.psB
                        boff = (t // 2) * c.D
                        gt = qc * qtile + t
                        self.vstep(v, v.tensor_add(out=s.xres[:, gt * c.D:(gt + 1) * c.D],
                                                   in0=bank[:, boff:boff + c.D],
                                                   in1=x1b[:, t * c.D:(t + 1) * c.D]))
        es2.close()



def prep_inputs(cfg, L0_attn_x, L0_attn_w_qkv, L0_attn_b_qkv, L0_attn_w_proj,
                L0_attn_b_proj, L0_ffn_w_gate, L0_ffn_w_up, L0_ffn_w_down):
    c = cfg
    P = c.P
    x = np.ascontiguousarray(np.asarray(L0_attn_x, dtype=np.float32))
    D = c.D
    wqkv = np.asarray(L0_attn_w_qkv, dtype=np.float32)

    def tile_lhsT(wT, n_m):
        arr = wT.reshape(c.DC, P, n_m, P)
        return np.ascontiguousarray(arr.transpose(2, 1, 0, 3))

    wqk = tile_lhsT(wqkv[:, 0:2 * D], 2 * c.DC)
    wv = np.ascontiguousarray(
        wqkv[:, 2 * D:3 * D].reshape(c.DC, P, c.ND2, c.NW).transpose(2, 0, 1, 3))
    bqkv = np.asarray(L0_attn_b_qkv, dtype=np.float32)
    bqk = np.ascontiguousarray(bqkv[0:2 * D].reshape(2 * c.DC, P))
    bv = np.ascontiguousarray(bqkv[2 * D:3 * D])
    wp = np.ascontiguousarray(np.asarray(L0_attn_w_proj, dtype=np.float32))
    bp = np.ascontiguousarray(np.asarray(L0_attn_b_proj, dtype=np.float32))
    wu = tile_lhsT(np.ascontiguousarray(np.asarray(L0_ffn_w_up, dtype=np.float32).T), c.HC)
    wg = tile_lhsT(np.ascontiguousarray(np.asarray(L0_ffn_w_gate, dtype=np.float32).T), c.HC)
    wd = np.ascontiguousarray(np.asarray(L0_ffn_w_down, dtype=np.float32).T)

    in_maps = []
    for core in range(c.n_cores):
        b, h = core // 2, core % 2
        xs = np.ascontiguousarray(x[b, h * c.T:(h + 1) * c.T, :])
        in_maps.append(dict(x0=xs, wqk=wqk, wv=wv, bqk=bqk, bv=bv, wp=wp, bp=bp,
                            wu=wu, wg=wg, wd=wd))
    return in_maps


def build_program(cfg, dbg=None):
    b = Builder(cfg, dbg=dbg)
    b.emit()
    return b.finish()


def kernel(L0_attn_x, L0_attn_w_qkv, L0_attn_b_qkv, L0_attn_w_proj, L0_attn_b_proj,
           L0_ffn_w_gate, L0_ffn_w_up, L0_ffn_w_down):
    cfg = FULL
    in_maps = prep_inputs(cfg, L0_attn_x, L0_attn_w_qkv, L0_attn_b_qkv,
                          L0_attn_w_proj, L0_attn_b_proj, L0_ffn_w_gate,
                          L0_ffn_w_up, L0_ffn_w_down)
    nc = build_program(cfg)
    res = run_bass_kernel_spmd(nc, in_maps, core_ids=list(range(cfg.n_cores)))
    x = np.asarray(L0_attn_x)
    Bb, S, D = x.shape
    out = np.empty((Bb, S, D), dtype=np.float32)
    for core in range(cfg.n_cores):
        bb, h = core // 2, core % 2
        out[bb, h * cfg.T:(h + 1) * cfg.T, :] = res.results[core]["y"]
    return out


# version 07
# speedup vs baseline: 1.1688x; 1.0971x over previous
import numpy as np
from contextlib import ExitStack
from dataclasses import dataclass

import concourse.bass as bass
import concourse.mybir as mybir
from concourse.bass_utils import run_bass_kernel_spmd
from concourse.dve_spec import Spec, Src0, Src1, C0, C1, C2, Zero, One, maxx, minn, AluOp
from concourse.dve_spec import lower as dve_lower
from concourse.dve_ops import DveOp, OPS
import concourse.dve_ops as dve_ops_mod

F32 = mybir.dt.float32
I16 = mybir.dt.int16
I32 = mybir.dt.int32
AF = mybir.ActivationFunctionType
ALU = mybir.AluOpType
AX = mybir.AxisListType

MAGIC = float(np.float32(1.5 * 2.0 ** 23))


def _register_op(name, spec, reference):
    for op in OPS:
        if op.name == name:
            return op
    shas, perf_en = {}, {}
    for ver in ("v3", "v4"):
        try:
            shas[ver] = dve_lower(spec, ver=ver, perf_en=False, name=name).sha(ver)
            perf_en[ver] = False
        except Exception:
            pass
    op = DveOp(name, spec, subdim=False, uops_sha=shas, perf_en=perf_en,
               reference=reference)
    OPS.append(op)
    dve_ops_mod._SUB_OPCODE_FOR_NAME[name] = (
        dve_ops_mod._CUSTOM_DVE_ROW_BASE + len(OPS) - 1)
    dve_ops_mod.CUSTOM_DVE_REFS[name] = op.ref
    dve_ops_mod.CUSTOM_DVE_SPECS[name] = spec
    return op


RCS = _register_op(
    "MK_RCS",
    Spec(body=minn(maxx((Src0 * C0 + C2) - C2, Zero - (C1 + One)), C1)),
    lambda in0, in1, s0, s1, imm2: np.clip(np.rint(in0 * s0), -s1 - 1.0, s1),
)
SUBMUL = _register_op(
    "MK_SUBMUL",
    Spec(body=(Src0 - C0) * Src1, accum=AluOp.MAX),
    lambda in0, in1, s0, s1, imm2: (
        (in0 - s0) * in1,
        np.max((in0 - s0) * in1, axis=-1, keepdims=True),
    ),
)


@dataclass(frozen=True)
class Cfg:
    D: int = 1024
    H: int = 3072
    NH: int = 16
    T: int = 2048
    n_cores: int = 8
    eps: float = 1e-5
    stages: int = 3

    P = 128
    HDIM = 64

    @property
    def DC(self):
        return self.D // 128

    @property
    def HC(self):
        return self.H // 128

    @property
    def NTT(self):
        return self.T // 128

    @property
    def S(self):
        return 2 * self.T

    @property
    def NKT(self):
        return self.S // 128

    @property
    def TN(self):
        return min(512, self.T)

    @property
    def NTC(self):
        return self.T // self.TN

    @property
    def NHG(self):
        return self.NH // 2

    @property
    def NW(self):
        return min(512, self.D)

    @property
    def ND2(self):
        return self.D // self.NW


FULL = Cfg()
MINI = Cfg(D=256, H=768, NH=4, T=256)
MIDI = Cfg(D=1024, H=1536, NH=16, T=1024)


class Sem:
    def __init__(self, es, nc, name):
        self.h = es.enter_context(nc.semaphore(name))
        self.v = 0

    def inc(self, ins, n=1):
        ins.then_inc(self.h, n)
        self.v += n
        return self.v


class Builder:
    def __init__(self, cfg: Cfg, dbg=None):
        self.dbg = dbg
        self.ffn_i = 0
        self.dbg_done = False
        self.c = cfg
        nc = bass.Bass("TRN2", target_bir_lowering=False, debug=False)
        self.nc = nc
        self.uid = 0
        c, P = cfg, cfg.P
        self.x0_d = nc.dram_tensor("x0", [c.T, c.D], F32, kind="ExternalInput").ap()
        self.wqk_d = nc.dram_tensor("wqk", [2 * c.DC, P, c.DC, P], F32, kind="ExternalInput").ap()
        self.wv_d = nc.dram_tensor("wv", [c.ND2, c.DC, P, c.NW], F32, kind="ExternalInput").ap()
        self.bqk_d = nc.dram_tensor("bqk", [2 * c.DC, P], F32, kind="ExternalInput").ap()
        self.bv_d = nc.dram_tensor("bv", [c.D], F32, kind="ExternalInput").ap()
        self.wp_d = nc.dram_tensor("wp", [c.D, c.D], F32, kind="ExternalInput").ap()
        self.bp_d = nc.dram_tensor("bp", [c.D], F32, kind="ExternalInput").ap()
        self.wu_d = nc.dram_tensor("wu", [c.HC, P, c.DC, P], F32, kind="ExternalInput").ap()
        self.wg_d = nc.dram_tensor("wg", [c.HC, P, c.DC, P], F32, kind="ExternalInput").ap()
        self.wd_d = nc.dram_tensor("wd", [c.H, c.D], F32, kind="ExternalInput").ap()
        self.y_d = nc.dram_tensor("y", [c.T, c.D], F32, kind="ExternalOutput").ap()
        self.wuq_d = nc.dram_tensor("wuq", [c.HC, P, c.DC, P], F32).ap()
        self.wgq_d = nc.dram_tensor("wgq", [c.HC, P, c.DC, P], F32).ap()
        self.wdq_d = nc.dram_tensor("wdq", [c.H, c.D], F32).ap()
        self.mup_d = nc.dram_tensor("mup", [c.HC, P, c.T], F32).ap()
        self.mgate_d = nc.dram_tensor("mgate", [c.HC, P, c.T], F32).ap()
        self.kup_d = nc.dram_tensor("kup", [c.NTC, c.HC, P, c.TN], I16).ap()
        self.kgate_d = nc.dram_tensor("kgate", [c.NTC, c.HC, P, c.TN], I16).ap()
        self.kg2_d = nc.dram_tensor("kg2", [c.NTC, c.HC, P, c.TN], I16).ap()
        self.qT_d = nc.dram_tensor("qTd", [c.DC, P, c.T], F32).ap()
        self.kloc_d = nc.dram_tensor("kloc", [c.DC, P, c.T], F32).ap()
        self.vloc_d = nc.dram_tensor("vloc", [c.NTT, P, c.D], F32).ap()
        self.kpair_d = nc.dram_tensor("kpair", [2, c.DC, P, c.T], F32).ap()
        self.vpair_d = nc.dram_tensor("vpair", [2, c.NTT, P, c.D], F32).ap()
        self.lraw_d = nc.dram_tensor("lraw", [2 * 512], F32).ap()
        self.lrec_d = nc.dram_tensor("lrec", [2 * 512], F32).ap()
        self.ar_in = [nc.dram_tensor(f"arin{i}", [256], F32).ap() for i in range(8)]
        self.ar_out = [nc.dram_tensor(f"arout{i}", [256], F32).ap() for i in range(8)]
        self.n_ar = 0

        self.es = ExitStack()
        self.es.__enter__()
        es = self.es
        sb = lambda n, sh, dt=F32: es.enter_context(nc.sbuf_tensor(n, sh, dt))
        self.xres = sb("xres", [P, c.NTT * c.D])
        self.stat = sb("stat", [P, 8 * c.NTT])
        self.slots = sb("slots", [P, 256])
        self.scal = sb("scal", [P, 32])
        self.arb = sb("arb", [P, 256])
        self.ident = sb("ident", [P, P])
        self.bvb = sb("bvb", [P, c.D])
        self.bpb = sb("bpb", [P, c.D])
        self.bqks = sb("bqks", [P, 2 * c.DC])
        self.psA = es.enter_context(nc.psum_tensor("psA", [P, 2048], F32))
        self.psB = es.enter_context(nc.psum_tensor("psB", [P, 2048], F32))
        self.sy = Sem(es, nc, "sy")
        self.gd = Sem(es, nc, "gd")
        self.cc = Sem(es, nc, "cc")
        self.tm = Sem(es, nc, "tm")
        self.vv = Sem(es, nc, "vv")
        self.ss = Sem(es, nc, "ss")
        self.gg = Sem(es, nc, "gg")
        self.wsl = [Sem(es, nc, "wsl0"), Sem(es, nc, "wsl1")]
        self.isl = [Sem(es, nc, f"isl{i}") for i in range(4)]
        self.osl = [Sem(es, nc, f"osl{i}") for i in range(4)]

    def sc(self, i):
        return self.scal[:, i:i + 1]

    def name(self, stem):
        self.uid += 1
        return f"{stem}{self.uid}"

    def pslot(self, i, width):
        t = self.psA if i % 2 == 0 else self.psB
        off = ((i // 2) % 2) * 1024
        return t[:, off:off + width]

    def vstep(self, v, ins):
        ins.then_inc(self.vv.h, 1)
        self.vv.v += 1
        v.wait_ge(self.vv.h, self.vv.v)

    def finish(self):
        self.es.__exit__(None, None, None)
        return self.nc

    def emit(self):
        c, nc = self.c, self.nc
        with nc.Block() as blk:
            @blk.sync
            def _(sy):
                x0v = self.x0_d.rearrange("(t p) d -> t p d", p=c.P)
                for t in range(c.NTT):
                    self.sy.inc(sy.dma_start(out=self.xres[:, t * c.D:(t + 1) * c.D], in_=x0v[t]), 16)
                self.sy.inc(sy.dma_start(out=self.bvb[:], in_=self.bv_d.unsqueeze(0).to_broadcast([c.P, c.D])), 16)
                self.sy.inc(sy.dma_start(out=self.bpb[:], in_=self.bp_d.unsqueeze(0).to_broadcast([c.P, c.D])), 16)
                idd = nc.inline_tensor(np.eye(c.P, dtype=np.float32), "identnp").ap()
                self.sy.inc(sy.dma_start(out=self.ident[:], in_=idd), 16)
                for m in range(2 * c.DC):
                    self.sy.inc(sy.dma_start(out=self.bqks[:, m:m + 1], in_=self.bqk_d[m].unsqueeze(1)), 16)
                sy.wait_ge(self.sy.h, self.sy.v)

        self.emit_dequant()
        self.emit_ffn()
        if c.stages >= 2 and not self.dbg_done:
            self.emit_attention()
        if c.stages >= 3 and not self.dbg_done:
            self.emit_ffn()

        if not self.dbg_done:
            with nc.Block() as blk:
                @blk.sync
                def _(sy):
                    yv = self.y_d.rearrange("(t p) d -> t p d", p=c.P)
                    for t in range(c.NTT):
                        self.sy.inc(sy.dma_start(out=yv[t], in_=self.xres[:, t * c.D:(t + 1) * c.D]), 16)
                    sy.wait_ge(self.sy.h, self.sy.v)

    def emit_dequant(self):
        c, s, nc = self.c, self, self.nc
        P = c.P
        W = 512 if c.D >= 512 else c.D
        es2 = ExitStack()
        wa = es2.enter_context(nc.sbuf_tensor(self.name("dqa"), [P, 4 * W], F32))
        wt = es2.enter_context(nc.sbuf_tensor(self.name("dqt"), [P, W], F32))
        bm = es2.enter_context(nc.sbuf_tensor(self.name("dqm"), [P, W // 2], F32))
        ki = es2.enter_context(nc.sbuf_tensor(self.name("dqi"), [P, W], I16))
        nb = W // 32

        tiles = []
        for wsrc, wdst in [(s.wu_d, s.wuq_d), (s.wg_d, s.wgq_d)]:
            for m in range(c.HC):
                v4 = wsrc[m].rearrange("p cc pp -> p (cc pp)")
                d4 = wdst[m].rearrange("p cc pp -> p (cc pp)")
                for j in range(0, c.DC * P, W):
                    w_ = min(W, c.DC * P - j)
                    tiles.append((v4[:, j:j + w_], d4[:, j:j + w_], w_))
        wdv_s = s.wd_d.rearrange("(hc p) d -> hc p d", p=P)
        wdv_q = s.wdq_d.rearrange("(hc p) d -> hc p d", p=P)
        for hc in range(c.HC):
            for j in range(0, c.D, W):
                w_ = min(W, c.D - j)
                tiles.append((wdv_s[hc, :, j:j + w_], wdv_q[hc, :, j:j + w_], w_))

        with nc.Block() as blk:
            n = len(tiles)
            VPT = 7
            ld_evt = []
            st_evt = []
            vvb = s.vv.v

            @blk.sync
            def _(sy):
                for i in range(n + 1):
                    if i < n:
                        src_ap, dst_ap, w_ = tiles[i]
                        if i >= 4:
                            sy.wait_ge(s.osl[i % 4].h, st_evt[i - 4])
                        il = s.isl[i % 4]
                        il.inc(sy.dma_start(out=wa[:, (i % 4) * W:(i % 4) * W + w_], in_=src_ap), 16)
                        ld_evt.append(il.v)
                    if i >= 1:
                        j = i - 1
                        sy.wait_ge(s.vv.h, vvb + (j + 1) * VPT)
                        ol = s.osl[j % 4]
                        ol.inc(sy.dma_start(out=tiles[j][1],
                                            in_=wa[:, (j % 4) * W:(j % 4) * W + tiles[j][2]]), 16)
                        st_evt.append(ol.v)
                for q in range(4):
                    sy.wait_ge(s.osl[q].h, s.osl[q].v)

            @blk.vector
            def _(v):
                for i, (src_ap, dst_ap, w_) in enumerate(tiles):
                    nb_ = w_ // 32
                    o = (i % 4) * W
                    v.wait_ge(s.isl[i % 4].h, ld_evt[i])
                    self.vstep(v, v.transpose(out=wt[:, 0:w_], in_=wa[:, o:o + w_]))
                    self.vstep(v, v.reduce_max(
                        out=bm[:, 0:nb_],
                        in_=wt[:, 0:w_].rearrange("p (g e) -> p g e", e=32),
                        axis=AX.X, apply_absolute_value=True))
                    self.vstep(v, v.tensor_scalar(out=bm[:, 0:nb_], in0=bm[:, 0:nb_],
                                                  scalar1=1.0 / 7.0, scalar2=1e-12,
                                                  op0=ALU.mult, op1=ALU.max))
                    self.vstep(v, v.reciprocal(out=bm[:, nb:nb + nb_], in_=bm[:, 0:nb_]))
                    self.vstep(v, v.tensor_tensor(
                        out=ki[:, 0:w_].rearrange("p (g e) -> p g e", e=32),
                        in0=wt[:, 0:w_].rearrange("p (g e) -> p g e", e=32),
                        in1=bm[:, nb:nb + nb_].rearrange("p (g o) -> p g o", o=1).to_broadcast([P, nb_, 32]),
                        op=ALU.mult))
                    self.vstep(v, v.tensor_tensor(
                        out=wt[:, 0:w_].rearrange("p (g e) -> p g e", e=32),
                        in0=ki[:, 0:w_].rearrange("p (g e) -> p g e", e=32),
                        in1=bm[:, 0:nb_].rearrange("p (g o) -> p g o", o=1).to_broadcast([P, nb_, 32]),
                        op=ALU.mult))
                    self.vstep(v, v.transpose(out=wa[:, o:o + w_], in_=wt[:, 0:w_]))
        es2.close()

    def emit_ln_stats(self, lnsc):
        c, s = self.c, self
        N = c.NTT
        nc = self.nc
        with nc.Block() as blk:
            @blk.vector
            def _(v):
                for t in range(N):
                    xt = s.xres[:, t * c.D:(t + 1) * c.D]
                    ins = v.reduce_sum(out=s.stat[:, t:t + 1], in_=xt, axis=AX.X)
                s.vv.inc(ins)

            @blk.scalar
            def _(sc):
                ssb = s.ss.v
                for t in range(N):
                    xt = s.xres[:, t * c.D:(t + 1) * c.D]
                    if t >= 1:
                        sc.wait_ge(s.ss.h, ssb + t)
                    sc.activation(out=lnsc[:, 0:c.D], in_=xt, func=AF.Square,
                                  accum_out=s.stat[:, N + t:N + t + 1]).then_inc(s.ss.h, 1)
                    s.ss.v += 1

        with nc.Block() as blk:
            @blk.vector
            def _(v):
                st = s.stat
                self.vstep(v, v.tensor_scalar(out=st[:, 2 * N:3 * N], in0=st[:, 0:N],
                                              scalar1=1.0 / c.D, scalar2=None, op0=ALU.mult))
                self.vstep(v, v.scalar_tensor_tensor(out=st[:, 3 * N:4 * N], in0=st[:, 2 * N:3 * N],
                                                     scalar=-1.0, in1=st[:, 2 * N:3 * N],
                                                     op0=ALU.mult, op1=ALU.mult))
                self.vstep(v, v.scalar_tensor_tensor(out=st[:, 3 * N:4 * N], in0=st[:, N:2 * N],
                                                     scalar=1.0 / c.D, in1=st[:, 3 * N:4 * N],
                                                     op0=ALU.mult, op1=ALU.add))
                self.vstep(v, v.tensor_scalar(out=st[:, 3 * N:4 * N], in0=st[:, 3 * N:4 * N],
                                              scalar1=c.eps, scalar2=None, op0=ALU.add))
                s.vv.inc(v.engine_nop())

            @blk.scalar
            def _(sc):
                sc.wait_ge(s.vv.h, s.vv.v)
                st = s.stat
                s.ss.inc(sc.activation(out=st[:, 4 * N:5 * N], in_=st[:, 3 * N:4 * N],
                                       func=AF.Sqrt))

            @blk.vector
            def _(v):
                v.wait_ge(s.ss.h, s.ss.v)
                st = s.stat
                self.vstep(v, v.reciprocal(out=st[:, 5 * N:6 * N], in_=st[:, 4 * N:5 * N]))
                self.vstep(v, v.tensor_mul(out=st[:, 6 * N:7 * N], in0=st[:, 5 * N:6 * N],
                                           in1=st[:, 5 * N:6 * N]))
                self.vstep(v, v.tensor_mul(out=st[:, 6 * N:7 * N], in0=st[:, 6 * N:7 * N],
                                           in1=st[:, 3 * N:4 * N]))
                self.vstep(v, v.tensor_scalar(out=st[:, 6 * N:7 * N], in0=st[:, 6 * N:7 * N],
                                              scalar1=-0.5, scalar2=1.5, op0=ALU.mult, op1=ALU.add))
                self.vstep(v, v.tensor_mul(out=st[:, 4 * N:5 * N], in0=st[:, 5 * N:6 * N],
                                           in1=st[:, 6 * N:7 * N]))
                s.vv.inc(v.engine_nop())

    def emit_armax(self, ranges, outs):
        c, s, nc = self.c, self, self.nc
        idx = self.n_ar % 8
        self.n_ar += 1
        k = len(ranges)
        with nc.Block() as blk:
            @blk.vector
            def _(v):
                for i, (st0, n) in enumerate(ranges):
                    ins = v.reduce_max(out=s.scal[:, 30 - i:31 - i],
                                       in_=s.slots[:, st0:st0 + n], axis=AX.X)
                s.vv.inc(ins)

            @blk.gpsimd
            def _(g):
                g.wait_ge(s.vv.h, s.vv.v)
                for i in range(k):
                    s.gd.inc(g.dma_start(out=s.ar_in[idx][i * 128:(i + 1) * 128],
                                         in_=s.scal[:, 30 - i:31 - i]), 16)
                g.wait_ge(s.gd.h, s.gd.v)
                s.cc.inc(g.collective_compute(
                    "AllReduce", ALU.max, replica_groups=[list(range(c.n_cores))],
                    ins=[s.ar_in[idx][0:k * 128].opt()],
                    outs=[s.ar_out[idx][0:k * 128].opt()]), 1)
                g.wait_ge(s.cc.h, s.cc.v)
                for i in range(k):
                    s.gd.inc(g.dma_start(
                        out=s.arb[:, i * 128:(i + 1) * 128],
                        in_=s.ar_out[idx][i * 128:(i + 1) * 128].unsqueeze(0).to_broadcast([c.P, 128])), 16)

            gdv = s.gd.v

            @blk.vector
            def _(v):
                v.wait_ge(s.gd.h, gdv)
                for i in range(k):
                    ins = v.reduce_max(out=outs[i], in_=s.arb[:, i * 128:(i + 1) * 128], axis=AX.X)
                s.vv.inc(ins)

    def dump(self, ap, nelem):
        nc, s = self.nc, self
        yflat = self.y_d.rearrange("a b -> (a b)")
        with nc.Block() as blk:
            @blk.sync
            def _(sy):
                s.sy.inc(sy.dma_start(out=yflat[0:nelem], in_=ap), 16)
                sy.wait_ge(s.sy.h, s.sy.v)
        self.dbg_done = True

    def dbg_hit(self, tag):
        return (not self.dbg_done) and self.dbg is not None and \
            self.dbg[0] == tag and self.dbg[1] == self.ffn_i

    def emit_ffn(self):
        c, s, nc = self.c, self, self.nc
        self.ffn_i += 1
        P, N = c.P, c.NTT
        es2 = ExitStack()
        esln = ExitStack()
        fmA = es2.enter_context(nc.sbuf_tensor(self.name("fmA"), [P, c.DC * c.T], F32))
        lnsc = esln.enter_context(nc.sbuf_tensor(self.name("lnsc"), [P, 2 * c.D], F32))
        xlnh = esln.enter_context(nc.sbuf_tensor(self.name("xlnh"), [P, max(1, N // 2) * c.D], F32))

        self.emit_ln_stats(lnsc)

        with nc.Block() as blk:
            @blk.vector
            def _(v):
                for t in range(N):
                    self.vstep(v, v.scalar_tensor_tensor(
                        out=lnsc[:, 0:c.D],
                        in0=s.xres[:, t * c.D:(t + 1) * c.D],
                        scalar=s.stat[:, 2 * N + t:2 * N + t + 1],
                        in1=s.stat[:, 4 * N + t:4 * N + t + 1].to_broadcast([c.P, c.D]),
                        op0=ALU.subtract, op1=ALU.mult))
                    self.vstep(v, v.reduce_max(out=s.slots[:, t:t + 1], in_=lnsc[:, 0:c.D],
                                               axis=AX.X, apply_absolute_value=True))

        self.emit_armax([(0, N)], [self.sc(25)])
        with nc.Block() as blk:
            @blk.vector
            def _(v):
                v.wait_ge(s.vv.h, s.vv.v)
                self.vstep(v, v.tensor_scalar(out=self.sc(1), in0=self.sc(25), scalar1=1.0 / 32767.0,
                                              scalar2=1e-12, op0=ALU.mult, op1=ALU.max))
                self.vstep(v, v.reciprocal(out=self.sc(2), in_=self.sc(1)))
                s.vv.inc(v.engine_nop())

        self.emit_lnT(fmA, xlnh, rcs_scale=self.sc(2))
        esln.close()
        if self.dbg_hit("k1T"):
            self.dump(fmA[:], c.P * c.DC * c.T)
            es2.close()
            return

        self.emit_upgate(fmA)
        es2.close()

        self.emit_armax([(0, c.HC * c.NTC), (128, c.HC * c.NTC)],
                        [self.sc(30), self.sc(29)])
        with nc.Block() as blk:
            @blk.vector
            def _(v):
                v.wait_ge(s.vv.h, s.vv.v)
                self.vstep(v, v.scalar_tensor_tensor(out=self.sc(20), in0=self.sc(30),
                                                     scalar=1.0 / 32767.0, in1=self.sc(1),
                                                     op0=ALU.mult, op1=ALU.mult))
                self.vstep(v, v.tensor_scalar(out=self.sc(20), in0=self.sc(20), scalar1=1e-12,
                                              scalar2=None, op0=ALU.max))
                self.vstep(v, v.reciprocal(out=self.sc(28), in_=self.sc(20)))
                self.vstep(v, v.tensor_mul(out=self.sc(4), in0=self.sc(28), in1=self.sc(1)))
                self.vstep(v, v.scalar_tensor_tensor(out=self.sc(21), in0=self.sc(29),
                                                     scalar=1.0 / 32767.0, in1=self.sc(1),
                                                     op0=ALU.mult, op1=ALU.mult))
                self.vstep(v, v.tensor_scalar(out=self.sc(21), in0=self.sc(21), scalar1=1e-12,
                                              scalar2=None, op0=ALU.max))
                self.vstep(v, v.reciprocal(out=self.sc(28), in_=self.sc(21)))
                self.vstep(v, v.tensor_mul(out=self.sc(5), in0=self.sc(28), in1=self.sc(1)))
                self.vstep(v, v.tensor_copy(out=self.sc(6), in_=self.sc(21)))
                s.vv.inc(v.engine_nop())

        if self.dbg_hit("scal_ar1"):
            self.dump(s.scal[:], c.P * 32)
            return
        self.emit_f3()
        self.emit_armax([(0, c.HC)], [self.sc(30)])
        with nc.Block() as blk:
            @blk.vector
            def _(v):
                v.wait_ge(s.vv.h, s.vv.v)
                self.vstep(v, v.scalar_tensor_tensor(out=self.sc(22), in0=self.sc(6),
                                                     scalar=1.0 / 65536.0, in1=self.sc(30),
                                                     op0=ALU.mult, op1=ALU.mult))
                self.vstep(v, v.tensor_scalar(out=self.sc(22), in0=self.sc(22),
                                              scalar1=1.0 / 32767.0, scalar2=1e-12,
                                              op0=ALU.mult, op1=ALU.max))
                self.vstep(v, v.reciprocal(out=self.sc(28), in_=self.sc(22)))
                self.vstep(v, v.scalar_tensor_tensor(out=self.sc(7), in0=self.sc(6),
                                                     scalar=1.0 / 65536.0, in1=self.sc(28),
                                                     op0=ALU.mult, op1=ALU.mult))
                self.vstep(v, v.tensor_mul(out=self.sc(8), in0=self.sc(22), in1=self.sc(20)))
                s.vv.inc(v.engine_nop())

        if self.dbg_hit("scal_ar2"):
            self.dump(s.scal[:], c.P * 32)
            return
        self.emit_f4()
        self.emit_armax([(0, c.HC)], [self.sc(30)])
        with nc.Block() as blk:
            @blk.vector
            def _(v):
                v.wait_ge(s.vv.h, s.vv.v)
                self.vstep(v, v.scalar_tensor_tensor(out=self.sc(23), in0=self.sc(30),
                                                     scalar=1.0 / 32767.0, in1=self.sc(8),
                                                     op0=ALU.mult, op1=ALU.mult))
                self.vstep(v, v.tensor_scalar(out=self.sc(23), in0=self.sc(23), scalar1=1e-12,
                                              scalar2=None, op0=ALU.max))
                self.vstep(v, v.reciprocal(out=self.sc(28), in_=self.sc(23)))
                self.vstep(v, v.tensor_mul(out=self.sc(9), in0=self.sc(28), in1=self.sc(8)))
                self.vstep(v, v.tensor_scalar(out=self.sc(10), in0=self.sc(23), scalar1=0.5,
                                              scalar2=None, op0=ALU.mult))
                s.vv.inc(v.engine_nop())

        if self.dbg_hit("scal_ar3"):
            self.dump(s.scal[:], c.P * 32)
            return
        self.emit_f5()

    def emit_lnT(self, fmA, xlnh, rcs_scale):
        c, s, nc = self.c, self, self.nc
        P, N = c.P, c.NTT
        nh = max(1, N // 2)
        nbank = 4
        W = c.DC * P
        EV = 3 if rcs_scale is not None else 1
        esl = ExitStack()
        tmpf = esl.enter_context(nc.sbuf_tensor(self.name("ltf"), [P, W], F32))
        tmpi = esl.enter_context(nc.sbuf_tensor(self.name("lti"), [P, W], I16))
        for half in range(N // nh):
            t0 = half * nh
            with nc.Block() as blk:
                @blk.vector
                def _(v, t0=t0):
                    tmb = s.tm.v
                    for i in range(nh):
                        t = t0 + i
                        v.scalar_tensor_tensor(
                            out=xlnh[:, i * c.D:(i + 1) * c.D],
                            in0=s.xres[:, t * c.D:(t + 1) * c.D],
                            scalar=s.stat[:, 2 * N + t:2 * N + t + 1],
                            in1=s.stat[:, 4 * N + t:4 * N + t + 1].to_broadcast([P, c.D]),
                            op0=ALU.subtract, op1=ALU.mult).then_inc(s.vv.h, 1)
                        s.vv.v += 1
                    for i in range(nh):
                        t = t0 + i
                        v.wait_ge(s.tm.h, tmb + i + 1)
                        bank = self.pslot(i % nbank, W)
                        o = fmA.rearrange("p (cc tt) -> p cc tt", cc=c.DC)
                        if rcs_scale is not None:
                            self.vstep(v, v.tensor_scalar(out=tmpf[:], in0=bank,
                                                          scalar1=rcs_scale, scalar2=None,
                                                          op0=ALU.mult))
                            self.vstep(v, v.tensor_copy(out=tmpi[:], in_=tmpf[:]))
                            v.tensor_copy(out=o[:, :, t * P:(t + 1) * P],
                                          in_=tmpi[:].rearrange("p (cc tt) -> p cc tt", cc=c.DC)
                                          ).then_inc(s.vv.h, 1)
                            s.vv.v += 1
                        else:
                            v.tensor_copy(out=o[:, :, t * P:(t + 1) * P], in_=bank
                                          ).then_inc(s.vv.h, 1)
                            s.vv.v += 1

                vvb = s.vv.v - (1 + EV) * nh

                @blk.tensor
                def _(te, t0=t0, vvb=vvb):
                    for i in range(nh):
                        te.wait_ge(s.vv.h, vvb + i + 1)
                        if i >= nbank:
                            te.wait_ge(s.vv.h, vvb + nh + EV * (i - nbank) + 1)
                        bank = self.pslot(i % nbank, W)
                        for cc in range(c.DC):
                            ti = te.transpose(bank[:, cc * P:(cc + 1) * P],
                                              xlnh[:, i * c.D + cc * P:i * c.D + (cc + 1) * P],
                                              s.ident[:])
                        s.tm.inc(ti)
        esl.close()

    def emit_upgate(self, fmA):
        c, s, nc = self.c, self, self.nc
        P = c.P
        es2 = ExitStack()
        wbuf = es2.enter_context(nc.sbuf_tensor(self.name("wb"), [P, 2 * c.DC * P], F32))
        mst = es2.enter_context(nc.sbuf_tensor(self.name("ms"), [P, 2 * c.T], F32))
        nb = min(4, 2048 // c.TN)

        for wd_, md_, slot0 in [(s.wuq_d, s.mup_d, 0), (s.wgq_d, s.mgate_d, 128)]:
            with nc.Block() as blk:
                tmb0 = s.tm.v
                vvb0 = s.vv.v
                oslb = [s.osl[0].v, s.osl[1].v]
                wevt = []
                sevt = []

                @blk.sync
                def _(sy, wd_=wd_, md_=md_):
                    for M in range(c.HC + 1):
                        if M < c.HC:
                            if M >= 2:
                                sy.wait_ge(s.tm.h, tmb0 + (M - 1) * c.NTC)
                            w = s.wsl[M % 2]
                            w.inc(sy.dma_start(
                                out=wbuf[:, (M % 2) * c.DC * P:(M % 2 + 1) * c.DC * P],
                                in_=wd_[M]), 16)
                            wevt.append(w.v)
                        if M >= 1:
                            sy.wait_ge(s.vv.h, vvb0 + 2 * M * c.NTC)
                            o = s.osl[(M - 1) % 2]
                            o.inc(sy.dma_start(
                                out=md_[M - 1],
                                in_=mst[:, ((M - 1) % 2) * c.T:((M - 1) % 2 + 1) * c.T]), 16)
                            sevt.append(o.v)

                @blk.tensor
                def _(te):
                    for M in range(c.HC):
                        w = s.wsl[M % 2]
                        te.wait_ge(w.h, wevt[M])
                        for n in range(c.NTC):
                            g = M * c.NTC + n
                            if g >= nb:
                                te.wait_ge(s.vv.h, vvb0 + 2 * (g - nb) + 1)
                            bank = self.pslot(g % nb, c.TN)
                            for kc in range(c.DC):
                                ti = te.matmul(
                                    bank,
                                    wbuf[:, (M % 2) * c.DC * P + kc * P:(M % 2) * c.DC * P + (kc + 1) * P],
                                    fmA[:, kc * c.T + n * c.TN:kc * c.T + (n + 1) * c.TN],
                                    start=(kc == 0), stop=(kc == c.DC - 1))
                            s.tm.inc(ti)

                @blk.vector
                def _(v, slot0=slot0):
                    v.wait_ge(s.osl[0].h, oslb[0])
                    v.wait_ge(s.osl[1].h, oslb[1])
                    for M in range(c.HC):
                        if M >= 2:
                            o = s.osl[M % 2]
                            v.wait_ge(o.h, sevt[M - 2])
                        for n in range(c.NTC):
                            g = M * c.NTC + n
                            v.wait_ge(s.tm.h, tmb0 + g + 1)
                            bank = self.pslot(g % nb, c.TN)
                            dst = mst[:, (M % 2) * c.T + n * c.TN:(M % 2) * c.T + (n + 1) * c.TN]
                            self.vstep(v, v.tensor_scalar(
                                out=dst, in0=bank, scalar1=1.0, scalar2=None, op0=ALU.mult))
                            v.reduce_max(out=s.slots[:, slot0 + g:slot0 + g + 1],
                                         in_=dst, axis=AX.X,
                                         apply_absolute_value=True).then_inc(s.vv.h, 1)
                            s.vv.v += 1
        with nc.Block() as blk:
            @blk.sync
            def _(sy):
                sy.wait_ge(s.osl[0].h, s.osl[0].v)
                sy.wait_ge(s.osl[1].h, s.osl[1].v)
        es2.close()

    def emit_f3(self):
        c, s, nc = self.c, self, self.nc
        P = c.P
        es2 = ExitStack()
        mst = es2.enter_context(nc.sbuf_tensor(self.name("f3m"), [P, 4 * c.T], F32))
        kst = es2.enter_context(nc.sbuf_tensor(self.name("f3k"), [P, 4 * c.T], I16))
        sg = es2.enter_context(nc.sbuf_tensor(self.name("f3s"), [P, 2 * c.T], F32))
        jj = es2.enter_context(nc.sbuf_tensor(self.name("f3j"), [P, 2 * c.T], F32))
        pp = es2.enter_context(nc.sbuf_tensor(self.name("f3p"), [P, c.T], F32))
        tpf = es2.enter_context(nc.sbuf_tensor(self.name("f3t"), [P, c.T], F32))
        ju16 = es2.enter_context(nc.sbuf_tensor(self.name("f3u"), [P, c.T], mybir.dt.uint16))

        with nc.Block() as blk:
            vvb0 = s.vv.v
            ssb0 = s.ss.v
            VPC = 9
            ievt = []
            oevt = []

            @blk.sync
            def _(sy):
                for cc in range(c.HC + 1):
                    if cc < c.HC:
                        sl = cc % 2
                        if cc >= 2:
                            sy.wait_ge(s.vv.h, vvb0 + (cc - 2) * VPC + 3)
                        i = s.isl[sl]
                        i.inc(sy.dma_start(out=mst[:, sl * c.T:(sl + 1) * c.T], in_=s.mup_d[cc]), 16)
                        i.inc(sy.dma_start(out=mst[:, (2 + sl) * c.T:(3 + sl) * c.T], in_=s.mgate_d[cc]), 16)
                        ievt.append(i.v)
                    if cc >= 1:
                        sl = (cc - 1) % 2
                        sy.wait_ge(s.vv.h, vvb0 + (cc - 1) * VPC + 4)
                        o = s.osl[sl]
                        for n in range(c.NTC):
                            o.inc(sy.dma_start(
                                out=s.kup_d[n, cc - 1],
                                in_=kst[:, sl * c.T + n * c.TN:sl * c.T + (n + 1) * c.TN]), 16)
                            o.inc(sy.dma_start(
                                out=s.kgate_d[n, cc - 1],
                                in_=kst[:, (2 + sl) * c.T + n * c.TN:(2 + sl) * c.T + (n + 1) * c.TN]), 16)
                        oevt.append(o.v)

            @blk.vector
            def _(v):
                for cc in range(c.HC):
                    sl = cc % 2
                    v.wait_ge(s.isl[sl].h, ievt[cc])
                    if cc >= 2:
                        v.wait_ge(s.osl[sl].h, oevt[cc - 2])
                    self.vstep(v, v.tensor_scalar(out=tpf[:], in0=mst[:, sl * c.T:(sl + 1) * c.T],
                                                  scalar1=self.sc(4), scalar2=None, op0=ALU.mult))
                    self.vstep(v, v.tensor_copy(out=kst[:, sl * c.T:(sl + 1) * c.T], in_=tpf[:]))
                    self.vstep(v, v.tensor_scalar(out=tpf[:], in0=mst[:, (2 + sl) * c.T:(3 + sl) * c.T],
                                                  scalar1=self.sc(5), scalar2=None, op0=ALU.mult))
                    self.vstep(v, v.tensor_copy(out=kst[:, (2 + sl) * c.T:(3 + sl) * c.T], in_=tpf[:]))
                    v.wait_ge(s.ss.h, ssb0 + cc + 1)
                    self.vstep(v, v.tensor_scalar(out=tpf[:], in0=sg[:, sl * c.T:(sl + 1) * c.T],
                                                  scalar1=65536.0, scalar2=None, op0=ALU.mult))
                    self.vstep(v, v.tensor_copy(out=ju16[:], in_=tpf[:]))
                    self.vstep(v, v.tensor_copy(out=jj[:, sl * c.T:(sl + 1) * c.T], in_=ju16[:]))
                    self.vstep(v, v.tensor_mul(out=pp[:], in0=kst[:, (2 + sl) * c.T:(3 + sl) * c.T],
                                               in1=jj[:, sl * c.T:(sl + 1) * c.T]))
                    self.vstep(v, v.reduce_max(out=s.slots[:, cc:cc + 1], in_=pp[:], axis=AX.X,
                                               apply_absolute_value=True))

            @blk.scalar
            def _(sc):
                for cc in range(c.HC):
                    sl = cc % 2
                    sc.wait_ge(s.vv.h, vvb0 + cc * VPC + 4)
                    if cc >= 2:
                        sc.wait_ge(s.ss.h, ssb0 + cc - 1)
                    sc.activation(out=sg[:, sl * c.T:(sl + 1) * c.T],
                                  in_=kst[:, (2 + sl) * c.T:(3 + sl) * c.T],
                                  func=AF.Sigmoid, scale=self.sc(6)).then_inc(s.ss.h, 1)
                    s.ss.v += 1

        with nc.Block() as blk:
            @blk.sync
            def _(sy):
                sy.wait_ge(s.osl[0].h, s.osl[0].v)
                sy.wait_ge(s.osl[1].h, s.osl[1].v)
        es2.close()

    def emit_f4(self):
        c, s, nc = self.c, self, self.nc
        P = c.P
        es2 = ExitStack()
        kin = es2.enter_context(nc.sbuf_tensor(self.name("f4a"), [P, 4 * c.T], I16))
        sg = es2.enter_context(nc.sbuf_tensor(self.name("f4s"), [P, 2 * c.T], F32))
        jj = es2.enter_context(nc.sbuf_tensor(self.name("f4j"), [P, c.T], F32))
        pp = es2.enter_context(nc.sbuf_tensor(self.name("f4p"), [P, c.T], F32))
        k2 = es2.enter_context(nc.sbuf_tensor(self.name("f4k"), [P, 2 * c.T], I16))
        tpf = es2.enter_context(nc.sbuf_tensor(self.name("f4t"), [P, c.T], F32))
        ju16 = es2.enter_context(nc.sbuf_tensor(self.name("f4u"), [P, c.T], mybir.dt.uint16))

        with nc.Block() as blk:
            vvb0 = s.vv.v
            ssb0 = s.ss.v
            VPC = 8
            ievt = []
            oevt = []

            @blk.sync
            def _(sy):
                for cc in range(c.HC + 1):
                    if cc < c.HC:
                        sl = cc % 2
                        if cc >= 2:
                            sy.wait_ge(s.vv.h, vvb0 + (cc - 2) * VPC + 7)
                        i = s.isl[sl]
                        for n in range(c.NTC):
                            i.inc(sy.dma_start(
                                out=kin[:, sl * c.T + n * c.TN:sl * c.T + (n + 1) * c.TN],
                                in_=s.kgate_d[n, cc]), 16)
                            i.inc(sy.dma_start(
                                out=kin[:, (2 + sl) * c.T + n * c.TN:(2 + sl) * c.T + (n + 1) * c.TN],
                                in_=s.kup_d[n, cc]), 16)
                        ievt.append(i.v)
                    if cc >= 1:
                        sl = (cc - 1) % 2
                        sy.wait_ge(s.vv.h, vvb0 + (cc - 1) * VPC + 6)
                        o = s.osl[sl]
                        for n in range(c.NTC):
                            o.inc(sy.dma_start(
                                out=s.kg2_d[n, cc - 1],
                                in_=k2[:, sl * c.T + n * c.TN:sl * c.T + (n + 1) * c.TN]), 16)
                        oevt.append(o.v)

            @blk.scalar
            def _(sc):
                for cc in range(c.HC):
                    sl = cc % 2
                    sc.wait_ge(s.isl[sl].h, ievt[cc])
                    if cc >= 2:
                        sc.wait_ge(s.ss.h, ssb0 + cc - 1)
                        sc.wait_ge(s.vv.h, vvb0 + (cc - 2) * VPC + 1)
                    sc.activation(out=sg[:, sl * c.T:(sl + 1) * c.T],
                                  in_=kin[:, sl * c.T:(sl + 1) * c.T],
                                  func=AF.Sigmoid, scale=self.sc(6)).then_inc(s.ss.h, 1)
                    s.ss.v += 1

            @blk.vector
            def _(v):
                for cc in range(c.HC):
                    sl = cc % 2
                    v.wait_ge(s.ss.h, ssb0 + cc + 1)
                    self.vstep(v, v.tensor_scalar(out=tpf[:], in0=sg[:, sl * c.T:(sl + 1) * c.T],
                                                  scalar1=65536.0, scalar2=None, op0=ALU.mult))
                    self.vstep(v, v.tensor_copy(out=ju16[:], in_=tpf[:]))
                    self.vstep(v, v.tensor_copy(out=jj[:], in_=ju16[:]))
                    self.vstep(v, v.tensor_mul(out=pp[:], in0=kin[:, sl * c.T:(sl + 1) * c.T],
                                               in1=jj[:]))
                    if cc >= 2:
                        v.wait_ge(s.osl[sl].h, oevt[cc - 2])
                    self.vstep(v, v.tensor_scalar(out=tpf[:], in0=pp[:],
                                                  scalar1=self.sc(7), scalar2=None, op0=ALU.mult))
                    self.vstep(v, v.tensor_copy(out=k2[:, sl * c.T:(sl + 1) * c.T], in_=tpf[:]))
                    self.vstep(v, v.tensor_mul(out=pp[:], in0=k2[:, sl * c.T:(sl + 1) * c.T],
                                               in1=kin[:, (2 + sl) * c.T:(3 + sl) * c.T]))
                    self.vstep(v, v.reduce_max(out=s.slots[:, cc:cc + 1], in_=pp[:], axis=AX.X,
                                               apply_absolute_value=True))

        with nc.Block() as blk:
            @blk.sync
            def _(sy):
                sy.wait_ge(s.osl[0].h, s.osl[0].v)
                sy.wait_ge(s.osl[1].h, s.osl[1].v)
        es2.close()

    def emit_f5(self):
        c, s, nc = self.c, self, self.nc
        if self.dbg_hit("xres_pre_f5"):
            self.dump(self.xres[:, 0:4 * c.D], c.P * 4 * c.D)
            return
        P = c.P
        ntile = c.TN // P
        es2 = ExitStack()
        k2st = es2.enter_context(nc.sbuf_tensor(self.name("f5a"), [P, c.HC * c.TN], I16))
        kust = es2.enter_context(nc.sbuf_tensor(self.name("f5b"), [P, c.HC * c.TN], I16))
        qko = es2.enter_context(nc.sbuf_tensor(self.name("f5q"), [P, c.HC * c.TN], F32))
        ko2 = qko
        wdb = es2.enter_context(nc.sbuf_tensor(self.name("f5w"), [P, 2 * c.D], F32))
        wdv = s.wdq_d.rearrange("(hc p) d -> hc p d", p=P)

        for tc in range(c.NTC):
            with nc.Block() as blk:
                iv = []

                @blk.sync
                def _(sy, tc=tc):
                    i = s.isl[0]
                    for cc in range(c.HC):
                        i.inc(sy.dma_start(out=k2st[:, cc * c.TN:(cc + 1) * c.TN],
                                           in_=s.kg2_d[tc, cc]), 16)
                        i.inc(sy.dma_start(out=kust[:, cc * c.TN:(cc + 1) * c.TN],
                                           in_=s.kup_d[tc, cc]), 16)
                    iv.append(i.v)

                @blk.vector
                def _(v):
                    v.wait_ge(s.isl[0].h, iv[0])
                    self.vstep(v, v.tensor_mul(out=qko[:], in0=k2st[:], in1=kust[:]))
                    self.vstep(v, v.tensor_scalar(out=qko[:], in0=qko[:], scalar1=self.sc(9),
                                                  scalar2=None, op0=ALU.mult))
                    self.vstep(v, v.tensor_copy(out=k2st[:], in_=qko[:]))
                    self.vstep(v, v.tensor_copy(out=ko2[:], in_=k2st[:]))

            if self.dbg_hit("ko_tc") and tc == self.dbg[2]:
                self.dump(qko[:], c.P * c.HC * c.TN)
                break
            slow = self.dbg is not None and self.dbg[0] == "f5slow" and self.dbg[1] == self.ffn_i
            with nc.Block() as blk:
                tmb0 = s.tm.v
                wevt = []

                @blk.sync
                def _(sy, tc=tc, slow=slow):
                    for cc in range(c.HC):
                        if slow:
                            if cc >= 1:
                                sy.wait_ge(s.tm.h, tmb0 + cc)
                            w = s.wsl[0]
                            w.inc(sy.dma_start(out=wdb[:, 0:c.D], in_=wdv[cc]), 16)
                            wevt.append(w.v)
                            continue
                        if cc >= 2:
                            sy.wait_ge(s.tm.h, tmb0 + (cc - 1))
                        w = s.wsl[cc % 2]
                        w.inc(sy.dma_start(out=wdb[:, (cc % 2) * c.D:(cc % 2 + 1) * c.D],
                                           in_=wdv[cc]), 16)
                        wevt.append(w.v)

                @blk.tensor
                def _(te, tc=tc, slow=slow):
                    for cc in range(c.HC):
                        wsl_i = 0 if slow else cc % 2
                        woff = 0 if slow else (cc % 2) * c.D
                        te.wait_ge(s.wsl[wsl_i].h, wevt[cc])
                        for t in range(ntile):
                            bank = s.psA if t % 2 == 0 else s.psB
                            boff = (t // 2) * c.D
                            for nn in range(c.ND2):
                                ti = te.matmul(
                                    bank[:, boff + nn * c.NW:boff + (nn + 1) * c.NW],
                                    ko2[:, cc * c.TN + t * P:cc * c.TN + (t + 1) * P],
                                    wdb[:, woff + nn * c.NW:woff + (nn + 1) * c.NW],
                                    start=(cc == 0), stop=(cc == c.HC - 1))
                        s.tm.inc(ti)

                @blk.vector
                def _(v, tc=tc):
                    v.wait_ge(s.tm.h, s.tm.v)
                    abt = (self.dbg is not None and self.dbg[0] == "abtest"
                           and self.dbg[1] == self.ffn_i and tc == 0)
                    for t in range(ntile):
                        bank = s.psA if t % 2 == 0 else s.psB
                        boff = (t // 2) * c.D
                        gt = tc * ntile + t
                        if abt:
                            self.vstep(v, v.tensor_scalar(
                                out=qko[:, t * c.D:(t + 1) * c.D],
                                in0=bank[:, boff:boff + c.D], scalar1=self.sc(10),
                                scalar2=None, op0=ALU.mult))
                            self.vstep(v, v.scalar_tensor_tensor(
                                out=qko[:, (4 + t) * c.D:(5 + t) * c.D],
                                in0=bank[:, boff:boff + c.D], scalar=self.sc(10),
                                in1=s.xres[:, gt * c.D:(gt + 1) * c.D],
                                op0=ALU.mult, op1=ALU.add))
                            self.vstep(v, v.scalar_tensor_tensor(
                                out=s.xres[:, gt * c.D:(gt + 1) * c.D],
                                in0=bank[:, boff:boff + c.D], scalar=self.sc(10),
                                in1=s.xres[:, gt * c.D:(gt + 1) * c.D],
                                op0=ALU.mult, op1=ALU.add))
                            continue
                        if self.dbg is not None and self.dbg[0] == "mdown" and self.dbg[1] == self.ffn_i:
                            self.vstep(v, v.tensor_scalar(
                                out=s.xres[:, gt * c.D:(gt + 1) * c.D],
                                in0=bank[:, boff:boff + c.D], scalar1=self.sc(10),
                                scalar2=None, op0=ALU.mult))
                        else:
                            self.vstep(v, v.scalar_tensor_tensor(
                                out=s.xres[:, gt * c.D:(gt + 1) * c.D],
                                in0=bank[:, boff:boff + c.D], scalar=self.sc(10),
                                in1=s.xres[:, gt * c.D:(gt + 1) * c.D],
                                op0=ALU.mult, op1=ALU.add))
            if self.dbg is not None and self.dbg[0] == "mdown" and self.dbg[1] == self.ffn_i and tc == self.dbg[2] and not self.dbg_done:
                self.dump(s.xres[:, 0:4 * c.D], c.P * 4 * c.D)
                break
            if self.dbg is not None and self.dbg[0] == "abtest" and self.dbg[1] == self.ffn_i and tc == 0 and not self.dbg_done:
                self.dump(qko[:, 0:8 * c.D], c.P * 8 * c.D)
                break
            if self.dbg is not None and self.dbg[0] == "f5slow" and self.dbg[1] == self.ffn_i and tc == 0 and not self.dbg_done:
                self.dump(s.xres[:, 0:4 * c.D], c.P * 4 * c.D)
                break
            if self.dbg_hit("after_f5_tc") and tc == self.dbg[2]:
                self.dump(s.xres[:, 0:4 * c.D], c.P * 4 * c.D)
                break
        es2.close()

    def emit_attention(self):
        c, s, nc = self.c, self, self.nc
        P, N = c.P, c.NTT
        es2 = ExitStack()
        esln = ExitStack()
        fmA = es2.enter_context(nc.sbuf_tensor(self.name("afm"), [P, c.DC * c.T], F32))
        lnsc = esln.enter_context(nc.sbuf_tensor(self.name("alns"), [P, 2 * c.D], F32))
        xlnh = esln.enter_context(nc.sbuf_tensor(self.name("axln"), [P, max(1, N // 2) * c.D], F32))

        self.emit_ln_stats(lnsc)
        self.emit_lnT(fmA, xlnh, rcs_scale=None)
        esln.close()

        es3 = ExitStack()
        wbuf = es3.enter_context(nc.sbuf_tensor(self.name("awb"), [P, max(2 * c.DC * P, c.DC * c.NW)], F32))
        qst = es3.enter_context(nc.sbuf_tensor(self.name("aqs"), [P, 2 * c.T], F32))
        nb = min(4, 2048 // c.TN)
        with nc.Block() as blk:
            tmb0 = s.tm.v
            vvb0 = s.vv.v
            wevt = []
            sevt = []

            @blk.sync
            def _(sy):
                for M in range(2 * c.DC + 1):
                    if M < 2 * c.DC:
                        if M >= 2:
                            sy.wait_ge(s.tm.h, tmb0 + (M - 1) * c.NTC)
                        w = s.wsl[M % 2]
                        w.inc(sy.dma_start(
                            out=wbuf[:, (M % 2) * c.DC * P:(M % 2 + 1) * c.DC * P],
                            in_=s.wqk_d[M]), 16)
                        wevt.append(w.v)
                    if M >= 1:
                        sy.wait_ge(s.vv.h, vvb0 + M * c.NTC)
                        dst = s.qT_d[M - 1] if (M - 1) < c.DC else s.kloc_d[M - 1 - c.DC]
                        o = s.osl[(M - 1) % 2]
                        o.inc(sy.dma_start(
                            out=dst, in_=qst[:, ((M - 1) % 2) * c.T:((M - 1) % 2 + 1) * c.T]), 16)
                        sevt.append(o.v)

            @blk.tensor
            def _(te):
                for M in range(2 * c.DC):
                    te.wait_ge(s.wsl[M % 2].h, wevt[M])
                    for n in range(c.NTC):
                        g = M * c.NTC + n
                        if g >= nb:
                            te.wait_ge(s.vv.h, vvb0 + (g - nb) + 1)
                        bank = self.pslot(g % nb, c.TN)
                        for kc in range(c.DC):
                            ti = te.matmul(
                                bank,
                                wbuf[:, (M % 2) * c.DC * P + kc * P:(M % 2) * c.DC * P + (kc + 1) * P],
                                fmA[:, kc * c.T + n * c.TN:kc * c.T + (n + 1) * c.TN],
                                start=(kc == 0), stop=(kc == c.DC - 1))
                        s.tm.inc(ti)

            @blk.vector
            def _(v):
                for M in range(2 * c.DC):
                    if M >= 2:
                        v.wait_ge(s.osl[M % 2].h, sevt[M - 2])
                    for n in range(c.NTC):
                        g = M * c.NTC + n
                        v.wait_ge(s.tm.h, tmb0 + g + 1)
                        bank = self.pslot(g % nb, c.TN)
                        v.tensor_scalar(
                            out=qst[:, (M % 2) * c.T + n * c.TN:(M % 2) * c.T + (n + 1) * c.TN],
                            in0=bank, scalar1=s.bqks[:, M:M + 1], scalar2=None,
                            op0=ALU.add).then_inc(s.vv.h, 1)
                        s.vv.v += 1

        with nc.Block() as blk:
            tmb0 = s.tm.v
            vvb0 = s.vv.v
            oslb = [s.osl[0].v, s.osl[1].v]
            wevt = []
            sevt = []

            @blk.sync
            def _(sy):
                for h2 in range(c.ND2):
                    if h2 >= 1:
                        sy.wait_ge(s.tm.h, tmb0 + h2 * c.NTT)
                    w = s.wsl[h2 % 2]
                    for kc in range(c.DC):
                        w.inc(sy.dma_start(out=wbuf[:, kc * c.NW:(kc + 1) * c.NW],
                                           in_=s.wv_d[h2, kc]), 16)
                    wevt.append(w.v)
                    for Mt in range(c.NTT):
                        g = h2 * c.NTT + Mt
                        sy.wait_ge(s.vv.h, vvb0 + g + 1)
                        o = s.osl[g % 2]
                        o.inc(sy.dma_start(
                            out=s.vloc_d[Mt, :, h2 * c.NW:(h2 + 1) * c.NW],
                            in_=qst[:, (g % 2) * c.T:(g % 2) * c.T + c.NW]), 16)
                        sevt.append(o.v)

            @blk.tensor
            def _(te):
                for h2 in range(c.ND2):
                    te.wait_ge(s.wsl[h2 % 2].h, wevt[h2])
                    for Mt in range(c.NTT):
                        g = h2 * c.NTT + Mt
                        if g >= 4:
                            te.wait_ge(s.vv.h, vvb0 + (g - 4) + 1)
                        bank = self.pslot(g % 4, c.NW)
                        for kc in range(c.DC):
                            ti = te.matmul(bank,
                                           fmA[:, kc * c.T + Mt * P:kc * c.T + (Mt + 1) * P],
                                           wbuf[:, kc * c.NW:(kc + 1) * c.NW],
                                           start=(kc == 0), stop=(kc == c.DC - 1))
                        s.tm.inc(ti)

            @blk.vector
            def _(v):
                v.wait_ge(s.osl[0].h, oslb[0])
                v.wait_ge(s.osl[1].h, oslb[1])
                for h2 in range(c.ND2):
                    for Mt in range(c.NTT):
                        g = h2 * c.NTT + Mt
                        v.wait_ge(s.tm.h, tmb0 + g + 1)
                        if g >= 2:
                            v.wait_ge(s.osl[g % 2].h, sevt[g - 2])
                        bank = self.pslot(g % 4, c.NW)
                        v.tensor_add(out=qst[:, (g % 2) * c.T:(g % 2) * c.T + c.NW],
                                     in0=bank,
                                     in1=s.bvb[:, h2 * c.NW:(h2 + 1) * c.NW]).then_inc(s.vv.h, 1)
                        s.vv.v += 1

            @blk.gpsimd
            def _(g):
                g.wait_ge(s.osl[0].h, oslb[0])
                g.wait_ge(s.osl[1].h, oslb[1])
                s.cc.inc(g.collective_compute(
                    "AllGather", ALU.bypass, replica_groups=self.pair_groups(),
                    ins=[s.kloc_d.opt()], outs=[s.kpair_d.opt()]), 1)
                g.wait_ge(s.cc.h, s.cc.v)
        es3.close()
        es2.close()

        with nc.Block() as blk:
            @blk.gpsimd
            def _(g):
                g.wait_ge(s.osl[0].h, s.osl[0].v)
                g.wait_ge(s.osl[1].h, s.osl[1].v)
                s.cc.inc(g.collective_compute(
                    "AllGather", ALU.bypass, replica_groups=self.pair_groups(),
                    ins=[s.vloc_d.opt()], outs=[s.vpair_d.opt()]), 1)
                g.wait_ge(s.cc.h, s.cc.v)

        self.emit_flash()

    def pair_groups(self):
        return [[2 * b, 2 * b + 1] for b in range(self.c.n_cores // 2)]

    def emit_flash(self):
        c, s, nc = self.c, self, self.nc
        P, HD = c.P, c.HDIM
        QC = c.TN
        NQC = c.T // QC
        qtile = QC // P
        NKT = c.NKT
        half_kt = NKT // 2
        es2 = ExitStack()
        qcb = es2.enter_context(nc.sbuf_tensor(self.name("fqc"), [P, c.DC * QC], F32))
        kbl = es2.enter_context(nc.sbuf_tensor(self.name("fkb"), [P, 4 * P], F32))
        vbl = es2.enter_context(nc.sbuf_tensor(self.name("fvb"), [P, 4 * 130], F32))
        ptl = es2.enter_context(nc.sbuf_tensor(self.name("fpt"), [P, 2 * 2 * QC], F32))
        atn = es2.enter_context(nc.sbuf_tensor(self.name("fat"), [P, c.DC * QC], F32))
        x1b = es2.enter_context(nc.sbuf_tensor(self.name("fx1"), [P, qtile * c.D], F32))
        lsb = es2.enter_context(nc.sbuf_tensor(self.name("fls"), [P, 4 * QC], F32))
        lbc = es2.enter_context(nc.sbuf_tensor(self.name("flb"), [P, QC], F32))
        wpb = es2.enter_context(nc.sbuf_tensor(self.name("fwp"), [P, 2 * c.D], F32))
        wpv = s.wp_d.rearrange("(kc p) d -> kc p d", p=P)

        with nc.Block() as blk:
            @blk.vector
            def _(v):
                for sl in range(4):
                    for hh in range(2):
                        ins = v.memset(vbl[:, sl * 130 + hh * 65 + 64:sl * 130 + hh * 65 + 65], 1.0)
                s.vv.inc(ins)

        for qc in range(NQC):
            with nc.Block() as blk:
                @blk.sync
                def _(sy, qc=qc):
                    for cc in range(c.DC):
                        s.sy.inc(sy.dma_start(out=qcb[:, cc * QC:(cc + 1) * QC],
                                              in_=s.qT_d[cc, :, qc * QC:(qc + 1) * QC]), 16)
                    sy.wait_ge(s.sy.h, s.sy.v)

                @blk.gpsimd
                def _(g, qc=qc):
                    for t in range(qtile):
                        gt = qc * qtile + t
                        g.tensor_add(out=x1b[:, t * c.D:(t + 1) * c.D],
                                     in0=s.xres[:, gt * c.D:(gt + 1) * c.D],
                                     in1=s.bpb[:]).then_inc(s.gg.h, 1)
                        s.gg.v += 1

            pvb, scb = s.psB, s.psA
            with nc.Block() as blk:
                tmb0 = s.tm.v
                ssb0 = s.ss.v
                vvb0 = s.vv.v
                gdb0 = s.gd.v
                kevt = []

                @blk.sync
                def _(sy, qc=qc):
                    for hg in range(c.NHG):
                        for kt in range(NKT):
                            g = hg * NKT + kt
                            sp = kt // half_kt
                            ktl = kt % half_kt
                            slK = g % 4
                            if g >= 4:
                                sy.wait_ge(s.tm.h, tmb0 + 2 * (g - 4) + 2)
                            i = s.isl[slK]
                            i.inc(sy.dma_start(
                                out=kbl[:, slK * P:(slK + 1) * P],
                                in_=s.kpair_d[sp, hg, :, ktl * P:(ktl + 1) * P]), 16)
                            vsrc = s.vpair_d[sp, ktl, :, hg * P:hg * P + 2 * HD]
                            vsrc = vsrc.rearrange("p (h d) -> p h d", h=2)
                            vdst = vbl[:, slK * 130:(slK + 1) * 130].rearrange(
                                "p (h e) -> p h e", h=2)[:, :, 0:HD]
                            i.inc(sy.dma_start(out=vdst, in_=vsrc), 16)
                            kevt.append(i.v)

                @blk.tensor
                def _(te, qc=qc):
                    for hg in range(c.NHG):
                        pvo = (hg % 2) * 1024
                        for kt in range(NKT):
                            g = hg * NKT + kt
                            sl = g % 2
                            slK = g % 4
                            te.wait_ge(s.isl[slK].h, kevt[g])
                            if g >= 2:
                                te.wait_ge(s.ss.h, ssb0 + (g - 1))
                            for hh in range(2):
                                ti = te.matmul(
                                    scb[:, sl * 1024 + hh * 512:sl * 1024 + hh * 512 + QC],
                                    kbl[hh * HD:(hh + 1) * HD, slK * P:(slK + 1) * P],
                                    qcb[hh * HD:(hh + 1) * HD, hg * QC:(hg + 1) * QC],
                                    start=True, stop=True, tile_position=(hh * HD, 0))
                            s.tm.inc(ti)
                            te.wait_ge(s.ss.h, ssb0 + g + 1)
                            if kt == 0 and hg >= 2:
                                te.wait_ge(s.vv.h, vvb0 + 4 * (hg - 1))
                            for hh in range(2):
                                ti = te.matmul(
                                    pvb[0:65, pvo + hh * 512:pvo + hh * 512 + QC],
                                    vbl[:, slK * 130 + hh * 65:slK * 130 + (hh + 1) * 65],
                                    ptl[:, sl * 2 * QC + hh * QC:sl * 2 * QC + (hh + 1) * QC],
                                    start=(kt == 0), stop=(kt == NKT - 1))
                            s.tm.inc(ti)

                @blk.scalar
                def _(sc, qc=qc):
                    for g in range(c.NHG * NKT):
                        sl = g % 2
                        sc.wait_ge(s.tm.h, tmb0 + 2 * g + 1)
                        if g >= 2:
                            sc.wait_ge(s.tm.h, tmb0 + 2 * (g - 2) + 2)
                        sin = scb[:, sl * 1024:(sl + 1) * 1024].rearrange(
                            "p (h x) -> p h x", h=2)[:, :, 0:QC]
                        pout = ptl[:, sl * 2 * QC:(sl + 1) * 2 * QC].rearrange(
                            "p (h x) -> p h x", h=2)
                        sc.activation(out=pout, in_=sin,
                                      func=AF.Exp, scale=0.125).then_inc(s.ss.h, 1)
                        s.ss.v += 1

                @blk.vector
                def _(v, qc=qc):
                    for hg in range(c.NHG):
                        pvo = (hg % 2) * 1024
                        v.wait_ge(s.tm.h, tmb0 + 2 * (hg + 1) * NKT)
                        v.tensor_copy(out=lsb[64:65, 0:QC], in_=pvb[64:65, pvo:pvo + QC])
                        v.tensor_copy(out=lsb[64:65, QC:2 * QC],
                                      in_=pvb[64:65, pvo + 512:pvo + 512 + QC]
                                      ).then_inc(s.vv.h, 1)
                        s.vv.v += 1
                        v.wait_ge(s.gd.h, gdb0 + 80 * hg + 32)
                        v.reciprocal(out=lsb[0:2, QC:2 * QC],
                                     in_=lsb[0:2, 0:QC]).then_inc(s.vv.h, 1)
                        s.vv.v += 1
                        v.wait_ge(s.gd.h, gdb0 + 80 * hg + 80)
                        for hh in range(2):
                            h = 2 * hg + hh
                            cch, rr = h // 2, (h % 2) * HD
                            v.tensor_mul(out=atn[rr:rr + HD, cch * QC:(cch + 1) * QC],
                                         in0=pvb[0:HD, pvo + hh * 512:pvo + hh * 512 + QC],
                                         in1=lbc[hh * HD:(hh + 1) * HD, 0:QC]).then_inc(s.vv.h, 1)
                            s.vv.v += 1

                @blk.gpsimd
                def _(g_, qc=qc):
                    for hg in range(c.NHG):
                        if hg >= 1:
                            g_.wait_ge(s.gd.h, gdb0 + 80 * hg)
                        g_.wait_ge(s.vv.h, vvb0 + 4 * hg + 1)
                        s.gd.inc(g_.dma_start(out=s.lraw_d[0:2 * QC],
                                              in_=lsb[64:65, 0:2 * QC]), 16)
                        g_.wait_ge(s.gd.h, gdb0 + 80 * hg + 16)
                        lr = s.lraw_d[0:2 * QC].rearrange("(a b) -> a b", a=2)
                        s.gd.inc(g_.dma_start(out=lsb[0:2, 0:QC], in_=lr), 16)
                        g_.wait_ge(s.vv.h, vvb0 + 4 * hg + 2)
                        s.gd.inc(g_.dma_start(out=s.lrec_d[0:2 * QC],
                                              in_=lsb[0:2, QC:2 * QC]), 16)
                        g_.wait_ge(s.gd.h, gdb0 + 80 * hg + 48)
                        for hh in range(2):
                            s.gd.inc(g_.dma_start(
                                out=lbc[hh * HD:(hh + 1) * HD, 0:QC],
                                in_=s.lrec_d[hh * QC:(hh + 1) * QC].unsqueeze(0).to_broadcast([HD, QC])), 16)

            with nc.Block() as blk:
                wevt = []

                @blk.sync
                def _(sy, qc=qc):
                    tmb0 = s.tm.v
                    for kc in range(c.DC):
                        if kc >= 2:
                            sy.wait_ge(s.tm.h, tmb0 + (kc - 1))
                        w = s.wsl[kc % 2]
                        w.inc(sy.dma_start(out=wpb[:, (kc % 2) * c.D:(kc % 2 + 1) * c.D],
                                           in_=wpv[kc]), 16)
                        wevt.append(w.v)

                @blk.tensor
                def _(te, qc=qc):
                    for kc in range(c.DC):
                        te.wait_ge(s.wsl[kc % 2].h, wevt[kc])
                        for t in range(qtile):
                            bank = s.psA if t % 2 == 0 else s.psB
                            boff = (t // 2) * c.D
                            for nn in range(c.ND2):
                                ti = te.matmul(
                                    bank[:, boff + nn * c.NW:boff + (nn + 1) * c.NW],
                                    atn[:, kc * QC + t * P:kc * QC + (t + 1) * P],
                                    wpb[:, (kc % 2) * c.D + nn * c.NW:(kc % 2) * c.D + (nn + 1) * c.NW],
                                    start=(kc == 0), stop=(kc == c.DC - 1))
                        s.tm.inc(ti)

                @blk.vector
                def _(v, qc=qc):
                    v.wait_ge(s.tm.h, s.tm.v)
                    v.wait_ge(s.gg.h, s.gg.v)
                    for t in range(qtile):
                        bank = s.psA if t % 2 == 0 else s.psB
                        boff = (t // 2) * c.D
                        gt = qc * qtile + t
                        self.vstep(v, v.tensor_add(out=s.xres[:, gt * c.D:(gt + 1) * c.D],
                                                   in0=bank[:, boff:boff + c.D],
                                                   in1=x1b[:, t * c.D:(t + 1) * c.D]))
        es2.close()



def prep_inputs(cfg, L0_attn_x, L0_attn_w_qkv, L0_attn_b_qkv, L0_attn_w_proj,
                L0_attn_b_proj, L0_ffn_w_gate, L0_ffn_w_up, L0_ffn_w_down):
    c = cfg
    P = c.P
    x = np.ascontiguousarray(np.asarray(L0_attn_x, dtype=np.float32))
    D = c.D
    wqkv = np.asarray(L0_attn_w_qkv, dtype=np.float32)

    def tile_lhsT(wT, n_m):
        arr = wT.reshape(c.DC, P, n_m, P)
        return np.ascontiguousarray(arr.transpose(2, 1, 0, 3))

    wqk = tile_lhsT(wqkv[:, 0:2 * D], 2 * c.DC)
    wv = np.ascontiguousarray(
        wqkv[:, 2 * D:3 * D].reshape(c.DC, P, c.ND2, c.NW).transpose(2, 0, 1, 3))
    bqkv = np.asarray(L0_attn_b_qkv, dtype=np.float32)
    bqk = np.ascontiguousarray(bqkv[0:2 * D].reshape(2 * c.DC, P))
    bv = np.ascontiguousarray(bqkv[2 * D:3 * D])
    wp = np.ascontiguousarray(np.asarray(L0_attn_w_proj, dtype=np.float32))
    bp = np.ascontiguousarray(np.asarray(L0_attn_b_proj, dtype=np.float32))
    wu = tile_lhsT(np.ascontiguousarray(np.asarray(L0_ffn_w_up, dtype=np.float32).T), c.HC)
    wg = tile_lhsT(np.ascontiguousarray(np.asarray(L0_ffn_w_gate, dtype=np.float32).T), c.HC)
    wd = np.ascontiguousarray(np.asarray(L0_ffn_w_down, dtype=np.float32).T)

    in_maps = []
    for core in range(c.n_cores):
        b, h = core // 2, core % 2
        xs = np.ascontiguousarray(x[b, h * c.T:(h + 1) * c.T, :])
        in_maps.append(dict(x0=xs, wqk=wqk, wv=wv, bqk=bqk, bv=bv, wp=wp, bp=bp,
                            wu=wu, wg=wg, wd=wd))
    return in_maps


def build_program(cfg, dbg=None):
    b = Builder(cfg, dbg=dbg)
    b.emit()
    return b.finish()


def kernel(L0_attn_x, L0_attn_w_qkv, L0_attn_b_qkv, L0_attn_w_proj, L0_attn_b_proj,
           L0_ffn_w_gate, L0_ffn_w_up, L0_ffn_w_down):
    cfg = FULL
    in_maps = prep_inputs(cfg, L0_attn_x, L0_attn_w_qkv, L0_attn_b_qkv,
                          L0_attn_w_proj, L0_attn_b_proj, L0_ffn_w_gate,
                          L0_ffn_w_up, L0_ffn_w_down)
    nc = build_program(cfg)
    res = run_bass_kernel_spmd(nc, in_maps, core_ids=list(range(cfg.n_cores)))
    x = np.asarray(L0_attn_x)
    Bb, S, D = x.shape
    out = np.empty((Bb, S, D), dtype=np.float32)
    for core in range(cfg.n_cores):
        bb, h = core // 2, core % 2
        out[bb, h * cfg.T:(h + 1) * cfg.T, :] = res.results[core]["y"]
    return out


# version 09
# speedup vs baseline: 1.3689x; 1.1713x over previous
import numpy as np
from contextlib import ExitStack
from dataclasses import dataclass

import concourse.bass as bass
import concourse.mybir as mybir
from concourse.bass_utils import run_bass_kernel_spmd
from concourse.dve_spec import Spec, Src0, Src1, C0, C1, C2, Zero, One, maxx, minn, AluOp
from concourse.dve_spec import lower as dve_lower
from concourse.dve_ops import DveOp, OPS
import concourse.dve_ops as dve_ops_mod

F32 = mybir.dt.float32
I16 = mybir.dt.int16
BF16 = mybir.dt.bfloat16
I32 = mybir.dt.int32
AF = mybir.ActivationFunctionType
ALU = mybir.AluOpType
AX = mybir.AxisListType

MAGIC = float(np.float32(1.5 * 2.0 ** 23))


def _register_op(name, spec, reference):
    for op in OPS:
        if op.name == name:
            return op
    shas, perf_en = {}, {}
    for ver in ("v3", "v4"):
        try:
            shas[ver] = dve_lower(spec, ver=ver, perf_en=False, name=name).sha(ver)
            perf_en[ver] = False
        except Exception:
            pass
    op = DveOp(name, spec, subdim=False, uops_sha=shas, perf_en=perf_en,
               reference=reference)
    OPS.append(op)
    dve_ops_mod._SUB_OPCODE_FOR_NAME[name] = (
        dve_ops_mod._CUSTOM_DVE_ROW_BASE + len(OPS) - 1)
    dve_ops_mod.CUSTOM_DVE_REFS[name] = op.ref
    dve_ops_mod.CUSTOM_DVE_SPECS[name] = spec
    return op


RCS = _register_op(
    "MK_RCS",
    Spec(body=minn(maxx((Src0 * C0 + C2) - C2, Zero - (C1 + One)), C1)),
    lambda in0, in1, s0, s1, imm2: np.clip(np.rint(in0 * s0), -s1 - 1.0, s1),
)
SUBMUL = _register_op(
    "MK_SUBMUL",
    Spec(body=(Src0 - C0) * Src1, accum=AluOp.MAX),
    lambda in0, in1, s0, s1, imm2: (
        (in0 - s0) * in1,
        np.max((in0 - s0) * in1, axis=-1, keepdims=True),
    ),
)


@dataclass(frozen=True)
class Cfg:
    D: int = 1024
    H: int = 3072
    NH: int = 16
    T: int = 2048
    n_cores: int = 8
    eps: float = 1e-5
    stages: int = 3

    P = 128
    HDIM = 64

    @property
    def DC(self):
        return self.D // 128

    @property
    def HC(self):
        return self.H // 128

    @property
    def NTT(self):
        return self.T // 128

    @property
    def S(self):
        return 2 * self.T

    @property
    def NKT(self):
        return self.S // 128

    @property
    def TN(self):
        return min(512, self.T)

    @property
    def NTC(self):
        return self.T // self.TN

    @property
    def NHG(self):
        return self.NH // 2

    @property
    def NW(self):
        return min(512, self.D)

    @property
    def ND2(self):
        return self.D // self.NW


FULL = Cfg()
MINI = Cfg(D=256, H=768, NH=4, T=256)
MIDI = Cfg(D=1024, H=1536, NH=16, T=1024)


class Sem:
    def __init__(self, es, nc, name):
        self.h = es.enter_context(nc.semaphore(name))
        self.v = 0

    def inc(self, ins, n=1):
        ins.then_inc(self.h, n)
        self.v += n
        return self.v


class Builder:
    def __init__(self, cfg: Cfg, dbg=None):
        self.dbg = dbg
        self.ffn_i = 0
        self.dbg_done = False
        self.c = cfg
        nc = bass.Bass("TRN2", target_bir_lowering=False, debug=False)
        self.nc = nc
        self.uid = 0
        c, P = cfg, cfg.P
        self.x0_d = nc.dram_tensor("x0", [c.T, c.D], F32, kind="ExternalInput").ap()
        self.wqk_d = nc.dram_tensor("wqk", [2 * c.DC, P, c.DC, P], F32, kind="ExternalInput").ap()
        self.wv_d = nc.dram_tensor("wv", [c.ND2, c.DC, P, c.NW], F32, kind="ExternalInput").ap()
        self.bqk_d = nc.dram_tensor("bqk", [2 * c.DC, P], F32, kind="ExternalInput").ap()
        self.bv_d = nc.dram_tensor("bv", [c.D], F32, kind="ExternalInput").ap()
        self.wp_d = nc.dram_tensor("wp", [c.D, c.D], F32, kind="ExternalInput").ap()
        self.bp_d = nc.dram_tensor("bp", [c.D], F32, kind="ExternalInput").ap()
        self.wu_d = nc.dram_tensor("wu", [c.HC, P, c.DC, P], F32, kind="ExternalInput").ap()
        self.wg_d = nc.dram_tensor("wg", [c.HC, P, c.DC, P], F32, kind="ExternalInput").ap()
        self.wd_d = nc.dram_tensor("wd", [c.H, c.D], F32, kind="ExternalInput").ap()
        self.y_d = nc.dram_tensor("y", [c.T, c.D], F32, kind="ExternalOutput").ap()
        self.wuq_h = nc.dram_tensor("wuqh", [c.HC, P, c.DC, P], BF16).ap()
        self.wuq_l = nc.dram_tensor("wuql", [c.HC, P, c.DC, P], BF16).ap()
        self.wgq_h = nc.dram_tensor("wgqh", [c.HC, P, c.DC, P], BF16).ap()
        self.wgq_l = nc.dram_tensor("wgql", [c.HC, P, c.DC, P], BF16).ap()
        self.wdq_d = nc.dram_tensor("wdq", [c.H, c.D], F32).ap()
        self.mup_d = nc.dram_tensor("mup", [c.HC, P, c.T], F32).ap()
        self.mgate_d = nc.dram_tensor("mgate", [c.HC, P, c.T], F32).ap()
        self.kup_d = nc.dram_tensor("kup", [c.NTC, c.HC, P, c.TN], I16).ap()
        self.kgate_d = nc.dram_tensor("kgate", [c.NTC, c.HC, P, c.TN], I16).ap()
        self.kg2_d = nc.dram_tensor("kg2", [c.NTC, c.HC, P, c.TN], I16).ap()
        self.qT_d = nc.dram_tensor("qTd", [c.DC, P, c.T], F32).ap()
        self.kloc_d = nc.dram_tensor("kloc", [c.DC, P, c.T], F32).ap()
        self.vloc_d = nc.dram_tensor("vloc", [c.NTT, P, c.D], F32).ap()
        self.kpair_d = nc.dram_tensor("kpair", [2, c.DC, P, c.T], F32).ap()
        self.vpair_d = nc.dram_tensor("vpair", [2, c.NTT, P, c.D], F32).ap()
        self.lraw_d = nc.dram_tensor("lraw", [2 * 512], F32).ap()
        self.lrec_d = nc.dram_tensor("lrec", [2 * 512], F32).ap()
        self.ar_in = [nc.dram_tensor(f"arin{i}", [256], F32).ap() for i in range(8)]
        self.ar_out = [nc.dram_tensor(f"arout{i}", [256], F32).ap() for i in range(8)]
        self.n_ar = 0

        self.es = ExitStack()
        self.es.__enter__()
        es = self.es
        sb = lambda n, sh, dt=F32: es.enter_context(nc.sbuf_tensor(n, sh, dt))
        self.xres = sb("xres", [P, c.NTT * c.D])
        self.stat = sb("stat", [P, 8 * c.NTT])
        self.slots = sb("slots", [P, 256])
        self.scal = sb("scal", [P, 32])
        self.arb = sb("arb", [P, 256])
        self.ident = sb("ident", [P, P])
        self.bvb = sb("bvb", [P, c.D])
        self.bpb = sb("bpb", [P, c.D])
        self.bqks = sb("bqks", [P, 2 * c.DC])
        self.psA = es.enter_context(nc.psum_tensor("psA", [P, 2048], F32))
        self.psB = es.enter_context(nc.psum_tensor("psB", [P, 2048], F32))
        self.sy = Sem(es, nc, "sy")
        self.gd = Sem(es, nc, "gd")
        self.cc = Sem(es, nc, "cc")
        self.tm = Sem(es, nc, "tm")
        self.vv = Sem(es, nc, "vv")
        self.ss = Sem(es, nc, "ss")
        self.gg = Sem(es, nc, "gg")
        self.wsl = [Sem(es, nc, "wsl0"), Sem(es, nc, "wsl1")]
        self.isl = [Sem(es, nc, f"isl{i}") for i in range(4)]
        self.osl = [Sem(es, nc, f"osl{i}") for i in range(4)]

    def sc(self, i):
        return self.scal[:, i:i + 1]

    def name(self, stem):
        self.uid += 1
        return f"{stem}{self.uid}"

    def pslot(self, i, width):
        t = self.psA if i % 2 == 0 else self.psB
        off = ((i // 2) % 2) * 1024
        return t[:, off:off + width]

    def vstep(self, v, ins):
        ins.then_inc(self.vv.h, 1)
        self.vv.v += 1
        v.wait_ge(self.vv.h, self.vv.v)

    def finish(self):
        self.es.__exit__(None, None, None)
        return self.nc

    def emit(self):
        c, nc = self.c, self.nc
        with nc.Block() as blk:
            @blk.sync
            def _(sy):
                x0v = self.x0_d.rearrange("(t p) d -> t p d", p=c.P)
                for t in range(c.NTT):
                    self.sy.inc(sy.dma_start(out=self.xres[:, t * c.D:(t + 1) * c.D], in_=x0v[t]), 16)
                self.sy.inc(sy.dma_start(out=self.bvb[:], in_=self.bv_d.unsqueeze(0).to_broadcast([c.P, c.D])), 16)
                self.sy.inc(sy.dma_start(out=self.bpb[:], in_=self.bp_d.unsqueeze(0).to_broadcast([c.P, c.D])), 16)
                idd = nc.inline_tensor(np.eye(c.P, dtype=np.float32), "identnp").ap()
                self.sy.inc(sy.dma_start(out=self.ident[:], in_=idd), 16)
                for m in range(2 * c.DC):
                    self.sy.inc(sy.dma_start(out=self.bqks[:, m:m + 1], in_=self.bqk_d[m].unsqueeze(1)), 16)
                sy.wait_ge(self.sy.h, self.sy.v)

        self.emit_dequant()
        self.emit_ffn()
        if c.stages >= 2 and not self.dbg_done:
            self.emit_attention()
        if c.stages >= 3 and not self.dbg_done:
            self.emit_ffn()

        if not self.dbg_done:
            with nc.Block() as blk:
                @blk.sync
                def _(sy):
                    yv = self.y_d.rearrange("(t p) d -> t p d", p=c.P)
                    for t in range(c.NTT):
                        self.sy.inc(sy.dma_start(out=yv[t], in_=self.xres[:, t * c.D:(t + 1) * c.D]), 16)
                    sy.wait_ge(self.sy.h, self.sy.v)

    def emit_dequant(self):
        c, s, nc = self.c, self, self.nc
        P = c.P
        W = 512 if c.D >= 512 else c.D
        es2 = ExitStack()
        wa = es2.enter_context(nc.sbuf_tensor(self.name("dqa"), [P, 4 * W], F32))
        whb = es2.enter_context(nc.sbuf_tensor(self.name("dqh"), [P, 4 * W], BF16))
        wlb = es2.enter_context(nc.sbuf_tensor(self.name("dql"), [P, 4 * W], BF16))
        wt = es2.enter_context(nc.sbuf_tensor(self.name("dqt"), [P, W], F32))
        bm = es2.enter_context(nc.sbuf_tensor(self.name("dqm"), [P, W // 2], F32))
        ki = es2.enter_context(nc.sbuf_tensor(self.name("dqi"), [P, W], I16))
        nb = W // 32

        tiles = []
        for wsrc, wdh, wdl in [(s.wu_d, s.wuq_h, s.wuq_l), (s.wg_d, s.wgq_h, s.wgq_l)]:
            for m in range(c.HC):
                v4 = wsrc[m].rearrange("p cc pp -> p (cc pp)")
                d4h = wdh[m].rearrange("p cc pp -> p (cc pp)")
                d4l = wdl[m].rearrange("p cc pp -> p (cc pp)")
                for j in range(0, c.DC * P, W):
                    w_ = min(W, c.DC * P - j)
                    tiles.append((v4[:, j:j + w_], [d4h[:, j:j + w_], d4l[:, j:j + w_]],
                                  w_, True))
        wdv_s = s.wd_d.rearrange("(hc p) d -> hc p d", p=P)
        wdv_q = s.wdq_d.rearrange("(hc p) d -> hc p d", p=P)
        for hc in range(c.HC):
            for j in range(0, c.D, W):
                w_ = min(W, c.D - j)
                tiles.append((wdv_s[hc, :, j:j + w_], [wdv_q[hc, :, j:j + w_]], w_, False))

        with nc.Block() as blk:
            n = len(tiles)
            chain_end = []
            acc = s.vv.v
            for (_, _, _, bf) in tiles:
                acc += 9 if bf else 7
                chain_end.append(acc)
            ld_evt = []
            st_evt = []

            @blk.sync
            def _(sy):
                for i in range(n + 1):
                    if i < n:
                        src_ap, dsts, w_, bf = tiles[i]
                        if i >= 4:
                            sy.wait_ge(s.osl[i % 4].h, st_evt[i - 4])
                        il = s.isl[i % 4]
                        il.inc(sy.dma_start(out=wa[:, (i % 4) * W:(i % 4) * W + w_], in_=src_ap), 16)
                        ld_evt.append(il.v)
                    if i >= 1:
                        j = i - 1
                        src_ap, dsts, w_, bf = tiles[j]
                        sy.wait_ge(s.vv.h, chain_end[j])
                        ol = s.osl[j % 4]
                        o = (j % 4) * W
                        if bf:
                            ol.inc(sy.dma_start(out=dsts[0], in_=whb[:, o:o + w_]), 16)
                            ol.inc(sy.dma_start(out=dsts[1], in_=wlb[:, o:o + w_]), 16)
                        else:
                            ol.inc(sy.dma_start(out=dsts[0], in_=wa[:, o:o + w_]), 16)
                        st_evt.append(ol.v)
                for q in range(4):
                    sy.wait_ge(s.osl[q].h, s.osl[q].v)

            @blk.vector
            def _(v):
                for i, (src_ap, dsts, w_, bf) in enumerate(tiles):
                    nb_ = w_ // 32
                    o = (i % 4) * W
                    v.wait_ge(s.isl[i % 4].h, ld_evt[i])
                    self.vstep(v, v.transpose(out=wt[:, 0:w_], in_=wa[:, o:o + w_]))
                    self.vstep(v, v.reduce_max(
                        out=bm[:, 0:nb_],
                        in_=wt[:, 0:w_].rearrange("p (g e) -> p g e", e=32),
                        axis=AX.X, apply_absolute_value=True))
                    self.vstep(v, v.tensor_scalar(out=bm[:, 0:nb_], in0=bm[:, 0:nb_],
                                                  scalar1=1.0 / 7.0, scalar2=1e-12,
                                                  op0=ALU.mult, op1=ALU.max))
                    self.vstep(v, v.reciprocal(out=bm[:, nb:nb + nb_], in_=bm[:, 0:nb_]))
                    self.vstep(v, v.tensor_tensor(
                        out=ki[:, 0:w_].rearrange("p (g e) -> p g e", e=32),
                        in0=wt[:, 0:w_].rearrange("p (g e) -> p g e", e=32),
                        in1=bm[:, nb:nb + nb_].rearrange("p (g o) -> p g o", o=1).to_broadcast([P, nb_, 32]),
                        op=ALU.mult))
                    self.vstep(v, v.tensor_tensor(
                        out=wt[:, 0:w_].rearrange("p (g e) -> p g e", e=32),
                        in0=ki[:, 0:w_].rearrange("p (g e) -> p g e", e=32),
                        in1=bm[:, 0:nb_].rearrange("p (g o) -> p g o", o=1).to_broadcast([P, nb_, 32]),
                        op=ALU.mult))
                    self.vstep(v, v.transpose(out=wa[:, o:o + w_], in_=wt[:, 0:w_]))
                    if bf:
                        self.vstep(v, v.tensor_copy(out=whb[:, o:o + w_], in_=wa[:, o:o + w_]))
                        self.vstep(v, v.tensor_sub(out=wlb[:, o:o + w_], in0=wa[:, o:o + w_],
                                                   in1=whb[:, o:o + w_]))
        es2.close()

    def emit_ln_stats(self, lnsc):
        c, s = self.c, self
        N = c.NTT
        nc = self.nc
        with nc.Block() as blk:
            @blk.vector
            def _(v):
                for t in range(N):
                    xt = s.xres[:, t * c.D:(t + 1) * c.D]
                    ins = v.reduce_sum(out=s.stat[:, t:t + 1], in_=xt, axis=AX.X)
                s.vv.inc(ins)

            @blk.scalar
            def _(sc):
                ssb = s.ss.v
                for t in range(N):
                    xt = s.xres[:, t * c.D:(t + 1) * c.D]
                    if t >= 1:
                        sc.wait_ge(s.ss.h, ssb + t)
                    sc.activation(out=lnsc[:, 0:c.D], in_=xt, func=AF.Square,
                                  accum_out=s.stat[:, N + t:N + t + 1]).then_inc(s.ss.h, 1)
                    s.ss.v += 1

        with nc.Block() as blk:
            @blk.vector
            def _(v):
                st = s.stat
                self.vstep(v, v.tensor_scalar(out=st[:, 2 * N:3 * N], in0=st[:, 0:N],
                                              scalar1=1.0 / c.D, scalar2=None, op0=ALU.mult))
                self.vstep(v, v.scalar_tensor_tensor(out=st[:, 3 * N:4 * N], in0=st[:, 2 * N:3 * N],
                                                     scalar=-1.0, in1=st[:, 2 * N:3 * N],
                                                     op0=ALU.mult, op1=ALU.mult))
                self.vstep(v, v.scalar_tensor_tensor(out=st[:, 3 * N:4 * N], in0=st[:, N:2 * N],
                                                     scalar=1.0 / c.D, in1=st[:, 3 * N:4 * N],
                                                     op0=ALU.mult, op1=ALU.add))
                self.vstep(v, v.tensor_scalar(out=st[:, 3 * N:4 * N], in0=st[:, 3 * N:4 * N],
                                              scalar1=c.eps, scalar2=None, op0=ALU.add))
                s.vv.inc(v.engine_nop())

            @blk.scalar
            def _(sc):
                sc.wait_ge(s.vv.h, s.vv.v)
                st = s.stat
                s.ss.inc(sc.activation(out=st[:, 4 * N:5 * N], in_=st[:, 3 * N:4 * N],
                                       func=AF.Sqrt))

            @blk.vector
            def _(v):
                v.wait_ge(s.ss.h, s.ss.v)
                st = s.stat
                self.vstep(v, v.reciprocal(out=st[:, 5 * N:6 * N], in_=st[:, 4 * N:5 * N]))
                self.vstep(v, v.tensor_mul(out=st[:, 6 * N:7 * N], in0=st[:, 5 * N:6 * N],
                                           in1=st[:, 5 * N:6 * N]))
                self.vstep(v, v.tensor_mul(out=st[:, 6 * N:7 * N], in0=st[:, 6 * N:7 * N],
                                           in1=st[:, 3 * N:4 * N]))
                self.vstep(v, v.tensor_scalar(out=st[:, 6 * N:7 * N], in0=st[:, 6 * N:7 * N],
                                              scalar1=-0.5, scalar2=1.5, op0=ALU.mult, op1=ALU.add))
                self.vstep(v, v.tensor_mul(out=st[:, 4 * N:5 * N], in0=st[:, 5 * N:6 * N],
                                           in1=st[:, 6 * N:7 * N]))
                s.vv.inc(v.engine_nop())

    def emit_armax(self, ranges, outs):
        c, s, nc = self.c, self, self.nc
        idx = self.n_ar % 8
        self.n_ar += 1
        k = len(ranges)
        with nc.Block() as blk:
            @blk.vector
            def _(v):
                for i, (st0, n) in enumerate(ranges):
                    ins = v.reduce_max(out=s.scal[:, 30 - i:31 - i],
                                       in_=s.slots[:, st0:st0 + n], axis=AX.X)
                s.vv.inc(ins)

            @blk.gpsimd
            def _(g):
                g.wait_ge(s.vv.h, s.vv.v)
                for i in range(k):
                    s.gd.inc(g.dma_start(out=s.ar_in[idx][i * 128:(i + 1) * 128],
                                         in_=s.scal[:, 30 - i:31 - i]), 16)
                g.wait_ge(s.gd.h, s.gd.v)
                s.cc.inc(g.collective_compute(
                    "AllReduce", ALU.max, replica_groups=[list(range(c.n_cores))],
                    ins=[s.ar_in[idx][0:k * 128].opt()],
                    outs=[s.ar_out[idx][0:k * 128].opt()]), 1)
                g.wait_ge(s.cc.h, s.cc.v)
                for i in range(k):
                    s.gd.inc(g.dma_start(
                        out=s.arb[:, i * 128:(i + 1) * 128],
                        in_=s.ar_out[idx][i * 128:(i + 1) * 128].unsqueeze(0).to_broadcast([c.P, 128])), 16)

            gdv = s.gd.v

            @blk.vector
            def _(v):
                v.wait_ge(s.gd.h, gdv)
                for i in range(k):
                    ins = v.reduce_max(out=outs[i], in_=s.arb[:, i * 128:(i + 1) * 128], axis=AX.X)
                s.vv.inc(ins)

    def dump(self, ap, nelem):
        nc, s = self.nc, self
        yflat = self.y_d.rearrange("a b -> (a b)")
        with nc.Block() as blk:
            @blk.sync
            def _(sy):
                s.sy.inc(sy.dma_start(out=yflat[0:nelem], in_=ap), 16)
                sy.wait_ge(s.sy.h, s.sy.v)
        self.dbg_done = True

    def dbg_hit(self, tag):
        return (not self.dbg_done) and self.dbg is not None and \
            self.dbg[0] == tag and self.dbg[1] == self.ffn_i

    def emit_ffn(self):
        c, s, nc = self.c, self, self.nc
        self.ffn_i += 1
        P, N = c.P, c.NTT
        es2 = ExitStack()
        esln = ExitStack()
        fmh = es2.enter_context(nc.sbuf_tensor(self.name("fmh"), [P, c.DC * c.T], BF16))
        fml = es2.enter_context(nc.sbuf_tensor(self.name("fml"), [P, c.DC * c.T], BF16))
        lnsc = esln.enter_context(nc.sbuf_tensor(self.name("lnsc"), [P, 2 * c.D], F32))
        xlnh = esln.enter_context(nc.sbuf_tensor(self.name("xlnh"), [P, max(1, N // 2) * c.D], F32))

        self.emit_ln_stats(lnsc)

        with nc.Block() as blk:
            @blk.vector
            def _(v):
                for t in range(N):
                    self.vstep(v, v.scalar_tensor_tensor(
                        out=lnsc[:, 0:c.D],
                        in0=s.xres[:, t * c.D:(t + 1) * c.D],
                        scalar=s.stat[:, 2 * N + t:2 * N + t + 1],
                        in1=s.stat[:, 4 * N + t:4 * N + t + 1].to_broadcast([c.P, c.D]),
                        op0=ALU.subtract, op1=ALU.mult))
                    self.vstep(v, v.reduce_max(out=s.slots[:, t:t + 1], in_=lnsc[:, 0:c.D],
                                               axis=AX.X, apply_absolute_value=True))

        self.emit_armax([(0, N)], [self.sc(25)])
        with nc.Block() as blk:
            @blk.vector
            def _(v):
                v.wait_ge(s.vv.h, s.vv.v)
                self.vstep(v, v.tensor_scalar(out=self.sc(1), in0=self.sc(25), scalar1=1.0 / 32767.0,
                                              scalar2=1e-12, op0=ALU.mult, op1=ALU.max))
                self.vstep(v, v.reciprocal(out=self.sc(2), in_=self.sc(1)))
                s.vv.inc(v.engine_nop())

        self.emit_lnT(fmh, fml, xlnh, rcs_scale=self.sc(2))
        esln.close()

        self.emit_upgate(fmh, fml)
        es2.close()

        self.emit_armax([(0, c.HC * c.NTC), (128, c.HC * c.NTC)],
                        [self.sc(30), self.sc(29)])
        with nc.Block() as blk:
            @blk.vector
            def _(v):
                v.wait_ge(s.vv.h, s.vv.v)
                self.vstep(v, v.scalar_tensor_tensor(out=self.sc(20), in0=self.sc(30),
                                                     scalar=1.0 / 32767.0, in1=self.sc(1),
                                                     op0=ALU.mult, op1=ALU.mult))
                self.vstep(v, v.tensor_scalar(out=self.sc(20), in0=self.sc(20), scalar1=1e-12,
                                              scalar2=None, op0=ALU.max))
                self.vstep(v, v.reciprocal(out=self.sc(28), in_=self.sc(20)))
                self.vstep(v, v.tensor_mul(out=self.sc(4), in0=self.sc(28), in1=self.sc(1)))
                self.vstep(v, v.scalar_tensor_tensor(out=self.sc(21), in0=self.sc(29),
                                                     scalar=1.0 / 32767.0, in1=self.sc(1),
                                                     op0=ALU.mult, op1=ALU.mult))
                self.vstep(v, v.tensor_scalar(out=self.sc(21), in0=self.sc(21), scalar1=1e-12,
                                              scalar2=None, op0=ALU.max))
                self.vstep(v, v.reciprocal(out=self.sc(28), in_=self.sc(21)))
                self.vstep(v, v.tensor_mul(out=self.sc(5), in0=self.sc(28), in1=self.sc(1)))
                self.vstep(v, v.tensor_copy(out=self.sc(6), in_=self.sc(21)))
                s.vv.inc(v.engine_nop())

        if self.dbg_hit("scal_ar1"):
            self.dump(s.scal[:], c.P * 32)
            return
        self.emit_f3()
        self.emit_armax([(0, c.HC)], [self.sc(30)])
        with nc.Block() as blk:
            @blk.vector
            def _(v):
                v.wait_ge(s.vv.h, s.vv.v)
                self.vstep(v, v.scalar_tensor_tensor(out=self.sc(22), in0=self.sc(6),
                                                     scalar=1.0 / 65536.0, in1=self.sc(30),
                                                     op0=ALU.mult, op1=ALU.mult))
                self.vstep(v, v.tensor_scalar(out=self.sc(22), in0=self.sc(22),
                                              scalar1=1.0 / 32767.0, scalar2=1e-12,
                                              op0=ALU.mult, op1=ALU.max))
                self.vstep(v, v.reciprocal(out=self.sc(28), in_=self.sc(22)))
                self.vstep(v, v.scalar_tensor_tensor(out=self.sc(7), in0=self.sc(6),
                                                     scalar=1.0 / 65536.0, in1=self.sc(28),
                                                     op0=ALU.mult, op1=ALU.mult))
                self.vstep(v, v.tensor_mul(out=self.sc(8), in0=self.sc(22), in1=self.sc(20)))
                s.vv.inc(v.engine_nop())

        if self.dbg_hit("scal_ar2"):
            self.dump(s.scal[:], c.P * 32)
            return
        self.emit_f4()
        self.emit_armax([(0, c.HC)], [self.sc(30)])
        with nc.Block() as blk:
            @blk.vector
            def _(v):
                v.wait_ge(s.vv.h, s.vv.v)
                self.vstep(v, v.scalar_tensor_tensor(out=self.sc(23), in0=self.sc(30),
                                                     scalar=1.0 / 32767.0, in1=self.sc(8),
                                                     op0=ALU.mult, op1=ALU.mult))
                self.vstep(v, v.tensor_scalar(out=self.sc(23), in0=self.sc(23), scalar1=1e-12,
                                              scalar2=None, op0=ALU.max))
                self.vstep(v, v.reciprocal(out=self.sc(28), in_=self.sc(23)))
                self.vstep(v, v.tensor_mul(out=self.sc(9), in0=self.sc(28), in1=self.sc(8)))
                self.vstep(v, v.tensor_scalar(out=self.sc(10), in0=self.sc(23), scalar1=0.5,
                                              scalar2=None, op0=ALU.mult))
                s.vv.inc(v.engine_nop())

        if self.dbg_hit("scal_ar3"):
            self.dump(s.scal[:], c.P * 32)
            return
        self.emit_f5()

    def emit_lnT(self, fmA, fmL, xlnh, rcs_scale):
        c, s, nc = self.c, self, self.nc
        P, N = c.P, c.NTT
        nh = max(1, N // 2)
        nbank = 4
        W = c.DC * P
        EV = 5 if rcs_scale is not None else 1
        esl = ExitStack()
        tmpf = esl.enter_context(nc.sbuf_tensor(self.name("ltf"), [P, W], F32))
        tmpi = esl.enter_context(nc.sbuf_tensor(self.name("lti"), [P, W], I16))
        tmpj = esl.enter_context(nc.sbuf_tensor(self.name("ltj"), [P, W], I16))
        for half in range(N // nh):
            t0 = half * nh
            with nc.Block() as blk:
                @blk.vector
                def _(v, t0=t0):
                    tmb = s.tm.v
                    for i in range(nh):
                        t = t0 + i
                        v.scalar_tensor_tensor(
                            out=xlnh[:, i * c.D:(i + 1) * c.D],
                            in0=s.xres[:, t * c.D:(t + 1) * c.D],
                            scalar=s.stat[:, 2 * N + t:2 * N + t + 1],
                            in1=s.stat[:, 4 * N + t:4 * N + t + 1].to_broadcast([P, c.D]),
                            op0=ALU.subtract, op1=ALU.mult).then_inc(s.vv.h, 1)
                        s.vv.v += 1
                    for i in range(nh):
                        t = t0 + i
                        v.wait_ge(s.tm.h, tmb + i + 1)
                        bank = self.pslot(i % nbank, W)
                        o = fmA.rearrange("p (cc tt) -> p cc tt", cc=c.DC)
                        if rcs_scale is not None:
                            ol = fmL.rearrange("p (cc tt) -> p cc tt", cc=c.DC)
                            oh_t = o[:, :, t * P:(t + 1) * P]
                            ol_t = ol[:, :, t * P:(t + 1) * P]
                            r3 = lambda ap: ap.rearrange("p (cc tt) -> p cc tt", cc=c.DC)
                            self.vstep(v, v.tensor_scalar(out=tmpf[:], in0=bank,
                                                          scalar1=rcs_scale, scalar2=None,
                                                          op0=ALU.mult))
                            self.vstep(v, v.tensor_copy(out=tmpi[:], in_=tmpf[:]))
                            self.vstep(v, v.tensor_scalar(out=tmpj[:], in0=tmpi[:],
                                                          scalar1=1.0 / 256.0, scalar2=None,
                                                          op0=ALU.mult))
                            self.vstep(v, v.tensor_scalar(out=oh_t, in0=r3(tmpj[:]),
                                                          scalar1=256.0, scalar2=None,
                                                          op0=ALU.mult))
                            v.tensor_sub(out=ol_t, in0=r3(tmpi[:]),
                                         in1=oh_t).then_inc(s.vv.h, 1)
                            s.vv.v += 1
                        else:
                            v.tensor_copy(out=o[:, :, t * P:(t + 1) * P], in_=bank
                                          ).then_inc(s.vv.h, 1)
                            s.vv.v += 1

                vvb = s.vv.v - (1 + EV) * nh

                @blk.tensor
                def _(te, t0=t0, vvb=vvb):
                    for i in range(nh):
                        te.wait_ge(s.vv.h, vvb + i + 1)
                        if i >= nbank:
                            te.wait_ge(s.vv.h, vvb + nh + EV * (i - nbank) + 1)
                        bank = self.pslot(i % nbank, W)
                        for cc in range(c.DC):
                            ti = te.transpose(bank[:, cc * P:(cc + 1) * P],
                                              xlnh[:, i * c.D + cc * P:i * c.D + (cc + 1) * P],
                                              s.ident[:])
                        s.tm.inc(ti)
        esl.close()

    def emit_upgate(self, fmh, fml):
        c, s, nc = self.c, self, self.nc
        P = c.P
        es2 = ExitStack()
        wbh = es2.enter_context(nc.sbuf_tensor(self.name("wbh"), [P, 2 * c.DC * P], BF16))
        wbl = es2.enter_context(nc.sbuf_tensor(self.name("wbl"), [P, 2 * c.DC * P], BF16))
        mst = es2.enter_context(nc.sbuf_tensor(self.name("ms"), [P, 2 * c.T], F32))
        nb = min(4, 2048 // c.TN)

        for wh_, wl_, md_, slot0 in [(s.wuq_h, s.wuq_l, s.mup_d, 0),
                                     (s.wgq_h, s.wgq_l, s.mgate_d, 128)]:
            with nc.Block() as blk:
                tmb0 = s.tm.v
                vvb0 = s.vv.v
                oslb = [s.osl[0].v, s.osl[1].v]
                wevt = []
                sevt = []

                @blk.sync
                def _(sy, wh_=wh_, wl_=wl_, md_=md_):
                    for M in range(c.HC + 1):
                        if M < c.HC:
                            if M >= 2:
                                sy.wait_ge(s.tm.h, tmb0 + (M - 1) * c.NTC)
                            w = s.wsl[M % 2]
                            w.inc(sy.dma_start(
                                out=wbh[:, (M % 2) * c.DC * P:(M % 2 + 1) * c.DC * P],
                                in_=wh_[M]), 16)
                            w.inc(sy.dma_start(
                                out=wbl[:, (M % 2) * c.DC * P:(M % 2 + 1) * c.DC * P],
                                in_=wl_[M]), 16)
                            wevt.append(w.v)
                        if M >= 1:
                            sy.wait_ge(s.vv.h, vvb0 + 2 * M * c.NTC)
                            o = s.osl[(M - 1) % 2]
                            o.inc(sy.dma_start(
                                out=md_[M - 1],
                                in_=mst[:, ((M - 1) % 2) * c.T:((M - 1) % 2 + 1) * c.T]), 16)
                            sevt.append(o.v)

                @blk.tensor
                def _(te):
                    for M in range(c.HC):
                        w = s.wsl[M % 2]
                        te.wait_ge(w.h, wevt[M])
                        wo = (M % 2) * c.DC * P
                        for n in range(c.NTC):
                            g = M * c.NTC + n
                            if g >= nb:
                                te.wait_ge(s.vv.h, vvb0 + 2 * (g - nb) + 1)
                            bank = self.pslot(g % nb, c.TN)
                            first = True
                            for kc in range(c.DC):
                                for wb_, fm_ in ((wbh, fmh), (wbh, fml), (wbl, fmh)):
                                    ti = te.matmul(
                                        bank,
                                        wb_[:, wo + kc * P:wo + (kc + 1) * P],
                                        fm_[:, kc * c.T + n * c.TN:kc * c.T + (n + 1) * c.TN],
                                        start=first,
                                        stop=(kc == c.DC - 1 and fm_ is fmh and wb_ is wbl))
                                    first = False
                            s.tm.inc(ti)

                @blk.vector
                def _(v, slot0=slot0):
                    v.wait_ge(s.osl[0].h, oslb[0])
                    v.wait_ge(s.osl[1].h, oslb[1])
                    for M in range(c.HC):
                        if M >= 2:
                            o = s.osl[M % 2]
                            v.wait_ge(o.h, sevt[M - 2])
                        for n in range(c.NTC):
                            g = M * c.NTC + n
                            v.wait_ge(s.tm.h, tmb0 + g + 1)
                            bank = self.pslot(g % nb, c.TN)
                            dst = mst[:, (M % 2) * c.T + n * c.TN:(M % 2) * c.T + (n + 1) * c.TN]
                            self.vstep(v, v.tensor_scalar(
                                out=dst, in0=bank, scalar1=1.0, scalar2=None, op0=ALU.mult))
                            v.reduce_max(out=s.slots[:, slot0 + g:slot0 + g + 1],
                                         in_=dst, axis=AX.X,
                                         apply_absolute_value=True).then_inc(s.vv.h, 1)
                            s.vv.v += 1
        with nc.Block() as blk:
            @blk.sync
            def _(sy):
                sy.wait_ge(s.osl[0].h, s.osl[0].v)
                sy.wait_ge(s.osl[1].h, s.osl[1].v)
        es2.close()

    def emit_f3(self):
        c, s, nc = self.c, self, self.nc
        P = c.P
        es2 = ExitStack()
        mst = es2.enter_context(nc.sbuf_tensor(self.name("f3m"), [P, 4 * c.T], F32))
        kst = es2.enter_context(nc.sbuf_tensor(self.name("f3k"), [P, 4 * c.T], I16))
        sg = es2.enter_context(nc.sbuf_tensor(self.name("f3s"), [P, 2 * c.T], F32))
        jj = es2.enter_context(nc.sbuf_tensor(self.name("f3j"), [P, 2 * c.T], F32))
        pp = es2.enter_context(nc.sbuf_tensor(self.name("f3p"), [P, c.T], F32))
        tpf = es2.enter_context(nc.sbuf_tensor(self.name("f3t"), [P, c.T], F32))
        ju16 = es2.enter_context(nc.sbuf_tensor(self.name("f3u"), [P, c.T], mybir.dt.uint16))

        with nc.Block() as blk:
            vvb0 = s.vv.v
            ssb0 = s.ss.v
            VPC = 9
            ievt = []
            oevt = []

            @blk.sync
            def _(sy):
                for cc in range(c.HC + 1):
                    if cc < c.HC:
                        sl = cc % 2
                        if cc >= 2:
                            sy.wait_ge(s.vv.h, vvb0 + (cc - 2) * VPC + 3)
                        i = s.isl[sl]
                        i.inc(sy.dma_start(out=mst[:, sl * c.T:(sl + 1) * c.T], in_=s.mup_d[cc]), 16)
                        i.inc(sy.dma_start(out=mst[:, (2 + sl) * c.T:(3 + sl) * c.T], in_=s.mgate_d[cc]), 16)
                        ievt.append(i.v)
                    if cc >= 1:
                        sl = (cc - 1) % 2
                        sy.wait_ge(s.vv.h, vvb0 + (cc - 1) * VPC + 4)
                        o = s.osl[sl]
                        for n in range(c.NTC):
                            o.inc(sy.dma_start(
                                out=s.kup_d[n, cc - 1],
                                in_=kst[:, sl * c.T + n * c.TN:sl * c.T + (n + 1) * c.TN]), 16)
                            o.inc(sy.dma_start(
                                out=s.kgate_d[n, cc - 1],
                                in_=kst[:, (2 + sl) * c.T + n * c.TN:(2 + sl) * c.T + (n + 1) * c.TN]), 16)
                        oevt.append(o.v)

            @blk.vector
            def _(v):
                for cc in range(c.HC):
                    sl = cc % 2
                    v.wait_ge(s.isl[sl].h, ievt[cc])
                    if cc >= 2:
                        v.wait_ge(s.osl[sl].h, oevt[cc - 2])
                    self.vstep(v, v.tensor_scalar(out=tpf[:], in0=mst[:, sl * c.T:(sl + 1) * c.T],
                                                  scalar1=self.sc(4), scalar2=None, op0=ALU.mult))
                    self.vstep(v, v.tensor_copy(out=kst[:, sl * c.T:(sl + 1) * c.T], in_=tpf[:]))
                    self.vstep(v, v.tensor_scalar(out=tpf[:], in0=mst[:, (2 + sl) * c.T:(3 + sl) * c.T],
                                                  scalar1=self.sc(5), scalar2=None, op0=ALU.mult))
                    self.vstep(v, v.tensor_copy(out=kst[:, (2 + sl) * c.T:(3 + sl) * c.T], in_=tpf[:]))
                    v.wait_ge(s.ss.h, ssb0 + cc + 1)
                    self.vstep(v, v.tensor_scalar(out=tpf[:], in0=sg[:, sl * c.T:(sl + 1) * c.T],
                                                  scalar1=65536.0, scalar2=None, op0=ALU.mult))
                    self.vstep(v, v.tensor_copy(out=ju16[:], in_=tpf[:]))
                    self.vstep(v, v.tensor_copy(out=jj[:, sl * c.T:(sl + 1) * c.T], in_=ju16[:]))
                    self.vstep(v, v.tensor_mul(out=pp[:], in0=kst[:, (2 + sl) * c.T:(3 + sl) * c.T],
                                               in1=jj[:, sl * c.T:(sl + 1) * c.T]))
                    self.vstep(v, v.reduce_max(out=s.slots[:, cc:cc + 1], in_=pp[:], axis=AX.X,
                                               apply_absolute_value=True))

            @blk.scalar
            def _(sc):
                for cc in range(c.HC):
                    sl = cc % 2
                    sc.wait_ge(s.vv.h, vvb0 + cc * VPC + 4)
                    if cc >= 2:
                        sc.wait_ge(s.ss.h, ssb0 + cc - 1)
                    sc.activation(out=sg[:, sl * c.T:(sl + 1) * c.T],
                                  in_=kst[:, (2 + sl) * c.T:(3 + sl) * c.T],
                                  func=AF.Sigmoid, scale=self.sc(6)).then_inc(s.ss.h, 1)
                    s.ss.v += 1

        with nc.Block() as blk:
            @blk.sync
            def _(sy):
                sy.wait_ge(s.osl[0].h, s.osl[0].v)
                sy.wait_ge(s.osl[1].h, s.osl[1].v)
        es2.close()

    def emit_f4(self):
        c, s, nc = self.c, self, self.nc
        P = c.P
        es2 = ExitStack()
        kin = es2.enter_context(nc.sbuf_tensor(self.name("f4a"), [P, 4 * c.T], I16))
        sg = es2.enter_context(nc.sbuf_tensor(self.name("f4s"), [P, 2 * c.T], F32))
        jj = es2.enter_context(nc.sbuf_tensor(self.name("f4j"), [P, c.T], F32))
        pp = es2.enter_context(nc.sbuf_tensor(self.name("f4p"), [P, c.T], F32))
        k2 = es2.enter_context(nc.sbuf_tensor(self.name("f4k"), [P, 2 * c.T], I16))
        tpf = es2.enter_context(nc.sbuf_tensor(self.name("f4t"), [P, c.T], F32))
        ju16 = es2.enter_context(nc.sbuf_tensor(self.name("f4u"), [P, c.T], mybir.dt.uint16))

        with nc.Block() as blk:
            vvb0 = s.vv.v
            ssb0 = s.ss.v
            VPC = 8
            ievt = []
            oevt = []

            @blk.sync
            def _(sy):
                for cc in range(c.HC + 1):
                    if cc < c.HC:
                        sl = cc % 2
                        if cc >= 2:
                            sy.wait_ge(s.vv.h, vvb0 + (cc - 2) * VPC + 7)
                        i = s.isl[sl]
                        for n in range(c.NTC):
                            i.inc(sy.dma_start(
                                out=kin[:, sl * c.T + n * c.TN:sl * c.T + (n + 1) * c.TN],
                                in_=s.kgate_d[n, cc]), 16)
                            i.inc(sy.dma_start(
                                out=kin[:, (2 + sl) * c.T + n * c.TN:(2 + sl) * c.T + (n + 1) * c.TN],
                                in_=s.kup_d[n, cc]), 16)
                        ievt.append(i.v)
                    if cc >= 1:
                        sl = (cc - 1) % 2
                        sy.wait_ge(s.vv.h, vvb0 + (cc - 1) * VPC + 6)
                        o = s.osl[sl]
                        for n in range(c.NTC):
                            o.inc(sy.dma_start(
                                out=s.kg2_d[n, cc - 1],
                                in_=k2[:, sl * c.T + n * c.TN:sl * c.T + (n + 1) * c.TN]), 16)
                        oevt.append(o.v)

            @blk.scalar
            def _(sc):
                for cc in range(c.HC):
                    sl = cc % 2
                    sc.wait_ge(s.isl[sl].h, ievt[cc])
                    if cc >= 2:
                        sc.wait_ge(s.ss.h, ssb0 + cc - 1)
                        sc.wait_ge(s.vv.h, vvb0 + (cc - 2) * VPC + 1)
                    sc.activation(out=sg[:, sl * c.T:(sl + 1) * c.T],
                                  in_=kin[:, sl * c.T:(sl + 1) * c.T],
                                  func=AF.Sigmoid, scale=self.sc(6)).then_inc(s.ss.h, 1)
                    s.ss.v += 1

            @blk.vector
            def _(v):
                for cc in range(c.HC):
                    sl = cc % 2
                    v.wait_ge(s.ss.h, ssb0 + cc + 1)
                    self.vstep(v, v.tensor_scalar(out=tpf[:], in0=sg[:, sl * c.T:(sl + 1) * c.T],
                                                  scalar1=65536.0, scalar2=None, op0=ALU.mult))
                    self.vstep(v, v.tensor_copy(out=ju16[:], in_=tpf[:]))
                    self.vstep(v, v.tensor_copy(out=jj[:], in_=ju16[:]))
                    self.vstep(v, v.tensor_mul(out=pp[:], in0=kin[:, sl * c.T:(sl + 1) * c.T],
                                               in1=jj[:]))
                    if cc >= 2:
                        v.wait_ge(s.osl[sl].h, oevt[cc - 2])
                    self.vstep(v, v.tensor_scalar(out=tpf[:], in0=pp[:],
                                                  scalar1=self.sc(7), scalar2=None, op0=ALU.mult))
                    self.vstep(v, v.tensor_copy(out=k2[:, sl * c.T:(sl + 1) * c.T], in_=tpf[:]))
                    self.vstep(v, v.tensor_mul(out=pp[:], in0=k2[:, sl * c.T:(sl + 1) * c.T],
                                               in1=kin[:, (2 + sl) * c.T:(3 + sl) * c.T]))
                    self.vstep(v, v.reduce_max(out=s.slots[:, cc:cc + 1], in_=pp[:], axis=AX.X,
                                               apply_absolute_value=True))

        with nc.Block() as blk:
            @blk.sync
            def _(sy):
                sy.wait_ge(s.osl[0].h, s.osl[0].v)
                sy.wait_ge(s.osl[1].h, s.osl[1].v)
        es2.close()

    def emit_f5(self):
        c, s, nc = self.c, self, self.nc
        if self.dbg_hit("xres_pre_f5"):
            self.dump(self.xres[:, 0:4 * c.D], c.P * 4 * c.D)
            return
        P = c.P
        ntile = c.TN // P
        es2 = ExitStack()
        k2st = es2.enter_context(nc.sbuf_tensor(self.name("f5a"), [P, c.HC * c.TN], I16))
        kust = es2.enter_context(nc.sbuf_tensor(self.name("f5b"), [P, c.HC * c.TN], I16))
        qko = es2.enter_context(nc.sbuf_tensor(self.name("f5q"), [P, c.HC * c.TN], F32))
        ko2 = qko
        wdb = es2.enter_context(nc.sbuf_tensor(self.name("f5w"), [P, 2 * c.D], F32))
        wdv = s.wdq_d.rearrange("(hc p) d -> hc p d", p=P)

        for tc in range(c.NTC):
            with nc.Block() as blk:
                iv = []

                @blk.sync
                def _(sy, tc=tc):
                    i = s.isl[0]
                    for cc in range(c.HC):
                        i.inc(sy.dma_start(out=k2st[:, cc * c.TN:(cc + 1) * c.TN],
                                           in_=s.kg2_d[tc, cc]), 16)
                        i.inc(sy.dma_start(out=kust[:, cc * c.TN:(cc + 1) * c.TN],
                                           in_=s.kup_d[tc, cc]), 16)
                    iv.append(i.v)

                @blk.vector
                def _(v):
                    v.wait_ge(s.isl[0].h, iv[0])
                    self.vstep(v, v.tensor_mul(out=qko[:], in0=k2st[:], in1=kust[:]))
                    self.vstep(v, v.tensor_scalar(out=qko[:], in0=qko[:], scalar1=self.sc(9),
                                                  scalar2=None, op0=ALU.mult))
                    self.vstep(v, v.tensor_copy(out=k2st[:], in_=qko[:]))
                    self.vstep(v, v.tensor_copy(out=ko2[:], in_=k2st[:]))

            if self.dbg_hit("ko_tc") and tc == self.dbg[2]:
                self.dump(qko[:], c.P * c.HC * c.TN)
                break
            slow = self.dbg is not None and self.dbg[0] == "f5slow" and self.dbg[1] == self.ffn_i
            with nc.Block() as blk:
                tmb0 = s.tm.v
                wevt = []

                @blk.sync
                def _(sy, tc=tc, slow=slow):
                    for cc in range(c.HC):
                        if slow:
                            if cc >= 1:
                                sy.wait_ge(s.tm.h, tmb0 + cc)
                            w = s.wsl[0]
                            w.inc(sy.dma_start(out=wdb[:, 0:c.D], in_=wdv[cc]), 16)
                            wevt.append(w.v)
                            continue
                        if cc >= 2:
                            sy.wait_ge(s.tm.h, tmb0 + (cc - 1))
                        w = s.wsl[cc % 2]
                        w.inc(sy.dma_start(out=wdb[:, (cc % 2) * c.D:(cc % 2 + 1) * c.D],
                                           in_=wdv[cc]), 16)
                        wevt.append(w.v)

                @blk.tensor
                def _(te, tc=tc, slow=slow):
                    for cc in range(c.HC):
                        wsl_i = 0 if slow else cc % 2
                        woff = 0 if slow else (cc % 2) * c.D
                        te.wait_ge(s.wsl[wsl_i].h, wevt[cc])
                        for t in range(ntile):
                            bank = s.psA if t % 2 == 0 else s.psB
                            boff = (t // 2) * c.D
                            for nn in range(c.ND2):
                                ti = te.matmul(
                                    bank[:, boff + nn * c.NW:boff + (nn + 1) * c.NW],
                                    ko2[:, cc * c.TN + t * P:cc * c.TN + (t + 1) * P],
                                    wdb[:, woff + nn * c.NW:woff + (nn + 1) * c.NW],
                                    start=(cc == 0), stop=(cc == c.HC - 1))
                        s.tm.inc(ti)

                @blk.vector
                def _(v, tc=tc):
                    v.wait_ge(s.tm.h, s.tm.v)
                    abt = (self.dbg is not None and self.dbg[0] == "abtest"
                           and self.dbg[1] == self.ffn_i and tc == 0)
                    for t in range(ntile):
                        bank = s.psA if t % 2 == 0 else s.psB
                        boff = (t // 2) * c.D
                        gt = tc * ntile + t
                        if abt:
                            self.vstep(v, v.tensor_scalar(
                                out=qko[:, t * c.D:(t + 1) * c.D],
                                in0=bank[:, boff:boff + c.D], scalar1=self.sc(10),
                                scalar2=None, op0=ALU.mult))
                            self.vstep(v, v.scalar_tensor_tensor(
                                out=qko[:, (4 + t) * c.D:(5 + t) * c.D],
                                in0=bank[:, boff:boff + c.D], scalar=self.sc(10),
                                in1=s.xres[:, gt * c.D:(gt + 1) * c.D],
                                op0=ALU.mult, op1=ALU.add))
                            self.vstep(v, v.scalar_tensor_tensor(
                                out=s.xres[:, gt * c.D:(gt + 1) * c.D],
                                in0=bank[:, boff:boff + c.D], scalar=self.sc(10),
                                in1=s.xres[:, gt * c.D:(gt + 1) * c.D],
                                op0=ALU.mult, op1=ALU.add))
                            continue
                        if self.dbg is not None and self.dbg[0] == "mdown" and self.dbg[1] == self.ffn_i:
                            self.vstep(v, v.tensor_scalar(
                                out=s.xres[:, gt * c.D:(gt + 1) * c.D],
                                in0=bank[:, boff:boff + c.D], scalar1=self.sc(10),
                                scalar2=None, op0=ALU.mult))
                        else:
                            self.vstep(v, v.scalar_tensor_tensor(
                                out=s.xres[:, gt * c.D:(gt + 1) * c.D],
                                in0=bank[:, boff:boff + c.D], scalar=self.sc(10),
                                in1=s.xres[:, gt * c.D:(gt + 1) * c.D],
                                op0=ALU.mult, op1=ALU.add))
            if self.dbg is not None and self.dbg[0] == "mdown" and self.dbg[1] == self.ffn_i and tc == self.dbg[2] and not self.dbg_done:
                self.dump(s.xres[:, 0:4 * c.D], c.P * 4 * c.D)
                break
            if self.dbg is not None and self.dbg[0] == "abtest" and self.dbg[1] == self.ffn_i and tc == 0 and not self.dbg_done:
                self.dump(qko[:, 0:8 * c.D], c.P * 8 * c.D)
                break
            if self.dbg is not None and self.dbg[0] == "f5slow" and self.dbg[1] == self.ffn_i and tc == 0 and not self.dbg_done:
                self.dump(s.xres[:, 0:4 * c.D], c.P * 4 * c.D)
                break
            if self.dbg_hit("after_f5_tc") and tc == self.dbg[2]:
                self.dump(s.xres[:, 0:4 * c.D], c.P * 4 * c.D)
                break
        es2.close()

    def emit_attention(self):
        c, s, nc = self.c, self, self.nc
        P, N = c.P, c.NTT
        es2 = ExitStack()
        esln = ExitStack()
        fmA = es2.enter_context(nc.sbuf_tensor(self.name("afm"), [P, c.DC * c.T], F32))
        lnsc = esln.enter_context(nc.sbuf_tensor(self.name("alns"), [P, 2 * c.D], F32))
        xlnh = esln.enter_context(nc.sbuf_tensor(self.name("axln"), [P, max(1, N // 2) * c.D], F32))

        self.emit_ln_stats(lnsc)
        self.emit_lnT(fmA, None, xlnh, rcs_scale=None)
        esln.close()

        es3 = ExitStack()
        wbuf = es3.enter_context(nc.sbuf_tensor(self.name("awb"), [P, max(2 * c.DC * P, c.DC * c.NW)], F32))
        qst = es3.enter_context(nc.sbuf_tensor(self.name("aqs"), [P, 2 * c.T], F32))
        nb = min(4, 2048 // c.TN)
        with nc.Block() as blk:
            tmb0 = s.tm.v
            vvb0 = s.vv.v
            wevt = []
            sevt = []

            @blk.sync
            def _(sy):
                for M in range(2 * c.DC + 1):
                    if M < 2 * c.DC:
                        if M >= 2:
                            sy.wait_ge(s.tm.h, tmb0 + (M - 1) * c.NTC)
                        w = s.wsl[M % 2]
                        w.inc(sy.dma_start(
                            out=wbuf[:, (M % 2) * c.DC * P:(M % 2 + 1) * c.DC * P],
                            in_=s.wqk_d[M]), 16)
                        wevt.append(w.v)
                    if M >= 1:
                        sy.wait_ge(s.vv.h, vvb0 + M * c.NTC)
                        dst = s.qT_d[M - 1] if (M - 1) < c.DC else s.kloc_d[M - 1 - c.DC]
                        o = s.osl[(M - 1) % 2]
                        o.inc(sy.dma_start(
                            out=dst, in_=qst[:, ((M - 1) % 2) * c.T:((M - 1) % 2 + 1) * c.T]), 16)
                        sevt.append(o.v)

            @blk.tensor
            def _(te):
                for M in range(2 * c.DC):
                    te.wait_ge(s.wsl[M % 2].h, wevt[M])
                    for n in range(c.NTC):
                        g = M * c.NTC + n
                        if g >= nb:
                            te.wait_ge(s.vv.h, vvb0 + (g - nb) + 1)
                        bank = self.pslot(g % nb, c.TN)
                        for kc in range(c.DC):
                            ti = te.matmul(
                                bank,
                                wbuf[:, (M % 2) * c.DC * P + kc * P:(M % 2) * c.DC * P + (kc + 1) * P],
                                fmA[:, kc * c.T + n * c.TN:kc * c.T + (n + 1) * c.TN],
                                start=(kc == 0), stop=(kc == c.DC - 1))
                        s.tm.inc(ti)

            @blk.vector
            def _(v):
                for M in range(2 * c.DC):
                    if M >= 2:
                        v.wait_ge(s.osl[M % 2].h, sevt[M - 2])
                    for n in range(c.NTC):
                        g = M * c.NTC + n
                        v.wait_ge(s.tm.h, tmb0 + g + 1)
                        bank = self.pslot(g % nb, c.TN)
                        v.tensor_scalar(
                            out=qst[:, (M % 2) * c.T + n * c.TN:(M % 2) * c.T + (n + 1) * c.TN],
                            in0=bank, scalar1=s.bqks[:, M:M + 1], scalar2=None,
                            op0=ALU.add).then_inc(s.vv.h, 1)
                        s.vv.v += 1

        with nc.Block() as blk:
            tmb0 = s.tm.v
            vvb0 = s.vv.v
            oslb = [s.osl[0].v, s.osl[1].v]
            wevt = []
            sevt = []

            @blk.sync
            def _(sy):
                for h2 in range(c.ND2):
                    if h2 >= 1:
                        sy.wait_ge(s.tm.h, tmb0 + h2 * c.NTT)
                    w = s.wsl[h2 % 2]
                    for kc in range(c.DC):
                        w.inc(sy.dma_start(out=wbuf[:, kc * c.NW:(kc + 1) * c.NW],
                                           in_=s.wv_d[h2, kc]), 16)
                    wevt.append(w.v)
                    for Mt in range(c.NTT):
                        g = h2 * c.NTT + Mt
                        sy.wait_ge(s.vv.h, vvb0 + g + 1)
                        o = s.osl[g % 2]
                        o.inc(sy.dma_start(
                            out=s.vloc_d[Mt, :, h2 * c.NW:(h2 + 1) * c.NW],
                            in_=qst[:, (g % 2) * c.T:(g % 2) * c.T + c.NW]), 16)
                        sevt.append(o.v)

            @blk.tensor
            def _(te):
                for h2 in range(c.ND2):
                    te.wait_ge(s.wsl[h2 % 2].h, wevt[h2])
                    for Mt in range(c.NTT):
                        g = h2 * c.NTT + Mt
                        if g >= 4:
                            te.wait_ge(s.vv.h, vvb0 + (g - 4) + 1)
                        bank = self.pslot(g % 4, c.NW)
                        for kc in range(c.DC):
                            ti = te.matmul(bank,
                                           fmA[:, kc * c.T + Mt * P:kc * c.T + (Mt + 1) * P],
                                           wbuf[:, kc * c.NW:(kc + 1) * c.NW],
                                           start=(kc == 0), stop=(kc == c.DC - 1))
                        s.tm.inc(ti)

            @blk.vector
            def _(v):
                v.wait_ge(s.osl[0].h, oslb[0])
                v.wait_ge(s.osl[1].h, oslb[1])
                for h2 in range(c.ND2):
                    for Mt in range(c.NTT):
                        g = h2 * c.NTT + Mt
                        v.wait_ge(s.tm.h, tmb0 + g + 1)
                        if g >= 2:
                            v.wait_ge(s.osl[g % 2].h, sevt[g - 2])
                        bank = self.pslot(g % 4, c.NW)
                        v.tensor_add(out=qst[:, (g % 2) * c.T:(g % 2) * c.T + c.NW],
                                     in0=bank,
                                     in1=s.bvb[:, h2 * c.NW:(h2 + 1) * c.NW]).then_inc(s.vv.h, 1)
                        s.vv.v += 1

            @blk.gpsimd
            def _(g):
                g.wait_ge(s.osl[0].h, oslb[0])
                g.wait_ge(s.osl[1].h, oslb[1])
                s.cc.inc(g.collective_compute(
                    "AllGather", ALU.bypass, replica_groups=self.pair_groups(),
                    ins=[s.kloc_d.opt()], outs=[s.kpair_d.opt()]), 1)
                g.wait_ge(s.cc.h, s.cc.v)
        es3.close()
        es2.close()

        with nc.Block() as blk:
            @blk.gpsimd
            def _(g):
                g.wait_ge(s.osl[0].h, s.osl[0].v)
                g.wait_ge(s.osl[1].h, s.osl[1].v)
                s.cc.inc(g.collective_compute(
                    "AllGather", ALU.bypass, replica_groups=self.pair_groups(),
                    ins=[s.vloc_d.opt()], outs=[s.vpair_d.opt()]), 1)
                g.wait_ge(s.cc.h, s.cc.v)

        self.emit_flash()

    def pair_groups(self):
        return [[2 * b, 2 * b + 1] for b in range(self.c.n_cores // 2)]

    def emit_flash(self):
        c, s, nc = self.c, self, self.nc
        P, HD = c.P, c.HDIM
        QC = c.TN
        NQC = c.T // QC
        qtile = QC // P
        NKT = c.NKT
        half_kt = NKT // 2
        es2 = ExitStack()
        qcb = es2.enter_context(nc.sbuf_tensor(self.name("fqc"), [P, c.DC * QC], F32))
        kbl = es2.enter_context(nc.sbuf_tensor(self.name("fkb"), [P, 4 * P], F32))
        vbl = es2.enter_context(nc.sbuf_tensor(self.name("fvb"), [P, 4 * 130], F32))
        ptl = es2.enter_context(nc.sbuf_tensor(self.name("fpt"), [P, 2 * 2 * QC], F32))
        atn = es2.enter_context(nc.sbuf_tensor(self.name("fat"), [P, c.DC * QC], F32))
        x1b = es2.enter_context(nc.sbuf_tensor(self.name("fx1"), [P, qtile * c.D], F32))
        lsb = es2.enter_context(nc.sbuf_tensor(self.name("fls"), [P, 4 * QC], F32))
        lbc = es2.enter_context(nc.sbuf_tensor(self.name("flb"), [P, QC], F32))
        wpb = es2.enter_context(nc.sbuf_tensor(self.name("fwp"), [P, 2 * c.D], F32))
        wpv = s.wp_d.rearrange("(kc p) d -> kc p d", p=P)

        with nc.Block() as blk:
            @blk.vector
            def _(v):
                for sl in range(4):
                    for hh in range(2):
                        ins = v.memset(vbl[:, sl * 130 + hh * 65 + 64:sl * 130 + hh * 65 + 65], 1.0)
                s.vv.inc(ins)

        for qc in range(NQC):
            with nc.Block() as blk:
                @blk.sync
                def _(sy, qc=qc):
                    for cc in range(c.DC):
                        s.sy.inc(sy.dma_start(out=qcb[:, cc * QC:(cc + 1) * QC],
                                              in_=s.qT_d[cc, :, qc * QC:(qc + 1) * QC]), 16)
                    sy.wait_ge(s.sy.h, s.sy.v)

                @blk.gpsimd
                def _(g, qc=qc):
                    for t in range(qtile):
                        gt = qc * qtile + t
                        g.tensor_add(out=x1b[:, t * c.D:(t + 1) * c.D],
                                     in0=s.xres[:, gt * c.D:(gt + 1) * c.D],
                                     in1=s.bpb[:]).then_inc(s.gg.h, 1)
                        s.gg.v += 1

            pvb, scb = s.psB, s.psA
            with nc.Block() as blk:
                tmb0 = s.tm.v
                ssb0 = s.ss.v
                vvb0 = s.vv.v
                gdb0 = s.gd.v
                kevt = []

                @blk.sync
                def _(sy, qc=qc):
                    for hg in range(c.NHG):
                        for kt in range(NKT):
                            g = hg * NKT + kt
                            sp = kt // half_kt
                            ktl = kt % half_kt
                            slK = g % 4
                            if g >= 4:
                                sy.wait_ge(s.tm.h, tmb0 + 2 * (g - 4) + 3)
                            i = s.isl[slK]
                            i.inc(sy.dma_start(
                                out=kbl[:, slK * P:(slK + 1) * P],
                                in_=s.kpair_d[sp, hg, :, ktl * P:(ktl + 1) * P]), 16)
                            vsrc = s.vpair_d[sp, ktl, :, hg * P:hg * P + 2 * HD]
                            vsrc = vsrc.rearrange("p (h d) -> p h d", h=2)
                            vdst = vbl[:, slK * 130:(slK + 1) * 130].rearrange(
                                "p (h e) -> p h e", h=2)[:, :, 0:HD]
                            i.inc(sy.dma_start(out=vdst, in_=vsrc), 16)
                            kevt.append(i.v)

                G = c.NHG * NKT

                def emit_scores(te, g):
                    sl, slK, hg = g % 2, g % 4, g // NKT
                    te.wait_ge(s.isl[slK].h, kevt[g])
                    if g >= 2:
                        te.wait_ge(s.ss.h, ssb0 + (g - 1))
                    for hh in range(2):
                        ti = te.matmul(
                            scb[:, sl * 1024 + hh * 512:sl * 1024 + hh * 512 + QC],
                            kbl[hh * HD:(hh + 1) * HD, slK * P:(slK + 1) * P],
                            qcb[hh * HD:(hh + 1) * HD, hg * QC:(hg + 1) * QC],
                            start=True, stop=True, tile_position=(hh * HD, 0))
                    s.tm.inc(ti)

                def emit_pv(te, g):
                    sl, slK, hg, kt = g % 2, g % 4, g // NKT, g % NKT
                    pvo = (hg % 2) * 1024
                    te.wait_ge(s.ss.h, ssb0 + g + 1)
                    if kt == 0 and hg >= 2:
                        te.wait_ge(s.vv.h, vvb0 + 4 * (hg - 1))
                    for hh in range(2):
                        ti = te.matmul(
                            pvb[0:65, pvo + hh * 512:pvo + hh * 512 + QC],
                            vbl[:, slK * 130 + hh * 65:slK * 130 + (hh + 1) * 65],
                            ptl[:, sl * 2 * QC + hh * QC:sl * 2 * QC + (hh + 1) * QC],
                            start=(kt == 0), stop=(kt == NKT - 1))
                    s.tm.inc(ti)

                @blk.tensor
                def _(te, qc=qc):
                    emit_scores(te, 0)
                    for g in range(1, G):
                        emit_scores(te, g)
                        emit_pv(te, g - 1)
                    emit_pv(te, G - 1)

                @blk.scalar
                def _(sc, qc=qc):
                    for g in range(G):
                        sl = g % 2
                        sc.wait_ge(s.tm.h, tmb0 + max(2 * g, 1))
                        sin = scb[:, sl * 1024:(sl + 1) * 1024].rearrange(
                            "p (h x) -> p h x", h=2)[:, :, 0:QC]
                        pout = ptl[:, sl * 2 * QC:(sl + 1) * 2 * QC].rearrange(
                            "p (h x) -> p h x", h=2)
                        sc.activation(out=pout, in_=sin,
                                      func=AF.Exp, scale=0.125).then_inc(s.ss.h, 1)
                        s.ss.v += 1

                @blk.vector
                def _(v, qc=qc):
                    for hg in range(c.NHG):
                        pvo = (hg % 2) * 1024
                        pv_stop = (2 * c.NHG * NKT if hg == c.NHG - 1
                                   else 2 * (hg + 1) * NKT + 1)
                        v.wait_ge(s.tm.h, tmb0 + pv_stop)
                        v.tensor_copy(out=lsb[64:65, 0:QC], in_=pvb[64:65, pvo:pvo + QC])
                        v.tensor_copy(out=lsb[64:65, QC:2 * QC],
                                      in_=pvb[64:65, pvo + 512:pvo + 512 + QC]
                                      ).then_inc(s.vv.h, 1)
                        s.vv.v += 1
                        v.wait_ge(s.gd.h, gdb0 + 80 * hg + 32)
                        v.reciprocal(out=lsb[0:2, QC:2 * QC],
                                     in_=lsb[0:2, 0:QC]).then_inc(s.vv.h, 1)
                        s.vv.v += 1
                        v.wait_ge(s.gd.h, gdb0 + 80 * hg + 80)
                        for hh in range(2):
                            h = 2 * hg + hh
                            cch, rr = h // 2, (h % 2) * HD
                            v.tensor_mul(out=atn[rr:rr + HD, cch * QC:(cch + 1) * QC],
                                         in0=pvb[0:HD, pvo + hh * 512:pvo + hh * 512 + QC],
                                         in1=lbc[hh * HD:(hh + 1) * HD, 0:QC]).then_inc(s.vv.h, 1)
                            s.vv.v += 1

                @blk.gpsimd
                def _(g_, qc=qc):
                    for hg in range(c.NHG):
                        if hg >= 1:
                            g_.wait_ge(s.gd.h, gdb0 + 80 * hg)
                        g_.wait_ge(s.vv.h, vvb0 + 4 * hg + 1)
                        s.gd.inc(g_.dma_start(out=s.lraw_d[0:2 * QC],
                                              in_=lsb[64:65, 0:2 * QC]), 16)
                        g_.wait_ge(s.gd.h, gdb0 + 80 * hg + 16)
                        lr = s.lraw_d[0:2 * QC].rearrange("(a b) -> a b", a=2)
                        s.gd.inc(g_.dma_start(out=lsb[0:2, 0:QC], in_=lr), 16)
                        g_.wait_ge(s.vv.h, vvb0 + 4 * hg + 2)
                        s.gd.inc(g_.dma_start(out=s.lrec_d[0:2 * QC],
                                              in_=lsb[0:2, QC:2 * QC]), 16)
                        g_.wait_ge(s.gd.h, gdb0 + 80 * hg + 48)
                        for hh in range(2):
                            s.gd.inc(g_.dma_start(
                                out=lbc[hh * HD:(hh + 1) * HD, 0:QC],
                                in_=s.lrec_d[hh * QC:(hh + 1) * QC].unsqueeze(0).to_broadcast([HD, QC])), 16)

            with nc.Block() as blk:
                wevt = []

                @blk.sync
                def _(sy, qc=qc):
                    tmb0 = s.tm.v
                    for kc in range(c.DC):
                        if kc >= 2:
                            sy.wait_ge(s.tm.h, tmb0 + (kc - 1))
                        w = s.wsl[kc % 2]
                        w.inc(sy.dma_start(out=wpb[:, (kc % 2) * c.D:(kc % 2 + 1) * c.D],
                                           in_=wpv[kc]), 16)
                        wevt.append(w.v)

                @blk.tensor
                def _(te, qc=qc):
                    for kc in range(c.DC):
                        te.wait_ge(s.wsl[kc % 2].h, wevt[kc])
                        for t in range(qtile):
                            bank = s.psA if t % 2 == 0 else s.psB
                            boff = (t // 2) * c.D
                            for nn in range(c.ND2):
                                ti = te.matmul(
                                    bank[:, boff + nn * c.NW:boff + (nn + 1) * c.NW],
                                    atn[:, kc * QC + t * P:kc * QC + (t + 1) * P],
                                    wpb[:, (kc % 2) * c.D + nn * c.NW:(kc % 2) * c.D + (nn + 1) * c.NW],
                                    start=(kc == 0), stop=(kc == c.DC - 1))
                        s.tm.inc(ti)

                @blk.vector
                def _(v, qc=qc):
                    v.wait_ge(s.tm.h, s.tm.v)
                    v.wait_ge(s.gg.h, s.gg.v)
                    for t in range(qtile):
                        bank = s.psA if t % 2 == 0 else s.psB
                        boff = (t // 2) * c.D
                        gt = qc * qtile + t
                        self.vstep(v, v.tensor_add(out=s.xres[:, gt * c.D:(gt + 1) * c.D],
                                                   in0=bank[:, boff:boff + c.D],
                                                   in1=x1b[:, t * c.D:(t + 1) * c.D]))
        es2.close()



def prep_inputs(cfg, L0_attn_x, L0_attn_w_qkv, L0_attn_b_qkv, L0_attn_w_proj,
                L0_attn_b_proj, L0_ffn_w_gate, L0_ffn_w_up, L0_ffn_w_down):
    c = cfg
    P = c.P
    x = np.ascontiguousarray(np.asarray(L0_attn_x, dtype=np.float32))
    D = c.D
    wqkv = np.asarray(L0_attn_w_qkv, dtype=np.float32)

    def tile_lhsT(wT, n_m):
        arr = wT.reshape(c.DC, P, n_m, P)
        return np.ascontiguousarray(arr.transpose(2, 1, 0, 3))

    wqk = tile_lhsT(wqkv[:, 0:2 * D], 2 * c.DC)
    wv = np.ascontiguousarray(
        wqkv[:, 2 * D:3 * D].reshape(c.DC, P, c.ND2, c.NW).transpose(2, 0, 1, 3))
    bqkv = np.asarray(L0_attn_b_qkv, dtype=np.float32)
    bqk = np.ascontiguousarray(bqkv[0:2 * D].reshape(2 * c.DC, P))
    bv = np.ascontiguousarray(bqkv[2 * D:3 * D])
    wp = np.ascontiguousarray(np.asarray(L0_attn_w_proj, dtype=np.float32))
    bp = np.ascontiguousarray(np.asarray(L0_attn_b_proj, dtype=np.float32))
    wu = tile_lhsT(np.ascontiguousarray(np.asarray(L0_ffn_w_up, dtype=np.float32).T), c.HC)
    wg = tile_lhsT(np.ascontiguousarray(np.asarray(L0_ffn_w_gate, dtype=np.float32).T), c.HC)
    wd = np.ascontiguousarray(np.asarray(L0_ffn_w_down, dtype=np.float32).T)

    in_maps = []
    for core in range(c.n_cores):
        b, h = core // 2, core % 2
        xs = np.ascontiguousarray(x[b, h * c.T:(h + 1) * c.T, :])
        in_maps.append(dict(x0=xs, wqk=wqk, wv=wv, bqk=bqk, bv=bv, wp=wp, bp=bp,
                            wu=wu, wg=wg, wd=wd))
    return in_maps


def build_program(cfg, dbg=None):
    b = Builder(cfg, dbg=dbg)
    b.emit()
    return b.finish()


def kernel(L0_attn_x, L0_attn_w_qkv, L0_attn_b_qkv, L0_attn_w_proj, L0_attn_b_proj,
           L0_ffn_w_gate, L0_ffn_w_up, L0_ffn_w_down):
    cfg = FULL
    in_maps = prep_inputs(cfg, L0_attn_x, L0_attn_w_qkv, L0_attn_b_qkv,
                          L0_attn_w_proj, L0_attn_b_proj, L0_ffn_w_gate,
                          L0_ffn_w_up, L0_ffn_w_down)
    nc = build_program(cfg)
    res = run_bass_kernel_spmd(nc, in_maps, core_ids=list(range(cfg.n_cores)))
    x = np.asarray(L0_attn_x)
    Bb, S, D = x.shape
    out = np.empty((Bb, S, D), dtype=np.float32)
    for core in range(cfg.n_cores):
        bb, h = core // 2, core % 2
        out[bb, h * cfg.T:(h + 1) * cfg.T, :] = res.results[core]["y"]
    return out


# version 10
# speedup vs baseline: 1.4296x; 1.0443x over previous
import numpy as np
from contextlib import ExitStack
from dataclasses import dataclass

import concourse.bass as bass
import concourse.mybir as mybir
from concourse.bass_utils import run_bass_kernel_spmd
from concourse.dve_spec import Spec, Src0, Src1, C0, C1, C2, Zero, One, maxx, minn, AluOp
from concourse.dve_spec import lower as dve_lower
from concourse.dve_ops import DveOp, OPS
import concourse.dve_ops as dve_ops_mod

F32 = mybir.dt.float32
I16 = mybir.dt.int16
BF16 = mybir.dt.bfloat16
I32 = mybir.dt.int32
AF = mybir.ActivationFunctionType
ALU = mybir.AluOpType
AX = mybir.AxisListType

MAGIC = float(np.float32(1.5 * 2.0 ** 23))


def _register_op(name, spec, reference):
    for op in OPS:
        if op.name == name:
            return op
    shas, perf_en = {}, {}
    for ver in ("v3", "v4"):
        try:
            shas[ver] = dve_lower(spec, ver=ver, perf_en=False, name=name).sha(ver)
            perf_en[ver] = False
        except Exception:
            pass
    op = DveOp(name, spec, subdim=False, uops_sha=shas, perf_en=perf_en,
               reference=reference)
    OPS.append(op)
    dve_ops_mod._SUB_OPCODE_FOR_NAME[name] = (
        dve_ops_mod._CUSTOM_DVE_ROW_BASE + len(OPS) - 1)
    dve_ops_mod.CUSTOM_DVE_REFS[name] = op.ref
    dve_ops_mod.CUSTOM_DVE_SPECS[name] = spec
    return op


RCS = _register_op(
    "MK_RCS",
    Spec(body=minn(maxx((Src0 * C0 + C2) - C2, Zero - (C1 + One)), C1)),
    lambda in0, in1, s0, s1, imm2: np.clip(np.rint(in0 * s0), -s1 - 1.0, s1),
)
SUBMUL = _register_op(
    "MK_SUBMUL",
    Spec(body=(Src0 - C0) * Src1, accum=AluOp.MAX),
    lambda in0, in1, s0, s1, imm2: (
        (in0 - s0) * in1,
        np.max((in0 - s0) * in1, axis=-1, keepdims=True),
    ),
)


@dataclass(frozen=True)
class Cfg:
    D: int = 1024
    H: int = 3072
    NH: int = 16
    T: int = 2048
    n_cores: int = 8
    eps: float = 1e-5
    stages: int = 3

    P = 128
    HDIM = 64

    @property
    def DC(self):
        return self.D // 128

    @property
    def HC(self):
        return self.H // 128

    @property
    def NTT(self):
        return self.T // 128

    @property
    def S(self):
        return 2 * self.T

    @property
    def NKT(self):
        return self.S // 128

    @property
    def TN(self):
        return min(512, self.T)

    @property
    def NTC(self):
        return self.T // self.TN

    @property
    def NHG(self):
        return self.NH // 2

    @property
    def NW(self):
        return min(512, self.D)

    @property
    def ND2(self):
        return self.D // self.NW


FULL = Cfg()
MINI = Cfg(D=256, H=768, NH=4, T=256)
MIDI = Cfg(D=1024, H=1536, NH=16, T=1024)


class Sem:
    def __init__(self, es, nc, name):
        self.h = es.enter_context(nc.semaphore(name))
        self.v = 0

    def inc(self, ins, n=1):
        ins.then_inc(self.h, n)
        self.v += n
        return self.v


class Builder:
    def __init__(self, cfg: Cfg, dbg=None):
        self.dbg = dbg
        self.ffn_i = 0
        self.dbg_done = False
        self.c = cfg
        nc = bass.Bass("TRN2", target_bir_lowering=False, debug=False)
        self.nc = nc
        self.uid = 0
        c, P = cfg, cfg.P
        self.x0_d = nc.dram_tensor("x0", [c.T, c.D], F32, kind="ExternalInput").ap()
        self.wqk_d = nc.dram_tensor("wqk", [2 * c.DC, P, c.DC, P], F32, kind="ExternalInput").ap()
        self.wv_d = nc.dram_tensor("wv", [c.ND2, c.DC, P, c.NW], F32, kind="ExternalInput").ap()
        self.bqk_d = nc.dram_tensor("bqk", [2 * c.DC, P], F32, kind="ExternalInput").ap()
        self.bv_d = nc.dram_tensor("bv", [c.D], F32, kind="ExternalInput").ap()
        self.wp_d = nc.dram_tensor("wp", [c.D, c.D], F32, kind="ExternalInput").ap()
        self.bp_d = nc.dram_tensor("bp", [c.D], F32, kind="ExternalInput").ap()
        self.wu_d = nc.dram_tensor("wu", [c.HC, P, c.DC, P], F32, kind="ExternalInput").ap()
        self.wg_d = nc.dram_tensor("wg", [c.HC, P, c.DC, P], F32, kind="ExternalInput").ap()
        self.wd_d = nc.dram_tensor("wd", [c.H, c.D], F32, kind="ExternalInput").ap()
        self.y_d = nc.dram_tensor("y", [c.T, c.D], F32, kind="ExternalOutput").ap()
        self.wuq_h = nc.dram_tensor("wuqh", [c.HC, P, c.DC, P], BF16).ap()
        self.wuq_l = nc.dram_tensor("wuql", [c.HC, P, c.DC, P], BF16).ap()
        self.wgq_h = nc.dram_tensor("wgqh", [c.HC, P, c.DC, P], BF16).ap()
        self.wgq_l = nc.dram_tensor("wgql", [c.HC, P, c.DC, P], BF16).ap()
        self.wdq_h = nc.dram_tensor("wdqh", [c.H, c.D], BF16).ap()
        self.wdq_l = nc.dram_tensor("wdql", [c.H, c.D], BF16).ap()
        self.mup_d = nc.dram_tensor("mup", [c.HC, P, c.T], F32).ap()
        self.mgate_d = nc.dram_tensor("mgate", [c.HC, P, c.T], F32).ap()
        self.kup_d = nc.dram_tensor("kup", [c.NTC, c.HC, P, c.TN], I16).ap()
        self.kgate_d = nc.dram_tensor("kgate", [c.NTC, c.HC, P, c.TN], I16).ap()
        self.kg2_d = nc.dram_tensor("kg2", [c.NTC, c.HC, P, c.TN], I16).ap()
        self.qT_d = nc.dram_tensor("qTd", [c.DC, P, c.T], F32).ap()
        self.kloc_d = nc.dram_tensor("kloc", [c.DC, P, c.T], F32).ap()
        self.vloc_d = nc.dram_tensor("vloc", [c.NTT, P, c.D], F32).ap()
        self.kpair_d = nc.dram_tensor("kpair", [2, c.DC, P, c.T], F32).ap()
        self.vpair_d = nc.dram_tensor("vpair", [2, c.NTT, P, c.D], F32).ap()
        self.lraw_d = nc.dram_tensor("lraw", [2 * 512], F32).ap()
        self.lrec_d = nc.dram_tensor("lrec", [2 * 512], F32).ap()
        self.ar_in = [nc.dram_tensor(f"arin{i}", [256], F32).ap() for i in range(8)]
        self.ar_out = [nc.dram_tensor(f"arout{i}", [256], F32).ap() for i in range(8)]
        self.n_ar = 0

        self.es = ExitStack()
        self.es.__enter__()
        es = self.es
        sb = lambda n, sh, dt=F32: es.enter_context(nc.sbuf_tensor(n, sh, dt))
        self.xres = sb("xres", [P, c.NTT * c.D])
        self.stat = sb("stat", [P, 8 * c.NTT])
        self.slots = sb("slots", [P, 256])
        self.scal = sb("scal", [P, 32])
        self.arb = sb("arb", [P, 256])
        self.ident = sb("ident", [P, P])
        self.bvb = sb("bvb", [P, c.D])
        self.bpb = sb("bpb", [P, c.D])
        self.bqks = sb("bqks", [P, 2 * c.DC])
        self.psA = es.enter_context(nc.psum_tensor("psA", [P, 2048], F32))
        self.psB = es.enter_context(nc.psum_tensor("psB", [P, 2048], F32))
        self.sy = Sem(es, nc, "sy")
        self.gd = Sem(es, nc, "gd")
        self.cc = Sem(es, nc, "cc")
        self.tm = Sem(es, nc, "tm")
        self.vv = Sem(es, nc, "vv")
        self.ss = Sem(es, nc, "ss")
        self.gg = Sem(es, nc, "gg")
        self.wsl = [Sem(es, nc, "wsl0"), Sem(es, nc, "wsl1")]
        self.isl = [Sem(es, nc, f"isl{i}") for i in range(4)]
        self.osl = [Sem(es, nc, f"osl{i}") for i in range(4)]

    def sc(self, i):
        return self.scal[:, i:i + 1]

    def name(self, stem):
        self.uid += 1
        return f"{stem}{self.uid}"

    def pslot(self, i, width):
        t = self.psA if i % 2 == 0 else self.psB
        off = ((i // 2) % 2) * 1024
        return t[:, off:off + width]

    def vstep(self, v, ins):
        ins.then_inc(self.vv.h, 1)
        self.vv.v += 1
        v.wait_ge(self.vv.h, self.vv.v)

    def finish(self):
        self.es.__exit__(None, None, None)
        return self.nc

    def emit(self):
        c, nc = self.c, self.nc
        with nc.Block() as blk:
            @blk.sync
            def _(sy):
                x0v = self.x0_d.rearrange("(t p) d -> t p d", p=c.P)
                for t in range(c.NTT):
                    self.sy.inc(sy.dma_start(out=self.xres[:, t * c.D:(t + 1) * c.D], in_=x0v[t]), 16)
                self.sy.inc(sy.dma_start(out=self.bvb[:], in_=self.bv_d.unsqueeze(0).to_broadcast([c.P, c.D])), 16)
                self.sy.inc(sy.dma_start(out=self.bpb[:], in_=self.bp_d.unsqueeze(0).to_broadcast([c.P, c.D])), 16)
                idd = nc.inline_tensor(np.eye(c.P, dtype=np.float32), "identnp").ap()
                self.sy.inc(sy.dma_start(out=self.ident[:], in_=idd), 16)
                for m in range(2 * c.DC):
                    self.sy.inc(sy.dma_start(out=self.bqks[:, m:m + 1], in_=self.bqk_d[m].unsqueeze(1)), 16)
                sy.wait_ge(self.sy.h, self.sy.v)

        self.emit_dequant()
        self.emit_ffn()
        if c.stages >= 2 and not self.dbg_done:
            self.emit_attention()
        if c.stages >= 3 and not self.dbg_done:
            self.emit_ffn()

        if not self.dbg_done:
            with nc.Block() as blk:
                @blk.sync
                def _(sy):
                    yv = self.y_d.rearrange("(t p) d -> t p d", p=c.P)
                    for t in range(c.NTT):
                        self.sy.inc(sy.dma_start(out=yv[t], in_=self.xres[:, t * c.D:(t + 1) * c.D]), 16)
                    sy.wait_ge(self.sy.h, self.sy.v)

    def emit_dequant(self):
        c, s, nc = self.c, self, self.nc
        P = c.P
        W = 512 if c.D >= 512 else c.D
        es2 = ExitStack()
        wa = es2.enter_context(nc.sbuf_tensor(self.name("dqa"), [P, 4 * W], F32))
        whb = es2.enter_context(nc.sbuf_tensor(self.name("dqh"), [P, 4 * W], BF16))
        wlb = es2.enter_context(nc.sbuf_tensor(self.name("dql"), [P, 4 * W], BF16))
        wt = es2.enter_context(nc.sbuf_tensor(self.name("dqt"), [P, W], F32))
        bm = es2.enter_context(nc.sbuf_tensor(self.name("dqm"), [P, W // 2], F32))
        ki = es2.enter_context(nc.sbuf_tensor(self.name("dqi"), [P, W], I16))
        nb = W // 32

        tiles = []
        for wsrc, wdh, wdl in [(s.wu_d, s.wuq_h, s.wuq_l), (s.wg_d, s.wgq_h, s.wgq_l)]:
            for m in range(c.HC):
                v4 = wsrc[m].rearrange("p cc pp -> p (cc pp)")
                d4h = wdh[m].rearrange("p cc pp -> p (cc pp)")
                d4l = wdl[m].rearrange("p cc pp -> p (cc pp)")
                for j in range(0, c.DC * P, W):
                    w_ = min(W, c.DC * P - j)
                    tiles.append((v4[:, j:j + w_], [d4h[:, j:j + w_], d4l[:, j:j + w_]],
                                  w_, True))
        wdv_s = s.wd_d.rearrange("(hc p) d -> hc p d", p=P)
        wdv_h = s.wdq_h.rearrange("(hc p) d -> hc p d", p=P)
        wdv_l = s.wdq_l.rearrange("(hc p) d -> hc p d", p=P)
        for hc in range(c.HC):
            for j in range(0, c.D, W):
                w_ = min(W, c.D - j)
                tiles.append((wdv_s[hc, :, j:j + w_],
                              [wdv_h[hc, :, j:j + w_], wdv_l[hc, :, j:j + w_]], w_, True))

        with nc.Block() as blk:
            n = len(tiles)
            chain_end = []
            acc = s.vv.v
            for (_, _, _, bf) in tiles:
                acc += 9 if bf else 7
                chain_end.append(acc)
            ld_evt = []
            st_evt = []

            @blk.sync
            def _(sy):
                for i in range(n + 1):
                    if i < n:
                        src_ap, dsts, w_, bf = tiles[i]
                        if i >= 4:
                            sy.wait_ge(s.osl[i % 4].h, st_evt[i - 4])
                        il = s.isl[i % 4]
                        il.inc(sy.dma_start(out=wa[:, (i % 4) * W:(i % 4) * W + w_], in_=src_ap), 16)
                        ld_evt.append(il.v)
                    if i >= 1:
                        j = i - 1
                        src_ap, dsts, w_, bf = tiles[j]
                        sy.wait_ge(s.vv.h, chain_end[j])
                        ol = s.osl[j % 4]
                        o = (j % 4) * W
                        if bf:
                            ol.inc(sy.dma_start(out=dsts[0], in_=whb[:, o:o + w_]), 16)
                            ol.inc(sy.dma_start(out=dsts[1], in_=wlb[:, o:o + w_]), 16)
                        else:
                            ol.inc(sy.dma_start(out=dsts[0], in_=wa[:, o:o + w_]), 16)
                        st_evt.append(ol.v)
                for q in range(4):
                    sy.wait_ge(s.osl[q].h, s.osl[q].v)

            @blk.vector
            def _(v):
                for i, (src_ap, dsts, w_, bf) in enumerate(tiles):
                    nb_ = w_ // 32
                    o = (i % 4) * W
                    v.wait_ge(s.isl[i % 4].h, ld_evt[i])
                    self.vstep(v, v.transpose(out=wt[:, 0:w_], in_=wa[:, o:o + w_]))
                    self.vstep(v, v.reduce_max(
                        out=bm[:, 0:nb_],
                        in_=wt[:, 0:w_].rearrange("p (g e) -> p g e", e=32),
                        axis=AX.X, apply_absolute_value=True))
                    self.vstep(v, v.tensor_scalar(out=bm[:, 0:nb_], in0=bm[:, 0:nb_],
                                                  scalar1=1.0 / 7.0, scalar2=1e-12,
                                                  op0=ALU.mult, op1=ALU.max))
                    self.vstep(v, v.reciprocal(out=bm[:, nb:nb + nb_], in_=bm[:, 0:nb_]))
                    self.vstep(v, v.tensor_tensor(
                        out=ki[:, 0:w_].rearrange("p (g e) -> p g e", e=32),
                        in0=wt[:, 0:w_].rearrange("p (g e) -> p g e", e=32),
                        in1=bm[:, nb:nb + nb_].rearrange("p (g o) -> p g o", o=1).to_broadcast([P, nb_, 32]),
                        op=ALU.mult))
                    self.vstep(v, v.tensor_tensor(
                        out=wt[:, 0:w_].rearrange("p (g e) -> p g e", e=32),
                        in0=ki[:, 0:w_].rearrange("p (g e) -> p g e", e=32),
                        in1=bm[:, 0:nb_].rearrange("p (g o) -> p g o", o=1).to_broadcast([P, nb_, 32]),
                        op=ALU.mult))
                    self.vstep(v, v.transpose(out=wa[:, o:o + w_], in_=wt[:, 0:w_]))
                    if bf:
                        self.vstep(v, v.tensor_copy(out=whb[:, o:o + w_], in_=wa[:, o:o + w_]))
                        self.vstep(v, v.tensor_sub(out=wlb[:, o:o + w_], in0=wa[:, o:o + w_],
                                                   in1=whb[:, o:o + w_]))
        es2.close()

    def emit_ln_stats(self, lnsc):
        c, s = self.c, self
        N = c.NTT
        nc = self.nc
        with nc.Block() as blk:
            @blk.vector
            def _(v):
                for t in range(N):
                    xt = s.xres[:, t * c.D:(t + 1) * c.D]
                    ins = v.reduce_sum(out=s.stat[:, t:t + 1], in_=xt, axis=AX.X)
                s.vv.inc(ins)

            @blk.scalar
            def _(sc):
                ssb = s.ss.v
                for t in range(N):
                    xt = s.xres[:, t * c.D:(t + 1) * c.D]
                    if t >= 1:
                        sc.wait_ge(s.ss.h, ssb + t)
                    sc.activation(out=lnsc[:, 0:c.D], in_=xt, func=AF.Square,
                                  accum_out=s.stat[:, N + t:N + t + 1]).then_inc(s.ss.h, 1)
                    s.ss.v += 1

        with nc.Block() as blk:
            @blk.vector
            def _(v):
                st = s.stat
                self.vstep(v, v.tensor_scalar(out=st[:, 2 * N:3 * N], in0=st[:, 0:N],
                                              scalar1=1.0 / c.D, scalar2=None, op0=ALU.mult))
                self.vstep(v, v.scalar_tensor_tensor(out=st[:, 3 * N:4 * N], in0=st[:, 2 * N:3 * N],
                                                     scalar=-1.0, in1=st[:, 2 * N:3 * N],
                                                     op0=ALU.mult, op1=ALU.mult))
                self.vstep(v, v.scalar_tensor_tensor(out=st[:, 3 * N:4 * N], in0=st[:, N:2 * N],
                                                     scalar=1.0 / c.D, in1=st[:, 3 * N:4 * N],
                                                     op0=ALU.mult, op1=ALU.add))
                self.vstep(v, v.tensor_scalar(out=st[:, 3 * N:4 * N], in0=st[:, 3 * N:4 * N],
                                              scalar1=c.eps, scalar2=None, op0=ALU.add))
                s.vv.inc(v.engine_nop())

            @blk.scalar
            def _(sc):
                sc.wait_ge(s.vv.h, s.vv.v)
                st = s.stat
                s.ss.inc(sc.activation(out=st[:, 4 * N:5 * N], in_=st[:, 3 * N:4 * N],
                                       func=AF.Sqrt))

            @blk.vector
            def _(v):
                v.wait_ge(s.ss.h, s.ss.v)
                st = s.stat
                self.vstep(v, v.reciprocal(out=st[:, 5 * N:6 * N], in_=st[:, 4 * N:5 * N]))
                self.vstep(v, v.tensor_mul(out=st[:, 6 * N:7 * N], in0=st[:, 5 * N:6 * N],
                                           in1=st[:, 5 * N:6 * N]))
                self.vstep(v, v.tensor_mul(out=st[:, 6 * N:7 * N], in0=st[:, 6 * N:7 * N],
                                           in1=st[:, 3 * N:4 * N]))
                self.vstep(v, v.tensor_scalar(out=st[:, 6 * N:7 * N], in0=st[:, 6 * N:7 * N],
                                              scalar1=-0.5, scalar2=1.5, op0=ALU.mult, op1=ALU.add))
                self.vstep(v, v.tensor_mul(out=st[:, 4 * N:5 * N], in0=st[:, 5 * N:6 * N],
                                           in1=st[:, 6 * N:7 * N]))
                s.vv.inc(v.engine_nop())

    def emit_armax(self, ranges, outs):
        c, s, nc = self.c, self, self.nc
        idx = self.n_ar % 8
        self.n_ar += 1
        k = len(ranges)
        with nc.Block() as blk:
            @blk.vector
            def _(v):
                for i, (st0, n) in enumerate(ranges):
                    ins = v.reduce_max(out=s.scal[:, 30 - i:31 - i],
                                       in_=s.slots[:, st0:st0 + n], axis=AX.X)
                s.vv.inc(ins)

            @blk.gpsimd
            def _(g):
                g.wait_ge(s.vv.h, s.vv.v)
                for i in range(k):
                    s.gd.inc(g.dma_start(out=s.ar_in[idx][i * 128:(i + 1) * 128],
                                         in_=s.scal[:, 30 - i:31 - i]), 16)
                g.wait_ge(s.gd.h, s.gd.v)
                s.cc.inc(g.collective_compute(
                    "AllReduce", ALU.max, replica_groups=[list(range(c.n_cores))],
                    ins=[s.ar_in[idx][0:k * 128].opt()],
                    outs=[s.ar_out[idx][0:k * 128].opt()]), 1)
                g.wait_ge(s.cc.h, s.cc.v)
                for i in range(k):
                    s.gd.inc(g.dma_start(
                        out=s.arb[:, i * 128:(i + 1) * 128],
                        in_=s.ar_out[idx][i * 128:(i + 1) * 128].unsqueeze(0).to_broadcast([c.P, 128])), 16)

            gdv = s.gd.v

            @blk.vector
            def _(v):
                v.wait_ge(s.gd.h, gdv)
                for i in range(k):
                    ins = v.reduce_max(out=outs[i], in_=s.arb[:, i * 128:(i + 1) * 128], axis=AX.X)
                s.vv.inc(ins)

    def dump(self, ap, nelem):
        nc, s = self.nc, self
        yflat = self.y_d.rearrange("a b -> (a b)")
        with nc.Block() as blk:
            @blk.sync
            def _(sy):
                s.sy.inc(sy.dma_start(out=yflat[0:nelem], in_=ap), 16)
                sy.wait_ge(s.sy.h, s.sy.v)
        self.dbg_done = True

    def dbg_hit(self, tag):
        return (not self.dbg_done) and self.dbg is not None and \
            self.dbg[0] == tag and self.dbg[1] == self.ffn_i

    def emit_ffn(self):
        c, s, nc = self.c, self, self.nc
        self.ffn_i += 1
        P, N = c.P, c.NTT
        es2 = ExitStack()
        esln = ExitStack()
        fmh = es2.enter_context(nc.sbuf_tensor(self.name("fmh"), [P, c.DC * c.T], BF16))
        fml = es2.enter_context(nc.sbuf_tensor(self.name("fml"), [P, c.DC * c.T], BF16))
        lnsc = esln.enter_context(nc.sbuf_tensor(self.name("lnsc"), [P, 2 * c.D], F32))
        xlnh = esln.enter_context(nc.sbuf_tensor(self.name("xlnh"), [P, max(1, N // 2) * c.D], F32))

        self.emit_ln_stats(lnsc)

        with nc.Block() as blk:
            @blk.vector
            def _(v):
                for t in range(N):
                    self.vstep(v, v.scalar_tensor_tensor(
                        out=lnsc[:, 0:c.D],
                        in0=s.xres[:, t * c.D:(t + 1) * c.D],
                        scalar=s.stat[:, 2 * N + t:2 * N + t + 1],
                        in1=s.stat[:, 4 * N + t:4 * N + t + 1].to_broadcast([c.P, c.D]),
                        op0=ALU.subtract, op1=ALU.mult))
                    self.vstep(v, v.reduce_max(out=s.slots[:, t:t + 1], in_=lnsc[:, 0:c.D],
                                               axis=AX.X, apply_absolute_value=True))

        self.emit_armax([(0, N)], [self.sc(25)])
        with nc.Block() as blk:
            @blk.vector
            def _(v):
                v.wait_ge(s.vv.h, s.vv.v)
                self.vstep(v, v.tensor_scalar(out=self.sc(1), in0=self.sc(25), scalar1=1.0 / 32767.0,
                                              scalar2=1e-12, op0=ALU.mult, op1=ALU.max))
                self.vstep(v, v.reciprocal(out=self.sc(2), in_=self.sc(1)))
                s.vv.inc(v.engine_nop())

        self.emit_lnT(fmh, fml, xlnh, rcs_scale=self.sc(2))
        esln.close()

        self.emit_upgate(fmh, fml)
        es2.close()

        self.emit_armax([(0, c.HC * c.NTC), (128, c.HC * c.NTC)],
                        [self.sc(30), self.sc(29)])
        with nc.Block() as blk:
            @blk.vector
            def _(v):
                v.wait_ge(s.vv.h, s.vv.v)
                self.vstep(v, v.scalar_tensor_tensor(out=self.sc(20), in0=self.sc(30),
                                                     scalar=1.0 / 32767.0, in1=self.sc(1),
                                                     op0=ALU.mult, op1=ALU.mult))
                self.vstep(v, v.tensor_scalar(out=self.sc(20), in0=self.sc(20), scalar1=1e-12,
                                              scalar2=None, op0=ALU.max))
                self.vstep(v, v.reciprocal(out=self.sc(28), in_=self.sc(20)))
                self.vstep(v, v.tensor_mul(out=self.sc(4), in0=self.sc(28), in1=self.sc(1)))
                self.vstep(v, v.scalar_tensor_tensor(out=self.sc(21), in0=self.sc(29),
                                                     scalar=1.0 / 32767.0, in1=self.sc(1),
                                                     op0=ALU.mult, op1=ALU.mult))
                self.vstep(v, v.tensor_scalar(out=self.sc(21), in0=self.sc(21), scalar1=1e-12,
                                              scalar2=None, op0=ALU.max))
                self.vstep(v, v.reciprocal(out=self.sc(28), in_=self.sc(21)))
                self.vstep(v, v.tensor_mul(out=self.sc(5), in0=self.sc(28), in1=self.sc(1)))
                self.vstep(v, v.tensor_copy(out=self.sc(6), in_=self.sc(21)))
                s.vv.inc(v.engine_nop())

        if self.dbg_hit("scal_ar1"):
            self.dump(s.scal[:], c.P * 32)
            return
        self.emit_f3()
        self.emit_armax([(0, c.HC)], [self.sc(30)])
        with nc.Block() as blk:
            @blk.vector
            def _(v):
                v.wait_ge(s.vv.h, s.vv.v)
                self.vstep(v, v.scalar_tensor_tensor(out=self.sc(22), in0=self.sc(6),
                                                     scalar=1.0 / 65536.0, in1=self.sc(30),
                                                     op0=ALU.mult, op1=ALU.mult))
                self.vstep(v, v.tensor_scalar(out=self.sc(22), in0=self.sc(22),
                                              scalar1=1.0 / 32767.0, scalar2=1e-12,
                                              op0=ALU.mult, op1=ALU.max))
                self.vstep(v, v.reciprocal(out=self.sc(28), in_=self.sc(22)))
                self.vstep(v, v.scalar_tensor_tensor(out=self.sc(7), in0=self.sc(6),
                                                     scalar=1.0 / 65536.0, in1=self.sc(28),
                                                     op0=ALU.mult, op1=ALU.mult))
                self.vstep(v, v.tensor_mul(out=self.sc(8), in0=self.sc(22), in1=self.sc(20)))
                s.vv.inc(v.engine_nop())

        if self.dbg_hit("scal_ar2"):
            self.dump(s.scal[:], c.P * 32)
            return
        self.emit_f4()
        self.emit_armax([(0, c.HC)], [self.sc(30)])
        with nc.Block() as blk:
            @blk.vector
            def _(v):
                v.wait_ge(s.vv.h, s.vv.v)
                self.vstep(v, v.scalar_tensor_tensor(out=self.sc(23), in0=self.sc(30),
                                                     scalar=1.0 / 32767.0, in1=self.sc(8),
                                                     op0=ALU.mult, op1=ALU.mult))
                self.vstep(v, v.tensor_scalar(out=self.sc(23), in0=self.sc(23), scalar1=1e-12,
                                              scalar2=None, op0=ALU.max))
                self.vstep(v, v.reciprocal(out=self.sc(28), in_=self.sc(23)))
                self.vstep(v, v.tensor_mul(out=self.sc(9), in0=self.sc(28), in1=self.sc(8)))
                self.vstep(v, v.tensor_scalar(out=self.sc(10), in0=self.sc(23), scalar1=0.5,
                                              scalar2=None, op0=ALU.mult))
                s.vv.inc(v.engine_nop())

        if self.dbg_hit("scal_ar3"):
            self.dump(s.scal[:], c.P * 32)
            return
        self.emit_f5()

    def emit_lnT(self, fmA, fmL, xlnh, rcs_scale):
        c, s, nc = self.c, self, self.nc
        P, N = c.P, c.NTT
        nh = max(1, N // 2)
        nbank = 4
        W = c.DC * P
        EV = 5 if rcs_scale is not None else 1
        esl = ExitStack()
        tmpf = esl.enter_context(nc.sbuf_tensor(self.name("ltf"), [P, W], F32))
        tmpi = esl.enter_context(nc.sbuf_tensor(self.name("lti"), [P, W], I16))
        tmpj = esl.enter_context(nc.sbuf_tensor(self.name("ltj"), [P, W], I16))
        for half in range(N // nh):
            t0 = half * nh
            with nc.Block() as blk:
                @blk.vector
                def _(v, t0=t0):
                    tmb = s.tm.v
                    for i in range(nh):
                        t = t0 + i
                        v.scalar_tensor_tensor(
                            out=xlnh[:, i * c.D:(i + 1) * c.D],
                            in0=s.xres[:, t * c.D:(t + 1) * c.D],
                            scalar=s.stat[:, 2 * N + t:2 * N + t + 1],
                            in1=s.stat[:, 4 * N + t:4 * N + t + 1].to_broadcast([P, c.D]),
                            op0=ALU.subtract, op1=ALU.mult).then_inc(s.vv.h, 1)
                        s.vv.v += 1
                    for i in range(nh):
                        t = t0 + i
                        v.wait_ge(s.tm.h, tmb + i + 1)
                        bank = self.pslot(i % nbank, W)
                        o = fmA.rearrange("p (cc tt) -> p cc tt", cc=c.DC)
                        if rcs_scale is not None:
                            ol = fmL.rearrange("p (cc tt) -> p cc tt", cc=c.DC)
                            oh_t = o[:, :, t * P:(t + 1) * P]
                            ol_t = ol[:, :, t * P:(t + 1) * P]
                            r3 = lambda ap: ap.rearrange("p (cc tt) -> p cc tt", cc=c.DC)
                            self.vstep(v, v.tensor_scalar(out=tmpf[:], in0=bank,
                                                          scalar1=rcs_scale, scalar2=None,
                                                          op0=ALU.mult))
                            self.vstep(v, v.tensor_copy(out=tmpi[:], in_=tmpf[:]))
                            self.vstep(v, v.tensor_scalar(out=tmpj[:], in0=tmpi[:],
                                                          scalar1=1.0 / 256.0, scalar2=None,
                                                          op0=ALU.mult))
                            self.vstep(v, v.tensor_scalar(out=oh_t, in0=r3(tmpj[:]),
                                                          scalar1=256.0, scalar2=None,
                                                          op0=ALU.mult))
                            v.tensor_sub(out=ol_t, in0=r3(tmpi[:]),
                                         in1=oh_t).then_inc(s.vv.h, 1)
                            s.vv.v += 1
                        else:
                            v.tensor_copy(out=o[:, :, t * P:(t + 1) * P], in_=bank
                                          ).then_inc(s.vv.h, 1)
                            s.vv.v += 1

                vvb = s.vv.v - (1 + EV) * nh

                @blk.tensor
                def _(te, t0=t0, vvb=vvb):
                    for i in range(nh):
                        te.wait_ge(s.vv.h, vvb + i + 1)
                        if i >= nbank:
                            te.wait_ge(s.vv.h, vvb + nh + EV * (i - nbank) + 1)
                        bank = self.pslot(i % nbank, W)
                        for cc in range(c.DC):
                            ti = te.transpose(bank[:, cc * P:(cc + 1) * P],
                                              xlnh[:, i * c.D + cc * P:i * c.D + (cc + 1) * P],
                                              s.ident[:])
                        s.tm.inc(ti)
        esl.close()

    def emit_upgate(self, fmh, fml):
        c, s, nc = self.c, self, self.nc
        P = c.P
        es2 = ExitStack()
        wbh = es2.enter_context(nc.sbuf_tensor(self.name("wbh"), [P, 2 * c.DC * P], BF16))
        wbl = es2.enter_context(nc.sbuf_tensor(self.name("wbl"), [P, 2 * c.DC * P], BF16))
        mst = es2.enter_context(nc.sbuf_tensor(self.name("ms"), [P, 2 * c.T], F32))
        nb = min(4, 2048 // c.TN)

        for wh_, wl_, md_, slot0 in [(s.wuq_h, s.wuq_l, s.mup_d, 0),
                                     (s.wgq_h, s.wgq_l, s.mgate_d, 128)]:
            with nc.Block() as blk:
                tmb0 = s.tm.v
                vvb0 = s.vv.v
                oslb = [s.osl[0].v, s.osl[1].v]
                wevt = []
                sevt = []

                @blk.sync
                def _(sy, wh_=wh_, wl_=wl_, md_=md_):
                    for M in range(c.HC + 1):
                        if M < c.HC:
                            if M >= 2:
                                sy.wait_ge(s.tm.h, tmb0 + (M - 1) * c.NTC)
                            w = s.wsl[M % 2]
                            w.inc(sy.dma_start(
                                out=wbh[:, (M % 2) * c.DC * P:(M % 2 + 1) * c.DC * P],
                                in_=wh_[M]), 16)
                            w.inc(sy.dma_start(
                                out=wbl[:, (M % 2) * c.DC * P:(M % 2 + 1) * c.DC * P],
                                in_=wl_[M]), 16)
                            wevt.append(w.v)
                        if M >= 1:
                            sy.wait_ge(s.vv.h, vvb0 + 2 * M * c.NTC)
                            o = s.osl[(M - 1) % 2]
                            o.inc(sy.dma_start(
                                out=md_[M - 1],
                                in_=mst[:, ((M - 1) % 2) * c.T:((M - 1) % 2 + 1) * c.T]), 16)
                            sevt.append(o.v)

                @blk.tensor
                def _(te):
                    for M in range(c.HC):
                        w = s.wsl[M % 2]
                        te.wait_ge(w.h, wevt[M])
                        wo = (M % 2) * c.DC * P
                        for n in range(c.NTC):
                            g = M * c.NTC + n
                            if g >= nb:
                                te.wait_ge(s.vv.h, vvb0 + 2 * (g - nb) + 1)
                            bank = self.pslot(g % nb, c.TN)
                            first = True
                            for kc in range(c.DC):
                                for wb_, fm_ in ((wbh, fmh), (wbh, fml), (wbl, fmh)):
                                    ti = te.matmul(
                                        bank,
                                        wb_[:, wo + kc * P:wo + (kc + 1) * P],
                                        fm_[:, kc * c.T + n * c.TN:kc * c.T + (n + 1) * c.TN],
                                        start=first,
                                        stop=(kc == c.DC - 1 and fm_ is fmh and wb_ is wbl))
                                    first = False
                            s.tm.inc(ti)

                @blk.vector
                def _(v, slot0=slot0):
                    v.wait_ge(s.osl[0].h, oslb[0])
                    v.wait_ge(s.osl[1].h, oslb[1])
                    for M in range(c.HC):
                        if M >= 2:
                            o = s.osl[M % 2]
                            v.wait_ge(o.h, sevt[M - 2])
                        for n in range(c.NTC):
                            g = M * c.NTC + n
                            v.wait_ge(s.tm.h, tmb0 + g + 1)
                            bank = self.pslot(g % nb, c.TN)
                            dst = mst[:, (M % 2) * c.T + n * c.TN:(M % 2) * c.T + (n + 1) * c.TN]
                            self.vstep(v, v.tensor_scalar(
                                out=dst, in0=bank, scalar1=1.0, scalar2=None, op0=ALU.mult))
                            v.reduce_max(out=s.slots[:, slot0 + g:slot0 + g + 1],
                                         in_=dst, axis=AX.X,
                                         apply_absolute_value=True).then_inc(s.vv.h, 1)
                            s.vv.v += 1
        with nc.Block() as blk:
            @blk.sync
            def _(sy):
                sy.wait_ge(s.osl[0].h, s.osl[0].v)
                sy.wait_ge(s.osl[1].h, s.osl[1].v)
        es2.close()

    def emit_f3(self):
        c, s, nc = self.c, self, self.nc
        P = c.P
        es2 = ExitStack()
        mst = es2.enter_context(nc.sbuf_tensor(self.name("f3m"), [P, 4 * c.T], F32))
        kst = es2.enter_context(nc.sbuf_tensor(self.name("f3k"), [P, 4 * c.T], I16))
        sg = es2.enter_context(nc.sbuf_tensor(self.name("f3s"), [P, 2 * c.T], F32))
        jj = es2.enter_context(nc.sbuf_tensor(self.name("f3j"), [P, 2 * c.T], F32))
        pp = es2.enter_context(nc.sbuf_tensor(self.name("f3p"), [P, c.T], F32))
        tpf = es2.enter_context(nc.sbuf_tensor(self.name("f3t"), [P, c.T], F32))
        ju16 = es2.enter_context(nc.sbuf_tensor(self.name("f3u"), [P, c.T], mybir.dt.uint16))

        with nc.Block() as blk:
            vvb0 = s.vv.v
            ssb0 = s.ss.v
            VPC = 9
            ievt = []
            oevt = []

            @blk.sync
            def _(sy):
                for cc in range(c.HC + 1):
                    if cc < c.HC:
                        sl = cc % 2
                        if cc >= 2:
                            sy.wait_ge(s.vv.h, vvb0 + (cc - 2) * VPC + 3)
                        i = s.isl[sl]
                        i.inc(sy.dma_start(out=mst[:, sl * c.T:(sl + 1) * c.T], in_=s.mup_d[cc]), 16)
                        i.inc(sy.dma_start(out=mst[:, (2 + sl) * c.T:(3 + sl) * c.T], in_=s.mgate_d[cc]), 16)
                        ievt.append(i.v)
                    if cc >= 1:
                        sl = (cc - 1) % 2
                        sy.wait_ge(s.vv.h, vvb0 + (cc - 1) * VPC + 4)
                        o = s.osl[sl]
                        for n in range(c.NTC):
                            o.inc(sy.dma_start(
                                out=s.kup_d[n, cc - 1],
                                in_=kst[:, sl * c.T + n * c.TN:sl * c.T + (n + 1) * c.TN]), 16)
                            o.inc(sy.dma_start(
                                out=s.kgate_d[n, cc - 1],
                                in_=kst[:, (2 + sl) * c.T + n * c.TN:(2 + sl) * c.T + (n + 1) * c.TN]), 16)
                        oevt.append(o.v)

            @blk.vector
            def _(v):
                for cc in range(c.HC):
                    sl = cc % 2
                    v.wait_ge(s.isl[sl].h, ievt[cc])
                    if cc >= 2:
                        v.wait_ge(s.osl[sl].h, oevt[cc - 2])
                    self.vstep(v, v.tensor_scalar(out=tpf[:], in0=mst[:, sl * c.T:(sl + 1) * c.T],
                                                  scalar1=self.sc(4), scalar2=None, op0=ALU.mult))
                    self.vstep(v, v.tensor_copy(out=kst[:, sl * c.T:(sl + 1) * c.T], in_=tpf[:]))
                    self.vstep(v, v.tensor_scalar(out=tpf[:], in0=mst[:, (2 + sl) * c.T:(3 + sl) * c.T],
                                                  scalar1=self.sc(5), scalar2=None, op0=ALU.mult))
                    self.vstep(v, v.tensor_copy(out=kst[:, (2 + sl) * c.T:(3 + sl) * c.T], in_=tpf[:]))
                    v.wait_ge(s.ss.h, ssb0 + cc + 1)
                    self.vstep(v, v.tensor_scalar(out=tpf[:], in0=sg[:, sl * c.T:(sl + 1) * c.T],
                                                  scalar1=65536.0, scalar2=None, op0=ALU.mult))
                    self.vstep(v, v.tensor_copy(out=ju16[:], in_=tpf[:]))
                    self.vstep(v, v.tensor_copy(out=jj[:, sl * c.T:(sl + 1) * c.T], in_=ju16[:]))
                    self.vstep(v, v.tensor_mul(out=pp[:], in0=kst[:, (2 + sl) * c.T:(3 + sl) * c.T],
                                               in1=jj[:, sl * c.T:(sl + 1) * c.T]))
                    self.vstep(v, v.reduce_max(out=s.slots[:, cc:cc + 1], in_=pp[:], axis=AX.X,
                                               apply_absolute_value=True))

            @blk.scalar
            def _(sc):
                for cc in range(c.HC):
                    sl = cc % 2
                    sc.wait_ge(s.vv.h, vvb0 + cc * VPC + 4)
                    if cc >= 2:
                        sc.wait_ge(s.ss.h, ssb0 + cc - 1)
                    sc.activation(out=sg[:, sl * c.T:(sl + 1) * c.T],
                                  in_=kst[:, (2 + sl) * c.T:(3 + sl) * c.T],
                                  func=AF.Sigmoid, scale=self.sc(6)).then_inc(s.ss.h, 1)
                    s.ss.v += 1

        with nc.Block() as blk:
            @blk.sync
            def _(sy):
                sy.wait_ge(s.osl[0].h, s.osl[0].v)
                sy.wait_ge(s.osl[1].h, s.osl[1].v)
        es2.close()

    def emit_f4(self):
        c, s, nc = self.c, self, self.nc
        P = c.P
        es2 = ExitStack()
        kin = es2.enter_context(nc.sbuf_tensor(self.name("f4a"), [P, 4 * c.T], I16))
        sg = es2.enter_context(nc.sbuf_tensor(self.name("f4s"), [P, 2 * c.T], F32))
        jj = es2.enter_context(nc.sbuf_tensor(self.name("f4j"), [P, c.T], F32))
        pp = es2.enter_context(nc.sbuf_tensor(self.name("f4p"), [P, c.T], F32))
        k2 = es2.enter_context(nc.sbuf_tensor(self.name("f4k"), [P, 2 * c.T], I16))
        tpf = es2.enter_context(nc.sbuf_tensor(self.name("f4t"), [P, c.T], F32))
        ju16 = es2.enter_context(nc.sbuf_tensor(self.name("f4u"), [P, c.T], mybir.dt.uint16))

        with nc.Block() as blk:
            vvb0 = s.vv.v
            ssb0 = s.ss.v
            VPC = 8
            ievt = []
            oevt = []

            @blk.sync
            def _(sy):
                for cc in range(c.HC + 1):
                    if cc < c.HC:
                        sl = cc % 2
                        if cc >= 2:
                            sy.wait_ge(s.vv.h, vvb0 + (cc - 2) * VPC + 7)
                        i = s.isl[sl]
                        for n in range(c.NTC):
                            i.inc(sy.dma_start(
                                out=kin[:, sl * c.T + n * c.TN:sl * c.T + (n + 1) * c.TN],
                                in_=s.kgate_d[n, cc]), 16)
                            i.inc(sy.dma_start(
                                out=kin[:, (2 + sl) * c.T + n * c.TN:(2 + sl) * c.T + (n + 1) * c.TN],
                                in_=s.kup_d[n, cc]), 16)
                        ievt.append(i.v)
                    if cc >= 1:
                        sl = (cc - 1) % 2
                        sy.wait_ge(s.vv.h, vvb0 + (cc - 1) * VPC + 6)
                        o = s.osl[sl]
                        for n in range(c.NTC):
                            o.inc(sy.dma_start(
                                out=s.kg2_d[n, cc - 1],
                                in_=k2[:, sl * c.T + n * c.TN:sl * c.T + (n + 1) * c.TN]), 16)
                        oevt.append(o.v)

            @blk.scalar
            def _(sc):
                for cc in range(c.HC):
                    sl = cc % 2
                    sc.wait_ge(s.isl[sl].h, ievt[cc])
                    if cc >= 2:
                        sc.wait_ge(s.ss.h, ssb0 + cc - 1)
                        sc.wait_ge(s.vv.h, vvb0 + (cc - 2) * VPC + 1)
                    sc.activation(out=sg[:, sl * c.T:(sl + 1) * c.T],
                                  in_=kin[:, sl * c.T:(sl + 1) * c.T],
                                  func=AF.Sigmoid, scale=self.sc(6)).then_inc(s.ss.h, 1)
                    s.ss.v += 1

            @blk.vector
            def _(v):
                for cc in range(c.HC):
                    sl = cc % 2
                    v.wait_ge(s.ss.h, ssb0 + cc + 1)
                    self.vstep(v, v.tensor_scalar(out=tpf[:], in0=sg[:, sl * c.T:(sl + 1) * c.T],
                                                  scalar1=65536.0, scalar2=None, op0=ALU.mult))
                    self.vstep(v, v.tensor_copy(out=ju16[:], in_=tpf[:]))
                    self.vstep(v, v.tensor_copy(out=jj[:], in_=ju16[:]))
                    self.vstep(v, v.tensor_mul(out=pp[:], in0=kin[:, sl * c.T:(sl + 1) * c.T],
                                               in1=jj[:]))
                    if cc >= 2:
                        v.wait_ge(s.osl[sl].h, oevt[cc - 2])
                    self.vstep(v, v.tensor_scalar(out=tpf[:], in0=pp[:],
                                                  scalar1=self.sc(7), scalar2=None, op0=ALU.mult))
                    self.vstep(v, v.tensor_copy(out=k2[:, sl * c.T:(sl + 1) * c.T], in_=tpf[:]))
                    self.vstep(v, v.tensor_mul(out=pp[:], in0=k2[:, sl * c.T:(sl + 1) * c.T],
                                               in1=kin[:, (2 + sl) * c.T:(3 + sl) * c.T]))
                    self.vstep(v, v.reduce_max(out=s.slots[:, cc:cc + 1], in_=pp[:], axis=AX.X,
                                               apply_absolute_value=True))

        with nc.Block() as blk:
            @blk.sync
            def _(sy):
                sy.wait_ge(s.osl[0].h, s.osl[0].v)
                sy.wait_ge(s.osl[1].h, s.osl[1].v)
        es2.close()

    def emit_f5(self):
        c, s, nc = self.c, self, self.nc
        P = c.P
        ntile = c.TN // P
        es2 = ExitStack()
        k2st = es2.enter_context(nc.sbuf_tensor(self.name("f5a"), [P, c.HC * c.TN], I16))
        kust = es2.enter_context(nc.sbuf_tensor(self.name("f5b"), [P, c.HC * c.TN], I16))
        koh = es2.enter_context(nc.sbuf_tensor(self.name("f5h"), [P, c.HC * c.TN], BF16))
        kol = es2.enter_context(nc.sbuf_tensor(self.name("f5l"), [P, c.HC * c.TN], BF16))
        qt = es2.enter_context(nc.sbuf_tensor(self.name("f5q"), [P, c.TN], F32))
        ji = es2.enter_context(nc.sbuf_tensor(self.name("f5j"), [P, c.TN], I16))
        wdh = es2.enter_context(nc.sbuf_tensor(self.name("f5wh"), [P, 2 * c.D], BF16))
        wdl = es2.enter_context(nc.sbuf_tensor(self.name("f5wl"), [P, 2 * c.D], BF16))
        wvh = s.wdq_h.rearrange("(hc p) d -> hc p d", p=P)
        wvl = s.wdq_l.rearrange("(hc p) d -> hc p d", p=P)
        slots = s.isl + s.osl

        for tc in range(c.NTC):
            with nc.Block() as blk:
                tmb0 = s.tm.v
                vvb0 = s.vv.v
                ievt = []
                wevt = []

                @blk.sync
                def _(sy, tc=tc):
                    for cc in range(c.HC):
                        if cc >= 8:
                            sy.wait_ge(s.vv.h, vvb0 + 6 * (cc - 7))
                        i = slots[cc % 8]
                        i.inc(sy.dma_start(out=k2st[:, cc * c.TN:(cc + 1) * c.TN],
                                           in_=s.kg2_d[tc, cc]), 16)
                        i.inc(sy.dma_start(out=kust[:, cc * c.TN:(cc + 1) * c.TN],
                                           in_=s.kup_d[tc, cc]), 16)
                        ievt.append(i.v)
                        if cc >= 2:
                            sy.wait_ge(s.tm.h, tmb0 + (cc - 1))
                        w = s.wsl[cc % 2]
                        w.inc(sy.dma_start(out=wdh[:, (cc % 2) * c.D:(cc % 2 + 1) * c.D],
                                           in_=wvh[cc]), 16)
                        w.inc(sy.dma_start(out=wdl[:, (cc % 2) * c.D:(cc % 2 + 1) * c.D],
                                           in_=wvl[cc]), 16)
                        wevt.append(w.v)

                @blk.vector
                def _(v, tc=tc):
                    for cc in range(c.HC):
                        v.wait_ge(slots[cc % 8].h, ievt[cc])
                        a = cc * c.TN
                        sl = k2st[:, a:a + c.TN]
                        su = kust[:, a:a + c.TN]
                        self.vstep(v, v.tensor_mul(out=qt[:], in0=sl, in1=su))
                        self.vstep(v, v.tensor_scalar(out=qt[:], in0=qt[:], scalar1=self.sc(9),
                                                      scalar2=None, op0=ALU.mult))
                        self.vstep(v, v.tensor_copy(out=sl, in_=qt[:]))
                        self.vstep(v, v.tensor_scalar(out=ji[:], in0=sl, scalar1=1.0 / 256.0,
                                                      scalar2=None, op0=ALU.mult))
                        self.vstep(v, v.tensor_scalar(out=koh[:, a:a + c.TN], in0=ji[:],
                                                      scalar1=256.0, scalar2=None, op0=ALU.mult))
                        v.tensor_sub(out=kol[:, a:a + c.TN], in0=sl,
                                     in1=koh[:, a:a + c.TN]).then_inc(s.vv.h, 1)
                        s.vv.v += 1
                    v.wait_ge(s.tm.h, tmb0 + c.HC)
                    for t in range(ntile):
                        bank = s.psA if t % 2 == 0 else s.psB
                        boff = (t // 2) * c.D
                        gt = tc * ntile + t
                        self.vstep(v, v.scalar_tensor_tensor(
                            out=s.xres[:, gt * c.D:(gt + 1) * c.D],
                            in0=bank[:, boff:boff + c.D], scalar=self.sc(10),
                            in1=s.xres[:, gt * c.D:(gt + 1) * c.D],
                            op0=ALU.mult, op1=ALU.add))

                @blk.tensor
                def _(te, tc=tc):
                    for cc in range(c.HC):
                        te.wait_ge(s.wsl[cc % 2].h, wevt[cc])
                        te.wait_ge(s.vv.h, vvb0 + 6 * (cc + 1))
                        woff = (cc % 2) * c.D
                        a = cc * c.TN
                        for t in range(ntile):
                            bank = s.psA if t % 2 == 0 else s.psB
                            boff = (t // 2) * c.D
                            for nn in range(c.ND2):
                                for ko_, wd_ in ((koh, wdh), (kol, wdh), (koh, wdl)):
                                    ti = te.matmul(
                                        bank[:, boff + nn * c.NW:boff + (nn + 1) * c.NW],
                                        ko_[:, a + t * P:a + (t + 1) * P],
                                        wd_[:, woff + nn * c.NW:woff + (nn + 1) * c.NW],
                                        start=(cc == 0 and ko_ is koh and wd_ is wdh),
                                        stop=(cc == c.HC - 1 and ko_ is koh and wd_ is wdl))
                        s.tm.inc(ti)
        es2.close()

    def emit_attention(self):
        c, s, nc = self.c, self, self.nc
        P, N = c.P, c.NTT
        es2 = ExitStack()
        esln = ExitStack()
        fmA = es2.enter_context(nc.sbuf_tensor(self.name("afm"), [P, c.DC * c.T], F32))
        lnsc = esln.enter_context(nc.sbuf_tensor(self.name("alns"), [P, 2 * c.D], F32))
        xlnh = esln.enter_context(nc.sbuf_tensor(self.name("axln"), [P, max(1, N // 2) * c.D], F32))

        self.emit_ln_stats(lnsc)
        self.emit_lnT(fmA, None, xlnh, rcs_scale=None)
        esln.close()

        es3 = ExitStack()
        wbuf = es3.enter_context(nc.sbuf_tensor(self.name("awb"), [P, max(2 * c.DC * P, c.DC * c.NW)], F32))
        qst = es3.enter_context(nc.sbuf_tensor(self.name("aqs"), [P, 2 * c.T], F32))
        nb = min(4, 2048 // c.TN)
        with nc.Block() as blk:
            tmb0 = s.tm.v
            vvb0 = s.vv.v
            wevt = []
            sevt = []

            @blk.sync
            def _(sy):
                for M in range(2 * c.DC + 1):
                    if M < 2 * c.DC:
                        if M >= 2:
                            sy.wait_ge(s.tm.h, tmb0 + (M - 1) * c.NTC)
                        w = s.wsl[M % 2]
                        w.inc(sy.dma_start(
                            out=wbuf[:, (M % 2) * c.DC * P:(M % 2 + 1) * c.DC * P],
                            in_=s.wqk_d[M]), 16)
                        wevt.append(w.v)
                    if M >= 1:
                        sy.wait_ge(s.vv.h, vvb0 + M * c.NTC)
                        dst = s.qT_d[M - 1] if (M - 1) < c.DC else s.kloc_d[M - 1 - c.DC]
                        o = s.osl[(M - 1) % 2]
                        o.inc(sy.dma_start(
                            out=dst, in_=qst[:, ((M - 1) % 2) * c.T:((M - 1) % 2 + 1) * c.T]), 16)
                        sevt.append(o.v)

            @blk.tensor
            def _(te):
                for M in range(2 * c.DC):
                    te.wait_ge(s.wsl[M % 2].h, wevt[M])
                    for n in range(c.NTC):
                        g = M * c.NTC + n
                        if g >= nb:
                            te.wait_ge(s.vv.h, vvb0 + (g - nb) + 1)
                        bank = self.pslot(g % nb, c.TN)
                        for kc in range(c.DC):
                            ti = te.matmul(
                                bank,
                                wbuf[:, (M % 2) * c.DC * P + kc * P:(M % 2) * c.DC * P + (kc + 1) * P],
                                fmA[:, kc * c.T + n * c.TN:kc * c.T + (n + 1) * c.TN],
                                start=(kc == 0), stop=(kc == c.DC - 1))
                        s.tm.inc(ti)

            @blk.vector
            def _(v):
                for M in range(2 * c.DC):
                    if M >= 2:
                        v.wait_ge(s.osl[M % 2].h, sevt[M - 2])
                    for n in range(c.NTC):
                        g = M * c.NTC + n
                        v.wait_ge(s.tm.h, tmb0 + g + 1)
                        bank = self.pslot(g % nb, c.TN)
                        v.tensor_scalar(
                            out=qst[:, (M % 2) * c.T + n * c.TN:(M % 2) * c.T + (n + 1) * c.TN],
                            in0=bank, scalar1=s.bqks[:, M:M + 1], scalar2=None,
                            op0=ALU.add).then_inc(s.vv.h, 1)
                        s.vv.v += 1

        with nc.Block() as blk:
            tmb0 = s.tm.v
            vvb0 = s.vv.v
            oslb = [s.osl[0].v, s.osl[1].v]
            wevt = []
            sevt = []

            @blk.sync
            def _(sy):
                for h2 in range(c.ND2):
                    if h2 >= 1:
                        sy.wait_ge(s.tm.h, tmb0 + h2 * c.NTT)
                    w = s.wsl[h2 % 2]
                    for kc in range(c.DC):
                        w.inc(sy.dma_start(out=wbuf[:, kc * c.NW:(kc + 1) * c.NW],
                                           in_=s.wv_d[h2, kc]), 16)
                    wevt.append(w.v)
                    for Mt in range(c.NTT):
                        g = h2 * c.NTT + Mt
                        sy.wait_ge(s.vv.h, vvb0 + g + 1)
                        o = s.osl[g % 2]
                        o.inc(sy.dma_start(
                            out=s.vloc_d[Mt, :, h2 * c.NW:(h2 + 1) * c.NW],
                            in_=qst[:, (g % 2) * c.T:(g % 2) * c.T + c.NW]), 16)
                        sevt.append(o.v)

            @blk.tensor
            def _(te):
                for h2 in range(c.ND2):
                    te.wait_ge(s.wsl[h2 % 2].h, wevt[h2])
                    for Mt in range(c.NTT):
                        g = h2 * c.NTT + Mt
                        if g >= 4:
                            te.wait_ge(s.vv.h, vvb0 + (g - 4) + 1)
                        bank = self.pslot(g % 4, c.NW)
                        for kc in range(c.DC):
                            ti = te.matmul(bank,
                                           fmA[:, kc * c.T + Mt * P:kc * c.T + (Mt + 1) * P],
                                           wbuf[:, kc * c.NW:(kc + 1) * c.NW],
                                           start=(kc == 0), stop=(kc == c.DC - 1))
                        s.tm.inc(ti)

            @blk.vector
            def _(v):
                v.wait_ge(s.osl[0].h, oslb[0])
                v.wait_ge(s.osl[1].h, oslb[1])
                for h2 in range(c.ND2):
                    for Mt in range(c.NTT):
                        g = h2 * c.NTT + Mt
                        v.wait_ge(s.tm.h, tmb0 + g + 1)
                        if g >= 2:
                            v.wait_ge(s.osl[g % 2].h, sevt[g - 2])
                        bank = self.pslot(g % 4, c.NW)
                        v.tensor_add(out=qst[:, (g % 2) * c.T:(g % 2) * c.T + c.NW],
                                     in0=bank,
                                     in1=s.bvb[:, h2 * c.NW:(h2 + 1) * c.NW]).then_inc(s.vv.h, 1)
                        s.vv.v += 1

            @blk.gpsimd
            def _(g):
                g.wait_ge(s.osl[0].h, oslb[0])
                g.wait_ge(s.osl[1].h, oslb[1])
                s.cc.inc(g.collective_compute(
                    "AllGather", ALU.bypass, replica_groups=self.pair_groups(),
                    ins=[s.kloc_d.opt()], outs=[s.kpair_d.opt()]), 1)
                g.wait_ge(s.cc.h, s.cc.v)
        es3.close()
        es2.close()

        with nc.Block() as blk:
            @blk.gpsimd
            def _(g):
                g.wait_ge(s.osl[0].h, s.osl[0].v)
                g.wait_ge(s.osl[1].h, s.osl[1].v)
                s.cc.inc(g.collective_compute(
                    "AllGather", ALU.bypass, replica_groups=self.pair_groups(),
                    ins=[s.vloc_d.opt()], outs=[s.vpair_d.opt()]), 1)
                g.wait_ge(s.cc.h, s.cc.v)

        self.emit_flash()

    def pair_groups(self):
        return [[2 * b, 2 * b + 1] for b in range(self.c.n_cores // 2)]

    def emit_flash(self):
        c, s, nc = self.c, self, self.nc
        P, HD = c.P, c.HDIM
        QC = c.TN
        NQC = c.T // QC
        qtile = QC // P
        NKT = c.NKT
        half_kt = NKT // 2
        es2 = ExitStack()
        qcb = es2.enter_context(nc.sbuf_tensor(self.name("fqc"), [P, c.DC * QC], F32))
        kbl = es2.enter_context(nc.sbuf_tensor(self.name("fkb"), [P, 4 * P], F32))
        vbl = es2.enter_context(nc.sbuf_tensor(self.name("fvb"), [P, 4 * 130], F32))
        ptl = es2.enter_context(nc.sbuf_tensor(self.name("fpt"), [P, 2 * 2 * QC], F32))
        atn = es2.enter_context(nc.sbuf_tensor(self.name("fat"), [P, c.DC * QC], F32))
        x1b = es2.enter_context(nc.sbuf_tensor(self.name("fx1"), [P, qtile * c.D], F32))
        lsb = es2.enter_context(nc.sbuf_tensor(self.name("fls"), [P, 4 * QC], F32))
        lbc = es2.enter_context(nc.sbuf_tensor(self.name("flb"), [P, QC], F32))
        wpb = es2.enter_context(nc.sbuf_tensor(self.name("fwp"), [P, 2 * c.D], F32))
        wpv = s.wp_d.rearrange("(kc p) d -> kc p d", p=P)

        with nc.Block() as blk:
            @blk.vector
            def _(v):
                for sl in range(4):
                    for hh in range(2):
                        ins = v.memset(vbl[:, sl * 130 + hh * 65 + 64:sl * 130 + hh * 65 + 65], 1.0)
                s.vv.inc(ins)

        for qc in range(NQC):
            with nc.Block() as blk:
                @blk.sync
                def _(sy, qc=qc):
                    for cc in range(c.DC):
                        s.sy.inc(sy.dma_start(out=qcb[:, cc * QC:(cc + 1) * QC],
                                              in_=s.qT_d[cc, :, qc * QC:(qc + 1) * QC]), 16)
                    sy.wait_ge(s.sy.h, s.sy.v)

                @blk.gpsimd
                def _(g, qc=qc):
                    for t in range(qtile):
                        gt = qc * qtile + t
                        g.tensor_add(out=x1b[:, t * c.D:(t + 1) * c.D],
                                     in0=s.xres[:, gt * c.D:(gt + 1) * c.D],
                                     in1=s.bpb[:]).then_inc(s.gg.h, 1)
                        s.gg.v += 1

            pvb, scb = s.psB, s.psA
            with nc.Block() as blk:
                tmb0 = s.tm.v
                ssb0 = s.ss.v
                vvb0 = s.vv.v
                gdb0 = s.gd.v
                kevt = []

                @blk.sync
                def _(sy, qc=qc):
                    for hg in range(c.NHG):
                        for kt in range(NKT):
                            g = hg * NKT + kt
                            sp = kt // half_kt
                            ktl = kt % half_kt
                            slK = g % 4
                            if g >= 4:
                                sy.wait_ge(s.tm.h, tmb0 + 2 * (g - 4) + 3)
                            i = s.isl[slK]
                            i.inc(sy.dma_start(
                                out=kbl[:, slK * P:(slK + 1) * P],
                                in_=s.kpair_d[sp, hg, :, ktl * P:(ktl + 1) * P]), 16)
                            vsrc = s.vpair_d[sp, ktl, :, hg * P:hg * P + 2 * HD]
                            vsrc = vsrc.rearrange("p (h d) -> p h d", h=2)
                            vdst = vbl[:, slK * 130:(slK + 1) * 130].rearrange(
                                "p (h e) -> p h e", h=2)[:, :, 0:HD]
                            i.inc(sy.dma_start(out=vdst, in_=vsrc), 16)
                            kevt.append(i.v)

                G = c.NHG * NKT

                def emit_scores(te, g):
                    sl, slK, hg = g % 2, g % 4, g // NKT
                    te.wait_ge(s.isl[slK].h, kevt[g])
                    if g >= 2:
                        te.wait_ge(s.ss.h, ssb0 + (g - 1))
                    for hh in range(2):
                        ti = te.matmul(
                            scb[:, sl * 1024 + hh * 512:sl * 1024 + hh * 512 + QC],
                            kbl[hh * HD:(hh + 1) * HD, slK * P:(slK + 1) * P],
                            qcb[hh * HD:(hh + 1) * HD, hg * QC:(hg + 1) * QC],
                            start=True, stop=True, tile_position=(hh * HD, 0))
                    s.tm.inc(ti)

                def emit_pv(te, g):
                    sl, slK, hg, kt = g % 2, g % 4, g // NKT, g % NKT
                    pvo = (hg % 2) * 1024
                    te.wait_ge(s.ss.h, ssb0 + g + 1)
                    if kt == 0 and hg >= 2:
                        te.wait_ge(s.vv.h, vvb0 + 4 * (hg - 1))
                    for hh in range(2):
                        ti = te.matmul(
                            pvb[0:65, pvo + hh * 512:pvo + hh * 512 + QC],
                            vbl[:, slK * 130 + hh * 65:slK * 130 + (hh + 1) * 65],
                            ptl[:, sl * 2 * QC + hh * QC:sl * 2 * QC + (hh + 1) * QC],
                            start=(kt == 0), stop=(kt == NKT - 1))
                    s.tm.inc(ti)

                @blk.tensor
                def _(te, qc=qc):
                    emit_scores(te, 0)
                    for g in range(1, G):
                        emit_scores(te, g)
                        emit_pv(te, g - 1)
                    emit_pv(te, G - 1)

                @blk.scalar
                def _(sc, qc=qc):
                    for g in range(G):
                        sl = g % 2
                        sc.wait_ge(s.tm.h, tmb0 + max(2 * g, 1))
                        sin = scb[:, sl * 1024:(sl + 1) * 1024].rearrange(
                            "p (h x) -> p h x", h=2)[:, :, 0:QC]
                        pout = ptl[:, sl * 2 * QC:(sl + 1) * 2 * QC].rearrange(
                            "p (h x) -> p h x", h=2)
                        sc.activation(out=pout, in_=sin,
                                      func=AF.Exp, scale=0.125).then_inc(s.ss.h, 1)
                        s.ss.v += 1

                @blk.vector
                def _(v, qc=qc):
                    for hg in range(c.NHG):
                        pvo = (hg % 2) * 1024
                        pv_stop = (2 * c.NHG * NKT if hg == c.NHG - 1
                                   else 2 * (hg + 1) * NKT + 1)
                        v.wait_ge(s.tm.h, tmb0 + pv_stop)
                        v.tensor_copy(out=lsb[64:65, 0:QC], in_=pvb[64:65, pvo:pvo + QC])
                        v.tensor_copy(out=lsb[64:65, QC:2 * QC],
                                      in_=pvb[64:65, pvo + 512:pvo + 512 + QC]
                                      ).then_inc(s.vv.h, 1)
                        s.vv.v += 1
                        v.wait_ge(s.gd.h, gdb0 + 80 * hg + 32)
                        v.reciprocal(out=lsb[0:2, QC:2 * QC],
                                     in_=lsb[0:2, 0:QC]).then_inc(s.vv.h, 1)
                        s.vv.v += 1
                        v.wait_ge(s.gd.h, gdb0 + 80 * hg + 80)
                        for hh in range(2):
                            h = 2 * hg + hh
                            cch, rr = h // 2, (h % 2) * HD
                            v.tensor_mul(out=atn[rr:rr + HD, cch * QC:(cch + 1) * QC],
                                         in0=pvb[0:HD, pvo + hh * 512:pvo + hh * 512 + QC],
                                         in1=lbc[hh * HD:(hh + 1) * HD, 0:QC]).then_inc(s.vv.h, 1)
                            s.vv.v += 1

                @blk.gpsimd
                def _(g_, qc=qc):
                    for hg in range(c.NHG):
                        if hg >= 1:
                            g_.wait_ge(s.gd.h, gdb0 + 80 * hg)
                        g_.wait_ge(s.vv.h, vvb0 + 4 * hg + 1)
                        s.gd.inc(g_.dma_start(out=s.lraw_d[0:2 * QC],
                                              in_=lsb[64:65, 0:2 * QC]), 16)
                        g_.wait_ge(s.gd.h, gdb0 + 80 * hg + 16)
                        lr = s.lraw_d[0:2 * QC].rearrange("(a b) -> a b", a=2)
                        s.gd.inc(g_.dma_start(out=lsb[0:2, 0:QC], in_=lr), 16)
                        g_.wait_ge(s.vv.h, vvb0 + 4 * hg + 2)
                        s.gd.inc(g_.dma_start(out=s.lrec_d[0:2 * QC],
                                              in_=lsb[0:2, QC:2 * QC]), 16)
                        g_.wait_ge(s.gd.h, gdb0 + 80 * hg + 48)
                        for hh in range(2):
                            s.gd.inc(g_.dma_start(
                                out=lbc[hh * HD:(hh + 1) * HD, 0:QC],
                                in_=s.lrec_d[hh * QC:(hh + 1) * QC].unsqueeze(0).to_broadcast([HD, QC])), 16)

            with nc.Block() as blk:
                wevt = []

                @blk.sync
                def _(sy, qc=qc):
                    tmb0 = s.tm.v
                    for kc in range(c.DC):
                        if kc >= 2:
                            sy.wait_ge(s.tm.h, tmb0 + (kc - 1))
                        w = s.wsl[kc % 2]
                        w.inc(sy.dma_start(out=wpb[:, (kc % 2) * c.D:(kc % 2 + 1) * c.D],
                                           in_=wpv[kc]), 16)
                        wevt.append(w.v)

                @blk.tensor
                def _(te, qc=qc):
                    for kc in range(c.DC):
                        te.wait_ge(s.wsl[kc % 2].h, wevt[kc])
                        for t in range(qtile):
                            bank = s.psA if t % 2 == 0 else s.psB
                            boff = (t // 2) * c.D
                            for nn in range(c.ND2):
                                ti = te.matmul(
                                    bank[:, boff + nn * c.NW:boff + (nn + 1) * c.NW],
                                    atn[:, kc * QC + t * P:kc * QC + (t + 1) * P],
                                    wpb[:, (kc % 2) * c.D + nn * c.NW:(kc % 2) * c.D + (nn + 1) * c.NW],
                                    start=(kc == 0), stop=(kc == c.DC - 1))
                        s.tm.inc(ti)

                @blk.vector
                def _(v, qc=qc):
                    v.wait_ge(s.tm.h, s.tm.v)
                    v.wait_ge(s.gg.h, s.gg.v)
                    for t in range(qtile):
                        bank = s.psA if t % 2 == 0 else s.psB
                        boff = (t // 2) * c.D
                        gt = qc * qtile + t
                        self.vstep(v, v.tensor_add(out=s.xres[:, gt * c.D:(gt + 1) * c.D],
                                                   in0=bank[:, boff:boff + c.D],
                                                   in1=x1b[:, t * c.D:(t + 1) * c.D]))
        es2.close()



def prep_inputs(cfg, L0_attn_x, L0_attn_w_qkv, L0_attn_b_qkv, L0_attn_w_proj,
                L0_attn_b_proj, L0_ffn_w_gate, L0_ffn_w_up, L0_ffn_w_down):
    c = cfg
    P = c.P
    x = np.ascontiguousarray(np.asarray(L0_attn_x, dtype=np.float32))
    D = c.D
    wqkv = np.asarray(L0_attn_w_qkv, dtype=np.float32)

    def tile_lhsT(wT, n_m):
        arr = wT.reshape(c.DC, P, n_m, P)
        return np.ascontiguousarray(arr.transpose(2, 1, 0, 3))

    wqk = tile_lhsT(wqkv[:, 0:2 * D], 2 * c.DC)
    wv = np.ascontiguousarray(
        wqkv[:, 2 * D:3 * D].reshape(c.DC, P, c.ND2, c.NW).transpose(2, 0, 1, 3))
    bqkv = np.asarray(L0_attn_b_qkv, dtype=np.float32)
    bqk = np.ascontiguousarray(bqkv[0:2 * D].reshape(2 * c.DC, P))
    bv = np.ascontiguousarray(bqkv[2 * D:3 * D])
    wp = np.ascontiguousarray(np.asarray(L0_attn_w_proj, dtype=np.float32))
    bp = np.ascontiguousarray(np.asarray(L0_attn_b_proj, dtype=np.float32))
    wu = tile_lhsT(np.ascontiguousarray(np.asarray(L0_ffn_w_up, dtype=np.float32).T), c.HC)
    wg = tile_lhsT(np.ascontiguousarray(np.asarray(L0_ffn_w_gate, dtype=np.float32).T), c.HC)
    wd = np.ascontiguousarray(np.asarray(L0_ffn_w_down, dtype=np.float32).T)

    in_maps = []
    for core in range(c.n_cores):
        b, h = core // 2, core % 2
        xs = np.ascontiguousarray(x[b, h * c.T:(h + 1) * c.T, :])
        in_maps.append(dict(x0=xs, wqk=wqk, wv=wv, bqk=bqk, bv=bv, wp=wp, bp=bp,
                            wu=wu, wg=wg, wd=wd))
    return in_maps


def build_program(cfg, dbg=None):
    b = Builder(cfg, dbg=dbg)
    b.emit()
    return b.finish()


def kernel(L0_attn_x, L0_attn_w_qkv, L0_attn_b_qkv, L0_attn_w_proj, L0_attn_b_proj,
           L0_ffn_w_gate, L0_ffn_w_up, L0_ffn_w_down):
    cfg = FULL
    in_maps = prep_inputs(cfg, L0_attn_x, L0_attn_w_qkv, L0_attn_b_qkv,
                          L0_attn_w_proj, L0_attn_b_proj, L0_ffn_w_gate,
                          L0_ffn_w_up, L0_ffn_w_down)
    nc = build_program(cfg)
    res = run_bass_kernel_spmd(nc, in_maps, core_ids=list(range(cfg.n_cores)))
    x = np.asarray(L0_attn_x)
    Bb, S, D = x.shape
    out = np.empty((Bb, S, D), dtype=np.float32)
    for core in range(cfg.n_cores):
        bb, h = core // 2, core % 2
        out[bb, h * cfg.T:(h + 1) * cfg.T, :] = res.results[core]["y"]
    return out
